# version 02
# baseline (speedup 1.0000x reference)
import numpy as np
import concourse.bass as bass
import concourse.mybir as mybir
from concourse.bass_utils import run_bass_kernel_spmd

F32 = mybir.dt.float32
BF16 = mybir.dt.bfloat16
AF = mybir.ActivationFunctionType
ALU = mybir.AluOpType
AX = mybir.AxisListType

NCORE = 8
BPC = 2
T = 2048
DM = 1024
DI = 2048
DS = 16
TOK = BPC * T
EPS = 1e-5


class Chain:

    def __init__(self, nc, name):
        self.h = nc.alloc_semaphore(name)
        self.n = 0

    def push(self, inst, k=1):
        inst.then_inc(self.h, k)
        self.n += k
        return self.n

    def wait(self, eng, tok=None):
        eng.wait_ge(self.h, self.n if tok is None else tok)




class FC:

    def __init__(self, nc=None, prefix="", dmap=None, pmap=None, stack=None,
                 lstm_gather=False, out_swdge=False):
        self.nc = nc
        self.prefix = prefix
        self.dmap = dmap or {}
        self.pmap = pmap or {}
        self.stack = stack
        self.lstm_gather = lstm_gather
        self.out_swdge = out_swdge

    def dram(self, nc, name, shape, dtype, kind="ExternalInput"):
        if name in self.dmap:
            return self.dmap[name]
        return nc.dram_tensor(self.prefix + name, shape, dtype, kind=kind).ap()

    def sb(self, nc, name, shape, dtype):
        if self.stack is not None:
            return self.stack.enter_context(nc.sbuf_tensor(self.prefix + name, shape, dtype))
        return fc.sb(nc, name, shape, dtype)

    def ps(self, nc, name, shape, dtype):
        if name in self.pmap:
            return self.pmap[name]
        return fc.ps(nc, name, shape, dtype)

def _ln_emit(nc, bk, ch, xtile, xn_tm, lnscr, lnm, lns, lnr, eps_col=None):

    def vpart1(v):
        ch.wait(v)
        t1 = ch.push(v.tensor_reduce(lnm[:, :], xtile[:, :], AX.X, ALU.add))
        ch.wait(v, t1)
        t2 = ch.push(v.tensor_scalar(lnm[:, :], lnm[:, :], 1.0 / DM, None, ALU.mult))
        ch.wait(v, t2)
        ch.push(v.tensor_scalar(xtile[:, :], xtile[:, :], lnm[:, :], None, ALU.subtract))
    bk.vector(vpart1)

    def spart(s):
        ch.wait(s)
        t1 = ch.push(s.activation(lnscr[:, :], xtile[:, :], AF.Square, accum_out=lns[:, :]))
        ch.wait(s, t1)
        ch.push(s.activation(lnr[:, :], lns[:, :], AF.Sqrt, scale=1.0 / DM, bias=eps_col[:, :]))
    bk.scalar(spart)

    def vpart2(v):
        ch.wait(v)
        t1 = ch.push(v.reciprocal(lnr[:, :], lnr[:, :]))
        ch.wait(v, t1)
        ch.push(v.tensor_scalar(xn_tm[:, :], xtile[:, :], lnr[:, :], None, ALU.mult))
    bk.vector(vpart2)


def build_moe(dbg=False, fc=None):
    fc = fc or FC()
    nc = fc.nc if fc.nc is not None else bass.Bass("TRN2", target_bir_lowering=False, debug=False)
    x_in = fc.dram(nc, "x_in", [TOK, DM], F32)
    wr_d = fc.dram(nc, "wr", [DM, 8], F32)
    wg_d = fc.dram(nc, "wg", [8, DM, DI], F32)
    bg_d = fc.dram(nc, "bg", [128, 16, 8], F32)
    wu_d = fc.dram(nc, "wu", [8, DM, DI], F32)
    bu_d = fc.dram(nc, "bu", [128, 16, 8], F32)
    wo_d = fc.dram(nc, "wo", [8, DI, DM], F32)
    bo_d = fc.dram(nc, "bo", [8, DM], F32)
    id_d = fc.dram(nc, "ident", [128, 128], F32)
    x_out = fc.dram(nc, "x_out", [TOK, DM], F32, kind="ExternalOutput")
    if dbg:
        d_xq = fc.dram(nc, "d_xq", [128, 8, 1024], F32, kind="ExternalOutput")
        d_w = fc.dram(nc, "d_w", [128, 8, 8], F32, kind="ExternalOutput")
        d_acc0 = fc.dram(nc, "d_acc0", [128, 8, DM], F32, kind="ExternalOutput")
        d_acc = fc.dram(nc, "d_acc", [128, 8, DM], F32, kind="ExternalOutput")
        d_hw = fc.dram(nc, "d_hw", [128, 16, 512], F32, kind="ExternalOutput")
        d_acc1 = fc.dram(nc, "d_acc1", [128, 8, DM], F32, kind="ExternalOutput")

    QT = 1024
    NQ = TOK // QT
    NTT = QT // 128

    wr_sb = fc.sb(nc, "wr_sb", [128, 8, 8], BF16)
    bg_sb = fc.sb(nc, "bg_sb", [128, 16, 8], F32)
    bu_sb = fc.sb(nc, "bu_sb", [128, 16, 8], F32)
    bo_sb = fc.sb(nc, "bo_sb", [8, DM], BF16)
    id_bf = fc.sb(nc, "id_bf", [128, 128], BF16)
    xq_fm = fc.sb(nc, "xq_fm", [128, 8, QT], BF16)
    w_tm = fc.sb(nc, "w_tm", [128, NTT, 8], F32)
    w_fmT = fc.sb(nc, "w_fmT", [8, QT], BF16)
    acc = fc.sb(nc, "acc", [128, NTT, DM], F32)
    wg_sb = fc.sb(nc, "wg_sb", [128, 8, DI], BF16)
    wu_sb = fc.sb(nc, "wu_sb", [128, 8, DI], BF16)
    wo_sb = fc.sb(nc, "wo_sb", [128, 16, DM], BF16)
    hw_sb = fc.sb(nc, "hw_sb", [128, 16, 512], BF16)
    gs_sb = fc.sb(nc, "gs_sb", [128, 512], BF16)
    xtile = fc.sb(nc, "xtile", [128, DM], F32)
    xn_tm = fc.sb(nc, "xn_tm", [128, DM], BF16)
    lnscr = fc.sb(nc, "lnscr", [128, DM], BF16)
    lnm = fc.sb(nc, "lnm", [128, 1], F32)
    lns = fc.sb(nc, "lns", [128, 1], F32)
    lnr = fc.sb(nc, "lnr", [128, 1], F32)
    r_pr = fc.sb(nc, "r_pr", [128, 8], F32)
    r_m1 = fc.sb(nc, "r_m1", [128, 1], F32)
    r_m2 = fc.sb(nc, "r_m2", [128, 1], F32)
    r_k1 = fc.sb(nc, "r_k1", [128, 8], F32)
    r_k2 = fc.sb(nc, "r_k2", [128, 8], F32)
    r_t = fc.sb(nc, "r_t", [128, 8], F32)
    r_den = fc.sb(nc, "r_den", [128, 1], F32)
    w_bf = fc.sb(nc, "w_bf", [128, 8], BF16)
    eps_col = fc.sb(nc, "eps_col", [128, 1], F32)
    o_tm = fc.sb(nc, "o_tm", [128, DM], F32)

    ps_tr = [fc.ps(nc, f"ps_tr{i}", [128, 1024], BF16) for i in range(2)]
    ps_r = fc.ps(nc, "ps_r", [128, 8], F32)
    ps_g = fc.ps(nc, "ps_g", [128, 512], F32)
    ps_u = fc.ps(nc, "ps_u", [128, 512], F32)
    ps_o = [fc.ps(nc, "ps_o0", [128, DM], F32)]
    ps_o.append(ps_o[0])

    ch = Chain(nc, "ch")
    pe_s = Chain(nc, "pe_s")
    ev_s = Chain(nc, "ev_s")
    wdma = Chain(nc, "wdma")
    outs = Chain(nc, "outs")

    with nc.Block() as bk:
        def gsec(g):
            ch.push(g.dma_start(wr_sb[:, :, :], wr_d.rearrange("(k p) e -> p k e", p=128)), 16)
            ch.push(g.dma_start(bg_sb[:, :, :], bg_d), 16)
            ch.push(g.dma_start(bu_sb[:, :, :], bu_d), 16)
            ch.push(g.dma_start(bo_sb[:, :], bo_d), 16)
            ch.push(g.dma_start(id_bf[:, :], id_d), 16)
            g.memset(eps_col[:, :], EPS)
        bk.gpsimd(gsec)

    for q in range(NQ):
        with nc.Block() as bk:
            for it in range(NTT):
                tok0 = q * QT + it * 128

                def dmain(sy, tok0=tok0):
                    ch.wait(sy)
                    ch.push(sy.dma_start(xtile[:, :], x_in[tok0:tok0 + 128, :]), 16)
                bk.sync(dmain)

                _ln_emit(nc, bk, ch, xtile, xn_tm, lnscr, lnm, lns, lnr, eps_col)

                pe0, ev0 = pe_s.n, ev_s.n

                def trans(t, pe0=pe0, ev0=ev0):
                    ch.wait(t)
                    for j in range(8):
                        if j >= 2:
                            ev_s.wait(t, ev0 + (j - 2) + 1)
                        pe_s.push(t.transpose(ps_tr[j % 2][:, 0:128],
                                              xn_tm[:, j * 128:(j + 1) * 128], id_bf[:, :]))
                bk.tensor(trans)

                def evac(v, it=it, pe0=pe0, ev0=ev0):
                    for j in range(8):
                        pe_s.wait(v, pe0 + 1 + j)
                        ev_s.push(v.tensor_copy(xq_fm[:, j, it * 128:(it + 1) * 128],
                                                ps_tr[j % 2][:, 0:128]))
                bk.vector(evac)

                def joinv(v):
                    ev_s.wait(v)
                    ch.push(v.tensor_copy(lnm[:, :], lnm[:, :]))
                bk.vector(joinv)

                def rmm(t, it=it):
                    ch.wait(t)
                    for k in range(8):
                        inst = t.matmul(ps_r[:, :], xq_fm[:, k, it * 128:(it + 1) * 128],
                                        wr_sb[:, k, :], start=(k == 0), stop=(k == 7))
                    ch.push(inst)
                bk.tensor(rmm)

                def rsoft1(v):
                    ch.wait(v)
                    t1 = ch.push(v.tensor_copy(r_pr[:, :], ps_r[:, :]))
                    ch.wait(v, t1)
                    t2 = ch.push(v.tensor_reduce(r_m1[:, :], r_pr[:, :], AX.X, ALU.max))
                    ch.wait(v, t2)
                    ch.push(v.tensor_scalar(r_pr[:, :], r_pr[:, :], r_m1[:, :], None, ALU.subtract))
                bk.vector(rsoft1)

                def rexp(s):
                    ch.wait(s)
                    ch.push(s.activation(r_pr[:, :], r_pr[:, :], AF.Exp))
                bk.scalar(rexp)

                def rsoft2(v, it=it):
                    ch.wait(v)
                    seq = [
                        lambda: v.tensor_reduce(r_den[:, :], r_pr[:, :], AX.X, ALU.add),
                        lambda: v.reciprocal(r_den[:, :], r_den[:, :]),
                        lambda: v.tensor_scalar(r_pr[:, :], r_pr[:, :], r_den[:, :], None, ALU.mult),
                        lambda: v.tensor_reduce(r_m1[:, :], r_pr[:, :], AX.X, ALU.max),
                        lambda: v.tensor_scalar(r_k1[:, :], r_pr[:, :], r_m1[:, :], None, ALU.is_ge),
                        lambda: v.tensor_tensor(r_t[:, :], r_k1[:, :], r_pr[:, :], ALU.mult),
                        lambda: v.tensor_tensor(r_t[:, :], r_pr[:, :], r_t[:, :], ALU.subtract),
                        lambda: v.tensor_reduce(r_m2[:, :], r_t[:, :], AX.X, ALU.max),
                        lambda: v.tensor_scalar(r_k2[:, :], r_t[:, :], r_m2[:, :], None, ALU.is_ge),
                        lambda: v.tensor_tensor(r_k1[:, :], r_k1[:, :], r_k2[:, :], ALU.add),
                        lambda: v.tensor_scalar(r_den[:, :], r_m1[:, :], r_m2[:, :], 1e-8, ALU.add, ALU.add),
                        lambda: v.reciprocal(r_den[:, :], r_den[:, :]),
                        lambda: v.tensor_tensor(r_t[:, :], r_pr[:, :], r_k1[:, :], ALU.mult),
                        lambda: v.tensor_scalar(w_tm[:, it, :], r_t[:, :], r_den[:, :], None, ALU.mult),
                        lambda: v.tensor_copy(w_bf[:, :], w_tm[:, it, :]),
                    ]
                    for f in seq:
                        tk = ch.push(f())
                        ch.wait(v, tk)
                bk.vector(rsoft2)

                def wtr(t):
                    ch.wait(t)
                    ch.push(t.transpose(ps_tr[0][0:8, 0:128], w_bf[:, :], id_bf[:, :]))
                bk.tensor(wtr)

                def wev(v, it=it):
                    ch.wait(v)
                    ch.push(v.tensor_copy(w_fmT[0:8, it * 128:(it + 1) * 128],
                                          ps_tr[0][0:8, 0:128]))
                bk.vector(wev)

            for it in range(NTT):
                def bterm(t, it=it):
                    ch.wait(t)
                    for hh in range(2):
                        inst = t.matmul(ps_o[0][:, hh * 512:(hh + 1) * 512],
                                        w_fmT[:, it * 128:(it + 1) * 128],
                                        bo_sb[:, hh * 512:(hh + 1) * 512],
                                        start=True, stop=True)
                    ch.push(inst, 1)
                bk.tensor(bterm)

                def bev(v, it=it):
                    ch.wait(v)
                    ch.push(v.tensor_copy(acc[:, it, :], ps_o[0][:, :]))
                bk.vector(bev)

        if dbg and q == 0:
            with nc.Block() as bk:
                def dump1(g):
                    ch.wait(g)
                    outs.push(g.dma_start(d_xq[:, :, :], xq_fm[:, :, :]), 16)
                    outs.push(g.dma_start(d_w[:, :, :], w_tm[:, :, :]), 16)
                    outs.push(g.dma_start(d_acc0[:, :, :], acc[:, :, :]), 16)
                bk.gpsimd(dump1)

        for e in range(8):
            with nc.Block() as bk:
                def wload(g, e=e):
                    wdma.push(g.dma_start(wg_sb[:, :, :],
                                          wg_d[e].rearrange("(k p) m -> p k m", p=128)), 16)
                    wdma.push(g.dma_start(wu_sb[:, :, :],
                                          wu_d[e].rearrange("(k p) m -> p k m", p=128)), 16)
                    wdma.push(g.dma_start(wo_sb[:, :, :],
                                          wo_d[e].rearrange("(k p) m -> p k m", p=128)), 16)
                bk.gpsimd(wload)

                for h in range(2):
                    tok0 = h * 512
                    for m in range(16):
                        def gumm(t, e=e, m=m, tok0=tok0):
                            wdma.wait(t)
                            ch.wait(t)
                            for k in range(8):
                                ig = t.matmul(ps_g[:, :], wg_sb[:, k, m * 128:(m + 1) * 128],
                                              xq_fm[:, k, tok0:tok0 + 512],
                                              start=(k == 0), stop=(k == 7))
                            ch.push(ig)
                            for k in range(8):
                                iu = t.matmul(ps_u[:, :], wu_sb[:, k, m * 128:(m + 1) * 128],
                                              xq_fm[:, k, tok0:tok0 + 512],
                                              start=(k == 0), stop=(k == 7))
                            ch.push(iu)
                        bk.tensor(gumm)

                        def gsilu(s, e=e, m=m):
                            ch.wait(s)
                            ch.push(s.activation(gs_sb[:, :], ps_g[:, :], AF.Silu,
                                                 bias=bg_sb[:, m, e:e + 1]))
                        bk.scalar(gsilu)

                        def hmul(v, e=e, m=m):
                            ch.wait(v)
                            ch.push(v.scalar_tensor_tensor(hw_sb[:, m, :], ps_u[:, :],
                                                           bu_sb[:, m, e:e + 1],
                                                           gs_sb[:, :], ALU.add, ALU.mult))
                        bk.vector(hmul)

                    for tt in range(4):
                        it = h * 4 + tt
                        pso = ps_o[tt % 2]

                        def omm(t, tt=tt, pso=pso):
                            ch.wait(t)
                            for k in range(16):
                                for half in range(2):
                                    inst = t.matmul(pso[:, half * 512:(half + 1) * 512],
                                                    hw_sb[:, k, tt * 128:(tt + 1) * 128],
                                                    wo_sb[:, k, half * 512:(half + 1) * 512],
                                                    start=(k == 0), stop=(k == 15))
                            ch.push(inst)
                        bk.tensor(omm)

                        def oacc(v, e=e, it=it, pso=pso):
                            ch.wait(v)
                            ch.push(v.scalar_tensor_tensor(acc[:, it, :], pso[:, :],
                                                           w_tm[:, it, e:e + 1], acc[:, it, :],
                                                           ALU.mult, ALU.add))
                        bk.vector(oacc)
                if dbg and q == 0 and e == 0:
                    def dump3(g):
                        ch.wait(g)
                        outs.push(g.dma_start(d_acc1[:, :, :], acc[:, :, :]), 16)
                    bk.gpsimd(dump3)

        if dbg and q == 0:
            with nc.Block() as bk:
                def dump2(g):
                    ch.wait(g)
                    outs.push(g.dma_start(d_acc[:, :, :], acc[:, :, :]), 16)
                    outs.push(g.dma_start(d_hw[:, :, :], hw_sb[:, :, :]), 16)
                bk.gpsimd(dump2)

        with nc.Block() as bk:
            for it in range(NTT):
                tok0 = q * QT + it * 128

                def dmain(sy, tok0=tok0):
                    ch.wait(sy)
                    ch.push(sy.dma_start(xtile[:, :], x_in[tok0:tok0 + 128, :]), 16)
                bk.sync(dmain)

                def radd(v, it=it):
                    ch.wait(v)
                    outs.wait(v)
                    ch.push(v.tensor_tensor(o_tm[:, :], acc[:, it, :], xtile[:, :], ALU.add))
                bk.vector(radd)

                def dmaout(sy, tok0=tok0):
                    ch.wait(sy)
                    outs.push(sy.dma_start(x_out[tok0:tok0 + 128, :], o_tm[:, :]), 16)
                bk.sync(dmaout)

            def fin(sy):
                outs.wait(sy)
            bk.sync(fin)

    return nc


_CACHE = {}


def _get(name, builder):
    if name not in _CACHE:
        nc = builder()
        nc.finalize()
        _CACHE[name] = nc
    return _CACHE[name]


def _host_moe(inputs):
    ident = np.eye(128, dtype=np.float32)
    return {
            "wr": inputs["L0_ffn_W_router"],
            "wg": inputs["L0_ffn_W_gate"],
            "bg": np.ascontiguousarray(np.asarray(inputs["L0_ffn_b_gate"]).reshape(8, 16, 128).transpose(2, 1, 0)),
            "wu": inputs["L0_ffn_W_up"],
            "bu": np.ascontiguousarray(np.asarray(inputs["L0_ffn_b_up"]).reshape(8, 16, 128).transpose(2, 1, 0)),
            "wo": inputs["L0_ffn_W_out"],
            "bo": inputs["L0_ffn_b_out"],
            "ident": ident,
    }


def run_moe(x, inputs):
    nc = _get("moe", build_moe)
    base = _host_moe(inputs)
    in_maps = []
    for c in range(NCORE):
        m = dict(base)
        m["x_in"] = np.ascontiguousarray(x[2 * c:2 * c + 2].reshape(TOK, DM))
        in_maps.append(m)
    res = run_bass_kernel_spmd(nc, in_maps, core_ids=list(range(NCORE)))
    out = np.concatenate([r["x_out"].reshape(BPC, T, DM) for r in res.results], axis=0)
    return out


def build_full():
    import contextlib
    nc = bass.Bass("TRN2", target_bir_lowering=False, debug=False)
    ps_a = nc.alloc_psum_tensor("f_psa", [128, 1024], F32)
    ps_b = nc.alloc_psum_tensor("f_psb", [128, 1024], F32)
    ps_c = nc.alloc_psum_tensor("f_psc", [128, 512], F32)
    ps_d = nc.alloc_psum_tensor("f_psd", [128, 512], F32)
    trA = nc.alloc_psum_tensor("f_trA", [128, 1024], BF16)
    trB = nc.alloc_psum_tensor("f_trB", [128, 1024], BF16)

    x0 = nc.dram_tensor("x_in", [BPC, T, DM], F32, kind="ExternalInput").ap()
    x1_sc = nc.dram_tensor("x1_sc", [BPC, T, DM], F32, kind="Internal").ap()
    x2_sc = nc.dram_tensor("x2_sc", [BPC, T, DM], F32, kind="Internal").ap()
    ag_in = nc.dram_tensor("ag_in", [BPC * T, DM], BF16, kind="Internal").ap()
    ag_out = nc.dram_tensor("ag_out", [NCORE * BPC * T, DM], BF16, kind="Internal").ap()
    x_out = nc.dram_tensor("x_out", [16, 256, DM], F32, kind="ExternalOutput").ap()

    with contextlib.ExitStack() as st:
        fc = FC(nc=nc, prefix="m_", stack=st,
                dmap={"x_in": x0, "x_out": x1_sc},
                pmap={"mps_mm": ps_c, "mps_bc": ps_d,
                      "mps_dtr": ps_a[:, 0:512], "mps_dxr": ps_a[:, 512:1024],
                      "mps_y": ps_b[:, 0:512], "mps_tr0": trA, "mps_tr1": trB})
        build_mamba(fc=fc)

    with contextlib.ExitStack() as st:
        fc = FC(nc=nc, prefix="o_", stack=st,
                dmap={"x_in": x1_sc.rearrange("b t d -> (b t) d"),
                      "x_out": x2_sc.rearrange("b t d -> (b t) d")},
                pmap={"ps_tr0": trA, "ps_tr1": trB, "ps_r": ps_d[:, 0:8],
                      "ps_g": ps_c, "ps_u": ps_d, "ps_o0": ps_a})
        build_moe(fc=fc)

    with contextlib.ExitStack() as st:
        fc = FC(nc=nc, prefix="t_", stack=st, out_swdge=True,
                dmap={"x_in": x2_sc,
                      "x_out": ag_in.rearrange("(b t) d -> b t d", b=BPC)},
                pmap={"t_psbig": ps_a, "t_pssm": ps_c, "t_pstr0": trA, "t_pstr1": trB})
        build_ttt(fc=fc)

    ccs = nc.alloc_semaphore("f_ccs")
    with nc.Block() as bk:
        def ccsec(g):
            g.collective_compute("AllGather", mybir.AluOpType.bypass,
                                 replica_groups=[list(range(NCORE))],
                                 ins=[ag_in.opt()], outs=[ag_out.opt()]).then_inc(ccs, 1)
            g.wait_ge(ccs, 1)
        bk.gpsimd(ccsec)

    with contextlib.ExitStack() as st:
        fc = FC(nc=nc, prefix="l_", stack=st, lstm_gather=True,
                dmap={"agout": ag_out, "x_out": x_out},
                pmap={"l_psf": ps_a[:, 0:512], "l_psi": ps_a[:, 512:1024],
                      "l_psg": ps_b[:, 0:512], "l_pstr0": trA, "l_pstr1": trB})
        build_lstm(fc=fc)

    return nc


def kernel(**inputs):
    nc = _get("full", build_full)
    x0 = np.asarray(inputs["L0_attn_x"], dtype=np.float32)
    mb = {("m_" + k): v for k, v in _host_mamba(inputs).items()}
    ob = {("o_" + k): v for k, v in _host_moe(inputs).items()}
    tb = {("t_" + k): v for k, v in _host_ttt(inputs).items()}
    lb = {("l_" + k): v for k, v in _host_lstm(inputs).items()}
    in_maps = []
    for cc in range(NCORE):
        sel = np.zeros((128, 32), np.uint32)
        for bb in range(16):
            for tt in range(2):
                sel[:, bb * 2 + tt] = ((bb // 2) * 4096 + (bb % 2) * 2048
                                       + 256 * cc + tt * 128 + np.arange(128))
        m = {}
        m.update(mb)
        m.update(ob)
        m.update(tb)
        m.update(lb)
        m["x_in"] = np.ascontiguousarray(x0[2 * cc:2 * cc + 2])
        m["l_tsel"] = sel
        in_maps.append(m)
    res = run_bass_kernel_spmd(nc, in_maps, core_ids=list(range(NCORE)))
    return np.concatenate([r["x_out"] for r in res.results], axis=1)


def kernel_unfused(**inputs):
    x = np.asarray(inputs["L0_attn_x"], dtype=np.float32)
    x1 = run_mamba(x, inputs)
    x2 = run_moe(x1, inputs)
    x3 = run_ttt(x2, inputs)
    x4 = run_lstm(x3, inputs)
    return x4




def build_mamba(dbg=False, fc=None):
    fc = fc or FC()
    nc = fc.nc if fc.nc is not None else bass.Bass("TRN2", target_bir_lowering=False, debug=False)
    TC = 512
    NCH = T // TC
    NG = DI // 8

    x_in = fc.dram(nc, "x_in", [BPC, T, DM], F32)
    inw_d = fc.dram(nc, "in_w", [DM, 2 * DI], F32)
    inb_d = fc.dram(nc, "in_b", [128, 32], F32)
    cw_d = fc.dram(nc, "conv_w", [128, 16, 4], F32)
    cb_d = fc.dram(nc, "conv_b", [128, 16], F32)
    ap_d = fc.dram(nc, "A_perm", [128, 256], F32)
    dp_d = fc.dram(nc, "Dp_col", [128, 16], F32)
    dtw_d = fc.dram(nc, "dt_w", [DI, DI], F32)
    dtb_d = fc.dram(nc, "dt_b", [128, 16], F32)
    bcw_d = fc.dram(nc, "bc_w", [DI, 32], F32)
    bcb_d = fc.dram(nc, "bc_b", [32, 1], F32)
    ow_d = fc.dram(nc, "out_w", [DI, DM], F32)
    ob_d = fc.dram(nc, "out_b", [128, 8], F32)
    sel_d = fc.dram(nc, "SEL", [128, 16, 128], F32)
    red_d = fc.dram(nc, "RED", [128, 16, 128], F32)
    sbc_d = fc.dram(nc, "selbc", [32, 2, 128], F32)
    id_d = fc.dram(nc, "ident", [128, 128], F32)
    x_out = fc.dram(nc, "x_out", [BPC, T, DM], F32, kind="ExternalOutput")

    xz_sc = fc.dram(nc, "xz_sc", [BPC, 16, 128, T], BF16, kind="Internal")
    gt_sc = fc.dram(nc, "gt_sc", [BPC, 16, 128, T], BF16, kind="Internal")

    inb_sb = fc.sb(nc, "inb_sb", [128, 32], F32)
    cw_sb = fc.sb(nc, "cw_sb", [128, 16, 4], F32)
    cb_sb = fc.sb(nc, "cb_sb", [128, 16], F32)
    ap_sb = fc.sb(nc, "ap_sb", [128, 256], F32)
    dp_sb = fc.sb(nc, "dp_sb", [128, 16], F32)
    dtb_sb = fc.sb(nc, "dtb_sb", [128, 16], F32)
    bcb_sb = fc.sb(nc, "bcb_sb", [32, 1], F32)
    ob_sb = fc.sb(nc, "ob_sb", [128, 8], F32)
    sel_sb = fc.sb(nc, "sel_sb", [128, 16, 128], BF16)
    red_sb = fc.sb(nc, "red_sb", [128, 16, 128], BF16)
    sbc_sb = fc.sb(nc, "sbc_sb", [32, 2, 128], BF16)
    id_bf = fc.sb(nc, "id_bf", [128, 128], BF16)
    eps_col = fc.sb(nc, "eps_col", [128, 1], F32)
    A1 = fc.sb(nc, "m_A1", [128, 8 * 2 * DI], BF16)
    inw_sb = A1[:, :].rearrange("p (k m) -> p k m", k=8)
    dtw_sb = A1[:, :].rearrange("p (k m) -> p k m", k=16)
    ow_sb = fc.sb(nc, "m_ow", [128, 16, DM], BF16)
    bcw_sb = fc.sb(nc, "m_bcw", [128, 16, 32], BF16)
    ar1 = fc.sb(nc, "m_ar1", [128, 16, TC + 3], BF16)
    xz_fm = ar1[:, :, 0:TC]
    xzh = ar1[:, :, :]
    yg_fm = ar1[:, :, 0:TC]
    gtc_fm = fc.sb(nc, "m_gt_fm", [128, 16, TC], BF16)
    xc_fm = fc.sb(nc, "m_xc_fm", [128, 16, TC], BF16)
    dt_fm = fc.sb(nc, "m_dt_fm", [128, 16, TC], BF16)
    arof = fc.sb(nc, "m_arof", [128, 8, TC], BF16)
    xn_fm = arof[:, :, :]
    of_fm = arof[:, :, :]
    cvr = fc.sb(nc, "m_cvr", [128, TC], F32)
    sp_m = cvr[:, :]
    dtxJ = fc.sb(nc, "m_dtxJ", [128, TC], BF16)
    xtile = fc.sb(nc, "m_xtile", [128, DM], F32)
    xn_tm = fc.sb(nc, "m_xn_tm", [128, DM], BF16)
    lnscr = fc.sb(nc, "m_lnscr", [128, DM], BF16)
    lnm = fc.sb(nc, "m_lnm", [128, 1], F32)
    lns = fc.sb(nc, "m_lns", [128, 1], F32)
    lnr = fc.sb(nc, "m_lnr", [128, 1], F32)
    sp_a = fc.sb(nc, "m_sp_a", [128, TC], BF16)
    bc_sb = fc.sb(nc, "m_bc_sb", [32, TC], BF16)
    brep = fc.sb(nc, "m_brep", [128, TC], BF16)
    crep = fc.sb(nc, "m_crep", [128, TC], BF16)
    dA_sb = fc.sb(nc, "m_dA", [128, TC], F32)
    u_sb = fc.sb(nc, "m_u", [128, TC], F32)
    h_sb = fc.sb(nc, "m_h", [128, TC], BF16)
    hc_sb = fc.sb(nc, "m_hc", [128, TC], BF16)
    hcar = fc.sb(nc, "m_hcar", [128, 256], F32)
    sg_sb = fc.sb(nc, "m_sg", [128, TC], BF16)
    t1_sb = fc.sb(nc, "m_t1", [128, TC], BF16)
    o_tm = fc.sb(nc, "m_otm", [128, DM], F32)

    ps_mm = fc.ps(nc, "mps_mm", [128, 512], F32)
    ps_bc = fc.ps(nc, "mps_bc", [128, 512], F32)
    ps_dtr = fc.ps(nc, "mps_dtr", [128, 512], F32)
    ps_dxr = fc.ps(nc, "mps_dxr", [128, 512], F32)
    ps_y = fc.ps(nc, "mps_y", [128, 512], F32)
    ps_tr = [fc.ps(nc, f"mps_tr{i}", [128, 1024], BF16) for i in range(2)]

    ch = Chain(nc, "m_ch")
    pe_s = Chain(nc, "m_pe")
    ev_s = Chain(nc, "m_ev")
    dch = Chain(nc, "m_dch")
    wdma = Chain(nc, "m_wdma")
    outs = Chain(nc, "m_outs")

    with nc.Block() as bk:
        def gsec(g):
            dch.push(g.dma_start(inb_sb[:, :], inb_d), 16)
            dch.push(g.dma_start(cw_sb[:, :, :], cw_d), 16)
            dch.push(g.dma_start(cb_sb[:, :], cb_d), 16)
            dch.push(g.dma_start(ap_sb[:, :], ap_d), 16)
            dch.push(g.dma_start(dp_sb[:, :], dp_d), 16)
            dch.push(g.dma_start(dtb_sb[:, :], dtb_d), 16)
            dch.push(g.dma_start(bcb_sb[:, :], bcb_d), 16)
            dch.push(g.dma_start(ob_sb[:, :], ob_d), 16)
            dch.push(g.dma_start(sel_sb[:, :, :], sel_d), 16)
            dch.push(g.dma_start(red_sb[:, :, :], red_d), 16)
            dch.push(g.dma_start(sbc_sb[:, :, :], sbc_d), 16)
            dch.push(g.dma_start(id_bf[:, :], id_d), 16)
            dch.push(g.dma_start(bcw_sb[:, :, :], bcw_d.rearrange("(k p) c -> p k c", p=128)), 16)
            dch.push(g.dma_start(inw_sb, inw_d.rearrange("(k p) m -> p k m", p=128)), 16)
            g.memset(eps_col[:, :], EPS)
            g.memset(hcar[:, :], 0.0)
        bk.gpsimd(gsec)

    for b in range(BPC):
        for c in range(NCH):
            t0 = c * TC
            with nc.Block() as bk:
                for tt in range(4):
                    def dmain(sy, b=b, t0=t0, tt=tt):
                        ch.wait(sy)
                        dch.push(sy.dma_start(xtile[:, :], x_in[b, t0 + tt * 128:t0 + (tt + 1) * 128, :]), 16)
                    bk.sync(dmain)

                    def vstart(v):
                        dch.wait(v)
                        ch.push(v.tensor_copy(lnm[:, :], eps_col[:, :]))
                    bk.vector(vstart)

                    _ln_emit(nc, bk, ch, xtile, xn_tm, lnscr, lnm, lns, lnr, eps_col)

                    pe0, ev0 = pe_s.n, ev_s.n

                    def trans(t, pe0=pe0, ev0=ev0):
                        ch.wait(t)
                        for j in range(8):
                            if j >= 2:
                                ev_s.wait(t, ev0 + (j - 2) + 1)
                            pe_s.push(t.transpose(ps_tr[j % 2][:, 0:128],
                                                  xn_tm[:, j * 128:(j + 1) * 128], id_bf[:, :]))
                    bk.tensor(trans)

                    def evac(v, tt=tt, pe0=pe0, ev0=ev0):
                        for j in range(8):
                            pe_s.wait(v, pe0 + 1 + j)
                            ev_s.push(v.tensor_copy(xn_fm[:, j, tt * 128:(tt + 1) * 128],
                                                    ps_tr[j % 2][:, 0:128]))
                    bk.vector(evac)

                    def joinv(v):
                        ev_s.wait(v)
                        ch.push(v.tensor_copy(lnm[:, :], lnm[:, :]))
                    bk.vector(joinv)

                for m in range(32):
                    def ipmm(t, m=m):
                        ch.wait(t)
                        for k in range(8):
                            inst = t.matmul(ps_mm[:, :], inw_sb[:, k, m * 128:(m + 1) * 128],
                                            xn_fm[:, k, :], start=(k == 0), stop=(k == 7))
                        ch.push(inst)
                    bk.tensor(ipmm)

                    def ipev(v, m=m):
                        ch.wait(v)
                        dst = xz_fm[:, m, :] if m < 16 else gtc_fm[:, m - 16, :]
                        ch.push(v.tensor_scalar(dst, ps_mm[:, :], inb_sb[:, m:m + 1], None, ALU.add))
                    bk.vector(ipev)

                def st(g, b=b, t0=t0):
                    ch.wait(g)
                    outs.push(g.dma_start(xz_sc[b, :, :, t0:t0 + TC], xz_fm[:, :, :]), 16)
                    outs.push(g.dma_start(gt_sc[b, :, :, t0:t0 + TC], gtc_fm[:, :, :]), 16)
                bk.gpsimd(st)

                def fin(sy):
                    outs.wait(sy)
                bk.sync(fin)

    with nc.Block() as bk:
        def wload(g):
            wdma.push(g.dma_start(dtw_sb, dtw_d.rearrange("(k p) m -> p k m", p=128)), 16)
            wdma.push(g.dma_start(ow_sb[:, :, :], ow_d.rearrange("(k p) m -> p k m", p=128)), 16)
        bk.gpsimd(wload)

    for b in range(BPC):
        with nc.Block() as bk:
            def hrst(v, b=b):
                ch.push(v.tensor_scalar(hcar[:, :], hcar[:, :], 0.0, None, ALU.mult))
            bk.vector(hrst)

        for c in range(NCH):
            t0 = c * TC
            with nc.Block() as bk:
                lo = max(t0 - 1, 0)
                hi = min(t0 + TC + 2, T)
                off = lo - (t0 - 1)

                def ld(g, b=b, t0=t0, lo=lo, hi=hi, off=off):
                    g.memset(ar1[:, :, :], 0.0)
                    dch.push(g.dma_start(xzh[:, :, off:off + hi - lo], xz_sc[b, :, :, lo:hi]), 16)
                    dch.push(g.dma_start(gtc_fm[:, :, :], gt_sc[b, :, :, t0:t0 + TC]), 16)
                bk.gpsimd(ld)

                for j in range(16):
                    def conv(v, j=j):
                        if j == 0:
                            dch.wait(v)
                            wdma.wait(v)
                        ch.wait(v)
                        t1 = ch.push(v.tensor_scalar(cvr[:, :], xzh[:, j, 0:TC],
                                                     cw_sb[:, j, 0:1], None, ALU.mult))
                        for jj in range(1, 4):
                            ch.wait(v, t1)
                            t1 = ch.push(v.scalar_tensor_tensor(cvr[:, :], xzh[:, j, jj:jj + TC],
                                                                cw_sb[:, j, jj:jj + 1], cvr[:, :],
                                                                ALU.mult, ALU.add))
                    bk.vector(conv)

                    def csil(s, j=j):
                        ch.wait(s)
                        ch.push(s.activation(xc_fm[:, j, :], cvr[:, :], AF.Silu,
                                             bias=cb_sb[:, j:j + 1]))
                    bk.scalar(csil)

                def bcmm(t):
                    ch.wait(t)
                    for k in range(16):
                        inst = t.matmul(ps_bc[0:32, :], bcw_sb[:, k, :], xc_fm[:, k, :],
                                        start=(k == 0), stop=(k == 15))
                    ch.push(inst)
                bk.tensor(bcmm)

                def bcev(v):
                    ch.wait(v)
                    ch.push(v.tensor_scalar(bc_sb[:, :], ps_bc[0:32, :], bcb_sb[:, :], None, ALU.add))
                bk.vector(bcev)

                def repmm(t):
                    ch.wait(t)
                    ch.push(t.matmul(ps_bc[:, :], sbc_sb[:, 0, :], bc_sb[:, :], start=True, stop=True))
                bk.tensor(repmm)

                def repev(v):
                    ch.wait(v)
                    ch.push(v.tensor_copy(brep[:, :], ps_bc[:, :]))
                bk.vector(repev)

                def repmm2(t):
                    ch.wait(t)
                    ch.push(t.matmul(ps_bc[:, :], sbc_sb[:, 1, :], bc_sb[:, :], start=True, stop=True))
                bk.tensor(repmm2)

                def repev2(v):
                    ch.wait(v)
                    ch.push(v.tensor_copy(crep[:, :], ps_bc[:, :]))
                bk.vector(repev2)

                for m in range(16):
                    def dtmm(t, m=m):
                        ch.wait(t)
                        for k in range(16):
                            inst = t.matmul(ps_mm[:, :], dtw_sb[:, k, m * 128:(m + 1) * 128],
                                            xc_fm[:, k, :], start=(k == 0), stop=(k == 15))
                        ch.push(inst)
                    bk.tensor(dtmm)

                    def dtsp1(s, m=m):
                        ch.wait(s)
                        t1 = ch.push(s.activation(sp_a[:, :], ps_mm[:, :], AF.Abs,
                                                  bias=dtb_sb[:, m:m + 1]))
                        ch.wait(s, t1)
                        t2 = ch.push(s.activation(sp_a[:, :], sp_a[:, :], AF.Exp, scale=-1.0))
                        ch.wait(s, t2)
                        ch.push(s.activation(sp_a[:, :], sp_a[:, :], AF.Ln, bias=1.0))
                    bk.scalar(dtsp1)

                    def dtsp2(v, m=m):
                        ch.wait(v)
                        t1 = ch.push(v.tensor_scalar(sp_m[:, :], ps_mm[:, :], dtb_sb[:, m:m + 1],
                                                     0.0, ALU.add, ALU.max))
                        ch.wait(v, t1)
                        ch.push(v.tensor_tensor(dt_fm[:, m, :], sp_m[:, :], sp_a[:, :], ALU.add))
                    bk.vector(dtsp2)

                for J in range(16):
                    def dtxm(v, J=J):
                        ch.wait(v)
                        ch.push(v.tensor_tensor(dtxJ[:, :], dt_fm[:, J, :], xc_fm[:, J, :], ALU.mult))
                    bk.vector(dtxm)

                    for j in range(16):
                        g16 = J * 16 + j

                        def smm(t, J=J, j=j):
                            ch.wait(t)
                            ch.push(t.matmul(ps_dtr[:, :], sel_sb[:, j, :], dt_fm[:, J, :],
                                             start=True, stop=True))
                            ch.push(t.matmul(ps_dxr[:, :], sel_sb[:, j, :], dtxJ[:, :],
                                             start=True, stop=True))
                        bk.tensor(smm)

                        def sda(s, g16=g16):
                            ch.wait(s)
                            ch.push(s.activation(dA_sb[:, :], ps_dtr[:, :], AF.Exp,
                                                 scale=ap_sb[:, g16:g16 + 1]))
                        bk.scalar(sda)

                        def svec(v, g16=g16):
                            ch.wait(v)
                            t1 = ch.push(v.tensor_tensor(u_sb[:, :], ps_dxr[:, :], brep[:, :], ALU.mult))
                            ch.wait(v, t1)
                            t2 = ch.push(v.tensor_tensor_scan(h_sb[:, :], dA_sb[:, :], u_sb[:, :],
                                                              hcar[:, g16:g16 + 1], ALU.mult, ALU.add))
                            ch.wait(v, t2)
                            t3 = ch.push(v.tensor_copy(hcar[:, g16:g16 + 1], h_sb[:, TC - 1:TC]))
                            ch.wait(v, t3)
                            ch.push(v.tensor_tensor(hc_sb[:, :], h_sb[:, :], crep[:, :], ALU.mult))
                        bk.vector(svec)

                        def ymm(t, j=j):
                            ch.wait(t)
                            ch.push(t.matmul(ps_y[:, :], red_sb[:, j, :], hc_sb[:, :],
                                             start=(j == 0), stop=(j == 15)))
                        bk.tensor(ymm)

                    def gat1(s, J=J):
                        ch.wait(s)
                        ch.push(s.activation(sg_sb[:, :], gtc_fm[:, J, :], AF.Silu))
                    bk.scalar(gat1)

                    def gat2(v, J=J):
                        ch.wait(v)
                        t1 = ch.push(v.scalar_tensor_tensor(t1_sb[:, :], xc_fm[:, J, :],
                                                            dp_sb[:, J:J + 1], ps_y[:, :],
                                                            ALU.mult, ALU.add))
                        ch.wait(v, t1)
                        ch.push(v.tensor_tensor(yg_fm[:, J, :], t1_sb[:, :], sg_sb[:, :], ALU.mult))
                    bk.vector(gat2)

                for m in range(8):
                    def omm(t, m=m):
                        ch.wait(t)
                        for k in range(16):
                            inst = t.matmul(ps_mm[:, :], ow_sb[:, k, m * 128:(m + 1) * 128],
                                            yg_fm[:, k, :], start=(k == 0), stop=(k == 15))
                        ch.push(inst)
                    bk.tensor(omm)

                    def oev(v, m=m):
                        ch.wait(v)
                        ch.push(v.tensor_scalar(of_fm[:, m, :], ps_mm[:, :], ob_sb[:, m:m + 1],
                                                None, ALU.add))
                    bk.vector(oev)

                for tt in range(4):
                    def dmx(sy, b=b, t0=t0, tt=tt):
                        ch.wait(sy)
                        outs.wait(sy)
                        dch.push(sy.dma_start(xtile[:, :], x_in[b, t0 + tt * 128:t0 + (tt + 1) * 128, :]), 16)
                    bk.sync(dmx)

                    pe0, ev0 = pe_s.n, ev_s.n

                    def etr(t, tt=tt, pe0=pe0, ev0=ev0):
                        ch.wait(t)
                        for m in range(8):
                            if m >= 2:
                                ev_s.wait(t, ev0 + (m - 2) + 1)
                            pe_s.push(t.transpose(ps_tr[m % 2][:, 0:128],
                                                  of_fm[:, m, tt * 128:(tt + 1) * 128], id_bf[:, :]))
                    bk.tensor(etr)

                    def eev(v, tt=tt, pe0=pe0, ev0=ev0):
                        dch.wait(v)
                        for m in range(8):
                            pe_s.wait(v, pe0 + 1 + m)
                            ev_s.push(v.tensor_tensor(o_tm[:, m * 128:(m + 1) * 128],
                                                      ps_tr[m % 2][:, 0:128],
                                                      xtile[:, m * 128:(m + 1) * 128], ALU.add))
                    bk.vector(eev)

                    def ejoin(v):
                        ev_s.wait(v)
                        ch.push(v.tensor_copy(lnm[:, :], lnm[:, :]))
                    bk.vector(ejoin)

                    def dmo(sy, b=b, t0=t0, tt=tt):
                        ch.wait(sy)
                        outs.push(sy.dma_start(x_out[b, t0 + tt * 128:t0 + (tt + 1) * 128, :],
                                               o_tm[:, :]), 16)
                    bk.sync(dmo)

                def fin(sy):
                    outs.wait(sy)
                bk.sync(fin)

    return nc


def _host_mamba(inputs):
    ident = np.eye(128, dtype=np.float32)
    A = np.asarray(inputs["L0_attn_A"])
    A_perm = np.ascontiguousarray(A.reshape(256, 128).T)
    SEL = np.zeros((128, 16, 128), np.float32)
    for r in range(128):
        for jj in range(16):
            for p in range(128):
                if r == 8 * jj + p // 16:
                    SEL[r, jj, p] = 1.0
    RED = np.ascontiguousarray(SEL.transpose(2, 1, 0))
    selbc = np.zeros((32, 2, 128), np.float32)
    for p in range(128):
        selbc[p % 16, 0, p] = 1.0
        selbc[16 + p % 16, 1, p] = 1.0
    conv_w = np.asarray(inputs["L0_attn_conv_w"]).reshape(4, DI)
    base = {
        "in_w": inputs["L0_attn_in_w"],
        "in_b": np.ascontiguousarray(np.asarray(inputs["L0_attn_in_b"]).reshape(32, 128).T),
        "conv_w": np.ascontiguousarray(conv_w.reshape(4, 16, 128).transpose(2, 1, 0)),
        "conv_b": np.ascontiguousarray(np.asarray(inputs["L0_attn_conv_b"]).reshape(16, 128).T),
        "A_perm": A_perm,
        "Dp_col": np.ascontiguousarray(np.asarray(inputs["L0_attn_Dp"]).reshape(16, 128).T),
        "dt_w": inputs["L0_attn_dt_w"],
        "dt_b": np.ascontiguousarray(np.asarray(inputs["L0_attn_dt_b"]).reshape(16, 128).T),
        "bc_w": np.ascontiguousarray(np.concatenate([np.asarray(inputs["L0_attn_B_w"]),
                                                     np.asarray(inputs["L0_attn_C_w"])], axis=1)),
        "bc_b": np.ascontiguousarray(np.concatenate([np.asarray(inputs["L0_attn_B_b"]),
                                                     np.asarray(inputs["L0_attn_C_b"])]).reshape(32, 1)),
        "out_w": inputs["L0_attn_out_w"],
        "out_b": np.ascontiguousarray(np.asarray(inputs["L0_attn_out_b"]).reshape(8, 128).T),
        "SEL": SEL,
        "RED": RED,
        "selbc": selbc,
        "ident": ident,
    }
    return base


def run_mamba(x, inputs):
    nc = _get("mamba", build_mamba)
    base = _host_mamba(inputs)
    in_maps = []
    for cc in range(NCORE):
        m = dict(base)
        m["x_in"] = np.ascontiguousarray(x[2 * cc:2 * cc + 2])
        in_maps.append(m)
    res = run_bass_kernel_spmd(nc, in_maps, core_ids=list(range(NCORE)))
    return np.concatenate([r["x_out"] for r in res.results], axis=0)




def build_ttt(fc=None):
    fc = fc or FC()
    nc = fc.nc if fc.nc is not None else bass.Bass("TRN2", target_bir_lowering=False, debug=False)
    CK = 128
    NCK = T // CK
    LR = 0.01
    ALPHA = 2.0 / DM
    LAM = 2.0 * LR / DM

    x_in = fc.dram(nc, "x_in", [BPC, T, DM], F32)
    thk_d = fc.dram(nc, "thK_T", [DM, DM], F32)
    thv_d = fc.dram(nc, "thV_T", [DM, DM], F32)
    thq_d = fc.dram(nc, "thQ_T", [DM, DM], F32)
    w0t_d = fc.dram(nc, "W0_T", [DM, DM], F32)
    b0_d = fc.dram(nc, "b0", [1, DM], F32)
    mst_d = fc.dram(nc, "Mstrict", [128, 128], F32)
    min_d = fc.dram(nc, "Mincl", [128, 128], F32)
    id_d = fc.dram(nc, "ident", [128, 128], F32)
    x_out = fc.dram(nc, "x_out", [BPC, T, DM], F32, kind="ExternalOutput")

    thk = fc.sb(nc, "t_thk", [128, 8, DM], BF16)
    thq = fc.sb(nc, "t_thq", [128, 8, DM], BF16)
    thvn = fc.sb(nc, "t_thvn", [128, 8, DM], BF16)
    wt = [fc.sb(nc, f"t_wt{b}", [128, 8, DM], BF16) for b in range(BPC)]
    bst = fc.sb(nc, "t_bst", [1, DM], F32)
    bst_bf = fc.sb(nc, "t_bstbf", [1, DM], BF16)
    mstr = fc.sb(nc, "t_mstr", [128, 128], BF16)
    mincl = fc.sb(nc, "t_mincl", [128, 128], BF16)
    id_bf = fc.sb(nc, "t_id", [128, 128], BF16)
    ones_r = fc.sb(nc, "t_ones_r", [1, 128], BF16)
    ones_c = fc.sb(nc, "t_ones_c", [128, 1], BF16)
    eps_col = fc.sb(nc, "t_eps", [128, 1], F32)
    kfm = fc.sb(nc, "t_kfm", [128, 8, CK], BF16)
    qfm = fc.sb(nc, "t_qfm", [128, 8, CK], BF16)
    ktm = fc.sb(nc, "t_ktm", [128, 8, 128], BF16)
    base_sb = fc.sb(nc, "t_base", [128, DM], BF16)
    gs_tm = fc.sb(nc, "t_gs", [128, DM], BF16)
    nmat = fc.sb(nc, "t_n", [128, 128], BF16)
    pmat = fc.sb(nc, "t_p", [128, 128], BF16)
    ptmat = fc.sb(nc, "t_pt", [128, 128], BF16)
    tmat = fc.sb(nc, "t_t", [128, 128], BF16)
    umat = fc.sb(nc, "t_u", [128, 128], BF16)
    xn_fm = fc.sb(nc, "t_xnfm", [128, 8, CK], BF16)
    xtile = fc.sb(nc, "t_xtile", [128, DM], F32)
    xn_tm = fc.sb(nc, "t_xntm", [128, DM], BF16)
    lnscr = fc.sb(nc, "t_lnscr", [128, DM], BF16)
    lnm = fc.sb(nc, "t_lnm", [128, 1], F32)
    lns = fc.sb(nc, "t_lns", [128, 1], F32)
    lnr = fc.sb(nc, "t_lnr", [128, 1], F32)
    o_tm = fc.sb(nc, "t_otm", [128, DM], F32)

    ps_big = fc.ps(nc, "t_psbig", [128, DM], F32)
    ps_sm = fc.ps(nc, "t_pssm", [128, 512], F32)
    ps_tr = [fc.ps(nc, f"t_pstr{i}", [128, 1024], BF16) for i in range(2)]

    ch = Chain(nc, "t_ch")
    pe_s = Chain(nc, "t_pe")
    ev_s = Chain(nc, "t_ev")
    dch = Chain(nc, "t_dch")
    outs = Chain(nc, "t_outs")

    with nc.Block() as bk:
        def gsec(g):
            dch.push(g.dma_start(thk[:, :, :], thk_d.rearrange("(k p) m -> p k m", p=128)), 16)
            dch.push(g.dma_start(thq[:, :, :], thq_d.rearrange("(k p) m -> p k m", p=128)), 16)
            dch.push(g.dma_start(thvn[:, :, :], thv_d.rearrange("(k p) m -> p k m", p=128)), 16)
            for b in range(BPC):
                dch.push(g.dma_start(wt[b][:, :, :], w0t_d.rearrange("(k p) m -> p k m", p=128)), 16)
            dch.push(g.dma_start(mstr[:, :], mst_d), 16)
            dch.push(g.dma_start(mincl[:, :], min_d), 16)
            dch.push(g.dma_start(id_bf[:, :], id_d), 16)
            g.memset(ones_r[:, :], 1.0)
            g.memset(ones_c[:, :], 1.0)
            g.memset(eps_col[:, :], EPS)
        bk.gpsimd(gsec)

        def vneg(s):
            dch.wait(s)
            for j in range(8):
                ch.push(s.activation(thvn[:, j, :], thvn[:, j, :], AF.Identity, scale=-1.0))
        bk.scalar(vneg)

    for b in range(BPC):
        with nc.Block() as bk:
            def binit(g, b=b):
                dch.push(g.dma_start(bst[:, :], b0_d), 16)
            bk.gpsimd(binit)

            def binit2(v):
                dch.wait(v)
                ch.push(v.tensor_copy(bst_bf[:, :], bst[:, :]))
            bk.vector(binit2)

        for c in range(NCK):
            t0 = c * CK
            with nc.Block() as bk:
                def dmain(sy, b=b, t0=t0):
                    ch.wait(sy)
                    dch.push(sy.dma_start(xtile[:, :], x_in[b, t0:t0 + CK, :]), 16)
                bk.sync(dmain)

                def vstart(v):
                    dch.wait(v)
                    ch.push(v.tensor_copy(lnm[:, :], eps_col[:, :]))
                bk.vector(vstart)

                _ln_emit(nc, bk, ch, xtile, xn_tm, lnscr, lnm, lns, lnr, eps_col)

                pe0, ev0 = pe_s.n, ev_s.n

                def trans(t, pe0=pe0, ev0=ev0):
                    ch.wait(t)
                    for j in range(8):
                        if j >= 2:
                            ev_s.wait(t, ev0 + (j - 2) + 1)
                        pe_s.push(t.transpose(ps_tr[j % 2][:, 0:128],
                                              xn_tm[:, j * 128:(j + 1) * 128], id_bf[:, :]))
                bk.tensor(trans)

                def evac(v, pe0=pe0, ev0=ev0):
                    for j in range(8):
                        pe_s.wait(v, pe0 + 1 + j)
                        ev_s.push(v.tensor_copy(xn_fm[:, j, :], ps_tr[j % 2][:, 0:128]))
                bk.vector(evac)

                def joinv(v):
                    ev_s.wait(v)
                    ch.push(v.tensor_copy(lnm[:, :], lnm[:, :]))
                bk.vector(joinv)

                for dst, th in [(kfm, thk), (qfm, thq)]:
                    for i in range(8):
                        def pmm(t, dst=dst, th=th, i=i):
                            ch.wait(t)
                            for k in range(8):
                                inst = t.matmul(ps_sm[:, 0:128], th[:, k, i * 128:(i + 1) * 128],
                                                xn_fm[:, k, :], start=(k == 0), stop=(k == 7))
                            ch.push(inst)
                        bk.tensor(pmm)

                        def pev(v, dst=dst, i=i):
                            ch.wait(v)
                            ch.push(v.tensor_copy(dst[:, i, :], ps_sm[:, 0:128]))
                        bk.vector(pev)

                def basemm(t, b=b):
                    ch.wait(t)
                    for k in range(8):
                        for hh in range(2):
                            t.matmul(ps_big[:, hh * 512:(hh + 1) * 512], kfm[:, k, :],
                                     wt[b][:, k, hh * 512:(hh + 1) * 512],
                                     start=(k == 0), stop=False)
                    for hh in range(2):
                        t.matmul(ps_big[:, hh * 512:(hh + 1) * 512], ones_r[:, :],
                                 bst_bf[0:1, hh * 512:(hh + 1) * 512], start=False, stop=False)
                    for k in range(8):
                        for hh in range(2):
                            inst = t.matmul(ps_big[:, hh * 512:(hh + 1) * 512], xn_fm[:, k, :],
                                            thvn[:, k, hh * 512:(hh + 1) * 512],
                                            start=False, stop=(k == 7))
                    ch.push(inst)
                bk.tensor(basemm)

                def baseev(s):
                    ch.wait(s)
                    ch.push(s.activation(base_sb[:, :], ps_big[:, :], AF.Identity, scale=ALPHA))
                bk.scalar(baseev)

                def kkmm(t):
                    ch.wait(t)
                    for k in range(8):
                        t.matmul(ps_sm[:, 0:128], kfm[:, k, :], kfm[:, k, :],
                                 start=(k == 0), stop=False)
                    inst = t.matmul(ps_sm[:, 0:128], ones_r[:, :],
                                    ones_r[:, :], start=False, stop=True)
                    ch.push(inst)
                bk.tensor(kkmm)

                def nev(v):
                    ch.wait(v)
                    ch.push(v.scalar_tensor_tensor(nmat[:, :], ps_sm[:, 0:128], -LAM,
                                                   mstr[:, :], ALU.mult, ALU.mult))
                bk.vector(nev)

                def tinit(v):
                    ch.wait(v)
                    t1 = ch.push(v.tensor_tensor(tmat[:, :], nmat[:, :], id_bf[:, :], ALU.add))
                    ch.wait(v, t1)
                    ch.push(v.tensor_copy(pmat[:, :], nmat[:, :]))
                bk.vector(tinit)

                for it in range(6):
                    def sq1(t, it=it):
                        ch.wait(t)
                        ch.push(t.transpose(ps_tr[0][:, 0:128], pmat[:, :], id_bf[:, :]))
                    bk.tensor(sq1)

                    def sq2(v, it=it):
                        ch.wait(v)
                        ch.push(v.tensor_copy(ptmat[:, :], ps_tr[0][:, 0:128]))
                    bk.vector(sq2)

                    def sq3(t, it=it):
                        ch.wait(t)
                        ch.push(t.matmul(ps_sm[:, 0:128], ptmat[:, :], pmat[:, :],
                                         start=True, stop=True))
                        ch.push(t.matmul(ps_sm[:, 128:256], ptmat[:, :], tmat[:, :],
                                         start=True, stop=True))
                    bk.tensor(sq3)

                    def sq4(v, it=it):
                        ch.wait(v)
                        t1 = ch.push(v.tensor_copy(pmat[:, :], ps_sm[:, 0:128]))
                        ch.wait(v, t1)
                        ch.push(v.tensor_tensor(tmat[:, :], tmat[:, :], ps_sm[:, 128:256], ALU.add))
                    bk.vector(sq4)

                def kqmm(t):
                    ch.wait(t)
                    for k in range(8):
                        t.matmul(ps_sm[:, 0:128], kfm[:, k, :], qfm[:, k, :],
                                 start=(k == 0), stop=False)
                    inst = t.matmul(ps_sm[:, 0:128], ones_r[:, :], ones_r[:, :],
                                    start=False, stop=True)
                    ch.push(inst)
                bk.tensor(kqmm)

                def uev(v):
                    ch.wait(v)
                    ch.push(v.tensor_tensor(umat[:, :], ps_sm[:, 0:128], mincl[:, :], ALU.mult))
                bk.vector(uev)

                def gmm(t):
                    ch.wait(t)
                    for hh in range(2):
                        inst = t.matmul(ps_big[:, hh * 512:(hh + 1) * 512], tmat[:, :],
                                        base_sb[:, hh * 512:(hh + 1) * 512], start=True, stop=True)
                    ch.push(inst)
                bk.tensor(gmm)

                def gev(s):
                    ch.wait(s)
                    ch.push(s.activation(gs_tm[:, :], ps_big[:, :], AF.Identity, scale=-LR))
                bk.scalar(gev)

                def omm(t, b=b):
                    ch.wait(t)
                    for k in range(8):
                        for hh in range(2):
                            t.matmul(ps_big[:, hh * 512:(hh + 1) * 512], qfm[:, k, :],
                                     wt[b][:, k, hh * 512:(hh + 1) * 512],
                                     start=(k == 0), stop=False)
                    for hh in range(2):
                        t.matmul(ps_big[:, hh * 512:(hh + 1) * 512], ones_r[:, :],
                                 bst_bf[0:1, hh * 512:(hh + 1) * 512], start=False, stop=False)
                    for hh in range(2):
                        inst = t.matmul(ps_big[:, hh * 512:(hh + 1) * 512], umat[:, :],
                                        gs_tm[:, hh * 512:(hh + 1) * 512], start=False, stop=True)
                    ch.push(inst)
                bk.tensor(omm)

                def dmx(sy, b=b, t0=t0):
                    ch.wait(sy)
                    outs.wait(sy)
                    dch.push(sy.dma_start(xtile[:, :], x_in[b, t0:t0 + CK, :]), 16)
                bk.sync(dmx)

                def oev(v):
                    ch.wait(v)
                    dch.wait(v)
                    ch.push(v.tensor_tensor(o_tm[:, :], ps_big[:, :], xtile[:, :], ALU.add))
                bk.vector(oev)

                def dmo(e, b=b, t0=t0):
                    ch.wait(e)
                    outs.push(e.dma_start(x_out[b, t0:t0 + CK, :], o_tm[:, :]), 16)
                if fc.out_swdge:
                    bk.gpsimd(dmo)
                else:
                    bk.sync(dmo)

                pe0, ev0 = pe_s.n, ev_s.n

                def ktr(t, pe0=pe0, ev0=ev0):
                    ch.wait(t)
                    for i in range(8):
                        if i >= 2:
                            ev_s.wait(t, ev0 + (i - 2) + 1)
                        pe_s.push(t.transpose(ps_tr[i % 2][:, 0:128], kfm[:, i, :], id_bf[:, :]))
                bk.tensor(ktr)

                def kev(v, pe0=pe0, ev0=ev0):
                    for i in range(8):
                        pe_s.wait(v, pe0 + 1 + i)
                        ev_s.push(v.tensor_copy(ktm[:, i, :], ps_tr[i % 2][:, 0:128]))
                bk.vector(kev)

                def kjoin(v):
                    ev_s.wait(v)
                    ch.push(v.tensor_copy(lnm[:, :], lnm[:, :]))
                bk.vector(kjoin)

                for i in range(8):
                    def dwmm(t, i=i):
                        ch.wait(t)
                        for hh in range(2):
                            inst = t.matmul(ps_big[:, hh * 512:(hh + 1) * 512], ktm[:, i, :],
                                            gs_tm[:, hh * 512:(hh + 1) * 512], start=True, stop=True)
                        ch.push(inst)
                    bk.tensor(dwmm)

                    def dwup(v, b=b, i=i):
                        ch.wait(v)
                        ch.push(v.tensor_tensor(wt[b][:, i, :], ps_big[:, :], wt[b][:, i, :], ALU.add))
                    bk.vector(dwup)

                for hh in range(2):
                    def bmm(t, hh=hh):
                        ch.wait(t)
                        ch.push(t.matmul(ps_sm[0:1, :], ones_c[:, :],
                                         gs_tm[:, hh * 512:(hh + 1) * 512], start=True, stop=True))
                    bk.tensor(bmm)

                    def bup(v, hh=hh):
                        ch.wait(v)
                        ch.push(v.tensor_tensor(bst[0:1, hh * 512:(hh + 1) * 512], ps_sm[0:1, :],
                                                bst[0:1, hh * 512:(hh + 1) * 512], ALU.add))
                    bk.vector(bup)

                def bbf(v):
                    ch.wait(v)
                    ch.push(v.tensor_copy(bst_bf[:, :], bst[:, :]))
                bk.vector(bbf)

                def fin(sy):
                    outs.wait(sy)
                bk.sync(fin)

    return nc


def _host_ttt(inputs):
    ident = np.eye(128, dtype=np.float32)
    ii, jj = np.meshgrid(np.arange(128), np.arange(128), indexing="ij")
    Mstrict = (ii < jj).astype(np.float32)
    Mincl = (ii <= jj).astype(np.float32)
    base = {
        "thK_T": np.ascontiguousarray(np.asarray(inputs["L1_attn_theta_K"]).T),
        "thV_T": np.ascontiguousarray(np.asarray(inputs["L1_attn_theta_V"]).T),
        "thQ_T": np.ascontiguousarray(np.asarray(inputs["L1_attn_theta_Q"]).T),
        "W0_T": np.ascontiguousarray(np.asarray(inputs["L1_attn_W0"]).T),
        "b0": np.ascontiguousarray(np.asarray(inputs["L1_attn_b0"]).reshape(1, DM)),
        "Mstrict": Mstrict,
        "Mincl": Mincl,
        "ident": ident,
    }
    return base


def run_ttt(x, inputs):
    nc = _get("ttt", build_ttt)
    base = _host_ttt(inputs)
    in_maps = []
    for cc in range(NCORE):
        m = dict(base)
        m["x_in"] = np.ascontiguousarray(x[2 * cc:2 * cc + 2])
        in_maps.append(m)
    res = run_bass_kernel_spmd(nc, in_maps, core_ids=list(range(NCORE)))
    return np.concatenate([r["x_out"] for r in res.results], axis=0)




def build_lstm(fc=None):
    fc = fc or FC()
    nc = fc.nc if fc.nc is not None else bass.Bass("TRN2", target_bir_lowering=False, debug=False)
    NB = 16
    TS = 256
    if fc.lstm_gather:
        agout_d = fc.dmap["agout"]
        tsel_d = fc.dram(nc, "tsel", [128, 32], mybir.dt.uint32)
    else:
        x_in = fc.dram(nc, "x_in", [NB, TS, DM], F32)
    aw_d = fc.dram(nc, "aw", [6, DM, DM], F32)
    ow2_d = fc.dram(nc, "ow2", [2, DM, DM], F32)
    gb_d = fc.dram(nc, "gb", [128, 3, 8], F32)
    bo_d = fc.dram(nc, "bo", [128, 8], F32)
    ic_d = fc.dram(nc, "in_c", [128, 8], F32)
    id_d = fc.dram(nc, "ident", [128, 128], F32)
    x_out = fc.dram(nc, "x_out", [NB, TS, DM], F32, kind="ExternalOutput")

    AW = fc.sb(nc, "l_AW", [128, 6, 8, DM], BF16)
    wv = [AW[:, i, :, :] for i in range(6)]
    xn3 = fc.sb(nc, "l_xn3", [128, 8, NB * TS], BF16)
    c_fm = fc.sb(nc, "l_c", [128, 8, TS], F32)
    c_bf = [fc.sb(nc, f"l_cbf{i}", [128, 8, TS], BF16) for i in range(2)]
    f_sb = fc.sb(nc, "l_f", [128, TS], BF16)
    i_sb = fc.sb(nc, "l_i", [128, TS], BF16)
    g_sb = fc.sb(nc, "l_g", [128, TS], BF16)
    tt1 = fc.sb(nc, "l_tt1", [128, TS], F32)
    tt2 = fc.sb(nc, "l_tt2", [128, TS], F32)
    gb_sb = fc.sb(nc, "l_gb", [128, 3, 8], F32)
    bo_sb = fc.sb(nc, "l_bo", [128, 8], F32)
    ic_sb = fc.sb(nc, "l_ic", [128, 8], F32)
    onef = fc.sb(nc, "l_onef", [128, TS], F32)
    id_bf = fc.sb(nc, "l_id", [128, 128], BF16)
    eps_col = fc.sb(nc, "l_eps", [128, 1], F32)
    h_fm = fc.sb(nc, "l_hfm", [128, 8, TS], BF16)
    xtile = fc.sb(nc, "l_xtile", [128, DM], F32)
    xn_tm = fc.sb(nc, "l_xntm", [128, DM], BF16)
    lnscr = fc.sb(nc, "l_lnscr", [128, DM], BF16)
    lnm = fc.sb(nc, "l_lnm", [128, 1], F32)
    lns = fc.sb(nc, "l_lns", [128, 1], F32)
    lnr = fc.sb(nc, "l_lnr", [128, 1], F32)
    o_tm = fc.sb(nc, "l_otm", [128, DM], F32)
    if fc.lstm_gather:
        xg_bf = fc.sb(nc, "l_xgbf", [128, DM], BF16)
        tsel_sb = fc.sb(nc, "l_tsel", [128, 32], mybir.dt.uint32)

    ps_f = fc.ps(nc, "l_psf", [128, 512], F32)
    ps_i = fc.ps(nc, "l_psi", [128, 512], F32)
    ps_g = fc.ps(nc, "l_psg", [128, 512], F32)
    ps_tr = [fc.ps(nc, f"l_pstr{i}", [128, 1024], BF16) for i in range(2)]

    ch = Chain(nc, "l_ch")
    pe_s = Chain(nc, "l_pe")
    ev_s = Chain(nc, "l_ev")
    dch = Chain(nc, "l_dch")
    wdma = Chain(nc, "l_wdma")
    outs = Chain(nc, "l_outs")

    with nc.Block() as bk:
        def gsec(g):
            wdma.push(g.dma_start(AW[:, :, :, :],
                                  aw_d.rearrange("w (k p) m -> p w k m", p=128)), 16)
            dch.push(g.dma_start(gb_sb[:, :, :], gb_d), 16)
            dch.push(g.dma_start(bo_sb[:, :], bo_d), 16)
            dch.push(g.dma_start(ic_sb[:, :], ic_d), 16)
            dch.push(g.dma_start(id_bf[:, :], id_d), 16)
            g.memset(eps_col[:, :], EPS)
            g.memset(onef[:, :], 1.0)
            if fc.lstm_gather:
                dch.push(g.dma_start(tsel_sb[:, :], tsel_d), 16)
        bk.gpsimd(gsec)

        def cinit(s):
            dch.wait(s)
            for m in range(8):
                ch.push(s.activation(c_fm[:, m, :], onef[:, :], AF.Identity,
                                     scale=ic_sb[:, m:m + 1]))
        bk.scalar(cinit)

        def cbf0(v):
            ch.wait(v)
            ch.push(v.tensor_copy(c_bf[0][:, :, :], c_fm[:, :, :]))
        bk.vector(cbf0)

    for b in range(16):
        for tt in range(2):
            with nc.Block() as bk:
                if fc.lstm_gather:
                    def dmain(g, b=b, tt=tt):
                        ch.wait(g)
                        dch.push(g.indirect_dma_start(
                            xg_bf[:, :], None, agout_d,
                            bass.IndirectOffsetOnAxis(ap=tsel_sb[:, b * 2 + tt:b * 2 + tt + 1],
                                                      axis=0)), 16)
                    bk.gpsimd(dmain)

                    def vstart(v):
                        dch.wait(v)
                        ch.push(v.tensor_copy(xtile[:, :], xg_bf[:, :]))
                    bk.vector(vstart)
                else:
                    def dmain(sy, b=b, tt=tt):
                        ch.wait(sy)
                        dch.push(sy.dma_start(xtile[:, :], x_in[b, tt * 128:(tt + 1) * 128, :]), 16)
                    bk.sync(dmain)

                    def vstart(v):
                        dch.wait(v)
                        ch.push(v.tensor_copy(lnm[:, :], eps_col[:, :]))
                    bk.vector(vstart)

                _ln_emit(nc, bk, ch, xtile, xn_tm, lnscr, lnm, lns, lnr, eps_col)

                pe0, ev0 = pe_s.n, ev_s.n

                def trans(t, pe0=pe0, ev0=ev0):
                    ch.wait(t)
                    for j in range(8):
                        if j >= 2:
                            ev_s.wait(t, ev0 + (j - 2) + 1)
                        pe_s.push(t.transpose(ps_tr[j % 2][:, 0:128],
                                              xn_tm[:, j * 128:(j + 1) * 128], id_bf[:, :]))
                bk.tensor(trans)

                def evac(v, b=b, tt=tt, pe0=pe0, ev0=ev0):
                    col = b * TS + tt * 128
                    for j in range(8):
                        pe_s.wait(v, pe0 + 1 + j)
                        ev_s.push(v.tensor_copy(xn3[:, j, col:col + 128], ps_tr[j % 2][:, 0:128]))
                bk.vector(evac)

                def joinv(v):
                    ev_s.wait(v)
                    ch.push(v.tensor_copy(lnm[:, :], lnm[:, :]))
                bk.vector(joinv)

    for b in range(16):
        col = b * TS
        with nc.Block() as bk:
            for m in range(8):
                def gmm(t, b=b, m=m, col=col):
                    wdma.wait(t)
                    ch.wait(t)
                    for (ps, wi_, ui_) in [(ps_f, 0, 1), (ps_i, 2, 3), (ps_g, 4, 5)]:
                        for k in range(8):
                            t.matmul(ps[:, 0:TS], wv[wi_][:, k, m * 128:(m + 1) * 128],
                                     xn3[:, k, col:col + TS], start=(k == 0), stop=False)
                        for k in range(8):
                            inst = t.matmul(ps[:, 0:TS], wv[ui_][:, k, m * 128:(m + 1) * 128],
                                            c_bf[b % 2][:, k, :], start=False, stop=(k == 7))
                    ch.push(inst)
                bk.tensor(gmm)

                def gact(s, m=m):
                    ch.wait(s)
                    t1 = ch.push(s.activation(f_sb[:, :], ps_f[:, 0:TS], AF.Sigmoid,
                                              bias=gb_sb[:, 0, m:m + 1]))
                    t2 = ch.push(s.activation(i_sb[:, :], ps_i[:, 0:TS], AF.Sigmoid,
                                              bias=gb_sb[:, 1, m:m + 1]))
                    ch.push(s.activation(g_sb[:, :], ps_g[:, 0:TS], AF.Tanh,
                                         bias=gb_sb[:, 2, m:m + 1]))
                bk.scalar(gact)

                def cup(v, b=b, m=m):
                    ch.wait(v)
                    t1 = ch.push(v.tensor_tensor(tt1[:, :], f_sb[:, :], c_fm[:, m, :], ALU.mult))
                    t2 = ch.push(v.tensor_tensor(tt2[:, :], i_sb[:, :], g_sb[:, :], ALU.mult))
                    ch.wait(v, t2)
                    t3 = ch.push(v.tensor_tensor(c_fm[:, m, :], tt1[:, :], tt2[:, :], ALU.add))
                    ch.wait(v, t3)
                    ch.push(v.tensor_copy(c_bf[(b + 1) % 2][:, m, :], c_fm[:, m, :]))
                bk.vector(cup)

    with nc.Block() as bk:
        def wload2(g):
            wdma.push(g.dma_start(AW[:, 0:2, :, :],
                                  ow2_d.rearrange("w (k p) m -> p w k m", p=128)), 16)
        bk.gpsimd(wload2)

    for b in range(16):
        col = b * TS
        with nc.Block() as bk:
            for m in range(8):
                def hmm(t, b=b, m=m, col=col):
                    wdma.wait(t)
                    ch.wait(t)
                    for k in range(8):
                        t.matmul(ps_f[:, 0:TS], wv[0][:, k, m * 128:(m + 1) * 128],
                                 xn3[:, k, col:col + TS], start=(k == 0), stop=False)
                    for k in range(8):
                        inst = t.matmul(ps_f[:, 0:TS], wv[1][:, k, m * 128:(m + 1) * 128],
                                        c_bf[0][:, k, :], start=False, stop=(k == 7))
                    ch.push(inst)
                bk.tensor(hmm)

                def hact(s, m=m):
                    ch.wait(s)
                    ch.push(s.activation(h_fm[:, m, :], ps_f[:, 0:TS], AF.Tanh,
                                         bias=bo_sb[:, m:m + 1]))
                bk.scalar(hact)

            for tt in range(2):
                if fc.lstm_gather:
                    def dmx(g, b=b, tt=tt):
                        ch.wait(g)
                        outs.wait(g)
                        dch.push(g.indirect_dma_start(
                            xg_bf[:, :], None, agout_d,
                            bass.IndirectOffsetOnAxis(ap=tsel_sb[:, b * 2 + tt:b * 2 + tt + 1],
                                                      axis=0)), 16)
                    bk.gpsimd(dmx)

                    def vcast(v):
                        dch.wait(v)
                        ch.push(v.tensor_copy(xtile[:, :], xg_bf[:, :]))
                    bk.vector(vcast)
                else:
                    def dmx(sy, b=b, tt=tt):
                        ch.wait(sy)
                        outs.wait(sy)
                        dch.push(sy.dma_start(xtile[:, :], x_in[b, tt * 128:(tt + 1) * 128, :]), 16)
                    bk.sync(dmx)

                pe0, ev0 = pe_s.n, ev_s.n

                def etr(t, tt=tt, pe0=pe0, ev0=ev0):
                    ch.wait(t)
                    for m in range(8):
                        if m >= 2:
                            ev_s.wait(t, ev0 + (m - 2) + 1)
                        pe_s.push(t.transpose(ps_tr[m % 2][:, 0:128],
                                              h_fm[:, m, tt * 128:(tt + 1) * 128], id_bf[:, :]))
                bk.tensor(etr)

                def eev(v, tt=tt, pe0=pe0, ev0=ev0):
                    if not fc.lstm_gather:
                        dch.wait(v)
                    else:
                        ch.wait(v)
                    for m in range(8):
                        pe_s.wait(v, pe0 + 1 + m)
                        ev_s.push(v.tensor_tensor(o_tm[:, m * 128:(m + 1) * 128],
                                                  ps_tr[m % 2][:, 0:128],
                                                  xtile[:, m * 128:(m + 1) * 128], ALU.add))
                bk.vector(eev)

                def ejoin(v):
                    ev_s.wait(v)
                    ch.push(v.tensor_copy(lnm[:, :], lnm[:, :]))
                bk.vector(ejoin)

                def dmo(sy, b=b, tt=tt):
                    ch.wait(sy)
                    outs.push(sy.dma_start(x_out[b, tt * 128:(tt + 1) * 128, :], o_tm[:, :]), 16)
                bk.sync(dmo)

            def fin(sy):
                outs.wait(sy)
            bk.sync(fin)

    return nc


def _host_lstm(inputs):
    ident = np.eye(128, dtype=np.float32)
    aw = np.stack([np.asarray(inputs["L1_ffn_" + n]).T for n in
                   ["wf", "uf", "wi", "ui", "wc", "uc"]])
    ow2 = np.stack([np.asarray(inputs["L1_ffn_wo"]).T, np.asarray(inputs["L1_ffn_uo"]).T])
    gb = np.stack([np.asarray(inputs["L1_ffn_" + n]).reshape(8, 128).T for n in
                   ["bf", "bi", "bc"]], axis=1)
    base = {
        "aw": np.ascontiguousarray(aw),
        "ow2": np.ascontiguousarray(ow2),
        "gb": np.ascontiguousarray(gb),
        "bo": np.ascontiguousarray(np.asarray(inputs["L1_ffn_bo"]).reshape(8, 128).T),
        "in_c": np.ascontiguousarray(np.asarray(inputs["L1_ffn_in_c"]).reshape(8, 128).T),
        "ident": ident,
    }
    return base


def run_lstm(x, inputs):
    nc = _get("lstm", build_lstm)
    base = _host_lstm(inputs)
    in_maps = []
    for cc in range(NCORE):
        m = dict(base)
        m["x_in"] = np.ascontiguousarray(x[:, 256 * cc:256 * (cc + 1), :])
        in_maps.append(m)
    res = run_bass_kernel_spmd(nc, in_maps, core_ids=list(range(NCORE)))
    return np.concatenate([r["x_out"] for r in res.results], axis=1)


# version 03
# speedup vs baseline: 1.1915x; 1.1915x over previous
import numpy as np
import concourse.bass as bass
import concourse.mybir as mybir
from concourse.bass_utils import run_bass_kernel_spmd

F32 = mybir.dt.float32
BF16 = mybir.dt.bfloat16
AF = mybir.ActivationFunctionType
ALU = mybir.AluOpType
AX = mybir.AxisListType

NCORE = 8
BPC = 2
T = 2048
DM = 1024
DI = 2048
DS = 16
TOK = BPC * T
EPS = 1e-5


class Chain:

    def __init__(self, nc, name):
        self.h = nc.alloc_semaphore(name)
        self.n = 0

    def push(self, inst, k=1):
        inst.then_inc(self.h, k)
        self.n += k
        return self.n

    def wait(self, eng, tok=None):
        eng.wait_ge(self.h, self.n if tok is None else tok)




class FC:

    def __init__(self, nc=None, prefix="", dmap=None, pmap=None, stack=None,
                 lstm_gather=False, out_swdge=False):
        self.nc = nc
        self.prefix = prefix
        self.dmap = dmap or {}
        self.pmap = pmap or {}
        self.stack = stack
        self.lstm_gather = lstm_gather
        self.out_swdge = out_swdge

    def dram(self, nc, name, shape, dtype, kind="ExternalInput"):
        if name in self.dmap:
            return self.dmap[name]
        return nc.dram_tensor(self.prefix + name, shape, dtype, kind=kind).ap()

    def sb(self, nc, name, shape, dtype):
        if self.stack is not None:
            return self.stack.enter_context(nc.sbuf_tensor(self.prefix + name, shape, dtype))
        return nc.alloc_sbuf_tensor(name, shape, dtype)

    def ps(self, nc, name, shape, dtype):
        if name in self.pmap:
            return self.pmap[name]
        return nc.alloc_psum_tensor(name, shape, dtype)

def _ln_emit(nc, bk, ch, xtile, xn_tm, lnscr, lnm, lns, lnr, eps_col=None):

    def vpart1(v):
        ch.wait(v)
        t1 = ch.push(v.tensor_reduce(lnm[:, :], xtile[:, :], AX.X, ALU.add))
        ch.wait(v, t1)
        t2 = ch.push(v.tensor_scalar(lnm[:, :], lnm[:, :], 1.0 / DM, None, ALU.mult))
        ch.wait(v, t2)
        ch.push(v.tensor_scalar(xtile[:, :], xtile[:, :], lnm[:, :], None, ALU.subtract))
    bk.vector(vpart1)

    def spart(s):
        ch.wait(s)
        t1 = ch.push(s.activation(lnscr[:, :], xtile[:, :], AF.Square, accum_out=lns[:, :]))
        ch.wait(s, t1)
        ch.push(s.activation(lnr[:, :], lns[:, :], AF.Sqrt, scale=1.0 / DM, bias=eps_col[:, :]))
    bk.scalar(spart)

    def vpart2(v):
        ch.wait(v)
        t1 = ch.push(v.reciprocal(lnr[:, :], lnr[:, :]))
        ch.wait(v, t1)
        ch.push(v.tensor_scalar(xn_tm[:, :], xtile[:, :], lnr[:, :], None, ALU.mult))
    bk.vector(vpart2)


def build_moe(dbg=False, fc=None):
    fc = fc or FC()
    nc = fc.nc if fc.nc is not None else bass.Bass("TRN2", target_bir_lowering=False, debug=False)
    x_in = fc.dram(nc, "x_in", [TOK, DM], F32)
    wr_d = fc.dram(nc, "wr", [DM, 8], F32)
    wg_d = fc.dram(nc, "wg", [8, DM, DI], F32)
    bg_d = fc.dram(nc, "bg", [128, 16, 8], F32)
    wu_d = fc.dram(nc, "wu", [8, DM, DI], F32)
    bu_d = fc.dram(nc, "bu", [128, 16, 8], F32)
    wo_d = fc.dram(nc, "wo", [8, DI, DM], F32)
    bo_d = fc.dram(nc, "bo", [8, DM], F32)
    id_d = fc.dram(nc, "ident", [128, 128], F32)
    x_out = fc.dram(nc, "x_out", [TOK, DM], F32, kind="ExternalOutput")
    if dbg:
        d_xq = fc.dram(nc, "d_xq", [128, 8, 1024], F32, kind="ExternalOutput")
        d_w = fc.dram(nc, "d_w", [128, 8, 8], F32, kind="ExternalOutput")
        d_acc0 = fc.dram(nc, "d_acc0", [128, 8, DM], F32, kind="ExternalOutput")
        d_acc = fc.dram(nc, "d_acc", [128, 8, DM], F32, kind="ExternalOutput")
        d_hw = fc.dram(nc, "d_hw", [128, 16, 512], F32, kind="ExternalOutput")
        d_acc1 = fc.dram(nc, "d_acc1", [128, 8, DM], F32, kind="ExternalOutput")

    QT = 1024
    NQ = TOK // QT
    NTT = QT // 128

    wr_sb = fc.sb(nc, "wr_sb", [128, 8, 8], BF16)
    bg_sb = fc.sb(nc, "bg_sb", [128, 16, 8], F32)
    bu_sb = fc.sb(nc, "bu_sb", [128, 16, 8], F32)
    bo_sb = fc.sb(nc, "bo_sb", [8, DM], BF16)
    id_bf = fc.sb(nc, "id_bf", [128, 128], BF16)
    xq_fm = fc.sb(nc, "xq_fm", [128, 8, QT], BF16)
    w_tm = fc.sb(nc, "w_tm", [128, NTT, 8], F32)
    w_fmT = fc.sb(nc, "w_fmT", [8, QT], BF16)
    acc = fc.sb(nc, "acc", [128, NTT, DM], F32)
    wg_sb = fc.sb(nc, "wg_sb", [128, 8, DI], BF16)
    wu_sb = fc.sb(nc, "wu_sb", [128, 8, DI], BF16)
    wo_sb = fc.sb(nc, "wo_sb", [128, 16, DM], BF16)
    hw_sb = fc.sb(nc, "hw_sb", [128, 16, 512], BF16)
    gs_sb = fc.sb(nc, "gs_sb", [128, 512], BF16)
    xtile = fc.sb(nc, "xtile", [128, DM], F32)
    xn_tm = fc.sb(nc, "xn_tm", [128, DM], BF16)
    lnscr = fc.sb(nc, "lnscr", [128, DM], BF16)
    lnm = fc.sb(nc, "lnm", [128, 1], F32)
    lns = fc.sb(nc, "lns", [128, 1], F32)
    lnr = fc.sb(nc, "lnr", [128, 1], F32)
    r_pr = fc.sb(nc, "r_pr", [128, 8], F32)
    r_m1 = fc.sb(nc, "r_m1", [128, 1], F32)
    r_m2 = fc.sb(nc, "r_m2", [128, 1], F32)
    r_k1 = fc.sb(nc, "r_k1", [128, 8], F32)
    r_k2 = fc.sb(nc, "r_k2", [128, 8], F32)
    r_t = fc.sb(nc, "r_t", [128, 8], F32)
    r_den = fc.sb(nc, "r_den", [128, 1], F32)
    w_bf = fc.sb(nc, "w_bf", [128, 8], BF16)
    eps_col = fc.sb(nc, "eps_col", [128, 1], F32)
    o_tm = fc.sb(nc, "o_tm", [128, DM], F32)

    ps_tr = [fc.ps(nc, f"ps_tr{i}", [128, 1024], BF16) for i in range(2)]
    ps_r = fc.ps(nc, "ps_r", [128, 8], F32)
    ps_g = fc.ps(nc, "ps_g", [128, 512], F32)
    ps_u = fc.ps(nc, "ps_u", [128, 512], F32)
    ps_o = [fc.ps(nc, "ps_o0", [128, DM], F32)]
    ps_o.append(ps_o[0])

    ch = Chain(nc, "ch")
    pe_s = Chain(nc, "pe_s")
    ev_s = Chain(nc, "ev_s")
    o_dg = Chain(nc, "o_dg")
    o_dh = Chain(nc, "o_dh")
    wdma = Chain(nc, "wdma")
    outs = Chain(nc, "outs")

    with nc.Block() as bk:
        def gsec(g):
            o_dg.push(g.dma_start(wr_sb[:, :, :], wr_d.rearrange("(k p) e -> p k e", p=128)), 16)
            o_dg.push(g.dma_start(bg_sb[:, :, :], bg_d), 16)
            o_dg.push(g.dma_start(bu_sb[:, :, :], bu_d), 16)
            o_dg.push(g.dma_start(bo_sb[:, :], bo_d), 16)
            o_dg.push(g.dma_start(id_bf[:, :], id_d), 16)
            g.memset(eps_col[:, :], EPS)
        bk.gpsimd(gsec)

    for q in range(NQ):
        with nc.Block() as bk:
            for it in range(NTT):
                tok0 = q * QT + it * 128

                def dmain(sy, tok0=tok0):
                    ch.wait(sy)
                    o_dh.push(sy.dma_start(xtile[:, :], x_in[tok0:tok0 + 128, :]), 16)
                bk.sync(dmain)

                def vbridge(v):
                    o_dg.wait(v)
                    o_dh.wait(v)
                    ch.push(v.tensor_copy(lnm[:, :], eps_col[:, :]))
                bk.vector(vbridge)

                _ln_emit(nc, bk, ch, xtile, xn_tm, lnscr, lnm, lns, lnr, eps_col)

                pe0, ev0 = pe_s.n, ev_s.n

                def trans(t, pe0=pe0, ev0=ev0):
                    ch.wait(t)
                    for j in range(8):
                        if j >= 2:
                            ev_s.wait(t, ev0 + (j - 2) + 1)
                        pe_s.push(t.transpose(ps_tr[j % 2][:, 0:128],
                                              xn_tm[:, j * 128:(j + 1) * 128], id_bf[:, :]))
                bk.tensor(trans)

                def evac(v, it=it, pe0=pe0, ev0=ev0):
                    for j in range(8):
                        pe_s.wait(v, pe0 + 1 + j)
                        ev_s.push(v.tensor_copy(xq_fm[:, j, it * 128:(it + 1) * 128],
                                                ps_tr[j % 2][:, 0:128]))
                bk.vector(evac)

                def joinv(v):
                    ev_s.wait(v)
                    ch.push(v.tensor_copy(lnm[:, :], lnm[:, :]))
                bk.vector(joinv)

                def rmm(t, it=it):
                    ch.wait(t)
                    for k in range(8):
                        inst = t.matmul(ps_r[:, :], xq_fm[:, k, it * 128:(it + 1) * 128],
                                        wr_sb[:, k, :], start=(k == 0), stop=(k == 7))
                    ch.push(inst)
                bk.tensor(rmm)

                def rsoft1(v):
                    ch.wait(v)
                    t1 = ch.push(v.tensor_copy(r_pr[:, :], ps_r[:, :]))
                    ch.wait(v, t1)
                    t2 = ch.push(v.tensor_reduce(r_m1[:, :], r_pr[:, :], AX.X, ALU.max))
                    ch.wait(v, t2)
                    ch.push(v.tensor_scalar(r_pr[:, :], r_pr[:, :], r_m1[:, :], None, ALU.subtract))
                bk.vector(rsoft1)

                def rexp(s):
                    ch.wait(s)
                    ch.push(s.activation(r_pr[:, :], r_pr[:, :], AF.Exp))
                bk.scalar(rexp)

                def rsoft2(v, it=it):
                    ch.wait(v)
                    seq = [
                        lambda: v.tensor_reduce(r_den[:, :], r_pr[:, :], AX.X, ALU.add),
                        lambda: v.reciprocal(r_den[:, :], r_den[:, :]),
                        lambda: v.tensor_scalar(r_pr[:, :], r_pr[:, :], r_den[:, :], None, ALU.mult),
                        lambda: v.tensor_reduce(r_m1[:, :], r_pr[:, :], AX.X, ALU.max),
                        lambda: v.tensor_scalar(r_k1[:, :], r_pr[:, :], r_m1[:, :], None, ALU.is_ge),
                        lambda: v.tensor_tensor(r_t[:, :], r_k1[:, :], r_pr[:, :], ALU.mult),
                        lambda: v.tensor_tensor(r_t[:, :], r_pr[:, :], r_t[:, :], ALU.subtract),
                        lambda: v.tensor_reduce(r_m2[:, :], r_t[:, :], AX.X, ALU.max),
                        lambda: v.tensor_scalar(r_k2[:, :], r_t[:, :], r_m2[:, :], None, ALU.is_ge),
                        lambda: v.tensor_tensor(r_k1[:, :], r_k1[:, :], r_k2[:, :], ALU.add),
                        lambda: v.tensor_scalar(r_den[:, :], r_m1[:, :], r_m2[:, :], 1e-8, ALU.add, ALU.add),
                        lambda: v.reciprocal(r_den[:, :], r_den[:, :]),
                        lambda: v.tensor_tensor(r_t[:, :], r_pr[:, :], r_k1[:, :], ALU.mult),
                        lambda: v.tensor_scalar(w_tm[:, it, :], r_t[:, :], r_den[:, :], None, ALU.mult),
                        lambda: v.tensor_copy(w_bf[:, :], w_tm[:, it, :]),
                    ]
                    for f in seq:
                        tk = ch.push(f())
                        ch.wait(v, tk)
                bk.vector(rsoft2)

                def wtr(t):
                    ch.wait(t)
                    ch.push(t.transpose(ps_tr[0][0:8, 0:128], w_bf[:, :], id_bf[:, :]))
                bk.tensor(wtr)

                def wev(v, it=it):
                    ch.wait(v)
                    ch.push(v.tensor_copy(w_fmT[0:8, it * 128:(it + 1) * 128],
                                          ps_tr[0][0:8, 0:128]))
                bk.vector(wev)

            for it in range(NTT):
                def bterm(t, it=it):
                    ch.wait(t)
                    for hh in range(2):
                        inst = t.matmul(ps_o[0][:, hh * 512:(hh + 1) * 512],
                                        w_fmT[:, it * 128:(it + 1) * 128],
                                        bo_sb[:, hh * 512:(hh + 1) * 512],
                                        start=True, stop=True)
                    ch.push(inst, 1)
                bk.tensor(bterm)

                def bev(v, it=it):
                    ch.wait(v)
                    ch.push(v.tensor_copy(acc[:, it, :], ps_o[0][:, :]))
                bk.vector(bev)

        if dbg and q == 0:
            with nc.Block() as bk:
                def dump1(g):
                    ch.wait(g)
                    outs.push(g.dma_start(d_xq[:, :, :], xq_fm[:, :, :]), 16)
                    outs.push(g.dma_start(d_w[:, :, :], w_tm[:, :, :]), 16)
                    outs.push(g.dma_start(d_acc0[:, :, :], acc[:, :, :]), 16)
                bk.gpsimd(dump1)

        for e in range(8):
            with nc.Block() as bk:
                def wload(g, e=e):
                    wdma.push(g.dma_start(wg_sb[:, :, :],
                                          wg_d[e].rearrange("(k p) m -> p k m", p=128)), 16)
                    wdma.push(g.dma_start(wu_sb[:, :, :],
                                          wu_d[e].rearrange("(k p) m -> p k m", p=128)), 16)
                    wdma.push(g.dma_start(wo_sb[:, :, :],
                                          wo_d[e].rearrange("(k p) m -> p k m", p=128)), 16)
                bk.gpsimd(wload)

                for h in range(2):
                    tok0 = h * 512
                    for m in range(16):
                        def gumm(t, e=e, m=m, tok0=tok0):
                            wdma.wait(t)
                            ch.wait(t)
                            for k in range(8):
                                ig = t.matmul(ps_g[:, :], wg_sb[:, k, m * 128:(m + 1) * 128],
                                              xq_fm[:, k, tok0:tok0 + 512],
                                              start=(k == 0), stop=(k == 7))
                            ch.push(ig)
                            for k in range(8):
                                iu = t.matmul(ps_u[:, :], wu_sb[:, k, m * 128:(m + 1) * 128],
                                              xq_fm[:, k, tok0:tok0 + 512],
                                              start=(k == 0), stop=(k == 7))
                            ch.push(iu)
                        bk.tensor(gumm)

                        def gsilu(s, e=e, m=m):
                            ch.wait(s)
                            ch.push(s.activation(gs_sb[:, :], ps_g[:, :], AF.Silu,
                                                 bias=bg_sb[:, m, e:e + 1]))
                        bk.scalar(gsilu)

                        def hmul(v, e=e, m=m):
                            ch.wait(v)
                            ch.push(v.scalar_tensor_tensor(hw_sb[:, m, :], ps_u[:, :],
                                                           bu_sb[:, m, e:e + 1],
                                                           gs_sb[:, :], ALU.add, ALU.mult))
                        bk.vector(hmul)

                    for tt in range(4):
                        it = h * 4 + tt
                        pso = ps_o[tt % 2]

                        def omm(t, tt=tt, pso=pso):
                            ch.wait(t)
                            for k in range(16):
                                for half in range(2):
                                    inst = t.matmul(pso[:, half * 512:(half + 1) * 512],
                                                    hw_sb[:, k, tt * 128:(tt + 1) * 128],
                                                    wo_sb[:, k, half * 512:(half + 1) * 512],
                                                    start=(k == 0), stop=(k == 15))
                            ch.push(inst)
                        bk.tensor(omm)

                        def oacc(v, e=e, it=it, pso=pso):
                            ch.wait(v)
                            ch.push(v.scalar_tensor_tensor(acc[:, it, :], pso[:, :],
                                                           w_tm[:, it, e:e + 1], acc[:, it, :],
                                                           ALU.mult, ALU.add))
                        bk.vector(oacc)
                if dbg and q == 0 and e == 0:
                    def dump3(g):
                        ch.wait(g)
                        outs.push(g.dma_start(d_acc1[:, :, :], acc[:, :, :]), 16)
                    bk.gpsimd(dump3)

        if dbg and q == 0:
            with nc.Block() as bk:
                def dump2(g):
                    ch.wait(g)
                    outs.push(g.dma_start(d_acc[:, :, :], acc[:, :, :]), 16)
                    outs.push(g.dma_start(d_hw[:, :, :], hw_sb[:, :, :]), 16)
                bk.gpsimd(dump2)

        with nc.Block() as bk:
            for it in range(NTT):
                tok0 = q * QT + it * 128

                def dmain(sy, tok0=tok0):
                    ch.wait(sy)
                    o_dh.push(sy.dma_start(xtile[:, :], x_in[tok0:tok0 + 128, :]), 16)
                bk.sync(dmain)

                def radd(v, it=it):
                    ch.wait(v)
                    o_dh.wait(v)
                    outs.wait(v)
                    ch.push(v.tensor_tensor(o_tm[:, :], acc[:, it, :], xtile[:, :], ALU.add))
                bk.vector(radd)

                def dmaout(sy, tok0=tok0):
                    ch.wait(sy)
                    outs.push(sy.dma_start(x_out[tok0:tok0 + 128, :], o_tm[:, :]), 16)
                bk.sync(dmaout)

            def fin(sy):
                outs.wait(sy)
            bk.sync(fin)

    return nc


_CACHE = {}


def _get(name, builder):
    if name not in _CACHE:
        nc = builder()
        nc.finalize()
        _CACHE[name] = nc
    return _CACHE[name]


def _host_moe(inputs):
    ident = np.eye(128, dtype=np.float32)
    return {
            "wr": inputs["L0_ffn_W_router"],
            "wg": inputs["L0_ffn_W_gate"],
            "bg": np.ascontiguousarray(np.asarray(inputs["L0_ffn_b_gate"]).reshape(8, 16, 128).transpose(2, 1, 0)),
            "wu": inputs["L0_ffn_W_up"],
            "bu": np.ascontiguousarray(np.asarray(inputs["L0_ffn_b_up"]).reshape(8, 16, 128).transpose(2, 1, 0)),
            "wo": inputs["L0_ffn_W_out"],
            "bo": inputs["L0_ffn_b_out"],
            "ident": ident,
    }


def run_moe(x, inputs):
    nc = _get("moe", build_moe)
    base = _host_moe(inputs)
    in_maps = []
    for c in range(NCORE):
        m = dict(base)
        m["x_in"] = np.ascontiguousarray(x[2 * c:2 * c + 2].reshape(TOK, DM))
        in_maps.append(m)
    res = run_bass_kernel_spmd(nc, in_maps, core_ids=list(range(NCORE)))
    out = np.concatenate([r["x_out"].reshape(BPC, T, DM) for r in res.results], axis=0)
    return out


def build_full():
    import contextlib
    nc = bass.Bass("TRN2", target_bir_lowering=False, debug=False)
    ps_a = nc.alloc_psum_tensor("f_psa", [128, 1024], F32)
    ps_b = nc.alloc_psum_tensor("f_psb", [128, 1024], F32)
    ps_c = nc.alloc_psum_tensor("f_psc", [128, 512], F32)
    ps_d = nc.alloc_psum_tensor("f_psd", [128, 512], F32)
    trA = nc.alloc_psum_tensor("f_trA", [128, 1024], BF16)
    trB = nc.alloc_psum_tensor("f_trB", [128, 1024], BF16)
    mamba_pmap = {"mps_a0": ps_a[:, 0:512], "mps_a1": ps_a[:, 512:1024],
                  "mps_b0": ps_b[:, 0:512], "mps_b1": ps_b[:, 512:1024],
                  "mps_c": ps_c[:, :], "mps_d": ps_d[:, :],
                  "mps_tr0": trA, "mps_tr1": trB}

    x0 = nc.dram_tensor("x_in", [BPC, T, DM], F32, kind="ExternalInput").ap()
    x1_sc = nc.dram_tensor("x1_sc", [BPC, T, DM], F32, kind="Internal").ap()
    x2_sc = nc.dram_tensor("x2_sc", [BPC, T, DM], F32, kind="Internal").ap()
    ag_in = nc.dram_tensor("ag_in", [BPC * T, DM], BF16, kind="Internal").ap()
    ag_out = nc.dram_tensor("ag_out", [NCORE * BPC * T, DM], BF16, kind="Internal").ap()
    x_out = nc.dram_tensor("x_out", [16, 256, DM], F32, kind="ExternalOutput").ap()

    with contextlib.ExitStack() as st:
        fc = FC(nc=nc, prefix="m_", stack=st,
                dmap={"x_in": x0, "x_out": x1_sc},
                pmap=mamba_pmap)
        build_mamba(fc=fc)

    with contextlib.ExitStack() as st:
        fc = FC(nc=nc, prefix="o_", stack=st,
                dmap={"x_in": x1_sc.rearrange("b t d -> (b t) d"),
                      "x_out": x2_sc.rearrange("b t d -> (b t) d")},
                pmap={"ps_tr0": trA, "ps_tr1": trB, "ps_r": ps_b[:, 0:8],
                      "ps_g": ps_c, "ps_u": ps_d, "ps_o0": ps_a})
        build_moe(fc=fc)

    with contextlib.ExitStack() as st:
        fc = FC(nc=nc, prefix="t_", stack=st, out_swdge=True,
                dmap={"x_in": x2_sc,
                      "x_out": ag_in.rearrange("(b t) d -> b t d", b=BPC)},
                pmap={"t_psbig": ps_a, "t_pssm": ps_c, "t_pstr0": trA, "t_pstr1": trB})
        build_ttt(fc=fc)

    ccs = nc.alloc_semaphore("f_ccs")
    with nc.Block() as bk:
        def ccsec(g):
            g.collective_compute("AllGather", mybir.AluOpType.bypass,
                                 replica_groups=[list(range(NCORE))],
                                 ins=[ag_in.opt()], outs=[ag_out.opt()]).then_inc(ccs, 1)
            g.wait_ge(ccs, 1)
        bk.gpsimd(ccsec)

    with contextlib.ExitStack() as st:
        fc = FC(nc=nc, prefix="l_", stack=st, lstm_gather=True,
                dmap={"agout": ag_out, "x_out": x_out},
                pmap={"l_psf": ps_a[:, 0:512], "l_psi": ps_a[:, 512:1024],
                      "l_psg": ps_b[:, 0:512], "l_pstr0": trA, "l_pstr1": trB})
        build_lstm(fc=fc)

    return nc


def kernel(**inputs):
    nc = _get("full", build_full)
    x0 = np.asarray(inputs["L0_attn_x"], dtype=np.float32)
    mb = {("m_" + k): v for k, v in _host_mamba(inputs).items()}
    ob = {("o_" + k): v for k, v in _host_moe(inputs).items()}
    tb = {("t_" + k): v for k, v in _host_ttt(inputs).items()}
    lb = {("l_" + k): v for k, v in _host_lstm(inputs).items()}
    in_maps = []
    for cc in range(NCORE):
        sel = np.zeros((128, 32), np.uint32)
        for bb in range(16):
            for tt in range(2):
                sel[:, bb * 2 + tt] = ((bb // 2) * 4096 + (bb % 2) * 2048
                                       + 256 * cc + tt * 128 + np.arange(128))
        m = {}
        m.update(mb)
        m.update(ob)
        m.update(tb)
        m.update(lb)
        m["x_in"] = np.ascontiguousarray(x0[2 * cc:2 * cc + 2])
        m["l_tsel"] = sel
        in_maps.append(m)
    res = run_bass_kernel_spmd(nc, in_maps, core_ids=list(range(NCORE)))
    return np.concatenate([r["x_out"] for r in res.results], axis=1)


def kernel_unfused(**inputs):
    x = np.asarray(inputs["L0_attn_x"], dtype=np.float32)
    x1 = run_mamba(x, inputs)
    x2 = run_moe(x1, inputs)
    x3 = run_ttt(x2, inputs)
    x4 = run_lstm(x3, inputs)
    return x4




def build_mamba(dbg=False, fc=None):
    import os
    SKIP_SCAN = os.environ.get("SKIP_SCAN") == "1"
    SKIP_DT = os.environ.get("SKIP_DT") == "1"
    SKIP_IP = os.environ.get("SKIP_IP") == "1"
    fc = fc or FC()
    nc = fc.nc if fc.nc is not None else bass.Bass("TRN2", target_bir_lowering=False, debug=False)
    TC = 512
    NCH = T // TC
    NG = DI // 8

    x_in = fc.dram(nc, "x_in", [BPC, T, DM], F32)
    inw_d = fc.dram(nc, "in_w", [DM, 2 * DI], F32)
    inb_d = fc.dram(nc, "in_b", [128, 32], F32)
    cw_d = fc.dram(nc, "conv_w", [128, 16, 4], F32)
    cb_d = fc.dram(nc, "conv_b", [128, 16], F32)
    ap_d = fc.dram(nc, "A_perm", [128, 256], F32)
    dp_d = fc.dram(nc, "Dp_col", [128, 16], F32)
    dtw_d = fc.dram(nc, "dt_w", [DI, DI], F32)
    dtb_d = fc.dram(nc, "dt_b", [128, 16], F32)
    bcw_d = fc.dram(nc, "bc_w", [DI, 32], F32)
    bcb_d = fc.dram(nc, "bc_b", [32, 1], F32)
    ow_d = fc.dram(nc, "out_w", [DI, DM], F32)
    ob_d = fc.dram(nc, "out_b", [128, 8], F32)
    sel_d = fc.dram(nc, "SEL", [128, 16, 128], F32)
    red_d = fc.dram(nc, "RED", [128, 16, 128], F32)
    sbc_d = fc.dram(nc, "selbc", [32, 2, 128], F32)
    id_d = fc.dram(nc, "ident", [128, 128], F32)
    x_out = fc.dram(nc, "x_out", [BPC, T, DM], F32, kind="ExternalOutput")

    xz_sc = fc.dram(nc, "xz_sc", [BPC, 16, 128, T], BF16, kind="Internal")
    gt_sc = fc.dram(nc, "gt_sc", [BPC, 16, 128, T], BF16, kind="Internal")

    inb_sb = fc.sb(nc, "inb_sb", [128, 32], F32)
    cw_sb = fc.sb(nc, "cw_sb", [128, 16, 4], F32)
    cb_sb = fc.sb(nc, "cb_sb", [128, 16], F32)
    ap_sb = fc.sb(nc, "ap_sb", [128, 256], F32)
    dp_sb = fc.sb(nc, "dp_sb", [128, 16], F32)
    dtb_sb = fc.sb(nc, "dtb_sb", [128, 16], F32)
    bcb_sb = fc.sb(nc, "bcb_sb", [32, 1], F32)
    ob_sb = fc.sb(nc, "ob_sb", [128, 8], F32)
    sel_sb = fc.sb(nc, "sel_sb", [128, 16, 128], BF16)
    red_sb = fc.sb(nc, "red_sb", [128, 16, 128], BF16)
    sbc_sb = fc.sb(nc, "sbc_sb", [32, 2, 128], BF16)
    id_bf = fc.sb(nc, "id_bf", [128, 128], BF16)
    eps_col = fc.sb(nc, "eps_col", [128, 1], F32)
    A1 = fc.sb(nc, "m_A1", [128, 8 * 2 * DI], BF16)
    inw_sb = A1[:, :].rearrange("p (k m) -> p k m", k=8)
    dtw_sb = A1[:, :].rearrange("p (k m) -> p k m", k=16)
    ow_sb = fc.sb(nc, "m_ow", [128, 16, DM], BF16)
    bcw_sb = fc.sb(nc, "m_bcw", [128, 16, 32], BF16)
    ar1 = fc.sb(nc, "m_ar1", [128, 16, TC + 3], BF16)
    xz_fm = ar1[:, :, 0:TC]
    xzh = ar1[:, :, :]
    yg_fm = ar1[:, :, 0:TC]
    gtc_fm = fc.sb(nc, "m_gt_fm", [128, 16, TC], BF16)
    xc_fm = fc.sb(nc, "m_xc_fm", [128, 16, TC], BF16)
    dt_fm = fc.sb(nc, "m_dt_fm", [128, 16, TC], BF16)
    arof = fc.sb(nc, "m_arof", [128, 8, TC], BF16)
    xn_fm = arof[:, :, :]
    of_fm = arof[:, :, :]
    cvr = fc.sb(nc, "m_cvr", [128, TC], F32)
    xtile = fc.sb(nc, "m_xtile", [128, DM], F32)
    xn_tm = fc.sb(nc, "m_xn_tm", [128, DM], BF16)
    lnscr = fc.sb(nc, "m_lnscr", [128, DM], BF16)
    lnm = fc.sb(nc, "m_lnm", [128, 1], F32)
    lns = fc.sb(nc, "m_lns", [128, 1], F32)
    lnr = fc.sb(nc, "m_lnr", [128, 1], F32)
    bc_sb = fc.sb(nc, "m_bc_sb", [32, TC], BF16)
    brep = fc.sb(nc, "m_brep", [128, TC], BF16)
    crep = fc.sb(nc, "m_crep", [128, TC], BF16)
    dA_sb = [fc.sb(nc, f"m_dA{i}", [128, TC], F32) for i in range(2)]
    u_sb = fc.sb(nc, "m_u", [128, TC], F32)
    h_sb = fc.sb(nc, "m_h", [128, TC], BF16)
    hc_sb = [fc.sb(nc, f"m_hc{i}", [128, TC], BF16) for i in range(2)]
    hcar = fc.sb(nc, "m_hcar", [128, 256], F32)
    sg_sb = [fc.sb(nc, f"m_sg{i}", [128, TC], BF16) for i in range(2)]
    t1_sb = fc.sb(nc, "m_t1", [128, TC], BF16)
    dtxJ2 = [xn_tm[:, 0:TC], xn_tm[:, TC:2 * TC]]
    sp_a2 = [lnscr[:, 0:TC], lnscr[:, TC:2 * TC]]
    sp_m2 = [cvr[:, :], xtile[:, 0:TC]]


    mps_a0 = fc.ps(nc, "mps_a0", [128, 512], F32)
    mps_a1 = fc.ps(nc, "mps_a1", [128, 512], F32)
    mps_b0 = fc.ps(nc, "mps_b0", [128, 512], F32)
    mps_b1 = fc.ps(nc, "mps_b1", [128, 512], F32)
    mps_c = fc.ps(nc, "mps_c", [128, 512], F32)
    mps_d = fc.ps(nc, "mps_d", [128, 512], F32)
    ps_dtr = [mps_a0[:, :], mps_a1[:, :]]
    ps_dxr = [mps_b0[:, :], mps_b1[:, :]]
    ps_y = [mps_c[:, :], mps_d[:, :]]
    ps_mm = [mps_a0[:, :], mps_a1[:, :]]
    ps_bc = mps_d[:, :]
    ps_tr = [fc.ps(nc, f"mps_tr{i}", [128, 1024], BF16) for i in range(2)]

    ch = Chain(nc, "m_ch")
    pe_s = Chain(nc, "m_pe")
    ev_s = Chain(nc, "m_ev")
    ac_s = Chain(nc, "m_ac")
    dch = Chain(nc, "m_dch")
    hch = Chain(nc, "m_hch")
    wdma = Chain(nc, "m_wdma")
    outs = Chain(nc, "m_outs")
    outs_h = Chain(nc, "m_outsh")
    ms = Chain(nc, "m_ms")

    with nc.Block() as bk:
        def gsec(g):
            dch.push(g.dma_start(inb_sb[:, :], inb_d), 16)
            dch.push(g.dma_start(cw_sb[:, :, :], cw_d), 16)
            dch.push(g.dma_start(cb_sb[:, :], cb_d), 16)
            dch.push(g.dma_start(ap_sb[:, :], ap_d), 16)
            dch.push(g.dma_start(dp_sb[:, :], dp_d), 16)
            dch.push(g.dma_start(dtb_sb[:, :], dtb_d), 16)
            dch.push(g.dma_start(bcb_sb[:, :], bcb_d), 16)
            dch.push(g.dma_start(ob_sb[:, :], ob_d), 16)
            dch.push(g.dma_start(sel_sb[:, :, :], sel_d), 16)
            dch.push(g.dma_start(red_sb[:, :, :], red_d), 16)
            dch.push(g.dma_start(sbc_sb[:, :, :], sbc_d), 16)
            dch.push(g.dma_start(id_bf[:, :], id_d), 16)
            dch.push(g.dma_start(bcw_sb[:, :, :], bcw_d.rearrange("(k p) c -> p k c", p=128)), 16)
            dch.push(g.dma_start(inw_sb, inw_d.rearrange("(k p) m -> p k m", p=128)), 16)
            g.memset(eps_col[:, :], EPS)
            g.memset(hcar[:, :], 0.0)
        bk.gpsimd(gsec)

    for b in range(BPC):
        for c in range(NCH):
            t0 = c * TC
            with nc.Block() as bk:
                for tt in range(4):
                    def dmain(sy, b=b, t0=t0, tt=tt):
                        ch.wait(sy)
                        hch.push(sy.dma_start(xtile[:, :], x_in[b, t0 + tt * 128:t0 + (tt + 1) * 128, :]), 16)
                    bk.sync(dmain)

                    def vstart(v):
                        dch.wait(v)
                        hch.wait(v)
                        ch.push(v.tensor_copy(lnm[:, :], eps_col[:, :]))
                    bk.vector(vstart)

                    _ln_emit(nc, bk, ch, xtile, xn_tm, lnscr, lnm, lns, lnr, eps_col)

                    pe0, ev0 = pe_s.n, ev_s.n

                    def trans(t, pe0=pe0, ev0=ev0):
                        ch.wait(t)
                        for j in range(8):
                            if j >= 2:
                                ev_s.wait(t, ev0 + (j - 2) + 1)
                            pe_s.push(t.transpose(ps_tr[j % 2][:, 0:128],
                                                  xn_tm[:, j * 128:(j + 1) * 128], id_bf[:, :]))
                    bk.tensor(trans)

                    def evac(v, tt=tt, pe0=pe0, ev0=ev0):
                        for j in range(8):
                            pe_s.wait(v, pe0 + 1 + j)
                            ev_s.push(v.tensor_copy(xn_fm[:, j, tt * 128:(tt + 1) * 128],
                                                    ps_tr[j % 2][:, 0:128]))
                    bk.vector(evac)

                    def joinv(v):
                        ev_s.wait(v)
                        ch.push(v.tensor_copy(lnm[:, :], lnm[:, :]))
                    bk.vector(joinv)

                P0, E0 = pe_s.n, ev_s.n

                def ipmm(t, P0=P0, E0=E0):
                    ch.wait(t)
                    for m in range(32):
                        if m >= 2:
                            ev_s.wait(t, E0 + (m - 2) + 1)
                        for k in range(8):
                            inst = t.matmul(ps_mm[m % 2], inw_sb[:, k, m * 128:(m + 1) * 128],
                                            xn_fm[:, k, :], start=(k == 0), stop=(k == 7),
                                            skip_group_check=True)
                        pe_s.push(inst)
                if not SKIP_IP: bk.tensor(ipmm)

                def ipev(v, P0=P0, E0=E0):
                    for m in range(32):
                        pe_s.wait(v, P0 + m + 1)
                        dst = xz_fm[:, m, :] if m < 16 else gtc_fm[:, m - 16, :]
                        ev_s.push(v.tensor_scalar(dst, ps_mm[m % 2], inb_sb[:, m:m + 1], None, ALU.add))
                if not SKIP_IP: bk.vector(ipev)

                def ipjoin(v):
                    if not SKIP_IP:
                        ev_s.wait(v)
                    ch.push(v.tensor_copy(lnm[:, :], lnm[:, :]))
                bk.vector(ipjoin)

                def st(g, b=b, t0=t0):
                    ch.wait(g)
                    outs.push(g.dma_start(xz_sc[b, :, :, t0:t0 + TC], xz_fm[:, :, :]), 16)
                    outs.push(g.dma_start(gt_sc[b, :, :, t0:t0 + TC], gtc_fm[:, :, :]), 16)
                bk.gpsimd(st)

                def fin(sy):
                    outs.wait(sy)
                bk.sync(fin)

    with nc.Block() as bk:
        def wload(g):
            wdma.push(g.dma_start(dtw_sb, dtw_d.rearrange("(k p) m -> p k m", p=128)), 16)
            wdma.push(g.dma_start(ow_sb[:, :, :], ow_d.rearrange("(k p) m -> p k m", p=128)), 16)
        bk.gpsimd(wload)

    for b in range(BPC):
        with nc.Block() as bk:
            def hrst(v, b=b):
                ch.push(v.tensor_scalar(hcar[:, :], hcar[:, :], 0.0, None, ALU.mult))
            bk.vector(hrst)

        for c in range(NCH):
            t0 = c * TC
            with nc.Block() as bk:
                lo = max(t0 - 1, 0)
                hi = min(t0 + TC + 2, T)
                off = lo - (t0 - 1)

                import os
                def ld(g, b=b, t0=t0, lo=lo, hi=hi, off=off):
                    if os.environ.get("MSFENCE", "1") == "1":
                        ms.push(g.memset(ar1[:, :, :], 0.0))
                        ms.wait(g)
                    else:
                        g.memset(ar1[:, :, :], 0.0)
                    dch.push(g.dma_start(xzh[:, :, off:off + hi - lo], xz_sc[b, :, :, lo:hi]), 16)
                    dch.push(g.dma_start(gtc_fm[:, :, :], gt_sc[b, :, :, t0:t0 + TC]), 16)
                bk.gpsimd(ld)

                for j in range(16):
                    def conv(v, j=j):
                        if j == 0:
                            dch.wait(v)
                            wdma.wait(v)
                        ch.wait(v)
                        t1 = ch.push(v.tensor_scalar(cvr[:, :], xzh[:, j, 0:TC],
                                                     cw_sb[:, j, 0:1], None, ALU.mult))
                        for jj in range(1, 4):
                            ch.wait(v, t1)
                            t1 = ch.push(v.scalar_tensor_tensor(cvr[:, :], xzh[:, j, jj:jj + TC],
                                                                cw_sb[:, j, jj:jj + 1], cvr[:, :],
                                                                ALU.mult, ALU.add))
                    bk.vector(conv)

                    def csil(s, j=j):
                        ch.wait(s)
                        ch.push(s.activation(xc_fm[:, j, :], cvr[:, :], AF.Silu,
                                             bias=cb_sb[:, j:j + 1]))
                    bk.scalar(csil)

                def bcmm(t):
                    ch.wait(t)
                    for k in range(16):
                        inst = t.matmul(ps_bc[0:32, :], bcw_sb[:, k, :], xc_fm[:, k, :],
                                        start=(k == 0), stop=(k == 15))
                    ch.push(inst)
                bk.tensor(bcmm)

                def bcev(v):
                    ch.wait(v)
                    ch.push(v.tensor_scalar(bc_sb[:, :], ps_bc[0:32, :], bcb_sb[:, :], None, ALU.add))
                bk.vector(bcev)

                def repmm(t):
                    ch.wait(t)
                    ch.push(t.matmul(ps_bc[:, :], sbc_sb[:, 0, :], bc_sb[:, :], start=True, stop=True))
                bk.tensor(repmm)

                def repev(v):
                    ch.wait(v)
                    ch.push(v.tensor_copy(brep[:, :], ps_bc[:, :]))
                bk.vector(repev)

                def repmm2(t):
                    ch.wait(t)
                    ch.push(t.matmul(ps_bc[:, :], sbc_sb[:, 1, :], bc_sb[:, :], start=True, stop=True))
                bk.tensor(repmm2)

                def repev2(v):
                    ch.wait(v)
                    ch.push(v.tensor_copy(crep[:, :], ps_bc[:, :]))
                bk.vector(repev2)

                dtP0, dtA0, dtD0 = pe_s.n, ac_s.n, ev_s.n
                a_ln = {m: dtA0 + 3 * m + 3 for m in range(16)}
                d_max = {m: dtD0 + 2 * m + 1 for m in range(16)}
                d_add = {m: dtD0 + 2 * m + 2 for m in range(16)}
                p_dt = {m: dtP0 + m + 1 for m in range(16)}

                def dtmm(t):
                    ch.wait(t)
                    for m in range(16):
                        if m >= 2:
                            ac_s.wait(t, a_ln[m - 2])
                            ev_s.wait(t, d_max[m - 2])
                        for k in range(16):
                            inst = t.matmul(ps_mm[m % 2], dtw_sb[:, k, m * 128:(m + 1) * 128],
                                            xc_fm[:, k, :], start=(k == 0), stop=(k == 15),
                                            skip_group_check=True)
                        tk = pe_s.push(inst)
                        assert tk == p_dt[m]
                if not SKIP_DT: bk.tensor(dtmm)

                def dtsp1(s):
                    for m in range(16):
                        pe_s.wait(s, p_dt[m])
                        if m >= 2:
                            ev_s.wait(s, d_add[m - 2])
                        ac_s.push(s.activation(sp_a2[m % 2][:, :], ps_mm[m % 2], AF.Abs,
                                               bias=dtb_sb[:, m:m + 1]))
                        ac_s.wait(s)
                        ac_s.push(s.activation(sp_a2[m % 2][:, :], sp_a2[m % 2][:, :], AF.Exp, scale=-1.0))
                        ac_s.wait(s)
                        tk = ac_s.push(s.activation(sp_a2[m % 2][:, :], sp_a2[m % 2][:, :], AF.Ln, bias=1.0))
                        assert tk == a_ln[m]
                if not SKIP_DT: bk.scalar(dtsp1)

                def dtsp2(v):
                    for m in range(16):
                        pe_s.wait(v, p_dt[m])
                        ac_s.wait(v, dtA0 + 3 * m + 1)
                        tk = ev_s.push(v.tensor_scalar(sp_m2[m % 2][:, :], ps_mm[m % 2],
                                                       dtb_sb[:, m:m + 1], 0.0, ALU.add, ALU.max))
                        assert tk == d_max[m]
                        ac_s.wait(v, a_ln[m])
                        ev_s.wait(v, d_max[m])
                        tk = ev_s.push(v.tensor_tensor(dt_fm[:, m, :], sp_m2[m % 2][:, :],
                                                       sp_a2[m % 2][:, :], ALU.add))
                        assert tk == d_add[m]
                if not SKIP_DT: bk.vector(dtsp2)

                def dtjoin(v):
                    if not SKIP_DT:
                        ev_s.wait(v)
                    ch.push(v.tensor_copy(lnm[:, :], lnm[:, :]))
                bk.vector(dtjoin)

                NGC = 256
                P0s, A0s, D0s = pe_s.n, ac_s.n, ev_s.n
                p_dtr, p_dxr, p_y = {}, {}, {}
                cnt = P0s
                for g in range(NGC):
                    if g >= 1:
                        cnt += 1; p_y[g - 1] = cnt
                    cnt += 1; p_dtr[g] = cnt
                    cnt += 1; p_dxr[g] = cnt
                cnt += 1; p_y[NGC - 1] = cnt
                a_dA, a_sg = {}, {}
                cnt = A0s
                for g in range(NGC):
                    cnt += 1; a_dA[g] = cnt
                    if g % 16 == 15:
                        cnt += 1; a_sg[g // 16] = cnt
                d_dtx, d_u, d_scan, d_carry, d_hc, d_t1, d_yg = {}, {}, {}, {}, {}, {}, {}
                cnt = D0s
                for g in range(NGC):
                    J, j = g // 16, g % 16
                    if j == 0:
                        cnt += 1; d_dtx[J] = cnt
                    cnt += 1; d_u[g] = cnt
                    cnt += 1; d_scan[g] = cnt
                    cnt += 1; d_carry[g] = cnt
                    cnt += 1; d_hc[g] = cnt
                    if j == 15:
                        cnt += 1; d_t1[J] = cnt
                        cnt += 1; d_yg[J] = cnt

                def sc_pe(t):
                    ch.wait(t)
                    for g in range(NGC):
                        J, j = g // 16, g % 16
                        if g >= 1:
                            gy = g - 1
                            Jy, jy = gy // 16, gy % 16
                            if jy == 0 and Jy >= 2:
                                ev_s.wait(t, d_yg[Jy - 2])
                            ev_s.wait(t, d_hc[gy])
                            tk = pe_s.push(t.matmul(ps_y[Jy % 2], red_sb[:, jy, :], hc_sb[gy % 2][:, :],
                                                    start=(jy == 0), stop=(jy == 15),
                                                    skip_group_check=True))
                            assert tk == p_y[gy]
                        if j == 0:
                            ev_s.wait(t, d_dtx[J])
                        if g >= 2:
                            ac_s.wait(t, a_dA[g - 2])
                            ev_s.wait(t, d_u[g - 2])
                        tk = pe_s.push(t.matmul(ps_dtr[g % 2], sel_sb[:, j, :], dt_fm[:, J, :],
                                                start=True, stop=True, skip_group_check=True))
                        assert tk == p_dtr[g]
                        tk = pe_s.push(t.matmul(ps_dxr[g % 2], sel_sb[:, j, :], dtxJ2[J % 2][:, :],
                                                start=True, stop=True, skip_group_check=True))
                        assert tk == p_dxr[g]
                    gy = NGC - 1
                    ev_s.wait(t, d_hc[gy])
                    tk = pe_s.push(t.matmul(ps_y[(gy // 16) % 2], red_sb[:, 15, :], hc_sb[gy % 2][:, :],
                                            start=False, stop=True, skip_group_check=True))
                    assert tk == p_y[gy]
                if not SKIP_SCAN: bk.tensor(sc_pe)

                def sc_ac(s, b=b):
                    for g in range(NGC):
                        J = g // 16
                        pe_s.wait(s, p_dtr[g])
                        if g >= 2:
                            ev_s.wait(s, d_scan[g - 2])
                        tk = ac_s.push(s.activation(dA_sb[g % 2][:, :], ps_dtr[g % 2], AF.Exp,
                                                    scale=ap_sb[:, g:g + 1]))
                        assert tk == a_dA[g]
                        if g % 16 == 15:
                            if J >= 2:
                                ev_s.wait(s, d_yg[J - 2])
                            tk = ac_s.push(s.activation(sg_sb[J % 2][:, :], gtc_fm[:, J, :], AF.Silu))
                            assert tk == a_sg[J]
                if not SKIP_SCAN: bk.scalar(sc_ac)

                def sc_dv(v):
                    for g in range(NGC):
                        J, j = g // 16, g % 16
                        if j == 0:
                            if J >= 2:
                                pe_s.wait(v, p_dxr[16 * (J - 2) + 15])
                            tk = ev_s.push(v.tensor_tensor(dtxJ2[J % 2][:, :], dt_fm[:, J, :],
                                                           xc_fm[:, J, :], ALU.mult))
                            assert tk == d_dtx[J]
                        pe_s.wait(v, p_dxr[g])
                        tk = ev_s.push(v.tensor_tensor(u_sb[:, :], ps_dxr[g % 2], brep[:, :], ALU.mult))
                        assert tk == d_u[g]
                        ac_s.wait(v, a_dA[g])
                        ev_s.wait(v, d_u[g])
                        tk = ev_s.push(v.tensor_tensor_scan(h_sb[:, :], dA_sb[g % 2][:, :], u_sb[:, :],
                                                            hcar[:, g:g + 1], ALU.mult, ALU.add))
                        assert tk == d_scan[g]
                        ev_s.wait(v, d_scan[g])
                        tk = ev_s.push(v.tensor_copy(hcar[:, g:g + 1], h_sb[:, TC - 1:TC]))
                        assert tk == d_carry[g]
                        if g >= 2:
                            pe_s.wait(v, p_y[g - 2])
                        tk = ev_s.push(v.tensor_tensor(hc_sb[g % 2][:, :], h_sb[:, :], crep[:, :], ALU.mult))
                        assert tk == d_hc[g]
                        if j == 15:
                            pe_s.wait(v, p_y[g])
                            ac_s.wait(v, a_sg[J])
                            tk = ev_s.push(v.scalar_tensor_tensor(t1_sb[:, :], xc_fm[:, J, :],
                                                                  dp_sb[:, J:J + 1], ps_y[J % 2],
                                                                  ALU.mult, ALU.add))
                            assert tk == d_t1[J]
                            ev_s.wait(v, d_t1[J])
                            tk = ev_s.push(v.tensor_tensor(yg_fm[:, J, :], t1_sb[:, :],
                                                           sg_sb[J % 2][:, :], ALU.mult))
                            assert tk == d_yg[J]
                if not SKIP_SCAN: bk.vector(sc_dv)

                def scjoin(v):
                    if not SKIP_SCAN:
                        ev_s.wait(v)
                    ch.push(v.tensor_copy(lnm[:, :], lnm[:, :]))
                bk.vector(scjoin)

                oP0, oE0 = pe_s.n, ev_s.n

                def omm(t, oP0=oP0, oE0=oE0):
                    ch.wait(t)
                    for m in range(8):
                        if m >= 2:
                            ev_s.wait(t, oE0 + (m - 2) + 1)
                        for k in range(16):
                            inst = t.matmul(ps_mm[m % 2], ow_sb[:, k, m * 128:(m + 1) * 128],
                                            yg_fm[:, k, :], start=(k == 0), stop=(k == 15),
                                            skip_group_check=True)
                        pe_s.push(inst)
                bk.tensor(omm)

                def oev(v, oP0=oP0, oE0=oE0):
                    for m in range(8):
                        pe_s.wait(v, oP0 + m + 1)
                        ev_s.push(v.tensor_scalar(of_fm[:, m, :], ps_mm[m % 2], ob_sb[:, m:m + 1],
                                                  None, ALU.add))
                bk.vector(oev)

                def ojoin(v):
                    ev_s.wait(v)
                    ch.push(v.tensor_copy(lnm[:, :], lnm[:, :]))
                bk.vector(ojoin)

                for tt in range(4):
                    def dmx(sy, b=b, t0=t0, tt=tt):
                        ch.wait(sy)
                        outs_h.wait(sy)
                        hch.push(sy.dma_start(xtile[:, :], x_in[b, t0 + tt * 128:t0 + (tt + 1) * 128, :]), 16)
                    bk.sync(dmx)

                    pe0, ev0 = pe_s.n, ev_s.n

                    def etr(t, tt=tt, pe0=pe0, ev0=ev0):
                        ch.wait(t)
                        for m in range(8):
                            if m >= 2:
                                ev_s.wait(t, ev0 + (m - 2) + 1)
                            pe_s.push(t.transpose(ps_tr[m % 2][:, 0:128],
                                                  of_fm[:, m, tt * 128:(tt + 1) * 128], id_bf[:, :]))
                    bk.tensor(etr)

                    def eev(v, tt=tt, pe0=pe0, ev0=ev0):
                        hch.wait(v)
                        for m in range(8):
                            pe_s.wait(v, pe0 + 1 + m)
                            ev_s.push(v.tensor_tensor(xtile[:, m * 128:(m + 1) * 128],
                                                      ps_tr[m % 2][:, 0:128],
                                                      xtile[:, m * 128:(m + 1) * 128], ALU.add))
                    bk.vector(eev)

                    def ejoin(v):
                        ev_s.wait(v)
                        ch.push(v.tensor_copy(lnm[:, :], lnm[:, :]))
                    bk.vector(ejoin)

                    def dmo(sy, b=b, t0=t0, tt=tt):
                        ch.wait(sy)
                        outs_h.push(sy.dma_start(x_out[b, t0 + tt * 128:t0 + (tt + 1) * 128, :],
                                                 xtile[:, :]), 16)
                    bk.sync(dmo)

                def fin(sy):
                    outs_h.wait(sy)
                bk.sync(fin)

    return nc


def _host_mamba(inputs):
    ident = np.eye(128, dtype=np.float32)
    A = np.asarray(inputs["L0_attn_A"])
    A_perm = np.ascontiguousarray(A.reshape(256, 128).T)
    SEL = np.zeros((128, 16, 128), np.float32)
    for r in range(128):
        for jj in range(16):
            for p in range(128):
                if r == 8 * jj + p // 16:
                    SEL[r, jj, p] = 1.0
    RED = np.ascontiguousarray(SEL.transpose(2, 1, 0))
    selbc = np.zeros((32, 2, 128), np.float32)
    for p in range(128):
        selbc[p % 16, 0, p] = 1.0
        selbc[16 + p % 16, 1, p] = 1.0
    conv_w = np.asarray(inputs["L0_attn_conv_w"]).reshape(4, DI)
    base = {
        "in_w": inputs["L0_attn_in_w"],
        "in_b": np.ascontiguousarray(np.asarray(inputs["L0_attn_in_b"]).reshape(32, 128).T),
        "conv_w": np.ascontiguousarray(conv_w.reshape(4, 16, 128).transpose(2, 1, 0)),
        "conv_b": np.ascontiguousarray(np.asarray(inputs["L0_attn_conv_b"]).reshape(16, 128).T),
        "A_perm": A_perm,
        "Dp_col": np.ascontiguousarray(np.asarray(inputs["L0_attn_Dp"]).reshape(16, 128).T),
        "dt_w": inputs["L0_attn_dt_w"],
        "dt_b": np.ascontiguousarray(np.asarray(inputs["L0_attn_dt_b"]).reshape(16, 128).T),
        "bc_w": np.ascontiguousarray(np.concatenate([np.asarray(inputs["L0_attn_B_w"]),
                                                     np.asarray(inputs["L0_attn_C_w"])], axis=1)),
        "bc_b": np.ascontiguousarray(np.concatenate([np.asarray(inputs["L0_attn_B_b"]),
                                                     np.asarray(inputs["L0_attn_C_b"])]).reshape(32, 1)),
        "out_w": inputs["L0_attn_out_w"],
        "out_b": np.ascontiguousarray(np.asarray(inputs["L0_attn_out_b"]).reshape(8, 128).T),
        "SEL": SEL,
        "RED": RED,
        "selbc": selbc,
        "ident": ident,
    }
    return base


def run_mamba(x, inputs):
    nc = _get("mamba", build_mamba)
    base = _host_mamba(inputs)
    in_maps = []
    for cc in range(NCORE):
        m = dict(base)
        m["x_in"] = np.ascontiguousarray(x[2 * cc:2 * cc + 2])
        in_maps.append(m)
    res = run_bass_kernel_spmd(nc, in_maps, core_ids=list(range(NCORE)))
    return np.concatenate([r["x_out"] for r in res.results], axis=0)




def build_ttt(fc=None):
    fc = fc or FC()
    nc = fc.nc if fc.nc is not None else bass.Bass("TRN2", target_bir_lowering=False, debug=False)
    CK = 128
    NCK = T // CK
    LR = 0.01
    ALPHA = 2.0 / DM
    LAM = 2.0 * LR / DM

    x_in = fc.dram(nc, "x_in", [BPC, T, DM], F32)
    thk_d = fc.dram(nc, "thK_T", [DM, DM], F32)
    thv_d = fc.dram(nc, "thV_T", [DM, DM], F32)
    thq_d = fc.dram(nc, "thQ_T", [DM, DM], F32)
    w0t_d = fc.dram(nc, "W0_T", [DM, DM], F32)
    b0_d = fc.dram(nc, "b0", [1, DM], F32)
    mst_d = fc.dram(nc, "Mstrict", [128, 128], F32)
    min_d = fc.dram(nc, "Mincl", [128, 128], F32)
    id_d = fc.dram(nc, "ident", [128, 128], F32)
    x_out = fc.dram(nc, "x_out", [BPC, T, DM], F32, kind="ExternalOutput")

    thk = fc.sb(nc, "t_thk", [128, 8, DM], BF16)
    thq = fc.sb(nc, "t_thq", [128, 8, DM], BF16)
    thvn = fc.sb(nc, "t_thvn", [128, 8, DM], BF16)
    wt = [fc.sb(nc, f"t_wt{b}", [128, 8, DM], BF16) for b in range(BPC)]
    bst = fc.sb(nc, "t_bst", [1, DM], F32)
    bst_bf = fc.sb(nc, "t_bstbf", [1, DM], BF16)
    mstr = fc.sb(nc, "t_mstr", [128, 128], BF16)
    mincl = fc.sb(nc, "t_mincl", [128, 128], BF16)
    id_bf = fc.sb(nc, "t_id", [128, 128], BF16)
    ones_r = fc.sb(nc, "t_ones_r", [1, 128], BF16)
    ones_c = fc.sb(nc, "t_ones_c", [128, 1], BF16)
    eps_col = fc.sb(nc, "t_eps", [128, 1], F32)
    kfm = fc.sb(nc, "t_kfm", [128, 8, CK], BF16)
    qfm = fc.sb(nc, "t_qfm", [128, 8, CK], BF16)
    ktm = fc.sb(nc, "t_ktm", [128, 8, 128], BF16)
    base_sb = fc.sb(nc, "t_base", [128, DM], BF16)
    gs_tm = fc.sb(nc, "t_gs", [128, DM], BF16)
    nmat = fc.sb(nc, "t_n", [128, 128], BF16)
    pmat = fc.sb(nc, "t_p", [128, 128], BF16)
    ptmat = fc.sb(nc, "t_pt", [128, 128], BF16)
    tmat = fc.sb(nc, "t_t", [128, 128], BF16)
    umat = fc.sb(nc, "t_u", [128, 128], BF16)
    xn_fm = fc.sb(nc, "t_xnfm", [128, 8, CK], BF16)
    xtile = fc.sb(nc, "t_xtile", [128, DM], F32)
    xn_tm = fc.sb(nc, "t_xntm", [128, DM], BF16)
    lnscr = fc.sb(nc, "t_lnscr", [128, DM], BF16)
    lnm = fc.sb(nc, "t_lnm", [128, 1], F32)
    lns = fc.sb(nc, "t_lns", [128, 1], F32)
    lnr = fc.sb(nc, "t_lnr", [128, 1], F32)
    o_tm = fc.sb(nc, "t_otm", [128, DM], F32)

    ps_big = fc.ps(nc, "t_psbig", [128, DM], F32)
    ps_sm = fc.ps(nc, "t_pssm", [128, 512], F32)
    ps_tr = [fc.ps(nc, f"t_pstr{i}", [128, 1024], BF16) for i in range(2)]

    ch = Chain(nc, "t_ch")
    pe_s = Chain(nc, "t_pe")
    ev_s = Chain(nc, "t_ev")
    dch = Chain(nc, "t_dch")
    hch = Chain(nc, "t_hch")
    outs = Chain(nc, "t_outs")

    with nc.Block() as bk:
        def gsec(g):
            dch.push(g.dma_start(thk[:, :, :], thk_d.rearrange("(k p) m -> p k m", p=128)), 16)
            dch.push(g.dma_start(thq[:, :, :], thq_d.rearrange("(k p) m -> p k m", p=128)), 16)
            dch.push(g.dma_start(thvn[:, :, :], thv_d.rearrange("(k p) m -> p k m", p=128)), 16)
            for b in range(BPC):
                dch.push(g.dma_start(wt[b][:, :, :], w0t_d.rearrange("(k p) m -> p k m", p=128)), 16)
            dch.push(g.dma_start(mstr[:, :], mst_d), 16)
            dch.push(g.dma_start(mincl[:, :], min_d), 16)
            dch.push(g.dma_start(id_bf[:, :], id_d), 16)
            g.memset(ones_r[:, :], 1.0)
            g.memset(ones_c[:, :], 1.0)
            g.memset(eps_col[:, :], EPS)
        bk.gpsimd(gsec)

        def vneg(s):
            dch.wait(s)
            for j in range(8):
                ch.push(s.activation(thvn[:, j, :], thvn[:, j, :], AF.Identity, scale=-1.0))
        bk.scalar(vneg)

    for b in range(BPC):
        with nc.Block() as bk:
            def binit(g, b=b):
                dch.push(g.dma_start(bst[:, :], b0_d), 16)
            bk.gpsimd(binit)

            def binit2(v):
                dch.wait(v)
                ch.push(v.tensor_copy(bst_bf[:, :], bst[:, :]))
            bk.vector(binit2)

        for c in range(NCK):
            t0 = c * CK
            with nc.Block() as bk:
                def dmain(sy, b=b, t0=t0):
                    ch.wait(sy)
                    hch.push(sy.dma_start(xtile[:, :], x_in[b, t0:t0 + CK, :]), 16)
                bk.sync(dmain)

                def vstart(v):
                    dch.wait(v)
                    hch.wait(v)
                    ch.push(v.tensor_copy(lnm[:, :], eps_col[:, :]))
                bk.vector(vstart)

                _ln_emit(nc, bk, ch, xtile, xn_tm, lnscr, lnm, lns, lnr, eps_col)

                pe0, ev0 = pe_s.n, ev_s.n

                def trans(t, pe0=pe0, ev0=ev0):
                    ch.wait(t)
                    for j in range(8):
                        if j >= 2:
                            ev_s.wait(t, ev0 + (j - 2) + 1)
                        pe_s.push(t.transpose(ps_tr[j % 2][:, 0:128],
                                              xn_tm[:, j * 128:(j + 1) * 128], id_bf[:, :]))
                bk.tensor(trans)

                def evac(v, pe0=pe0, ev0=ev0):
                    for j in range(8):
                        pe_s.wait(v, pe0 + 1 + j)
                        ev_s.push(v.tensor_copy(xn_fm[:, j, :], ps_tr[j % 2][:, 0:128]))
                bk.vector(evac)

                def joinv(v):
                    ev_s.wait(v)
                    ch.push(v.tensor_copy(lnm[:, :], lnm[:, :]))
                bk.vector(joinv)

                for dst, th in [(kfm, thk), (qfm, thq)]:
                    for i in range(8):
                        def pmm(t, dst=dst, th=th, i=i):
                            ch.wait(t)
                            for k in range(8):
                                inst = t.matmul(ps_sm[:, 0:128], th[:, k, i * 128:(i + 1) * 128],
                                                xn_fm[:, k, :], start=(k == 0), stop=(k == 7))
                            ch.push(inst)
                        bk.tensor(pmm)

                        def pev(v, dst=dst, i=i):
                            ch.wait(v)
                            ch.push(v.tensor_copy(dst[:, i, :], ps_sm[:, 0:128]))
                        bk.vector(pev)

                def basemm(t, b=b):
                    ch.wait(t)
                    for k in range(8):
                        for hh in range(2):
                            t.matmul(ps_big[:, hh * 512:(hh + 1) * 512], kfm[:, k, :],
                                     wt[b][:, k, hh * 512:(hh + 1) * 512],
                                     start=(k == 0), stop=False)
                    for hh in range(2):
                        t.matmul(ps_big[:, hh * 512:(hh + 1) * 512], ones_r[:, :],
                                 bst_bf[0:1, hh * 512:(hh + 1) * 512], start=False, stop=False)
                    for k in range(8):
                        for hh in range(2):
                            inst = t.matmul(ps_big[:, hh * 512:(hh + 1) * 512], xn_fm[:, k, :],
                                            thvn[:, k, hh * 512:(hh + 1) * 512],
                                            start=False, stop=(k == 7))
                    ch.push(inst)
                bk.tensor(basemm)

                def baseev(s):
                    ch.wait(s)
                    ch.push(s.activation(base_sb[:, :], ps_big[:, :], AF.Identity, scale=ALPHA))
                bk.scalar(baseev)

                def kkmm(t):
                    ch.wait(t)
                    for k in range(8):
                        t.matmul(ps_sm[:, 0:128], kfm[:, k, :], kfm[:, k, :],
                                 start=(k == 0), stop=False)
                    inst = t.matmul(ps_sm[:, 0:128], ones_r[:, :],
                                    ones_r[:, :], start=False, stop=True)
                    ch.push(inst)
                bk.tensor(kkmm)

                def nev(v):
                    ch.wait(v)
                    ch.push(v.scalar_tensor_tensor(nmat[:, :], ps_sm[:, 0:128], -LAM,
                                                   mstr[:, :], ALU.mult, ALU.mult))
                bk.vector(nev)

                def tinit(v):
                    ch.wait(v)
                    t1 = ch.push(v.tensor_tensor(tmat[:, :], nmat[:, :], id_bf[:, :], ALU.add))
                    ch.wait(v, t1)
                    ch.push(v.tensor_copy(pmat[:, :], nmat[:, :]))
                bk.vector(tinit)

                for it in range(6):
                    def sq1(t, it=it):
                        ch.wait(t)
                        ch.push(t.transpose(ps_tr[0][:, 0:128], pmat[:, :], id_bf[:, :]))
                    bk.tensor(sq1)

                    def sq2(v, it=it):
                        ch.wait(v)
                        ch.push(v.tensor_copy(ptmat[:, :], ps_tr[0][:, 0:128]))
                    bk.vector(sq2)

                    def sq3(t, it=it):
                        ch.wait(t)
                        ch.push(t.matmul(ps_sm[:, 0:128], ptmat[:, :], pmat[:, :],
                                         start=True, stop=True))
                        ch.push(t.matmul(ps_sm[:, 128:256], ptmat[:, :], tmat[:, :],
                                         start=True, stop=True))
                    bk.tensor(sq3)

                    def sq4(v, it=it):
                        ch.wait(v)
                        t1 = ch.push(v.tensor_copy(pmat[:, :], ps_sm[:, 0:128]))
                        ch.wait(v, t1)
                        ch.push(v.tensor_tensor(tmat[:, :], tmat[:, :], ps_sm[:, 128:256], ALU.add))
                    bk.vector(sq4)

                def kqmm(t):
                    ch.wait(t)
                    for k in range(8):
                        t.matmul(ps_sm[:, 0:128], kfm[:, k, :], qfm[:, k, :],
                                 start=(k == 0), stop=False)
                    inst = t.matmul(ps_sm[:, 0:128], ones_r[:, :], ones_r[:, :],
                                    start=False, stop=True)
                    ch.push(inst)
                bk.tensor(kqmm)

                def uev(v):
                    ch.wait(v)
                    ch.push(v.tensor_tensor(umat[:, :], ps_sm[:, 0:128], mincl[:, :], ALU.mult))
                bk.vector(uev)

                def gmm(t):
                    ch.wait(t)
                    for hh in range(2):
                        inst = t.matmul(ps_big[:, hh * 512:(hh + 1) * 512], tmat[:, :],
                                        base_sb[:, hh * 512:(hh + 1) * 512], start=True, stop=True)
                    ch.push(inst)
                bk.tensor(gmm)

                def gev(s):
                    ch.wait(s)
                    ch.push(s.activation(gs_tm[:, :], ps_big[:, :], AF.Identity, scale=-LR))
                bk.scalar(gev)

                def omm(t, b=b):
                    ch.wait(t)
                    for k in range(8):
                        for hh in range(2):
                            t.matmul(ps_big[:, hh * 512:(hh + 1) * 512], qfm[:, k, :],
                                     wt[b][:, k, hh * 512:(hh + 1) * 512],
                                     start=(k == 0), stop=False)
                    for hh in range(2):
                        t.matmul(ps_big[:, hh * 512:(hh + 1) * 512], ones_r[:, :],
                                 bst_bf[0:1, hh * 512:(hh + 1) * 512], start=False, stop=False)
                    for hh in range(2):
                        inst = t.matmul(ps_big[:, hh * 512:(hh + 1) * 512], umat[:, :],
                                        gs_tm[:, hh * 512:(hh + 1) * 512], start=False, stop=True)
                    ch.push(inst)
                bk.tensor(omm)

                def dmx(sy, b=b, t0=t0):
                    ch.wait(sy)
                    outs.wait(sy)
                    hch.push(sy.dma_start(xtile[:, :], x_in[b, t0:t0 + CK, :]), 16)
                bk.sync(dmx)

                def oev(v):
                    ch.wait(v)
                    hch.wait(v)
                    ch.push(v.tensor_tensor(o_tm[:, :], ps_big[:, :], xtile[:, :], ALU.add))
                bk.vector(oev)

                def dmo(e, b=b, t0=t0):
                    ch.wait(e)
                    outs.push(e.dma_start(x_out[b, t0:t0 + CK, :], o_tm[:, :]), 16)
                if fc.out_swdge:
                    bk.gpsimd(dmo)
                else:
                    bk.sync(dmo)

                pe0, ev0 = pe_s.n, ev_s.n

                def ktr(t, pe0=pe0, ev0=ev0):
                    ch.wait(t)
                    for i in range(8):
                        if i >= 2:
                            ev_s.wait(t, ev0 + (i - 2) + 1)
                        pe_s.push(t.transpose(ps_tr[i % 2][:, 0:128], kfm[:, i, :], id_bf[:, :]))
                bk.tensor(ktr)

                def kev(v, pe0=pe0, ev0=ev0):
                    for i in range(8):
                        pe_s.wait(v, pe0 + 1 + i)
                        ev_s.push(v.tensor_copy(ktm[:, i, :], ps_tr[i % 2][:, 0:128]))
                bk.vector(kev)

                def kjoin(v):
                    ev_s.wait(v)
                    ch.push(v.tensor_copy(lnm[:, :], lnm[:, :]))
                bk.vector(kjoin)

                for i in range(8):
                    def dwmm(t, i=i):
                        ch.wait(t)
                        for hh in range(2):
                            inst = t.matmul(ps_big[:, hh * 512:(hh + 1) * 512], ktm[:, i, :],
                                            gs_tm[:, hh * 512:(hh + 1) * 512], start=True, stop=True)
                        ch.push(inst)
                    bk.tensor(dwmm)

                    def dwup(v, b=b, i=i):
                        ch.wait(v)
                        ch.push(v.tensor_tensor(wt[b][:, i, :], ps_big[:, :], wt[b][:, i, :], ALU.add))
                    bk.vector(dwup)

                for hh in range(2):
                    def bmm(t, hh=hh):
                        ch.wait(t)
                        ch.push(t.matmul(ps_sm[0:1, :], ones_c[:, :],
                                         gs_tm[:, hh * 512:(hh + 1) * 512], start=True, stop=True))
                    bk.tensor(bmm)

                    def bup(v, hh=hh):
                        ch.wait(v)
                        ch.push(v.tensor_tensor(bst[0:1, hh * 512:(hh + 1) * 512], ps_sm[0:1, :],
                                                bst[0:1, hh * 512:(hh + 1) * 512], ALU.add))
                    bk.vector(bup)

                def bbf(v):
                    ch.wait(v)
                    ch.push(v.tensor_copy(bst_bf[:, :], bst[:, :]))
                bk.vector(bbf)

                def fin(sy):
                    outs.wait(sy)
                bk.sync(fin)

    return nc


def _host_ttt(inputs):
    ident = np.eye(128, dtype=np.float32)
    ii, jj = np.meshgrid(np.arange(128), np.arange(128), indexing="ij")
    Mstrict = (ii < jj).astype(np.float32)
    Mincl = (ii <= jj).astype(np.float32)
    base = {
        "thK_T": np.ascontiguousarray(np.asarray(inputs["L1_attn_theta_K"]).T),
        "thV_T": np.ascontiguousarray(np.asarray(inputs["L1_attn_theta_V"]).T),
        "thQ_T": np.ascontiguousarray(np.asarray(inputs["L1_attn_theta_Q"]).T),
        "W0_T": np.ascontiguousarray(np.asarray(inputs["L1_attn_W0"]).T),
        "b0": np.ascontiguousarray(np.asarray(inputs["L1_attn_b0"]).reshape(1, DM)),
        "Mstrict": Mstrict,
        "Mincl": Mincl,
        "ident": ident,
    }
    return base


def run_ttt(x, inputs):
    nc = _get("ttt", build_ttt)
    base = _host_ttt(inputs)
    in_maps = []
    for cc in range(NCORE):
        m = dict(base)
        m["x_in"] = np.ascontiguousarray(x[2 * cc:2 * cc + 2])
        in_maps.append(m)
    res = run_bass_kernel_spmd(nc, in_maps, core_ids=list(range(NCORE)))
    return np.concatenate([r["x_out"] for r in res.results], axis=0)




def build_lstm(fc=None):
    fc = fc or FC()
    nc = fc.nc if fc.nc is not None else bass.Bass("TRN2", target_bir_lowering=False, debug=False)
    NB = 16
    TS = 256
    if fc.lstm_gather:
        agout_d = fc.dmap["agout"]
        tsel_d = fc.dram(nc, "tsel", [128, 32], mybir.dt.uint32)
    else:
        x_in = fc.dram(nc, "x_in", [NB, TS, DM], F32)
    aw_d = fc.dram(nc, "aw", [6, DM, DM], F32)
    ow2_d = fc.dram(nc, "ow2", [2, DM, DM], F32)
    gb_d = fc.dram(nc, "gb", [128, 3, 8], F32)
    bo_d = fc.dram(nc, "bo", [128, 8], F32)
    ic_d = fc.dram(nc, "in_c", [128, 8], F32)
    id_d = fc.dram(nc, "ident", [128, 128], F32)
    x_out = fc.dram(nc, "x_out", [NB, TS, DM], F32, kind="ExternalOutput")

    AW = fc.sb(nc, "l_AW", [128, 6, 8, DM], BF16)
    wv = [AW[:, i, :, :] for i in range(6)]
    xn3 = fc.sb(nc, "l_xn3", [128, 8, NB * TS], BF16)
    c_fm = fc.sb(nc, "l_c", [128, 8, TS], F32)
    c_bf = [fc.sb(nc, f"l_cbf{i}", [128, 8, TS], BF16) for i in range(2)]
    f_sb = fc.sb(nc, "l_f", [128, TS], BF16)
    i_sb = fc.sb(nc, "l_i", [128, TS], BF16)
    g_sb = fc.sb(nc, "l_g", [128, TS], BF16)
    tt1 = fc.sb(nc, "l_tt1", [128, TS], F32)
    tt2 = fc.sb(nc, "l_tt2", [128, TS], F32)
    gb_sb = fc.sb(nc, "l_gb", [128, 3, 8], F32)
    bo_sb = fc.sb(nc, "l_bo", [128, 8], F32)
    ic_sb = fc.sb(nc, "l_ic", [128, 8], F32)
    onef = fc.sb(nc, "l_onef", [128, TS], F32)
    id_bf = fc.sb(nc, "l_id", [128, 128], BF16)
    eps_col = fc.sb(nc, "l_eps", [128, 1], F32)
    h_fm = fc.sb(nc, "l_hfm", [128, 8, TS], BF16)
    xtile = fc.sb(nc, "l_xtile", [128, DM], F32)
    xn_tm = fc.sb(nc, "l_xntm", [128, DM], BF16)
    lnscr = fc.sb(nc, "l_lnscr", [128, DM], BF16)
    lnm = fc.sb(nc, "l_lnm", [128, 1], F32)
    lns = fc.sb(nc, "l_lns", [128, 1], F32)
    lnr = fc.sb(nc, "l_lnr", [128, 1], F32)
    o_tm = fc.sb(nc, "l_otm", [128, DM], F32)
    if fc.lstm_gather:
        xg_bf = fc.sb(nc, "l_xgbf", [128, DM], BF16)
        tsel_sb = fc.sb(nc, "l_tsel", [128, 32], mybir.dt.uint32)

    ps_f = fc.ps(nc, "l_psf", [128, 512], F32)
    ps_i = fc.ps(nc, "l_psi", [128, 512], F32)
    ps_g = fc.ps(nc, "l_psg", [128, 512], F32)
    ps_tr = [fc.ps(nc, f"l_pstr{i}", [128, 1024], BF16) for i in range(2)]

    ch = Chain(nc, "l_ch")
    pe_s = Chain(nc, "l_pe")
    ev_s = Chain(nc, "l_ev")
    dch = Chain(nc, "l_dch")
    hch = Chain(nc, "l_hch")
    wdma = Chain(nc, "l_wdma")
    outs = Chain(nc, "l_outs")

    with nc.Block() as bk:
        def gsec(g):
            wdma.push(g.dma_start(AW[:, :, :, :],
                                  aw_d.rearrange("w (k p) m -> p w k m", p=128)), 16)
            dch.push(g.dma_start(gb_sb[:, :, :], gb_d), 16)
            dch.push(g.dma_start(bo_sb[:, :], bo_d), 16)
            dch.push(g.dma_start(ic_sb[:, :], ic_d), 16)
            dch.push(g.dma_start(id_bf[:, :], id_d), 16)
            g.memset(eps_col[:, :], EPS)
            g.memset(onef[:, :], 1.0)
            if fc.lstm_gather:
                dch.push(g.dma_start(tsel_sb[:, :], tsel_d), 16)
        bk.gpsimd(gsec)

        def cinit(s):
            dch.wait(s)
            for m in range(8):
                ch.push(s.activation(c_fm[:, m, :], onef[:, :], AF.Identity,
                                     scale=ic_sb[:, m:m + 1]))
        bk.scalar(cinit)

        def cbf0(v):
            ch.wait(v)
            ch.push(v.tensor_copy(c_bf[0][:, :, :], c_fm[:, :, :]))
        bk.vector(cbf0)

    for b in range(16):
        for tt in range(2):
            with nc.Block() as bk:
                if fc.lstm_gather:
                    def dmain(g, b=b, tt=tt):
                        ch.wait(g)
                        dch.push(g.indirect_dma_start(
                            xg_bf[:, :], None, agout_d,
                            bass.IndirectOffsetOnAxis(ap=tsel_sb[:, b * 2 + tt:b * 2 + tt + 1],
                                                      axis=0)), 16)
                    bk.gpsimd(dmain)

                    def vstart(v):
                        dch.wait(v)
                        ch.push(v.tensor_copy(xtile[:, :], xg_bf[:, :]))
                    bk.vector(vstart)
                else:
                    def dmain(sy, b=b, tt=tt):
                        ch.wait(sy)
                        hch.push(sy.dma_start(xtile[:, :], x_in[b, tt * 128:(tt + 1) * 128, :]), 16)
                    bk.sync(dmain)

                    def vstart(v):
                        dch.wait(v)
                        hch.wait(v)
                        ch.push(v.tensor_copy(lnm[:, :], eps_col[:, :]))
                    bk.vector(vstart)

                _ln_emit(nc, bk, ch, xtile, xn_tm, lnscr, lnm, lns, lnr, eps_col)

                pe0, ev0 = pe_s.n, ev_s.n

                def trans(t, pe0=pe0, ev0=ev0):
                    ch.wait(t)
                    for j in range(8):
                        if j >= 2:
                            ev_s.wait(t, ev0 + (j - 2) + 1)
                        pe_s.push(t.transpose(ps_tr[j % 2][:, 0:128],
                                              xn_tm[:, j * 128:(j + 1) * 128], id_bf[:, :]))
                bk.tensor(trans)

                def evac(v, b=b, tt=tt, pe0=pe0, ev0=ev0):
                    col = b * TS + tt * 128
                    for j in range(8):
                        pe_s.wait(v, pe0 + 1 + j)
                        ev_s.push(v.tensor_copy(xn3[:, j, col:col + 128], ps_tr[j % 2][:, 0:128]))
                bk.vector(evac)

                def joinv(v):
                    ev_s.wait(v)
                    ch.push(v.tensor_copy(lnm[:, :], lnm[:, :]))
                bk.vector(joinv)

    for b in range(16):
        col = b * TS
        with nc.Block() as bk:
            for m in range(8):
                def gmm(t, b=b, m=m, col=col):
                    wdma.wait(t)
                    ch.wait(t)
                    for (ps, wi_, ui_) in [(ps_f, 0, 1), (ps_i, 2, 3), (ps_g, 4, 5)]:
                        for k in range(8):
                            t.matmul(ps[:, 0:TS], wv[wi_][:, k, m * 128:(m + 1) * 128],
                                     xn3[:, k, col:col + TS], start=(k == 0), stop=False)
                        for k in range(8):
                            inst = t.matmul(ps[:, 0:TS], wv[ui_][:, k, m * 128:(m + 1) * 128],
                                            c_bf[b % 2][:, k, :], start=False, stop=(k == 7))
                    ch.push(inst)
                bk.tensor(gmm)

                def gact(s, m=m):
                    ch.wait(s)
                    t1 = ch.push(s.activation(f_sb[:, :], ps_f[:, 0:TS], AF.Sigmoid,
                                              bias=gb_sb[:, 0, m:m + 1]))
                    t2 = ch.push(s.activation(i_sb[:, :], ps_i[:, 0:TS], AF.Sigmoid,
                                              bias=gb_sb[:, 1, m:m + 1]))
                    ch.push(s.activation(g_sb[:, :], ps_g[:, 0:TS], AF.Tanh,
                                         bias=gb_sb[:, 2, m:m + 1]))
                bk.scalar(gact)

                def cup(v, b=b, m=m):
                    ch.wait(v)
                    t1 = ch.push(v.tensor_tensor(tt1[:, :], f_sb[:, :], c_fm[:, m, :], ALU.mult))
                    t2 = ch.push(v.tensor_tensor(tt2[:, :], i_sb[:, :], g_sb[:, :], ALU.mult))
                    ch.wait(v, t2)
                    t3 = ch.push(v.tensor_tensor(c_fm[:, m, :], tt1[:, :], tt2[:, :], ALU.add))
                    ch.wait(v, t3)
                    ch.push(v.tensor_copy(c_bf[(b + 1) % 2][:, m, :], c_fm[:, m, :]))
                bk.vector(cup)

    with nc.Block() as bk:
        def wload2(g):
            wdma.push(g.dma_start(AW[:, 0:2, :, :],
                                  ow2_d.rearrange("w (k p) m -> p w k m", p=128)), 16)
        bk.gpsimd(wload2)

    for b in range(16):
        col = b * TS
        with nc.Block() as bk:
            for m in range(8):
                def hmm(t, b=b, m=m, col=col):
                    wdma.wait(t)
                    ch.wait(t)
                    for k in range(8):
                        t.matmul(ps_f[:, 0:TS], wv[0][:, k, m * 128:(m + 1) * 128],
                                 xn3[:, k, col:col + TS], start=(k == 0), stop=False)
                    for k in range(8):
                        inst = t.matmul(ps_f[:, 0:TS], wv[1][:, k, m * 128:(m + 1) * 128],
                                        c_bf[0][:, k, :], start=False, stop=(k == 7))
                    ch.push(inst)
                bk.tensor(hmm)

                def hact(s, m=m):
                    ch.wait(s)
                    ch.push(s.activation(h_fm[:, m, :], ps_f[:, 0:TS], AF.Tanh,
                                         bias=bo_sb[:, m:m + 1]))
                bk.scalar(hact)

            for tt in range(2):
                if fc.lstm_gather:
                    def dmx(g, b=b, tt=tt):
                        ch.wait(g)
                        outs.wait(g)
                        dch.push(g.indirect_dma_start(
                            xg_bf[:, :], None, agout_d,
                            bass.IndirectOffsetOnAxis(ap=tsel_sb[:, b * 2 + tt:b * 2 + tt + 1],
                                                      axis=0)), 16)
                    bk.gpsimd(dmx)

                    def vcast(v):
                        dch.wait(v)
                        ch.push(v.tensor_copy(xtile[:, :], xg_bf[:, :]))
                    bk.vector(vcast)
                else:
                    def dmx(sy, b=b, tt=tt):
                        ch.wait(sy)
                        outs.wait(sy)
                        hch.push(sy.dma_start(xtile[:, :], x_in[b, tt * 128:(tt + 1) * 128, :]), 16)
                    bk.sync(dmx)

                pe0, ev0 = pe_s.n, ev_s.n

                def etr(t, tt=tt, pe0=pe0, ev0=ev0):
                    ch.wait(t)
                    for m in range(8):
                        if m >= 2:
                            ev_s.wait(t, ev0 + (m - 2) + 1)
                        pe_s.push(t.transpose(ps_tr[m % 2][:, 0:128],
                                              h_fm[:, m, tt * 128:(tt + 1) * 128], id_bf[:, :]))
                bk.tensor(etr)

                def eev(v, tt=tt, pe0=pe0, ev0=ev0):
                    if not fc.lstm_gather:
                        hch.wait(v)
                    else:
                        ch.wait(v)
                    for m in range(8):
                        pe_s.wait(v, pe0 + 1 + m)
                        ev_s.push(v.tensor_tensor(o_tm[:, m * 128:(m + 1) * 128],
                                                  ps_tr[m % 2][:, 0:128],
                                                  xtile[:, m * 128:(m + 1) * 128], ALU.add))
                bk.vector(eev)

                def ejoin(v):
                    ev_s.wait(v)
                    ch.push(v.tensor_copy(lnm[:, :], lnm[:, :]))
                bk.vector(ejoin)

                def dmo(sy, b=b, tt=tt):
                    ch.wait(sy)
                    outs.push(sy.dma_start(x_out[b, tt * 128:(tt + 1) * 128, :], o_tm[:, :]), 16)
                bk.sync(dmo)

            def fin(sy):
                outs.wait(sy)
            bk.sync(fin)

    return nc


def _host_lstm(inputs):
    ident = np.eye(128, dtype=np.float32)
    aw = np.stack([np.asarray(inputs["L1_ffn_" + n]).T for n in
                   ["wf", "uf", "wi", "ui", "wc", "uc"]])
    ow2 = np.stack([np.asarray(inputs["L1_ffn_wo"]).T, np.asarray(inputs["L1_ffn_uo"]).T])
    gb = np.stack([np.asarray(inputs["L1_ffn_" + n]).reshape(8, 128).T for n in
                   ["bf", "bi", "bc"]], axis=1)
    base = {
        "aw": np.ascontiguousarray(aw),
        "ow2": np.ascontiguousarray(ow2),
        "gb": np.ascontiguousarray(gb),
        "bo": np.ascontiguousarray(np.asarray(inputs["L1_ffn_bo"]).reshape(8, 128).T),
        "in_c": np.ascontiguousarray(np.asarray(inputs["L1_ffn_in_c"]).reshape(8, 128).T),
        "ident": ident,
    }
    return base


def run_lstm(x, inputs):
    nc = _get("lstm", build_lstm)
    base = _host_lstm(inputs)
    in_maps = []
    for cc in range(NCORE):
        m = dict(base)
        m["x_in"] = np.ascontiguousarray(x[:, 256 * cc:256 * (cc + 1), :])
        in_maps.append(m)
    res = run_bass_kernel_spmd(nc, in_maps, core_ids=list(range(NCORE)))
    return np.concatenate([r["x_out"] for r in res.results], axis=1)


# version 04
# speedup vs baseline: 1.2728x; 1.0682x over previous
import numpy as np
import concourse.bass as bass
import concourse.mybir as mybir
from concourse.bass_utils import run_bass_kernel_spmd

F32 = mybir.dt.float32
BF16 = mybir.dt.bfloat16
AF = mybir.ActivationFunctionType
ALU = mybir.AluOpType
AX = mybir.AxisListType

NCORE = 8
BPC = 2
T = 2048
DM = 1024
DI = 2048
DS = 16
TOK = BPC * T
EPS = 1e-5


class Chain:

    def __init__(self, nc, name):
        self.h = nc.alloc_semaphore(name)
        self.n = 0

    def push(self, inst, k=1):
        inst.then_inc(self.h, k)
        self.n += k
        return self.n

    def wait(self, eng, tok=None):
        eng.wait_ge(self.h, self.n if tok is None else tok)




class FC:

    def __init__(self, nc=None, prefix="", dmap=None, pmap=None, stack=None,
                 lstm_gather=False, out_swdge=False):
        self.nc = nc
        self.prefix = prefix
        self.dmap = dmap or {}
        self.pmap = pmap or {}
        self.stack = stack
        self.lstm_gather = lstm_gather
        self.out_swdge = out_swdge

    def dram(self, nc, name, shape, dtype, kind="ExternalInput"):
        if name in self.dmap:
            return self.dmap[name]
        return nc.dram_tensor(self.prefix + name, shape, dtype, kind=kind).ap()

    def sb(self, nc, name, shape, dtype):
        if self.stack is not None:
            return self.stack.enter_context(nc.sbuf_tensor(self.prefix + name, shape, dtype))
        return nc.alloc_sbuf_tensor(name, shape, dtype)

    def ps(self, nc, name, shape, dtype):
        if name in self.pmap:
            return self.pmap[name]
        return nc.alloc_psum_tensor(name, shape, dtype)

def _ln_emit(nc, bk, ch, xtile, xn_tm, lnscr, lnm, lns, lnr, eps_col=None):

    def vpart1(v):
        ch.wait(v)
        t1 = ch.push(v.tensor_reduce(lnm[:, :], xtile[:, :], AX.X, ALU.add))
        ch.wait(v, t1)
        t2 = ch.push(v.tensor_scalar(lnm[:, :], lnm[:, :], 1.0 / DM, None, ALU.mult))
        ch.wait(v, t2)
        ch.push(v.tensor_scalar(xtile[:, :], xtile[:, :], lnm[:, :], None, ALU.subtract))
    bk.vector(vpart1)

    def spart(s):
        ch.wait(s)
        t1 = ch.push(s.activation(lnscr[:, :], xtile[:, :], AF.Square, accum_out=lns[:, :]))
        ch.wait(s, t1)
        ch.push(s.activation(lnr[:, :], lns[:, :], AF.Sqrt, scale=1.0 / DM, bias=eps_col[:, :]))
    bk.scalar(spart)

    def vpart2(v):
        ch.wait(v)
        t1 = ch.push(v.reciprocal(lnr[:, :], lnr[:, :]))
        ch.wait(v, t1)
        ch.push(v.tensor_scalar(xn_tm[:, :], xtile[:, :], lnr[:, :], None, ALU.mult))
    bk.vector(vpart2)


def build_moe(dbg=False, fc=None):
    fc = fc or FC()
    nc = fc.nc if fc.nc is not None else bass.Bass("TRN2", target_bir_lowering=False, debug=False)
    x_in = fc.dram(nc, "x_in", [TOK, DM], F32)
    wr_d = fc.dram(nc, "wr", [DM, 8], F32)
    wg_d = fc.dram(nc, "wg", [8, DM, DI], F32)
    bg_d = fc.dram(nc, "bg", [128, 16, 8], F32)
    wu_d = fc.dram(nc, "wu", [8, DM, DI], F32)
    bu_d = fc.dram(nc, "bu", [128, 16, 8], F32)
    wo_d = fc.dram(nc, "wo", [8, DI, DM], F32)
    bo_d = fc.dram(nc, "bo", [8, DM], F32)
    id_d = fc.dram(nc, "ident", [128, 128], F32)
    x_out = fc.dram(nc, "x_out", [TOK, DM], F32, kind="ExternalOutput")
    if dbg:
        d_xq = fc.dram(nc, "d_xq", [128, 8, 1024], F32, kind="ExternalOutput")
        d_w = fc.dram(nc, "d_w", [128, 8, 8], F32, kind="ExternalOutput")
        d_acc0 = fc.dram(nc, "d_acc0", [128, 8, DM], F32, kind="ExternalOutput")
        d_acc = fc.dram(nc, "d_acc", [128, 8, DM], F32, kind="ExternalOutput")
        d_hw = fc.dram(nc, "d_hw", [128, 16, 512], F32, kind="ExternalOutput")
        d_acc1 = fc.dram(nc, "d_acc1", [128, 8, DM], F32, kind="ExternalOutput")

    QT = 1024
    NQ = TOK // QT
    NTT = QT // 128

    wr_sb = fc.sb(nc, "wr_sb", [128, 8, 8], BF16)
    bg_sb = fc.sb(nc, "bg_sb", [128, 16, 8], F32)
    bu_sb = fc.sb(nc, "bu_sb", [128, 16, 8], F32)
    bo_sb = fc.sb(nc, "bo_sb", [8, DM], BF16)
    id_bf = fc.sb(nc, "id_bf", [128, 128], BF16)
    xq_fm = fc.sb(nc, "xq_fm", [128, 8, QT], BF16)
    w_tm = fc.sb(nc, "w_tm", [128, NTT, 8], F32)
    w_fmT = fc.sb(nc, "w_fmT", [8, QT], BF16)
    acc = fc.sb(nc, "acc", [128, NTT, DM], F32)
    wg_sb = fc.sb(nc, "wg_sb", [128, 8, DI], BF16)
    wu_sb = fc.sb(nc, "wu_sb", [128, 8, DI], BF16)
    wo_sb = fc.sb(nc, "wo_sb", [128, 16, DM], BF16)
    hw_sb = fc.sb(nc, "hw_sb", [128, 16, 512], BF16)
    gs_sb = [fc.sb(nc, f"gs_sb{i}", [128, 512], BF16) for i in range(2)]
    xtile = fc.sb(nc, "xtile", [128, DM], F32)
    xn_tm = fc.sb(nc, "xn_tm", [128, DM], BF16)
    lnscr = fc.sb(nc, "lnscr", [128, DM], BF16)
    lnm = fc.sb(nc, "lnm", [128, 1], F32)
    lns = fc.sb(nc, "lns", [128, 1], F32)
    lnr = fc.sb(nc, "lnr", [128, 1], F32)
    r_pr = fc.sb(nc, "r_pr", [128, 8], F32)
    r_m1 = fc.sb(nc, "r_m1", [128, 1], F32)
    r_m2 = fc.sb(nc, "r_m2", [128, 1], F32)
    r_k1 = fc.sb(nc, "r_k1", [128, 8], F32)
    r_k2 = fc.sb(nc, "r_k2", [128, 8], F32)
    r_t = fc.sb(nc, "r_t", [128, 8], F32)
    r_den = fc.sb(nc, "r_den", [128, 1], F32)
    w_bf = fc.sb(nc, "w_bf", [128, 8], BF16)
    eps_col = fc.sb(nc, "eps_col", [128, 1], F32)
    o_tm = fc.sb(nc, "o_tm", [128, DM], F32)

    ps_tr = [fc.ps(nc, f"ps_tr{i}", [128, 1024], BF16) for i in range(2)]
    ps_g = fc.ps(nc, "ps_g", [128, 512], F32)
    ps_u = fc.ps(nc, "ps_u", [128, 512], F32)
    ps_o0 = fc.ps(nc, "ps_o0", [128, DM], F32)
    ps_o1 = fc.ps(nc, "ps_o1", [128, DM], F32)
    ps_o = [ps_o0, ps_o0]
    ps_gp = [ps_g[:, :], ps_o1[:, 0:512]]
    ps_up = [ps_u[:, :], ps_o1[:, 512:1024]]
    ps_r = ps_o1[:, 0:8]

    ch = Chain(nc, "ch")
    pe_s = Chain(nc, "pe_s")
    ev_s = Chain(nc, "ev_s")
    ac_s = Chain(nc, "ac_s")
    o_dg = Chain(nc, "o_dg")
    o_dh = Chain(nc, "o_dh")
    wdma = Chain(nc, "wdma")
    outs = Chain(nc, "outs")

    with nc.Block() as bk:
        def gsec(g):
            o_dg.push(g.dma_start(wr_sb[:, :, :], wr_d.rearrange("(k p) e -> p k e", p=128)), 16)
            o_dg.push(g.dma_start(bg_sb[:, :, :], bg_d), 16)
            o_dg.push(g.dma_start(bu_sb[:, :, :], bu_d), 16)
            o_dg.push(g.dma_start(bo_sb[:, :], bo_d), 16)
            o_dg.push(g.dma_start(id_bf[:, :], id_d), 16)
            g.memset(eps_col[:, :], EPS)
        bk.gpsimd(gsec)

    for q in range(NQ):
        with nc.Block() as bk:
            for it in range(NTT):
                tok0 = q * QT + it * 128

                def dmain(sy, tok0=tok0):
                    ch.wait(sy)
                    o_dh.push(sy.dma_start(xtile[:, :], x_in[tok0:tok0 + 128, :]), 16)
                bk.sync(dmain)

                def vbridge(v):
                    o_dg.wait(v)
                    o_dh.wait(v)
                    ch.push(v.tensor_copy(lnm[:, :], eps_col[:, :]))
                bk.vector(vbridge)

                _ln_emit(nc, bk, ch, xtile, xn_tm, lnscr, lnm, lns, lnr, eps_col)

                pe0, ev0 = pe_s.n, ev_s.n

                def trans(t, pe0=pe0, ev0=ev0):
                    ch.wait(t)
                    for j in range(8):
                        if j >= 2:
                            ev_s.wait(t, ev0 + (j - 2) + 1)
                        pe_s.push(t.transpose(ps_tr[j % 2][:, 0:128],
                                              xn_tm[:, j * 128:(j + 1) * 128], id_bf[:, :]))
                bk.tensor(trans)

                def evac(v, it=it, pe0=pe0, ev0=ev0):
                    for j in range(8):
                        pe_s.wait(v, pe0 + 1 + j)
                        ev_s.push(v.tensor_copy(xq_fm[:, j, it * 128:(it + 1) * 128],
                                                ps_tr[j % 2][:, 0:128]))
                bk.vector(evac)

                def joinv(v):
                    ev_s.wait(v)
                    ch.push(v.tensor_copy(lnm[:, :], lnm[:, :]))
                bk.vector(joinv)

                def rmm(t, it=it):
                    ch.wait(t)
                    for k in range(8):
                        inst = t.matmul(ps_r[:, :], xq_fm[:, k, it * 128:(it + 1) * 128],
                                        wr_sb[:, k, :], start=(k == 0), stop=(k == 7))
                    ch.push(inst)
                bk.tensor(rmm)

                def rsoft1(v):
                    ch.wait(v)
                    t1 = ch.push(v.tensor_copy(r_pr[:, :], ps_r[:, :]))
                    ch.wait(v, t1)
                    t2 = ch.push(v.tensor_reduce(r_m1[:, :], r_pr[:, :], AX.X, ALU.max))
                    ch.wait(v, t2)
                    ch.push(v.tensor_scalar(r_pr[:, :], r_pr[:, :], r_m1[:, :], None, ALU.subtract))
                bk.vector(rsoft1)

                def rexp(s):
                    ch.wait(s)
                    ch.push(s.activation(r_pr[:, :], r_pr[:, :], AF.Exp))
                bk.scalar(rexp)

                def rsoft2(v, it=it):
                    ch.wait(v)
                    seq = [
                        lambda: v.tensor_reduce(r_den[:, :], r_pr[:, :], AX.X, ALU.add),
                        lambda: v.reciprocal(r_den[:, :], r_den[:, :]),
                        lambda: v.tensor_scalar(r_pr[:, :], r_pr[:, :], r_den[:, :], None, ALU.mult),
                        lambda: v.tensor_reduce(r_m1[:, :], r_pr[:, :], AX.X, ALU.max),
                        lambda: v.tensor_scalar(r_k1[:, :], r_pr[:, :], r_m1[:, :], None, ALU.is_ge),
                        lambda: v.tensor_tensor(r_t[:, :], r_k1[:, :], r_pr[:, :], ALU.mult),
                        lambda: v.tensor_tensor(r_t[:, :], r_pr[:, :], r_t[:, :], ALU.subtract),
                        lambda: v.tensor_reduce(r_m2[:, :], r_t[:, :], AX.X, ALU.max),
                        lambda: v.tensor_scalar(r_k2[:, :], r_t[:, :], r_m2[:, :], None, ALU.is_ge),
                        lambda: v.tensor_tensor(r_k1[:, :], r_k1[:, :], r_k2[:, :], ALU.add),
                        lambda: v.tensor_scalar(r_den[:, :], r_m1[:, :], r_m2[:, :], 1e-8, ALU.add, ALU.add),
                        lambda: v.reciprocal(r_den[:, :], r_den[:, :]),
                        lambda: v.tensor_tensor(r_t[:, :], r_pr[:, :], r_k1[:, :], ALU.mult),
                        lambda: v.tensor_scalar(w_tm[:, it, :], r_t[:, :], r_den[:, :], None, ALU.mult),
                        lambda: v.tensor_copy(w_bf[:, :], w_tm[:, it, :]),
                    ]
                    for f in seq:
                        tk = ch.push(f())
                        ch.wait(v, tk)
                bk.vector(rsoft2)

                def wtr(t):
                    ch.wait(t)
                    ch.push(t.transpose(ps_tr[0][0:8, 0:128], w_bf[:, :], id_bf[:, :]))
                bk.tensor(wtr)

                def wev(v, it=it):
                    ch.wait(v)
                    ch.push(v.tensor_copy(w_fmT[0:8, it * 128:(it + 1) * 128],
                                          ps_tr[0][0:8, 0:128]))
                bk.vector(wev)

            for it in range(NTT):
                def bterm(t, it=it):
                    ch.wait(t)
                    for hh in range(2):
                        inst = t.matmul(ps_o[0][:, hh * 512:(hh + 1) * 512],
                                        w_fmT[:, it * 128:(it + 1) * 128],
                                        bo_sb[:, hh * 512:(hh + 1) * 512],
                                        start=True, stop=True)
                    ch.push(inst, 1)
                bk.tensor(bterm)

                def bev(v, it=it):
                    ch.wait(v)
                    ch.push(v.tensor_copy(acc[:, it, :], ps_o[0][:, :]))
                bk.vector(bev)

        if dbg and q == 0:
            with nc.Block() as bk:
                def dump1(g):
                    ch.wait(g)
                    outs.push(g.dma_start(d_xq[:, :, :], xq_fm[:, :, :]), 16)
                    outs.push(g.dma_start(d_w[:, :, :], w_tm[:, :, :]), 16)
                    outs.push(g.dma_start(d_acc0[:, :, :], acc[:, :, :]), 16)
                bk.gpsimd(dump1)

        with nc.Block() as bk:
            W0n = wdma.n
            w_wu = {e: W0n + 48 * e + 32 for e in range(8)}
            w_wo = {e: W0n + 48 * e + 48 for e in range(8)}
            P0n = pe_s.n
            p_g, p_u, p_out = {}, {}, {}
            cnt = P0n
            for e in range(8):
                for h in range(2):
                    for m in range(16):
                        gi = (e * 2 + h) * 16 + m
                        cnt += 1; p_g[gi] = cnt
                        cnt += 1; p_u[gi] = cnt
                    for tt in range(4):
                        cnt += 1; p_out[(e, h, tt)] = cnt
            A0n = ac_s.n
            a_si = {}
            cnt = A0n
            for gi in range(256):
                cnt += 1; a_si[gi] = cnt
            E0n = ev_s.n
            d_hm, d_oa = {}, {}
            cnt = E0n
            oi = 0
            oa_flat = {}
            for e in range(8):
                for h in range(2):
                    for m in range(16):
                        gi = (e * 2 + h) * 16 + m
                        cnt += 1; d_hm[gi] = cnt
                    for tt in range(4):
                        cnt += 1; d_oa[(e, h, tt)] = cnt
                        oa_flat[oi] = cnt
                        oi += 1

            def wsec(g, q=q):
                for e in range(8):
                    if e >= 1:
                        pe_s.wait(g, p_u[(e * 2) * 16 - 1 + 0] if False else p_u[(e * 2 - 1) * 16 + 15])
                    wdma.push(g.dma_start(wg_sb[:, :, :],
                                          wg_d[e].rearrange("(k p) m -> p k m", p=128)), 16)
                    wdma.push(g.dma_start(wu_sb[:, :, :],
                                          wu_d[e].rearrange("(k p) m -> p k m", p=128)), 16)
                    if e >= 1:
                        pe_s.wait(g, p_out[(e - 1, 1, 3)])
                    wdma.push(g.dma_start(wo_sb[:, :, :],
                                          wo_d[e].rearrange("(k p) m -> p k m", p=128)), 16)
            bk.gpsimd(wsec)

            def pesec(t, q=q):
                ch.wait(t)
                oi = 0
                for e in range(8):
                    wdma.wait(t, w_wu[e])
                    for h in range(2):
                        tok0 = h * 512
                        for m in range(16):
                            gi = (e * 2 + h) * 16 + m
                            if gi >= 2:
                                ac_s.wait(t, a_si[gi - 2])
                                ev_s.wait(t, d_hm[gi - 2])
                            for k in range(8):
                                inst = t.matmul(ps_gp[gi % 2], wg_sb[:, k, m * 128:(m + 1) * 128],
                                                xq_fm[:, k, tok0:tok0 + 512],
                                                start=(k == 0), stop=(k == 7),
                                                skip_group_check=True)
                            tk = pe_s.push(inst)
                            assert tk == p_g[gi]
                            for k in range(8):
                                inst = t.matmul(ps_up[gi % 2], wu_sb[:, k, m * 128:(m + 1) * 128],
                                                xq_fm[:, k, tok0:tok0 + 512],
                                                start=(k == 0), stop=(k == 7),
                                                skip_group_check=True)
                            tk = pe_s.push(inst)
                            assert tk == p_u[gi]
                        ev_s.wait(t, d_hm[(e * 2 + h) * 16 + 15])
                        wdma.wait(t, w_wo[e])
                        for tt in range(4):
                            if oi >= 1:
                                ev_s.wait(t, oa_flat[oi - 1])
                            for k in range(16):
                                for half in range(2):
                                    inst = t.matmul(ps_o0[:, half * 512:(half + 1) * 512],
                                                    hw_sb[:, k, tt * 128:(tt + 1) * 128],
                                                    wo_sb[:, k, half * 512:(half + 1) * 512],
                                                    start=(k == 0), stop=(k == 15),
                                                    skip_group_check=True)
                            tk = pe_s.push(inst)
                            assert tk == p_out[(e, h, tt)]
                            oi += 1
            bk.tensor(pesec)

            def acsec(s, q=q):
                for e in range(8):
                    for h in range(2):
                        for m in range(16):
                            gi = (e * 2 + h) * 16 + m
                            pe_s.wait(s, p_g[gi])
                            if gi >= 2:
                                ev_s.wait(s, d_hm[gi - 2])
                            tk = ac_s.push(s.activation(gs_sb[gi % 2][:, :], ps_gp[gi % 2],
                                                        AF.Silu, bias=bg_sb[:, m, e:e + 1]))
                            assert tk == a_si[gi]
            bk.scalar(acsec)

            def dvsec(v, q=q):
                for e in range(8):
                    for h in range(2):
                        for m in range(16):
                            gi = (e * 2 + h) * 16 + m
                            if m == 0 and (e, h) != (0, 0):
                                pem, phm = (e, h - 1) if h == 1 else (e - 1, 1)
                                pe_s.wait(v, p_out[(pem, phm, 3)])
                            pe_s.wait(v, p_u[gi])
                            ac_s.wait(v, a_si[gi])
                            tk = ev_s.push(v.scalar_tensor_tensor(hw_sb[:, m, :], ps_up[gi % 2],
                                                                  bu_sb[:, m, e:e + 1],
                                                                  gs_sb[gi % 2][:, :],
                                                                  ALU.add, ALU.mult))
                            assert tk == d_hm[gi]
                        for tt in range(4):
                            it = h * 4 + tt
                            pe_s.wait(v, p_out[(e, h, tt)])
                            tk = ev_s.push(v.scalar_tensor_tensor(acc[:, it, :], ps_o0[:, :],
                                                                  w_tm[:, it, e:e + 1], acc[:, it, :],
                                                                  ALU.mult, ALU.add))
                            assert tk == d_oa[(e, h, tt)]
            bk.vector(dvsec)

            def joinall(v):
                ev_s.wait(v)
                ch.push(v.tensor_copy(r_m1[:, :], r_m1[:, :]))
            bk.vector(joinall)

        with nc.Block() as bk:
            for it in range(NTT):
                tok0 = q * QT + it * 128

                def dmain(sy, tok0=tok0):
                    ch.wait(sy)
                    o_dh.push(sy.dma_start(xtile[:, :], x_in[tok0:tok0 + 128, :]), 16)
                bk.sync(dmain)

                def radd(v, it=it):
                    ch.wait(v)
                    o_dh.wait(v)
                    outs.wait(v)
                    ch.push(v.tensor_tensor(o_tm[:, :], acc[:, it, :], xtile[:, :], ALU.add))
                bk.vector(radd)

                def dmaout(sy, tok0=tok0):
                    ch.wait(sy)
                    outs.push(sy.dma_start(x_out[tok0:tok0 + 128, :], o_tm[:, :]), 16)
                bk.sync(dmaout)

            def fin(sy):
                outs.wait(sy)
            bk.sync(fin)

    return nc


_CACHE = {}


def _get(name, builder):
    if name not in _CACHE:
        nc = builder()
        nc.finalize()
        _CACHE[name] = nc
    return _CACHE[name]


def _host_moe(inputs):
    ident = np.eye(128, dtype=np.float32)
    return {
            "wr": inputs["L0_ffn_W_router"],
            "wg": inputs["L0_ffn_W_gate"],
            "bg": np.ascontiguousarray(np.asarray(inputs["L0_ffn_b_gate"]).reshape(8, 16, 128).transpose(2, 1, 0)),
            "wu": inputs["L0_ffn_W_up"],
            "bu": np.ascontiguousarray(np.asarray(inputs["L0_ffn_b_up"]).reshape(8, 16, 128).transpose(2, 1, 0)),
            "wo": inputs["L0_ffn_W_out"],
            "bo": inputs["L0_ffn_b_out"],
            "ident": ident,
    }


def run_moe(x, inputs):
    nc = _get("moe", build_moe)
    base = _host_moe(inputs)
    in_maps = []
    for c in range(NCORE):
        m = dict(base)
        m["x_in"] = np.ascontiguousarray(x[2 * c:2 * c + 2].reshape(TOK, DM))
        in_maps.append(m)
    res = run_bass_kernel_spmd(nc, in_maps, core_ids=list(range(NCORE)))
    out = np.concatenate([r["x_out"].reshape(BPC, T, DM) for r in res.results], axis=0)
    return out


def build_full():
    import contextlib
    nc = bass.Bass("TRN2", target_bir_lowering=False, debug=False)
    ps_a = nc.alloc_psum_tensor("f_psa", [128, 1024], F32)
    ps_b = nc.alloc_psum_tensor("f_psb", [128, 1024], F32)
    ps_c = nc.alloc_psum_tensor("f_psc", [128, 512], F32)
    ps_d = nc.alloc_psum_tensor("f_psd", [128, 512], F32)
    trA = nc.alloc_psum_tensor("f_trA", [128, 1024], BF16)
    trB = nc.alloc_psum_tensor("f_trB", [128, 1024], BF16)
    mamba_pmap = {"mps_a0": ps_a[:, 0:512], "mps_a1": ps_a[:, 512:1024],
                  "mps_b0": ps_b[:, 0:512], "mps_b1": ps_b[:, 512:1024],
                  "mps_c": ps_c[:, :], "mps_d": ps_d[:, :],
                  "mps_tr0": trA, "mps_tr1": trB}

    x0 = nc.dram_tensor("x_in", [BPC, T, DM], F32, kind="ExternalInput").ap()
    x1_sc = nc.dram_tensor("x1_sc", [BPC, T, DM], F32, kind="Internal").ap()
    x2_sc = nc.dram_tensor("x2_sc", [BPC, T, DM], F32, kind="Internal").ap()
    ag_in = nc.dram_tensor("ag_in", [BPC * T, DM], BF16, kind="Internal").ap()
    ag_out = nc.dram_tensor("ag_out", [NCORE * BPC * T, DM], BF16, kind="Internal").ap()
    x_out = nc.dram_tensor("x_out", [16, 256, DM], F32, kind="ExternalOutput").ap()

    with contextlib.ExitStack() as st:
        fc = FC(nc=nc, prefix="m_", stack=st,
                dmap={"x_in": x0, "x_out": x1_sc},
                pmap=mamba_pmap)
        build_mamba(fc=fc)

    with contextlib.ExitStack() as st:
        fc = FC(nc=nc, prefix="o_", stack=st,
                dmap={"x_in": x1_sc.rearrange("b t d -> (b t) d"),
                      "x_out": x2_sc.rearrange("b t d -> (b t) d")},
                pmap={"ps_tr0": trA, "ps_tr1": trB, "ps_r": ps_b[:, 0:8],
                      "ps_g": ps_c, "ps_u": ps_d, "ps_o0": ps_a, "ps_o1": ps_b})
        build_moe(fc=fc)

    with contextlib.ExitStack() as st:
        fc = FC(nc=nc, prefix="t_", stack=st, out_swdge=True,
                dmap={"x_in": x2_sc,
                      "x_out": ag_in.rearrange("(b t) d -> b t d", b=BPC)},
                pmap={"t_psbig": ps_a, "t_pssm": ps_c, "t_pstr0": trA, "t_pstr1": trB})
        build_ttt(fc=fc)

    ccs = nc.alloc_semaphore("f_ccs")
    with nc.Block() as bk:
        def ccsec(g):
            g.collective_compute("AllGather", mybir.AluOpType.bypass,
                                 replica_groups=[list(range(NCORE))],
                                 ins=[ag_in.opt()], outs=[ag_out.opt()]).then_inc(ccs, 1)
            g.wait_ge(ccs, 1)
        bk.gpsimd(ccsec)

    with contextlib.ExitStack() as st:
        fc = FC(nc=nc, prefix="l_", stack=st, lstm_gather=True,
                dmap={"agout": ag_out, "x_out": x_out},
                pmap={"l_psf": ps_a[:, 0:512], "l_psi": ps_a[:, 512:1024],
                      "l_psg": ps_b[:, 0:512], "l_pstr0": trA, "l_pstr1": trB})
        build_lstm(fc=fc)

    return nc


def kernel(**inputs):
    nc = _get("full", build_full)
    x0 = np.asarray(inputs["L0_attn_x"], dtype=np.float32)
    mb = {("m_" + k): v for k, v in _host_mamba(inputs).items()}
    ob = {("o_" + k): v for k, v in _host_moe(inputs).items()}
    tb = {("t_" + k): v for k, v in _host_ttt(inputs).items()}
    lb = {("l_" + k): v for k, v in _host_lstm(inputs).items()}
    in_maps = []
    for cc in range(NCORE):
        sel = np.zeros((128, 32), np.uint32)
        for bb in range(16):
            for tt in range(2):
                sel[:, bb * 2 + tt] = ((bb // 2) * 4096 + (bb % 2) * 2048
                                       + 256 * cc + tt * 128 + np.arange(128))
        m = {}
        m.update(mb)
        m.update(ob)
        m.update(tb)
        m.update(lb)
        m["x_in"] = np.ascontiguousarray(x0[2 * cc:2 * cc + 2])
        m["l_tsel"] = sel
        in_maps.append(m)
    res = run_bass_kernel_spmd(nc, in_maps, core_ids=list(range(NCORE)))
    return np.concatenate([r["x_out"] for r in res.results], axis=1)


def kernel_unfused(**inputs):
    x = np.asarray(inputs["L0_attn_x"], dtype=np.float32)
    x1 = run_mamba(x, inputs)
    x2 = run_moe(x1, inputs)
    x3 = run_ttt(x2, inputs)
    x4 = run_lstm(x3, inputs)
    return x4




def build_mamba(dbg=False, fc=None):
    import os
    SKIP_SCAN = os.environ.get("SKIP_SCAN") == "1"
    SKIP_DT = os.environ.get("SKIP_DT") == "1"
    SKIP_IP = os.environ.get("SKIP_IP") == "1"
    fc = fc or FC()
    nc = fc.nc if fc.nc is not None else bass.Bass("TRN2", target_bir_lowering=False, debug=False)
    TC = 512
    NCH = T // TC
    NG = DI // 8

    x_in = fc.dram(nc, "x_in", [BPC, T, DM], F32)
    inw_d = fc.dram(nc, "in_w", [DM, 2 * DI], F32)
    inb_d = fc.dram(nc, "in_b", [128, 32], F32)
    cw_d = fc.dram(nc, "conv_w", [128, 16, 4], F32)
    cb_d = fc.dram(nc, "conv_b", [128, 16], F32)
    ap_d = fc.dram(nc, "A_perm", [128, 256], F32)
    dp_d = fc.dram(nc, "Dp_col", [128, 16], F32)
    dtw_d = fc.dram(nc, "dt_w", [DI, DI], F32)
    dtb_d = fc.dram(nc, "dt_b", [128, 16], F32)
    bcw_d = fc.dram(nc, "bc_w", [DI, 32], F32)
    bcb_d = fc.dram(nc, "bc_b", [32, 1], F32)
    ow_d = fc.dram(nc, "out_w", [DI, DM], F32)
    ob_d = fc.dram(nc, "out_b", [128, 8], F32)
    sel_d = fc.dram(nc, "SEL", [128, 16, 128], F32)
    red_d = fc.dram(nc, "RED", [128, 16, 128], F32)
    sbc_d = fc.dram(nc, "selbc", [32, 2, 128], F32)
    id_d = fc.dram(nc, "ident", [128, 128], F32)
    x_out = fc.dram(nc, "x_out", [BPC, T, DM], F32, kind="ExternalOutput")

    xz_sc = fc.dram(nc, "xz_sc", [BPC, 16, 128, T], BF16, kind="Internal")
    gt_sc = fc.dram(nc, "gt_sc", [BPC, 16, 128, T], BF16, kind="Internal")

    inb_sb = fc.sb(nc, "inb_sb", [128, 32], F32)
    cw_sb = fc.sb(nc, "cw_sb", [128, 16, 4], F32)
    cb_sb = fc.sb(nc, "cb_sb", [128, 16], F32)
    ap_sb = fc.sb(nc, "ap_sb", [128, 256], F32)
    dp_sb = fc.sb(nc, "dp_sb", [128, 16], F32)
    dtb_sb = fc.sb(nc, "dtb_sb", [128, 16], F32)
    bcb_sb = fc.sb(nc, "bcb_sb", [32, 1], F32)
    ob_sb = fc.sb(nc, "ob_sb", [128, 8], F32)
    sel_sb = fc.sb(nc, "sel_sb", [128, 16, 128], BF16)
    red_sb = fc.sb(nc, "red_sb", [128, 16, 128], BF16)
    sbc_sb = fc.sb(nc, "sbc_sb", [32, 2, 128], BF16)
    id_bf = fc.sb(nc, "id_bf", [128, 128], BF16)
    eps_col = fc.sb(nc, "eps_col", [128, 1], F32)
    A1 = fc.sb(nc, "m_A1", [128, 8 * 2 * DI], BF16)
    inw_sb = A1[:, :].rearrange("p (k m) -> p k m", k=8)
    dtw_sb = A1[:, :].rearrange("p (k m) -> p k m", k=16)
    ow_sb = fc.sb(nc, "m_ow", [128, 16, DM], BF16)
    bcw_sb = fc.sb(nc, "m_bcw", [128, 16, 32], BF16)
    ar1 = fc.sb(nc, "m_ar1", [128, 16, TC + 3], BF16)
    xz_fm = ar1[:, :, 0:TC]
    xzh = ar1[:, :, :]
    yg_fm = ar1[:, :, 0:TC]
    gtc_fm = fc.sb(nc, "m_gt_fm", [128, 16, TC], BF16)
    xc_fm = fc.sb(nc, "m_xc_fm", [128, 16, TC], BF16)
    dt_fm = fc.sb(nc, "m_dt_fm", [128, 16, TC], BF16)
    arof = fc.sb(nc, "m_arof", [128, 8, TC], BF16)
    xn_fm = arof[:, :, :]
    of_fm = arof[:, :, :]
    cvr = fc.sb(nc, "m_cvr", [128, TC], F32)
    xtile = fc.sb(nc, "m_xtile", [128, DM], F32)
    xn_tm = fc.sb(nc, "m_xn_tm", [128, DM], BF16)
    lnscr = fc.sb(nc, "m_lnscr", [128, DM], BF16)
    lnm = fc.sb(nc, "m_lnm", [128, 1], F32)
    lns = fc.sb(nc, "m_lns", [128, 1], F32)
    lnr = fc.sb(nc, "m_lnr", [128, 1], F32)
    bc_sb = fc.sb(nc, "m_bc_sb", [32, TC], BF16)
    brep = fc.sb(nc, "m_brep", [128, TC], BF16)
    crep = fc.sb(nc, "m_crep", [128, TC], BF16)
    dA_sb = [fc.sb(nc, f"m_dA{i}", [128, TC], F32) for i in range(2)]
    u_sb = fc.sb(nc, "m_u", [128, TC], F32)
    h_sb = fc.sb(nc, "m_h", [128, TC], BF16)
    hc_sb = [fc.sb(nc, f"m_hc{i}", [128, TC], BF16) for i in range(2)]
    hcar = fc.sb(nc, "m_hcar", [128, 256], F32)
    sg_sb = [fc.sb(nc, f"m_sg{i}", [128, TC], BF16) for i in range(2)]
    t1_sb = fc.sb(nc, "m_t1", [128, TC], BF16)
    dtxJ2 = [xn_tm[:, 0:TC], xn_tm[:, TC:2 * TC]]
    sp_a2 = [lnscr[:, 0:TC], lnscr[:, TC:2 * TC]]
    sp_m2 = [cvr[:, :], xtile[:, 0:TC]]


    mps_a0 = fc.ps(nc, "mps_a0", [128, 512], F32)
    mps_a1 = fc.ps(nc, "mps_a1", [128, 512], F32)
    mps_b0 = fc.ps(nc, "mps_b0", [128, 512], F32)
    mps_b1 = fc.ps(nc, "mps_b1", [128, 512], F32)
    mps_c = fc.ps(nc, "mps_c", [128, 512], F32)
    mps_d = fc.ps(nc, "mps_d", [128, 512], F32)
    ps_dtr = [mps_a0[:, :], mps_a1[:, :]]
    ps_dxr = [mps_b0[:, :], mps_b1[:, :]]
    ps_y = [mps_c[:, :], mps_d[:, :]]
    ps_mm = [mps_a0[:, :], mps_a1[:, :]]
    ps_bc = mps_d[:, :]
    ps_tr = [fc.ps(nc, f"mps_tr{i}", [128, 1024], BF16) for i in range(2)]

    ch = Chain(nc, "m_ch")
    pe_s = Chain(nc, "m_pe")
    ev_s = Chain(nc, "m_ev")
    ac_s = Chain(nc, "m_ac")
    dch = Chain(nc, "m_dch")
    hch = Chain(nc, "m_hch")
    wdma = Chain(nc, "m_wdma")
    outs = Chain(nc, "m_outs")
    outs_h = Chain(nc, "m_outsh")
    ms = Chain(nc, "m_ms")

    with nc.Block() as bk:
        def gsec(g):
            dch.push(g.dma_start(inb_sb[:, :], inb_d), 16)
            dch.push(g.dma_start(cw_sb[:, :, :], cw_d), 16)
            dch.push(g.dma_start(cb_sb[:, :], cb_d), 16)
            dch.push(g.dma_start(ap_sb[:, :], ap_d), 16)
            dch.push(g.dma_start(dp_sb[:, :], dp_d), 16)
            dch.push(g.dma_start(dtb_sb[:, :], dtb_d), 16)
            dch.push(g.dma_start(bcb_sb[:, :], bcb_d), 16)
            dch.push(g.dma_start(ob_sb[:, :], ob_d), 16)
            dch.push(g.dma_start(sel_sb[:, :, :], sel_d), 16)
            dch.push(g.dma_start(red_sb[:, :, :], red_d), 16)
            dch.push(g.dma_start(sbc_sb[:, :, :], sbc_d), 16)
            dch.push(g.dma_start(id_bf[:, :], id_d), 16)
            dch.push(g.dma_start(bcw_sb[:, :, :], bcw_d.rearrange("(k p) c -> p k c", p=128)), 16)
            dch.push(g.dma_start(inw_sb, inw_d.rearrange("(k p) m -> p k m", p=128)), 16)
            g.memset(eps_col[:, :], EPS)
            g.memset(hcar[:, :], 0.0)
        bk.gpsimd(gsec)

    for b in range(BPC):
        for c in range(NCH):
            t0 = c * TC
            with nc.Block() as bk:
                for tt in range(4):
                    def dmain(sy, b=b, t0=t0, tt=tt):
                        ch.wait(sy)
                        hch.push(sy.dma_start(xtile[:, :], x_in[b, t0 + tt * 128:t0 + (tt + 1) * 128, :]), 16)
                    bk.sync(dmain)

                    def vstart(v):
                        dch.wait(v)
                        hch.wait(v)
                        ch.push(v.tensor_copy(lnm[:, :], eps_col[:, :]))
                    bk.vector(vstart)

                    _ln_emit(nc, bk, ch, xtile, xn_tm, lnscr, lnm, lns, lnr, eps_col)

                    pe0, ev0 = pe_s.n, ev_s.n

                    def trans(t, pe0=pe0, ev0=ev0):
                        ch.wait(t)
                        for j in range(8):
                            if j >= 2:
                                ev_s.wait(t, ev0 + (j - 2) + 1)
                            pe_s.push(t.transpose(ps_tr[j % 2][:, 0:128],
                                                  xn_tm[:, j * 128:(j + 1) * 128], id_bf[:, :]))
                    bk.tensor(trans)

                    def evac(v, tt=tt, pe0=pe0, ev0=ev0):
                        for j in range(8):
                            pe_s.wait(v, pe0 + 1 + j)
                            ev_s.push(v.tensor_copy(xn_fm[:, j, tt * 128:(tt + 1) * 128],
                                                    ps_tr[j % 2][:, 0:128]))
                    bk.vector(evac)

                    def joinv(v):
                        ev_s.wait(v)
                        ch.push(v.tensor_copy(lnm[:, :], lnm[:, :]))
                    bk.vector(joinv)

                P0, E0 = pe_s.n, ev_s.n

                def ipmm(t, P0=P0, E0=E0):
                    ch.wait(t)
                    for m in range(32):
                        if m >= 2:
                            ev_s.wait(t, E0 + (m - 2) + 1)
                        for k in range(8):
                            inst = t.matmul(ps_mm[m % 2], inw_sb[:, k, m * 128:(m + 1) * 128],
                                            xn_fm[:, k, :], start=(k == 0), stop=(k == 7),
                                            skip_group_check=True)
                        pe_s.push(inst)
                if not SKIP_IP: bk.tensor(ipmm)

                def ipev(v, P0=P0, E0=E0):
                    for m in range(32):
                        pe_s.wait(v, P0 + m + 1)
                        dst = xz_fm[:, m, :] if m < 16 else gtc_fm[:, m - 16, :]
                        ev_s.push(v.tensor_scalar(dst, ps_mm[m % 2], inb_sb[:, m:m + 1], None, ALU.add))
                if not SKIP_IP: bk.vector(ipev)

                def ipjoin(v):
                    if not SKIP_IP:
                        ev_s.wait(v)
                    ch.push(v.tensor_copy(lnm[:, :], lnm[:, :]))
                bk.vector(ipjoin)

                def st(g, b=b, t0=t0):
                    ch.wait(g)
                    outs.push(g.dma_start(xz_sc[b, :, :, t0:t0 + TC], xz_fm[:, :, :]), 16)
                    outs.push(g.dma_start(gt_sc[b, :, :, t0:t0 + TC], gtc_fm[:, :, :]), 16)
                bk.gpsimd(st)

                def fin(sy):
                    outs.wait(sy)
                bk.sync(fin)

    with nc.Block() as bk:
        def wload(g):
            wdma.push(g.dma_start(dtw_sb, dtw_d.rearrange("(k p) m -> p k m", p=128)), 16)
            wdma.push(g.dma_start(ow_sb[:, :, :], ow_d.rearrange("(k p) m -> p k m", p=128)), 16)
        bk.gpsimd(wload)

    for b in range(BPC):
        with nc.Block() as bk:
            def hrst(v, b=b):
                ch.push(v.tensor_scalar(hcar[:, :], hcar[:, :], 0.0, None, ALU.mult))
            bk.vector(hrst)

        for c in range(NCH):
            t0 = c * TC
            with nc.Block() as bk:
                lo = max(t0 - 1, 0)
                hi = min(t0 + TC + 2, T)
                off = lo - (t0 - 1)

                import os
                def ld(g, b=b, t0=t0, lo=lo, hi=hi, off=off):
                    if os.environ.get("MSFENCE", "1") == "1":
                        ms.push(g.memset(ar1[:, :, :], 0.0))
                        ms.wait(g)
                    else:
                        g.memset(ar1[:, :, :], 0.0)
                    dch.push(g.dma_start(xzh[:, :, off:off + hi - lo], xz_sc[b, :, :, lo:hi]), 16)
                    dch.push(g.dma_start(gtc_fm[:, :, :], gt_sc[b, :, :, t0:t0 + TC]), 16)
                bk.gpsimd(ld)

                for j in range(16):
                    def conv(v, j=j):
                        if j == 0:
                            dch.wait(v)
                            wdma.wait(v)
                        ch.wait(v)
                        t1 = ch.push(v.tensor_scalar(cvr[:, :], xzh[:, j, 0:TC],
                                                     cw_sb[:, j, 0:1], None, ALU.mult))
                        for jj in range(1, 4):
                            ch.wait(v, t1)
                            t1 = ch.push(v.scalar_tensor_tensor(cvr[:, :], xzh[:, j, jj:jj + TC],
                                                                cw_sb[:, j, jj:jj + 1], cvr[:, :],
                                                                ALU.mult, ALU.add))
                    bk.vector(conv)

                    def csil(s, j=j):
                        ch.wait(s)
                        ch.push(s.activation(xc_fm[:, j, :], cvr[:, :], AF.Silu,
                                             bias=cb_sb[:, j:j + 1]))
                    bk.scalar(csil)

                def bcmm(t):
                    ch.wait(t)
                    for k in range(16):
                        inst = t.matmul(ps_bc[0:32, :], bcw_sb[:, k, :], xc_fm[:, k, :],
                                        start=(k == 0), stop=(k == 15))
                    ch.push(inst)
                bk.tensor(bcmm)

                def bcev(v):
                    ch.wait(v)
                    ch.push(v.tensor_scalar(bc_sb[:, :], ps_bc[0:32, :], bcb_sb[:, :], None, ALU.add))
                bk.vector(bcev)

                def repmm(t):
                    ch.wait(t)
                    ch.push(t.matmul(ps_bc[:, :], sbc_sb[:, 0, :], bc_sb[:, :], start=True, stop=True))
                bk.tensor(repmm)

                def repev(v):
                    ch.wait(v)
                    ch.push(v.tensor_copy(brep[:, :], ps_bc[:, :]))
                bk.vector(repev)

                def repmm2(t):
                    ch.wait(t)
                    ch.push(t.matmul(ps_bc[:, :], sbc_sb[:, 1, :], bc_sb[:, :], start=True, stop=True))
                bk.tensor(repmm2)

                def repev2(v):
                    ch.wait(v)
                    ch.push(v.tensor_copy(crep[:, :], ps_bc[:, :]))
                bk.vector(repev2)

                dtP0, dtA0, dtD0 = pe_s.n, ac_s.n, ev_s.n
                a_ln = {m: dtA0 + 3 * m + 3 for m in range(16)}
                d_max = {m: dtD0 + 2 * m + 1 for m in range(16)}
                d_add = {m: dtD0 + 2 * m + 2 for m in range(16)}
                p_dt = {m: dtP0 + m + 1 for m in range(16)}

                def dtmm(t):
                    ch.wait(t)
                    for m in range(16):
                        if m >= 2:
                            ac_s.wait(t, a_ln[m - 2])
                            ev_s.wait(t, d_max[m - 2])
                        for k in range(16):
                            inst = t.matmul(ps_mm[m % 2], dtw_sb[:, k, m * 128:(m + 1) * 128],
                                            xc_fm[:, k, :], start=(k == 0), stop=(k == 15),
                                            skip_group_check=True)
                        tk = pe_s.push(inst)
                        assert tk == p_dt[m]
                if not SKIP_DT: bk.tensor(dtmm)

                def dtsp1(s):
                    for m in range(16):
                        pe_s.wait(s, p_dt[m])
                        if m >= 2:
                            ev_s.wait(s, d_add[m - 2])
                        ac_s.push(s.activation(sp_a2[m % 2][:, :], ps_mm[m % 2], AF.Abs,
                                               bias=dtb_sb[:, m:m + 1]))
                        ac_s.wait(s)
                        ac_s.push(s.activation(sp_a2[m % 2][:, :], sp_a2[m % 2][:, :], AF.Exp, scale=-1.0))
                        ac_s.wait(s)
                        tk = ac_s.push(s.activation(sp_a2[m % 2][:, :], sp_a2[m % 2][:, :], AF.Ln, bias=1.0))
                        assert tk == a_ln[m]
                if not SKIP_DT: bk.scalar(dtsp1)

                def dtsp2(v):
                    for m in range(16):
                        pe_s.wait(v, p_dt[m])
                        ac_s.wait(v, dtA0 + 3 * m + 1)
                        tk = ev_s.push(v.tensor_scalar(sp_m2[m % 2][:, :], ps_mm[m % 2],
                                                       dtb_sb[:, m:m + 1], 0.0, ALU.add, ALU.max))
                        assert tk == d_max[m]
                        ac_s.wait(v, a_ln[m])
                        ev_s.wait(v, d_max[m])
                        tk = ev_s.push(v.tensor_tensor(dt_fm[:, m, :], sp_m2[m % 2][:, :],
                                                       sp_a2[m % 2][:, :], ALU.add))
                        assert tk == d_add[m]
                if not SKIP_DT: bk.vector(dtsp2)

                def dtjoin(v):
                    if not SKIP_DT:
                        ev_s.wait(v)
                    ch.push(v.tensor_copy(lnm[:, :], lnm[:, :]))
                bk.vector(dtjoin)

                NGC = 256
                P0s, A0s, D0s = pe_s.n, ac_s.n, ev_s.n
                p_dtr, p_dxr, p_y = {}, {}, {}
                cnt = P0s
                for g in range(NGC):
                    if g >= 1:
                        cnt += 1; p_y[g - 1] = cnt
                    cnt += 1; p_dtr[g] = cnt
                    cnt += 1; p_dxr[g] = cnt
                cnt += 1; p_y[NGC - 1] = cnt
                a_dA, a_sg = {}, {}
                cnt = A0s
                for g in range(NGC):
                    cnt += 1; a_dA[g] = cnt
                    if g % 16 == 15:
                        cnt += 1; a_sg[g // 16] = cnt
                d_dtx, d_u, d_scan, d_carry, d_hc, d_t1, d_yg = {}, {}, {}, {}, {}, {}, {}
                cnt = D0s
                for g in range(NGC):
                    J, j = g // 16, g % 16
                    if j == 0:
                        cnt += 1; d_dtx[J] = cnt
                    cnt += 1; d_u[g] = cnt
                    cnt += 1; d_scan[g] = cnt
                    cnt += 1; d_carry[g] = cnt
                    cnt += 1; d_hc[g] = cnt
                    if j == 15:
                        cnt += 1; d_t1[J] = cnt
                        cnt += 1; d_yg[J] = cnt

                def sc_pe(t):
                    ch.wait(t)
                    for g in range(NGC):
                        J, j = g // 16, g % 16
                        if g >= 1:
                            gy = g - 1
                            Jy, jy = gy // 16, gy % 16
                            if jy == 0 and Jy >= 2:
                                ev_s.wait(t, d_yg[Jy - 2])
                            ev_s.wait(t, d_hc[gy])
                            tk = pe_s.push(t.matmul(ps_y[Jy % 2], red_sb[:, jy, :], hc_sb[gy % 2][:, :],
                                                    start=(jy == 0), stop=(jy == 15),
                                                    skip_group_check=True))
                            assert tk == p_y[gy]
                        if j == 0:
                            ev_s.wait(t, d_dtx[J])
                        if g >= 2:
                            ac_s.wait(t, a_dA[g - 2])
                            ev_s.wait(t, d_u[g - 2])
                        tk = pe_s.push(t.matmul(ps_dtr[g % 2], sel_sb[:, j, :], dt_fm[:, J, :],
                                                start=True, stop=True, skip_group_check=True))
                        assert tk == p_dtr[g]
                        tk = pe_s.push(t.matmul(ps_dxr[g % 2], sel_sb[:, j, :], dtxJ2[J % 2][:, :],
                                                start=True, stop=True, skip_group_check=True))
                        assert tk == p_dxr[g]
                    gy = NGC - 1
                    ev_s.wait(t, d_hc[gy])
                    tk = pe_s.push(t.matmul(ps_y[(gy // 16) % 2], red_sb[:, 15, :], hc_sb[gy % 2][:, :],
                                            start=False, stop=True, skip_group_check=True))
                    assert tk == p_y[gy]
                if not SKIP_SCAN: bk.tensor(sc_pe)

                def sc_ac(s, b=b):
                    for g in range(NGC):
                        J = g // 16
                        pe_s.wait(s, p_dtr[g])
                        if g >= 2:
                            ev_s.wait(s, d_scan[g - 2])
                        tk = ac_s.push(s.activation(dA_sb[g % 2][:, :], ps_dtr[g % 2], AF.Exp,
                                                    scale=ap_sb[:, g:g + 1]))
                        assert tk == a_dA[g]
                        if g % 16 == 15:
                            if J >= 2:
                                ev_s.wait(s, d_yg[J - 2])
                            tk = ac_s.push(s.activation(sg_sb[J % 2][:, :], gtc_fm[:, J, :], AF.Silu))
                            assert tk == a_sg[J]
                if not SKIP_SCAN: bk.scalar(sc_ac)

                def sc_dv(v):
                    for g in range(NGC):
                        J, j = g // 16, g % 16
                        if j == 0:
                            if J >= 2:
                                pe_s.wait(v, p_dxr[16 * (J - 2) + 15])
                            tk = ev_s.push(v.tensor_tensor(dtxJ2[J % 2][:, :], dt_fm[:, J, :],
                                                           xc_fm[:, J, :], ALU.mult))
                            assert tk == d_dtx[J]
                        pe_s.wait(v, p_dxr[g])
                        tk = ev_s.push(v.tensor_tensor(u_sb[:, :], ps_dxr[g % 2], brep[:, :], ALU.mult))
                        assert tk == d_u[g]
                        ac_s.wait(v, a_dA[g])
                        ev_s.wait(v, d_u[g])
                        tk = ev_s.push(v.tensor_tensor_scan(h_sb[:, :], dA_sb[g % 2][:, :], u_sb[:, :],
                                                            hcar[:, g:g + 1], ALU.mult, ALU.add))
                        assert tk == d_scan[g]
                        ev_s.wait(v, d_scan[g])
                        tk = ev_s.push(v.tensor_copy(hcar[:, g:g + 1], h_sb[:, TC - 1:TC]))
                        assert tk == d_carry[g]
                        if g >= 2:
                            pe_s.wait(v, p_y[g - 2])
                        tk = ev_s.push(v.tensor_tensor(hc_sb[g % 2][:, :], h_sb[:, :], crep[:, :], ALU.mult))
                        assert tk == d_hc[g]
                        if j == 15:
                            pe_s.wait(v, p_y[g])
                            ac_s.wait(v, a_sg[J])
                            tk = ev_s.push(v.scalar_tensor_tensor(t1_sb[:, :], xc_fm[:, J, :],
                                                                  dp_sb[:, J:J + 1], ps_y[J % 2],
                                                                  ALU.mult, ALU.add))
                            assert tk == d_t1[J]
                            ev_s.wait(v, d_t1[J])
                            tk = ev_s.push(v.tensor_tensor(yg_fm[:, J, :], t1_sb[:, :],
                                                           sg_sb[J % 2][:, :], ALU.mult))
                            assert tk == d_yg[J]
                if not SKIP_SCAN: bk.vector(sc_dv)

                def scjoin(v):
                    if not SKIP_SCAN:
                        ev_s.wait(v)
                    ch.push(v.tensor_copy(lnm[:, :], lnm[:, :]))
                bk.vector(scjoin)

                oP0, oE0 = pe_s.n, ev_s.n

                def omm(t, oP0=oP0, oE0=oE0):
                    ch.wait(t)
                    for m in range(8):
                        if m >= 2:
                            ev_s.wait(t, oE0 + (m - 2) + 1)
                        for k in range(16):
                            inst = t.matmul(ps_mm[m % 2], ow_sb[:, k, m * 128:(m + 1) * 128],
                                            yg_fm[:, k, :], start=(k == 0), stop=(k == 15),
                                            skip_group_check=True)
                        pe_s.push(inst)
                bk.tensor(omm)

                def oev(v, oP0=oP0, oE0=oE0):
                    for m in range(8):
                        pe_s.wait(v, oP0 + m + 1)
                        ev_s.push(v.tensor_scalar(of_fm[:, m, :], ps_mm[m % 2], ob_sb[:, m:m + 1],
                                                  None, ALU.add))
                bk.vector(oev)

                def ojoin(v):
                    ev_s.wait(v)
                    ch.push(v.tensor_copy(lnm[:, :], lnm[:, :]))
                bk.vector(ojoin)

                for tt in range(4):
                    def dmx(sy, b=b, t0=t0, tt=tt):
                        ch.wait(sy)
                        outs_h.wait(sy)
                        hch.push(sy.dma_start(xtile[:, :], x_in[b, t0 + tt * 128:t0 + (tt + 1) * 128, :]), 16)
                    bk.sync(dmx)

                    pe0, ev0 = pe_s.n, ev_s.n

                    def etr(t, tt=tt, pe0=pe0, ev0=ev0):
                        ch.wait(t)
                        for m in range(8):
                            if m >= 2:
                                ev_s.wait(t, ev0 + (m - 2) + 1)
                            pe_s.push(t.transpose(ps_tr[m % 2][:, 0:128],
                                                  of_fm[:, m, tt * 128:(tt + 1) * 128], id_bf[:, :]))
                    bk.tensor(etr)

                    def eev(v, tt=tt, pe0=pe0, ev0=ev0):
                        hch.wait(v)
                        for m in range(8):
                            pe_s.wait(v, pe0 + 1 + m)
                            ev_s.push(v.tensor_tensor(xtile[:, m * 128:(m + 1) * 128],
                                                      ps_tr[m % 2][:, 0:128],
                                                      xtile[:, m * 128:(m + 1) * 128], ALU.add))
                    bk.vector(eev)

                    def ejoin(v):
                        ev_s.wait(v)
                        ch.push(v.tensor_copy(lnm[:, :], lnm[:, :]))
                    bk.vector(ejoin)

                    def dmo(sy, b=b, t0=t0, tt=tt):
                        ch.wait(sy)
                        outs_h.push(sy.dma_start(x_out[b, t0 + tt * 128:t0 + (tt + 1) * 128, :],
                                                 xtile[:, :]), 16)
                    bk.sync(dmo)

                def fin(sy):
                    outs_h.wait(sy)
                bk.sync(fin)

    return nc


def _host_mamba(inputs):
    ident = np.eye(128, dtype=np.float32)
    A = np.asarray(inputs["L0_attn_A"])
    A_perm = np.ascontiguousarray(A.reshape(256, 128).T)
    SEL = np.zeros((128, 16, 128), np.float32)
    for r in range(128):
        for jj in range(16):
            for p in range(128):
                if r == 8 * jj + p // 16:
                    SEL[r, jj, p] = 1.0
    RED = np.ascontiguousarray(SEL.transpose(2, 1, 0))
    selbc = np.zeros((32, 2, 128), np.float32)
    for p in range(128):
        selbc[p % 16, 0, p] = 1.0
        selbc[16 + p % 16, 1, p] = 1.0
    conv_w = np.asarray(inputs["L0_attn_conv_w"]).reshape(4, DI)
    base = {
        "in_w": inputs["L0_attn_in_w"],
        "in_b": np.ascontiguousarray(np.asarray(inputs["L0_attn_in_b"]).reshape(32, 128).T),
        "conv_w": np.ascontiguousarray(conv_w.reshape(4, 16, 128).transpose(2, 1, 0)),
        "conv_b": np.ascontiguousarray(np.asarray(inputs["L0_attn_conv_b"]).reshape(16, 128).T),
        "A_perm": A_perm,
        "Dp_col": np.ascontiguousarray(np.asarray(inputs["L0_attn_Dp"]).reshape(16, 128).T),
        "dt_w": inputs["L0_attn_dt_w"],
        "dt_b": np.ascontiguousarray(np.asarray(inputs["L0_attn_dt_b"]).reshape(16, 128).T),
        "bc_w": np.ascontiguousarray(np.concatenate([np.asarray(inputs["L0_attn_B_w"]),
                                                     np.asarray(inputs["L0_attn_C_w"])], axis=1)),
        "bc_b": np.ascontiguousarray(np.concatenate([np.asarray(inputs["L0_attn_B_b"]),
                                                     np.asarray(inputs["L0_attn_C_b"])]).reshape(32, 1)),
        "out_w": inputs["L0_attn_out_w"],
        "out_b": np.ascontiguousarray(np.asarray(inputs["L0_attn_out_b"]).reshape(8, 128).T),
        "SEL": SEL,
        "RED": RED,
        "selbc": selbc,
        "ident": ident,
    }
    return base


def run_mamba(x, inputs):
    nc = _get("mamba", build_mamba)
    base = _host_mamba(inputs)
    in_maps = []
    for cc in range(NCORE):
        m = dict(base)
        m["x_in"] = np.ascontiguousarray(x[2 * cc:2 * cc + 2])
        in_maps.append(m)
    res = run_bass_kernel_spmd(nc, in_maps, core_ids=list(range(NCORE)))
    return np.concatenate([r["x_out"] for r in res.results], axis=0)




def build_ttt(fc=None):
    fc = fc or FC()
    nc = fc.nc if fc.nc is not None else bass.Bass("TRN2", target_bir_lowering=False, debug=False)
    CK = 128
    NCK = T // CK
    LR = 0.01
    ALPHA = 2.0 / DM
    LAM = 2.0 * LR / DM

    x_in = fc.dram(nc, "x_in", [BPC, T, DM], F32)
    thk_d = fc.dram(nc, "thK_T", [DM, DM], F32)
    thv_d = fc.dram(nc, "thV_T", [DM, DM], F32)
    thq_d = fc.dram(nc, "thQ_T", [DM, DM], F32)
    w0t_d = fc.dram(nc, "W0_T", [DM, DM], F32)
    b0_d = fc.dram(nc, "b0", [1, DM], F32)
    mst_d = fc.dram(nc, "Mstrict", [128, 128], F32)
    min_d = fc.dram(nc, "Mincl", [128, 128], F32)
    id_d = fc.dram(nc, "ident", [128, 128], F32)
    x_out = fc.dram(nc, "x_out", [BPC, T, DM], F32, kind="ExternalOutput")

    thk = fc.sb(nc, "t_thk", [128, 8, DM], BF16)
    thq = fc.sb(nc, "t_thq", [128, 8, DM], BF16)
    thvn = fc.sb(nc, "t_thvn", [128, 8, DM], BF16)
    wt = [fc.sb(nc, f"t_wt{b}", [128, 8, DM], BF16) for b in range(BPC)]
    bst = fc.sb(nc, "t_bst", [1, DM], F32)
    bst_bf = fc.sb(nc, "t_bstbf", [1, DM], BF16)
    mstr = fc.sb(nc, "t_mstr", [128, 128], BF16)
    mincl = fc.sb(nc, "t_mincl", [128, 128], BF16)
    id_bf = fc.sb(nc, "t_id", [128, 128], BF16)
    ones_r = fc.sb(nc, "t_ones_r", [1, 128], BF16)
    ones_c = fc.sb(nc, "t_ones_c", [128, 1], BF16)
    eps_col = fc.sb(nc, "t_eps", [128, 1], F32)
    kfm = fc.sb(nc, "t_kfm", [128, 8, CK], BF16)
    qfm = fc.sb(nc, "t_qfm", [128, 8, CK], BF16)
    ktm = fc.sb(nc, "t_ktm", [128, 8, 128], BF16)
    base_sb = fc.sb(nc, "t_base", [128, DM], BF16)
    gs_tm = fc.sb(nc, "t_gs", [128, DM], BF16)
    nmat = fc.sb(nc, "t_n", [128, 128], BF16)
    pmat = fc.sb(nc, "t_p", [128, 128], BF16)
    ptmat = fc.sb(nc, "t_pt", [128, 128], BF16)
    tmat = fc.sb(nc, "t_t", [128, 128], BF16)
    umat = fc.sb(nc, "t_u", [128, 128], BF16)
    xn_fm = fc.sb(nc, "t_xnfm", [128, 8, CK], BF16)
    xtile = fc.sb(nc, "t_xtile", [128, DM], F32)
    xn_tm = fc.sb(nc, "t_xntm", [128, DM], BF16)
    lnscr = fc.sb(nc, "t_lnscr", [128, DM], BF16)
    lnm = fc.sb(nc, "t_lnm", [128, 1], F32)
    lns = fc.sb(nc, "t_lns", [128, 1], F32)
    lnr = fc.sb(nc, "t_lnr", [128, 1], F32)
    o_tm = fc.sb(nc, "t_otm", [128, DM], F32)

    ps_big = fc.ps(nc, "t_psbig", [128, DM], F32)
    ps_sm = fc.ps(nc, "t_pssm", [128, 512], F32)
    ps_tr = [fc.ps(nc, f"t_pstr{i}", [128, 1024], BF16) for i in range(2)]

    ch = Chain(nc, "t_ch")
    pe_s = Chain(nc, "t_pe")
    ev_s = Chain(nc, "t_ev")
    dch = Chain(nc, "t_dch")
    hch = Chain(nc, "t_hch")
    outs = Chain(nc, "t_outs")

    with nc.Block() as bk:
        def gsec(g):
            dch.push(g.dma_start(thk[:, :, :], thk_d.rearrange("(k p) m -> p k m", p=128)), 16)
            dch.push(g.dma_start(thq[:, :, :], thq_d.rearrange("(k p) m -> p k m", p=128)), 16)
            dch.push(g.dma_start(thvn[:, :, :], thv_d.rearrange("(k p) m -> p k m", p=128)), 16)
            for b in range(BPC):
                dch.push(g.dma_start(wt[b][:, :, :], w0t_d.rearrange("(k p) m -> p k m", p=128)), 16)
            dch.push(g.dma_start(mstr[:, :], mst_d), 16)
            dch.push(g.dma_start(mincl[:, :], min_d), 16)
            dch.push(g.dma_start(id_bf[:, :], id_d), 16)
            g.memset(ones_r[:, :], 1.0)
            g.memset(ones_c[:, :], 1.0)
            g.memset(eps_col[:, :], EPS)
        bk.gpsimd(gsec)

        def vneg(s):
            dch.wait(s)
            for j in range(8):
                ch.push(s.activation(thvn[:, j, :], thvn[:, j, :], AF.Identity, scale=-1.0))
        bk.scalar(vneg)

    for b in range(BPC):
        with nc.Block() as bk:
            def binit(g, b=b):
                dch.push(g.dma_start(bst[:, :], b0_d), 16)
            bk.gpsimd(binit)

            def binit2(v):
                dch.wait(v)
                ch.push(v.tensor_copy(bst_bf[:, :], bst[:, :]))
            bk.vector(binit2)

        for c in range(NCK):
            t0 = c * CK
            with nc.Block() as bk:
                def dmain(sy, b=b, t0=t0):
                    ch.wait(sy)
                    hch.push(sy.dma_start(xtile[:, :], x_in[b, t0:t0 + CK, :]), 16)
                bk.sync(dmain)

                def vstart(v):
                    dch.wait(v)
                    hch.wait(v)
                    ch.push(v.tensor_copy(lnm[:, :], eps_col[:, :]))
                bk.vector(vstart)

                _ln_emit(nc, bk, ch, xtile, xn_tm, lnscr, lnm, lns, lnr, eps_col)

                pe0, ev0 = pe_s.n, ev_s.n

                def trans(t, pe0=pe0, ev0=ev0):
                    ch.wait(t)
                    for j in range(8):
                        if j >= 2:
                            ev_s.wait(t, ev0 + (j - 2) + 1)
                        pe_s.push(t.transpose(ps_tr[j % 2][:, 0:128],
                                              xn_tm[:, j * 128:(j + 1) * 128], id_bf[:, :]))
                bk.tensor(trans)

                def evac(v, pe0=pe0, ev0=ev0):
                    for j in range(8):
                        pe_s.wait(v, pe0 + 1 + j)
                        ev_s.push(v.tensor_copy(xn_fm[:, j, :], ps_tr[j % 2][:, 0:128]))
                bk.vector(evac)

                def joinv(v):
                    ev_s.wait(v)
                    ch.push(v.tensor_copy(lnm[:, :], lnm[:, :]))
                bk.vector(joinv)

                for dst, th in [(kfm, thk), (qfm, thq)]:
                    for i in range(8):
                        def pmm(t, dst=dst, th=th, i=i):
                            ch.wait(t)
                            for k in range(8):
                                inst = t.matmul(ps_sm[:, 0:128], th[:, k, i * 128:(i + 1) * 128],
                                                xn_fm[:, k, :], start=(k == 0), stop=(k == 7))
                            ch.push(inst)
                        bk.tensor(pmm)

                        def pev(v, dst=dst, i=i):
                            ch.wait(v)
                            ch.push(v.tensor_copy(dst[:, i, :], ps_sm[:, 0:128]))
                        bk.vector(pev)

                def basemm(t, b=b):
                    ch.wait(t)
                    for k in range(8):
                        for hh in range(2):
                            t.matmul(ps_big[:, hh * 512:(hh + 1) * 512], kfm[:, k, :],
                                     wt[b][:, k, hh * 512:(hh + 1) * 512],
                                     start=(k == 0), stop=False)
                    for hh in range(2):
                        t.matmul(ps_big[:, hh * 512:(hh + 1) * 512], ones_r[:, :],
                                 bst_bf[0:1, hh * 512:(hh + 1) * 512], start=False, stop=False)
                    for k in range(8):
                        for hh in range(2):
                            inst = t.matmul(ps_big[:, hh * 512:(hh + 1) * 512], xn_fm[:, k, :],
                                            thvn[:, k, hh * 512:(hh + 1) * 512],
                                            start=False, stop=(k == 7))
                    ch.push(inst)
                bk.tensor(basemm)

                def baseev(s):
                    ch.wait(s)
                    ch.push(s.activation(base_sb[:, :], ps_big[:, :], AF.Identity, scale=ALPHA))
                bk.scalar(baseev)

                def kkmm(t):
                    ch.wait(t)
                    for k in range(8):
                        t.matmul(ps_sm[:, 0:128], kfm[:, k, :], kfm[:, k, :],
                                 start=(k == 0), stop=False)
                    inst = t.matmul(ps_sm[:, 0:128], ones_r[:, :],
                                    ones_r[:, :], start=False, stop=True)
                    ch.push(inst)
                bk.tensor(kkmm)

                def nev(v):
                    ch.wait(v)
                    ch.push(v.scalar_tensor_tensor(nmat[:, :], ps_sm[:, 0:128], -LAM,
                                                   mstr[:, :], ALU.mult, ALU.mult))
                bk.vector(nev)

                def tinit(v):
                    ch.wait(v)
                    t1 = ch.push(v.tensor_tensor(tmat[:, :], nmat[:, :], id_bf[:, :], ALU.add))
                    ch.wait(v, t1)
                    ch.push(v.tensor_copy(pmat[:, :], nmat[:, :]))
                bk.vector(tinit)

                for it in range(6):
                    def sq1(t, it=it):
                        ch.wait(t)
                        ch.push(t.transpose(ps_tr[0][:, 0:128], pmat[:, :], id_bf[:, :]))
                    bk.tensor(sq1)

                    def sq2(v, it=it):
                        ch.wait(v)
                        ch.push(v.tensor_copy(ptmat[:, :], ps_tr[0][:, 0:128]))
                    bk.vector(sq2)

                    def sq3(t, it=it):
                        ch.wait(t)
                        ch.push(t.matmul(ps_sm[:, 0:128], ptmat[:, :], pmat[:, :],
                                         start=True, stop=True))
                        ch.push(t.matmul(ps_sm[:, 128:256], ptmat[:, :], tmat[:, :],
                                         start=True, stop=True))
                    bk.tensor(sq3)

                    def sq4(v, it=it):
                        ch.wait(v)
                        t1 = ch.push(v.tensor_copy(pmat[:, :], ps_sm[:, 0:128]))
                        ch.wait(v, t1)
                        ch.push(v.tensor_tensor(tmat[:, :], tmat[:, :], ps_sm[:, 128:256], ALU.add))
                    bk.vector(sq4)

                def kqmm(t):
                    ch.wait(t)
                    for k in range(8):
                        t.matmul(ps_sm[:, 0:128], kfm[:, k, :], qfm[:, k, :],
                                 start=(k == 0), stop=False)
                    inst = t.matmul(ps_sm[:, 0:128], ones_r[:, :], ones_r[:, :],
                                    start=False, stop=True)
                    ch.push(inst)
                bk.tensor(kqmm)

                def uev(v):
                    ch.wait(v)
                    ch.push(v.tensor_tensor(umat[:, :], ps_sm[:, 0:128], mincl[:, :], ALU.mult))
                bk.vector(uev)

                def gmm(t):
                    ch.wait(t)
                    for hh in range(2):
                        inst = t.matmul(ps_big[:, hh * 512:(hh + 1) * 512], tmat[:, :],
                                        base_sb[:, hh * 512:(hh + 1) * 512], start=True, stop=True)
                    ch.push(inst)
                bk.tensor(gmm)

                def gev(s):
                    ch.wait(s)
                    ch.push(s.activation(gs_tm[:, :], ps_big[:, :], AF.Identity, scale=-LR))
                bk.scalar(gev)

                def omm(t, b=b):
                    ch.wait(t)
                    for k in range(8):
                        for hh in range(2):
                            t.matmul(ps_big[:, hh * 512:(hh + 1) * 512], qfm[:, k, :],
                                     wt[b][:, k, hh * 512:(hh + 1) * 512],
                                     start=(k == 0), stop=False)
                    for hh in range(2):
                        t.matmul(ps_big[:, hh * 512:(hh + 1) * 512], ones_r[:, :],
                                 bst_bf[0:1, hh * 512:(hh + 1) * 512], start=False, stop=False)
                    for hh in range(2):
                        inst = t.matmul(ps_big[:, hh * 512:(hh + 1) * 512], umat[:, :],
                                        gs_tm[:, hh * 512:(hh + 1) * 512], start=False, stop=True)
                    ch.push(inst)
                bk.tensor(omm)

                def dmx(sy, b=b, t0=t0):
                    ch.wait(sy)
                    outs.wait(sy)
                    hch.push(sy.dma_start(xtile[:, :], x_in[b, t0:t0 + CK, :]), 16)
                bk.sync(dmx)

                def oev(v):
                    ch.wait(v)
                    hch.wait(v)
                    ch.push(v.tensor_tensor(o_tm[:, :], ps_big[:, :], xtile[:, :], ALU.add))
                bk.vector(oev)

                def dmo(e, b=b, t0=t0):
                    ch.wait(e)
                    outs.push(e.dma_start(x_out[b, t0:t0 + CK, :], o_tm[:, :]), 16)
                if fc.out_swdge:
                    bk.gpsimd(dmo)
                else:
                    bk.sync(dmo)

                pe0, ev0 = pe_s.n, ev_s.n

                def ktr(t, pe0=pe0, ev0=ev0):
                    ch.wait(t)
                    for i in range(8):
                        if i >= 2:
                            ev_s.wait(t, ev0 + (i - 2) + 1)
                        pe_s.push(t.transpose(ps_tr[i % 2][:, 0:128], kfm[:, i, :], id_bf[:, :]))
                bk.tensor(ktr)

                def kev(v, pe0=pe0, ev0=ev0):
                    for i in range(8):
                        pe_s.wait(v, pe0 + 1 + i)
                        ev_s.push(v.tensor_copy(ktm[:, i, :], ps_tr[i % 2][:, 0:128]))
                bk.vector(kev)

                def kjoin(v):
                    ev_s.wait(v)
                    ch.push(v.tensor_copy(lnm[:, :], lnm[:, :]))
                bk.vector(kjoin)

                for i in range(8):
                    def dwmm(t, i=i):
                        ch.wait(t)
                        for hh in range(2):
                            inst = t.matmul(ps_big[:, hh * 512:(hh + 1) * 512], ktm[:, i, :],
                                            gs_tm[:, hh * 512:(hh + 1) * 512], start=True, stop=True)
                        ch.push(inst)
                    bk.tensor(dwmm)

                    def dwup(v, b=b, i=i):
                        ch.wait(v)
                        ch.push(v.tensor_tensor(wt[b][:, i, :], ps_big[:, :], wt[b][:, i, :], ALU.add))
                    bk.vector(dwup)

                for hh in range(2):
                    def bmm(t, hh=hh):
                        ch.wait(t)
                        ch.push(t.matmul(ps_sm[0:1, :], ones_c[:, :],
                                         gs_tm[:, hh * 512:(hh + 1) * 512], start=True, stop=True))
                    bk.tensor(bmm)

                    def bup(v, hh=hh):
                        ch.wait(v)
                        ch.push(v.tensor_tensor(bst[0:1, hh * 512:(hh + 1) * 512], ps_sm[0:1, :],
                                                bst[0:1, hh * 512:(hh + 1) * 512], ALU.add))
                    bk.vector(bup)

                def bbf(v):
                    ch.wait(v)
                    ch.push(v.tensor_copy(bst_bf[:, :], bst[:, :]))
                bk.vector(bbf)

                def fin(sy):
                    outs.wait(sy)
                bk.sync(fin)

    return nc


def _host_ttt(inputs):
    ident = np.eye(128, dtype=np.float32)
    ii, jj = np.meshgrid(np.arange(128), np.arange(128), indexing="ij")
    Mstrict = (ii < jj).astype(np.float32)
    Mincl = (ii <= jj).astype(np.float32)
    base = {
        "thK_T": np.ascontiguousarray(np.asarray(inputs["L1_attn_theta_K"]).T),
        "thV_T": np.ascontiguousarray(np.asarray(inputs["L1_attn_theta_V"]).T),
        "thQ_T": np.ascontiguousarray(np.asarray(inputs["L1_attn_theta_Q"]).T),
        "W0_T": np.ascontiguousarray(np.asarray(inputs["L1_attn_W0"]).T),
        "b0": np.ascontiguousarray(np.asarray(inputs["L1_attn_b0"]).reshape(1, DM)),
        "Mstrict": Mstrict,
        "Mincl": Mincl,
        "ident": ident,
    }
    return base


def run_ttt(x, inputs):
    nc = _get("ttt", build_ttt)
    base = _host_ttt(inputs)
    in_maps = []
    for cc in range(NCORE):
        m = dict(base)
        m["x_in"] = np.ascontiguousarray(x[2 * cc:2 * cc + 2])
        in_maps.append(m)
    res = run_bass_kernel_spmd(nc, in_maps, core_ids=list(range(NCORE)))
    return np.concatenate([r["x_out"] for r in res.results], axis=0)




def build_lstm(fc=None):
    fc = fc or FC()
    nc = fc.nc if fc.nc is not None else bass.Bass("TRN2", target_bir_lowering=False, debug=False)
    NB = 16
    TS = 256
    if fc.lstm_gather:
        agout_d = fc.dmap["agout"]
        tsel_d = fc.dram(nc, "tsel", [128, 32], mybir.dt.uint32)
    else:
        x_in = fc.dram(nc, "x_in", [NB, TS, DM], F32)
    aw_d = fc.dram(nc, "aw", [6, DM, DM], F32)
    ow2_d = fc.dram(nc, "ow2", [2, DM, DM], F32)
    gb_d = fc.dram(nc, "gb", [128, 3, 8], F32)
    bo_d = fc.dram(nc, "bo", [128, 8], F32)
    ic_d = fc.dram(nc, "in_c", [128, 8], F32)
    id_d = fc.dram(nc, "ident", [128, 128], F32)
    x_out = fc.dram(nc, "x_out", [NB, TS, DM], F32, kind="ExternalOutput")

    AW = fc.sb(nc, "l_AW", [128, 6, 8, DM], BF16)
    wv = [AW[:, i, :, :] for i in range(6)]
    xn3 = fc.sb(nc, "l_xn3", [128, 8, NB * TS], BF16)
    c_fm = fc.sb(nc, "l_c", [128, 8, TS], F32)
    c_bf = [fc.sb(nc, f"l_cbf{i}", [128, 8, TS], BF16) for i in range(2)]
    f_sb = fc.sb(nc, "l_f", [128, TS], BF16)
    i_sb = fc.sb(nc, "l_i", [128, TS], BF16)
    g_sb = fc.sb(nc, "l_g", [128, TS], BF16)
    tt1 = fc.sb(nc, "l_tt1", [128, TS], F32)
    tt2 = fc.sb(nc, "l_tt2", [128, TS], F32)
    gb_sb = fc.sb(nc, "l_gb", [128, 3, 8], F32)
    bo_sb = fc.sb(nc, "l_bo", [128, 8], F32)
    ic_sb = fc.sb(nc, "l_ic", [128, 8], F32)
    onef = fc.sb(nc, "l_onef", [128, TS], F32)
    id_bf = fc.sb(nc, "l_id", [128, 128], BF16)
    eps_col = fc.sb(nc, "l_eps", [128, 1], F32)
    h_fm = fc.sb(nc, "l_hfm", [128, 8, TS], BF16)
    xtile = fc.sb(nc, "l_xtile", [128, DM], F32)
    xn_tm = fc.sb(nc, "l_xntm", [128, DM], BF16)
    lnscr = fc.sb(nc, "l_lnscr", [128, DM], BF16)
    lnm = fc.sb(nc, "l_lnm", [128, 1], F32)
    lns = fc.sb(nc, "l_lns", [128, 1], F32)
    lnr = fc.sb(nc, "l_lnr", [128, 1], F32)
    o_tm = fc.sb(nc, "l_otm", [128, DM], F32)
    if fc.lstm_gather:
        xg_bf = fc.sb(nc, "l_xgbf", [128, DM], BF16)
        tsel_sb = fc.sb(nc, "l_tsel", [128, 32], mybir.dt.uint32)

    ps_f = fc.ps(nc, "l_psf", [128, 512], F32)
    ps_i = fc.ps(nc, "l_psi", [128, 512], F32)
    ps_g = fc.ps(nc, "l_psg", [128, 512], F32)
    ps_tr = [fc.ps(nc, f"l_pstr{i}", [128, 1024], BF16) for i in range(2)]

    ch = Chain(nc, "l_ch")
    pe_s = Chain(nc, "l_pe")
    ev_s = Chain(nc, "l_ev")
    dch = Chain(nc, "l_dch")
    hch = Chain(nc, "l_hch")
    wdma = Chain(nc, "l_wdma")
    outs = Chain(nc, "l_outs")

    with nc.Block() as bk:
        def gsec(g):
            wdma.push(g.dma_start(AW[:, :, :, :],
                                  aw_d.rearrange("w (k p) m -> p w k m", p=128)), 16)
            dch.push(g.dma_start(gb_sb[:, :, :], gb_d), 16)
            dch.push(g.dma_start(bo_sb[:, :], bo_d), 16)
            dch.push(g.dma_start(ic_sb[:, :], ic_d), 16)
            dch.push(g.dma_start(id_bf[:, :], id_d), 16)
            g.memset(eps_col[:, :], EPS)
            g.memset(onef[:, :], 1.0)
            if fc.lstm_gather:
                dch.push(g.dma_start(tsel_sb[:, :], tsel_d), 16)
        bk.gpsimd(gsec)

        def cinit(s):
            dch.wait(s)
            for m in range(8):
                ch.push(s.activation(c_fm[:, m, :], onef[:, :], AF.Identity,
                                     scale=ic_sb[:, m:m + 1]))
        bk.scalar(cinit)

        def cbf0(v):
            ch.wait(v)
            ch.push(v.tensor_copy(c_bf[0][:, :, :], c_fm[:, :, :]))
        bk.vector(cbf0)

    for b in range(16):
        for tt in range(2):
            with nc.Block() as bk:
                if fc.lstm_gather:
                    def dmain(g, b=b, tt=tt):
                        ch.wait(g)
                        dch.push(g.indirect_dma_start(
                            xg_bf[:, :], None, agout_d,
                            bass.IndirectOffsetOnAxis(ap=tsel_sb[:, b * 2 + tt:b * 2 + tt + 1],
                                                      axis=0)), 16)
                    bk.gpsimd(dmain)

                    def vstart(v):
                        dch.wait(v)
                        ch.push(v.tensor_copy(xtile[:, :], xg_bf[:, :]))
                    bk.vector(vstart)
                else:
                    def dmain(sy, b=b, tt=tt):
                        ch.wait(sy)
                        hch.push(sy.dma_start(xtile[:, :], x_in[b, tt * 128:(tt + 1) * 128, :]), 16)
                    bk.sync(dmain)

                    def vstart(v):
                        dch.wait(v)
                        hch.wait(v)
                        ch.push(v.tensor_copy(lnm[:, :], eps_col[:, :]))
                    bk.vector(vstart)

                _ln_emit(nc, bk, ch, xtile, xn_tm, lnscr, lnm, lns, lnr, eps_col)

                pe0, ev0 = pe_s.n, ev_s.n

                def trans(t, pe0=pe0, ev0=ev0):
                    ch.wait(t)
                    for j in range(8):
                        if j >= 2:
                            ev_s.wait(t, ev0 + (j - 2) + 1)
                        pe_s.push(t.transpose(ps_tr[j % 2][:, 0:128],
                                              xn_tm[:, j * 128:(j + 1) * 128], id_bf[:, :]))
                bk.tensor(trans)

                def evac(v, b=b, tt=tt, pe0=pe0, ev0=ev0):
                    col = b * TS + tt * 128
                    for j in range(8):
                        pe_s.wait(v, pe0 + 1 + j)
                        ev_s.push(v.tensor_copy(xn3[:, j, col:col + 128], ps_tr[j % 2][:, 0:128]))
                bk.vector(evac)

                def joinv(v):
                    ev_s.wait(v)
                    ch.push(v.tensor_copy(lnm[:, :], lnm[:, :]))
                bk.vector(joinv)

    for b in range(16):
        col = b * TS
        with nc.Block() as bk:
            for m in range(8):
                def gmm(t, b=b, m=m, col=col):
                    wdma.wait(t)
                    ch.wait(t)
                    for (ps, wi_, ui_) in [(ps_f, 0, 1), (ps_i, 2, 3), (ps_g, 4, 5)]:
                        for k in range(8):
                            t.matmul(ps[:, 0:TS], wv[wi_][:, k, m * 128:(m + 1) * 128],
                                     xn3[:, k, col:col + TS], start=(k == 0), stop=False)
                        for k in range(8):
                            inst = t.matmul(ps[:, 0:TS], wv[ui_][:, k, m * 128:(m + 1) * 128],
                                            c_bf[b % 2][:, k, :], start=False, stop=(k == 7))
                    ch.push(inst)
                bk.tensor(gmm)

                def gact(s, m=m):
                    ch.wait(s)
                    t1 = ch.push(s.activation(f_sb[:, :], ps_f[:, 0:TS], AF.Sigmoid,
                                              bias=gb_sb[:, 0, m:m + 1]))
                    t2 = ch.push(s.activation(i_sb[:, :], ps_i[:, 0:TS], AF.Sigmoid,
                                              bias=gb_sb[:, 1, m:m + 1]))
                    ch.push(s.activation(g_sb[:, :], ps_g[:, 0:TS], AF.Tanh,
                                         bias=gb_sb[:, 2, m:m + 1]))
                bk.scalar(gact)

                def cup(v, b=b, m=m):
                    ch.wait(v)
                    t1 = ch.push(v.tensor_tensor(tt1[:, :], f_sb[:, :], c_fm[:, m, :], ALU.mult))
                    t2 = ch.push(v.tensor_tensor(tt2[:, :], i_sb[:, :], g_sb[:, :], ALU.mult))
                    ch.wait(v, t2)
                    t3 = ch.push(v.tensor_tensor(c_fm[:, m, :], tt1[:, :], tt2[:, :], ALU.add))
                    ch.wait(v, t3)
                    ch.push(v.tensor_copy(c_bf[(b + 1) % 2][:, m, :], c_fm[:, m, :]))
                bk.vector(cup)

    with nc.Block() as bk:
        def wload2(g):
            wdma.push(g.dma_start(AW[:, 0:2, :, :],
                                  ow2_d.rearrange("w (k p) m -> p w k m", p=128)), 16)
        bk.gpsimd(wload2)

    for b in range(16):
        col = b * TS
        with nc.Block() as bk:
            for m in range(8):
                def hmm(t, b=b, m=m, col=col):
                    wdma.wait(t)
                    ch.wait(t)
                    for k in range(8):
                        t.matmul(ps_f[:, 0:TS], wv[0][:, k, m * 128:(m + 1) * 128],
                                 xn3[:, k, col:col + TS], start=(k == 0), stop=False)
                    for k in range(8):
                        inst = t.matmul(ps_f[:, 0:TS], wv[1][:, k, m * 128:(m + 1) * 128],
                                        c_bf[0][:, k, :], start=False, stop=(k == 7))
                    ch.push(inst)
                bk.tensor(hmm)

                def hact(s, m=m):
                    ch.wait(s)
                    ch.push(s.activation(h_fm[:, m, :], ps_f[:, 0:TS], AF.Tanh,
                                         bias=bo_sb[:, m:m + 1]))
                bk.scalar(hact)

            for tt in range(2):
                if fc.lstm_gather:
                    def dmx(g, b=b, tt=tt):
                        ch.wait(g)
                        outs.wait(g)
                        dch.push(g.indirect_dma_start(
                            xg_bf[:, :], None, agout_d,
                            bass.IndirectOffsetOnAxis(ap=tsel_sb[:, b * 2 + tt:b * 2 + tt + 1],
                                                      axis=0)), 16)
                    bk.gpsimd(dmx)

                    def vcast(v):
                        dch.wait(v)
                        ch.push(v.tensor_copy(xtile[:, :], xg_bf[:, :]))
                    bk.vector(vcast)
                else:
                    def dmx(sy, b=b, tt=tt):
                        ch.wait(sy)
                        outs.wait(sy)
                        hch.push(sy.dma_start(xtile[:, :], x_in[b, tt * 128:(tt + 1) * 128, :]), 16)
                    bk.sync(dmx)

                pe0, ev0 = pe_s.n, ev_s.n

                def etr(t, tt=tt, pe0=pe0, ev0=ev0):
                    ch.wait(t)
                    for m in range(8):
                        if m >= 2:
                            ev_s.wait(t, ev0 + (m - 2) + 1)
                        pe_s.push(t.transpose(ps_tr[m % 2][:, 0:128],
                                              h_fm[:, m, tt * 128:(tt + 1) * 128], id_bf[:, :]))
                bk.tensor(etr)

                def eev(v, tt=tt, pe0=pe0, ev0=ev0):
                    if not fc.lstm_gather:
                        hch.wait(v)
                    else:
                        ch.wait(v)
                    for m in range(8):
                        pe_s.wait(v, pe0 + 1 + m)
                        ev_s.push(v.tensor_tensor(o_tm[:, m * 128:(m + 1) * 128],
                                                  ps_tr[m % 2][:, 0:128],
                                                  xtile[:, m * 128:(m + 1) * 128], ALU.add))
                bk.vector(eev)

                def ejoin(v):
                    ev_s.wait(v)
                    ch.push(v.tensor_copy(lnm[:, :], lnm[:, :]))
                bk.vector(ejoin)

                def dmo(sy, b=b, tt=tt):
                    ch.wait(sy)
                    outs.push(sy.dma_start(x_out[b, tt * 128:(tt + 1) * 128, :], o_tm[:, :]), 16)
                bk.sync(dmo)

            def fin(sy):
                outs.wait(sy)
            bk.sync(fin)

    return nc


def _host_lstm(inputs):
    ident = np.eye(128, dtype=np.float32)
    aw = np.stack([np.asarray(inputs["L1_ffn_" + n]).T for n in
                   ["wf", "uf", "wi", "ui", "wc", "uc"]])
    ow2 = np.stack([np.asarray(inputs["L1_ffn_wo"]).T, np.asarray(inputs["L1_ffn_uo"]).T])
    gb = np.stack([np.asarray(inputs["L1_ffn_" + n]).reshape(8, 128).T for n in
                   ["bf", "bi", "bc"]], axis=1)
    base = {
        "aw": np.ascontiguousarray(aw),
        "ow2": np.ascontiguousarray(ow2),
        "gb": np.ascontiguousarray(gb),
        "bo": np.ascontiguousarray(np.asarray(inputs["L1_ffn_bo"]).reshape(8, 128).T),
        "in_c": np.ascontiguousarray(np.asarray(inputs["L1_ffn_in_c"]).reshape(8, 128).T),
        "ident": ident,
    }
    return base


def run_lstm(x, inputs):
    nc = _get("lstm", build_lstm)
    base = _host_lstm(inputs)
    in_maps = []
    for cc in range(NCORE):
        m = dict(base)
        m["x_in"] = np.ascontiguousarray(x[:, 256 * cc:256 * (cc + 1), :])
        in_maps.append(m)
    res = run_bass_kernel_spmd(nc, in_maps, core_ids=list(range(NCORE)))
    return np.concatenate([r["x_out"] for r in res.results], axis=1)


# version 05
# speedup vs baseline: 1.5220x; 1.1958x over previous
import numpy as np
import concourse.bass as bass
import concourse.mybir as mybir
from concourse.bass_utils import run_bass_kernel_spmd

F32 = mybir.dt.float32
BF16 = mybir.dt.bfloat16
AF = mybir.ActivationFunctionType
ALU = mybir.AluOpType
AX = mybir.AxisListType

NCORE = 8
BPC = 2
T = 2048
DM = 1024
DI = 2048
DS = 16
TOK = BPC * T
EPS = 1e-5


class Chain:

    def __init__(self, nc, name):
        self.h = nc.alloc_semaphore(name)
        self.n = 0

    def push(self, inst, k=1):
        inst.then_inc(self.h, k)
        self.n += k
        return self.n

    def wait(self, eng, tok=None):
        eng.wait_ge(self.h, self.n if tok is None else tok)




class FC:

    def __init__(self, nc=None, prefix="", dmap=None, pmap=None, stack=None,
                 lstm_gather=False, out_swdge=False):
        self.nc = nc
        self.prefix = prefix
        self.dmap = dmap or {}
        self.pmap = pmap or {}
        self.stack = stack
        self.lstm_gather = lstm_gather
        self.out_swdge = out_swdge

    def dram(self, nc, name, shape, dtype, kind="ExternalInput"):
        if name in self.dmap:
            return self.dmap[name]
        return nc.dram_tensor(self.prefix + name, shape, dtype, kind=kind).ap()

    def sb(self, nc, name, shape, dtype):
        if self.stack is not None:
            return self.stack.enter_context(nc.sbuf_tensor(self.prefix + name, shape, dtype))
        return nc.alloc_sbuf_tensor(name, shape, dtype)

    def ps(self, nc, name, shape, dtype):
        if name in self.pmap:
            return self.pmap[name]
        return nc.alloc_psum_tensor(name, shape, dtype)

def _ln_emit(nc, bk, ch, xtile, xn_tm, lnscr, lnm, lns, lnr, eps_col=None):

    def vpart1(v):
        ch.wait(v)
        t1 = ch.push(v.tensor_reduce(lnm[:, :], xtile[:, :], AX.X, ALU.add))
        ch.wait(v, t1)
        t2 = ch.push(v.tensor_scalar(lnm[:, :], lnm[:, :], 1.0 / DM, None, ALU.mult))
        ch.wait(v, t2)
        ch.push(v.tensor_scalar(xtile[:, :], xtile[:, :], lnm[:, :], None, ALU.subtract))
    bk.vector(vpart1)

    def spart(s):
        ch.wait(s)
        t1 = ch.push(s.activation(lnscr[:, :], xtile[:, :], AF.Square, accum_out=lns[:, :]))
        ch.wait(s, t1)
        ch.push(s.activation(lnr[:, :], lns[:, :], AF.Sqrt, scale=1.0 / DM, bias=eps_col[:, :]))
    bk.scalar(spart)

    def vpart2(v):
        ch.wait(v)
        t1 = ch.push(v.reciprocal(lnr[:, :], lnr[:, :]))
        ch.wait(v, t1)
        ch.push(v.tensor_scalar(xn_tm[:, :], xtile[:, :], lnr[:, :], None, ALU.mult))
    bk.vector(vpart2)


def build_moe(dbg=False, fc=None):
    fc = fc or FC()
    nc = fc.nc if fc.nc is not None else bass.Bass("TRN2", target_bir_lowering=False, debug=False)
    x_in = fc.dram(nc, "x_in", [TOK, DM], F32)
    wr_d = fc.dram(nc, "wr", [DM, 8], F32)
    wg_d = fc.dram(nc, "wg", [8, DM, DI], F32)
    bg_d = fc.dram(nc, "bg", [128, 16, 8], F32)
    wu_d = fc.dram(nc, "wu", [8, DM, DI], F32)
    bu_d = fc.dram(nc, "bu", [128, 16, 8], F32)
    wo_d = fc.dram(nc, "wo", [8, DI, DM], F32)
    bo_d = fc.dram(nc, "bo", [8, DM], F32)
    id_d = fc.dram(nc, "ident", [128, 128], F32)
    x_out = fc.dram(nc, "x_out", [TOK, DM], F32, kind="ExternalOutput")
    if dbg:
        d_xq = fc.dram(nc, "d_xq", [128, 8, 1024], F32, kind="ExternalOutput")
        d_w = fc.dram(nc, "d_w", [128, 8, 8], F32, kind="ExternalOutput")
        d_acc0 = fc.dram(nc, "d_acc0", [128, 8, DM], F32, kind="ExternalOutput")
        d_acc = fc.dram(nc, "d_acc", [128, 8, DM], F32, kind="ExternalOutput")
        d_hw = fc.dram(nc, "d_hw", [128, 16, 512], F32, kind="ExternalOutput")
        d_acc1 = fc.dram(nc, "d_acc1", [128, 8, DM], F32, kind="ExternalOutput")

    QT = 1024
    NQ = TOK // QT
    NTT = QT // 128

    wr_sb = fc.sb(nc, "wr_sb", [128, 8, 8], BF16)
    bg_sb = fc.sb(nc, "bg_sb", [128, 16, 8], F32)
    bu_sb = fc.sb(nc, "bu_sb", [128, 16, 8], F32)
    bo_sb = fc.sb(nc, "bo_sb", [8, DM], BF16)
    id_bf = fc.sb(nc, "id_bf", [128, 128], BF16)
    xq_fm = fc.sb(nc, "xq_fm", [128, 8, QT], BF16)
    w_tm = fc.sb(nc, "w_tm", [128, NTT, 8], F32)
    w_fmT = fc.sb(nc, "w_fmT", [8, QT], BF16)
    acc = fc.sb(nc, "acc", [128, NTT, DM], F32)
    wg_sb = fc.sb(nc, "wg_sb", [128, 8, DI], BF16)
    wu_sb = fc.sb(nc, "wu_sb", [128, 8, DI], BF16)
    wo_sb = fc.sb(nc, "wo_sb", [128, 16, DM], BF16)
    hw_sb = fc.sb(nc, "hw_sb", [128, 16, 512], BF16)
    gs_sb = [fc.sb(nc, f"gs_sb{i}", [128, 512], BF16) for i in range(2)]
    xtile = fc.sb(nc, "xtile", [128, DM], F32)
    xn_tm = fc.sb(nc, "xn_tm", [128, DM], BF16)
    lnscr = fc.sb(nc, "lnscr", [128, DM], BF16)
    lnm = fc.sb(nc, "lnm", [128, 1], F32)
    lns = fc.sb(nc, "lns", [128, 1], F32)
    lnr = fc.sb(nc, "lnr", [128, 1], F32)
    r_pr = fc.sb(nc, "r_pr", [128, 8], F32)
    r_m1 = fc.sb(nc, "r_m1", [128, 1], F32)
    r_m2 = fc.sb(nc, "r_m2", [128, 1], F32)
    r_k1 = fc.sb(nc, "r_k1", [128, 8], F32)
    r_k2 = fc.sb(nc, "r_k2", [128, 8], F32)
    r_t = fc.sb(nc, "r_t", [128, 8], F32)
    r_den = fc.sb(nc, "r_den", [128, 1], F32)
    w_bf = fc.sb(nc, "w_bf", [128, 8], BF16)
    eps_col = fc.sb(nc, "eps_col", [128, 1], F32)
    o_tm = fc.sb(nc, "o_tm", [128, DM], F32)

    ps_tr = [fc.ps(nc, f"ps_tr{i}", [128, 1024], BF16) for i in range(2)]
    ps_g = fc.ps(nc, "ps_g", [128, 512], F32)
    ps_u = fc.ps(nc, "ps_u", [128, 512], F32)
    ps_o0 = fc.ps(nc, "ps_o0", [128, DM], F32)
    ps_o1 = fc.ps(nc, "ps_o1", [128, DM], F32)
    ps_o = [ps_o0, ps_o0]
    ps_gp = [ps_g[:, :], ps_o1[:, 0:512]]
    ps_up = [ps_u[:, :], ps_o1[:, 512:1024]]
    ps_r = ps_o1[:, 0:8]

    ch = Chain(nc, "ch")
    pe_s = Chain(nc, "pe_s")
    ev_s = Chain(nc, "ev_s")
    ac_s = Chain(nc, "ac_s")
    o_dg = Chain(nc, "o_dg")
    o_dh = Chain(nc, "o_dh")
    wdma = Chain(nc, "wdma")
    outs = Chain(nc, "outs")

    with nc.Block() as bk:
        def gsec(g):
            o_dg.push(g.dma_start(wr_sb[:, :, :], wr_d.rearrange("(k p) e -> p k e", p=128)), 16)
            o_dg.push(g.dma_start(bg_sb[:, :, :], bg_d), 16)
            o_dg.push(g.dma_start(bu_sb[:, :, :], bu_d), 16)
            o_dg.push(g.dma_start(bo_sb[:, :], bo_d), 16)
            o_dg.push(g.dma_start(id_bf[:, :], id_d), 16)
            g.memset(eps_col[:, :], EPS)
        bk.gpsimd(gsec)

    for q in range(NQ):
        with nc.Block() as bk:
            for it in range(NTT):
                tok0 = q * QT + it * 128

                def dmain(sy, tok0=tok0):
                    ch.wait(sy)
                    o_dh.push(sy.dma_start(xtile[:, :], x_in[tok0:tok0 + 128, :]), 16)
                bk.sync(dmain)

                def vbridge(v):
                    o_dg.wait(v)
                    o_dh.wait(v)
                    ch.push(v.tensor_copy(lnm[:, :], eps_col[:, :]))
                bk.vector(vbridge)

                _ln_emit(nc, bk, ch, xtile, xn_tm, lnscr, lnm, lns, lnr, eps_col)

                pe0, ev0 = pe_s.n, ev_s.n

                def trans(t, pe0=pe0, ev0=ev0):
                    ch.wait(t)
                    for j in range(8):
                        if j >= 2:
                            ev_s.wait(t, ev0 + (j - 2) + 1)
                        pe_s.push(t.transpose(ps_tr[j % 2][:, 0:128],
                                              xn_tm[:, j * 128:(j + 1) * 128], id_bf[:, :]))
                bk.tensor(trans)

                def evac(v, it=it, pe0=pe0, ev0=ev0):
                    for j in range(8):
                        pe_s.wait(v, pe0 + 1 + j)
                        ev_s.push(v.tensor_copy(xq_fm[:, j, it * 128:(it + 1) * 128],
                                                ps_tr[j % 2][:, 0:128]))
                bk.vector(evac)

                def joinv(v):
                    ev_s.wait(v)
                    ch.push(v.tensor_copy(lnm[:, :], lnm[:, :]))
                bk.vector(joinv)

                def rmm(t, it=it):
                    ch.wait(t)
                    for k in range(8):
                        inst = t.matmul(ps_r[:, :], xq_fm[:, k, it * 128:(it + 1) * 128],
                                        wr_sb[:, k, :], start=(k == 0), stop=(k == 7))
                    ch.push(inst)
                bk.tensor(rmm)

                def rsoft1(v):
                    ch.wait(v)
                    t1 = ch.push(v.tensor_copy(r_pr[:, :], ps_r[:, :]))
                    ch.wait(v, t1)
                    t2 = ch.push(v.tensor_reduce(r_m1[:, :], r_pr[:, :], AX.X, ALU.max))
                    ch.wait(v, t2)
                    ch.push(v.tensor_scalar(r_pr[:, :], r_pr[:, :], r_m1[:, :], None, ALU.subtract))
                bk.vector(rsoft1)

                def rexp(s):
                    ch.wait(s)
                    ch.push(s.activation(r_pr[:, :], r_pr[:, :], AF.Exp))
                bk.scalar(rexp)

                def rsoft2(v, it=it):
                    ch.wait(v)
                    seq = [
                        lambda: v.tensor_reduce(r_den[:, :], r_pr[:, :], AX.X, ALU.add),
                        lambda: v.reciprocal(r_den[:, :], r_den[:, :]),
                        lambda: v.tensor_scalar(r_pr[:, :], r_pr[:, :], r_den[:, :], None, ALU.mult),
                        lambda: v.tensor_reduce(r_m1[:, :], r_pr[:, :], AX.X, ALU.max),
                        lambda: v.tensor_scalar(r_k1[:, :], r_pr[:, :], r_m1[:, :], None, ALU.is_ge),
                        lambda: v.tensor_tensor(r_t[:, :], r_k1[:, :], r_pr[:, :], ALU.mult),
                        lambda: v.tensor_tensor(r_t[:, :], r_pr[:, :], r_t[:, :], ALU.subtract),
                        lambda: v.tensor_reduce(r_m2[:, :], r_t[:, :], AX.X, ALU.max),
                        lambda: v.tensor_scalar(r_k2[:, :], r_t[:, :], r_m2[:, :], None, ALU.is_ge),
                        lambda: v.tensor_tensor(r_k1[:, :], r_k1[:, :], r_k2[:, :], ALU.add),
                        lambda: v.tensor_scalar(r_den[:, :], r_m1[:, :], r_m2[:, :], 1e-8, ALU.add, ALU.add),
                        lambda: v.reciprocal(r_den[:, :], r_den[:, :]),
                        lambda: v.tensor_tensor(r_t[:, :], r_pr[:, :], r_k1[:, :], ALU.mult),
                        lambda: v.tensor_scalar(w_tm[:, it, :], r_t[:, :], r_den[:, :], None, ALU.mult),
                        lambda: v.tensor_copy(w_bf[:, :], w_tm[:, it, :]),
                    ]
                    for f in seq:
                        tk = ch.push(f())
                        ch.wait(v, tk)
                bk.vector(rsoft2)

                def wtr(t):
                    ch.wait(t)
                    ch.push(t.transpose(ps_tr[0][0:8, 0:128], w_bf[:, :], id_bf[:, :]))
                bk.tensor(wtr)

                def wev(v, it=it):
                    ch.wait(v)
                    ch.push(v.tensor_copy(w_fmT[0:8, it * 128:(it + 1) * 128],
                                          ps_tr[0][0:8, 0:128]))
                bk.vector(wev)

            for it in range(NTT):
                def bterm(t, it=it):
                    ch.wait(t)
                    for hh in range(2):
                        inst = t.matmul(ps_o[0][:, hh * 512:(hh + 1) * 512],
                                        w_fmT[:, it * 128:(it + 1) * 128],
                                        bo_sb[:, hh * 512:(hh + 1) * 512],
                                        start=True, stop=True)
                    ch.push(inst, 1)
                bk.tensor(bterm)

                def bev(v, it=it):
                    ch.wait(v)
                    ch.push(v.tensor_copy(acc[:, it, :], ps_o[0][:, :]))
                bk.vector(bev)

        if dbg and q == 0:
            with nc.Block() as bk:
                def dump1(g):
                    ch.wait(g)
                    outs.push(g.dma_start(d_xq[:, :, :], xq_fm[:, :, :]), 16)
                    outs.push(g.dma_start(d_w[:, :, :], w_tm[:, :, :]), 16)
                    outs.push(g.dma_start(d_acc0[:, :, :], acc[:, :, :]), 16)
                bk.gpsimd(dump1)

        with nc.Block() as bk:
            W0n = wdma.n
            w_wu = {e: W0n + 48 * e + 32 for e in range(8)}
            w_wo = {e: W0n + 48 * e + 48 for e in range(8)}
            P0n = pe_s.n
            p_g, p_u, p_out = {}, {}, {}
            cnt = P0n
            for e in range(8):
                for h in range(2):
                    for m in range(16):
                        gi = (e * 2 + h) * 16 + m
                        cnt += 1; p_g[gi] = cnt
                        cnt += 1; p_u[gi] = cnt
                    for tt in range(4):
                        cnt += 1; p_out[(e, h, tt)] = cnt
            A0n = ac_s.n
            a_si = {}
            cnt = A0n
            for gi in range(256):
                cnt += 1; a_si[gi] = cnt
            E0n = ev_s.n
            d_hm, d_oa = {}, {}
            cnt = E0n
            oi = 0
            oa_flat = {}
            for e in range(8):
                for h in range(2):
                    for m in range(16):
                        gi = (e * 2 + h) * 16 + m
                        cnt += 1; d_hm[gi] = cnt
                    for tt in range(4):
                        cnt += 1; d_oa[(e, h, tt)] = cnt
                        oa_flat[oi] = cnt
                        oi += 1

            def wsec(g, q=q):
                for e in range(8):
                    if e >= 1:
                        pe_s.wait(g, p_u[(e * 2) * 16 - 1 + 0] if False else p_u[(e * 2 - 1) * 16 + 15])
                    wdma.push(g.dma_start(wg_sb[:, :, :],
                                          wg_d[e].rearrange("(k p) m -> p k m", p=128)), 16)
                    wdma.push(g.dma_start(wu_sb[:, :, :],
                                          wu_d[e].rearrange("(k p) m -> p k m", p=128)), 16)
                    if e >= 1:
                        pe_s.wait(g, p_out[(e - 1, 1, 3)])
                    wdma.push(g.dma_start(wo_sb[:, :, :],
                                          wo_d[e].rearrange("(k p) m -> p k m", p=128)), 16)
            bk.gpsimd(wsec)

            def pesec(t, q=q):
                ch.wait(t)
                oi = 0
                for e in range(8):
                    wdma.wait(t, w_wu[e])
                    for h in range(2):
                        tok0 = h * 512
                        for m in range(16):
                            gi = (e * 2 + h) * 16 + m
                            if gi >= 2:
                                ac_s.wait(t, a_si[gi - 2])
                                ev_s.wait(t, d_hm[gi - 2])
                            for k in range(8):
                                inst = t.matmul(ps_gp[gi % 2], wg_sb[:, k, m * 128:(m + 1) * 128],
                                                xq_fm[:, k, tok0:tok0 + 512],
                                                start=(k == 0), stop=(k == 7),
                                                skip_group_check=True)
                            tk = pe_s.push(inst)
                            assert tk == p_g[gi]
                            for k in range(8):
                                inst = t.matmul(ps_up[gi % 2], wu_sb[:, k, m * 128:(m + 1) * 128],
                                                xq_fm[:, k, tok0:tok0 + 512],
                                                start=(k == 0), stop=(k == 7),
                                                skip_group_check=True)
                            tk = pe_s.push(inst)
                            assert tk == p_u[gi]
                        ev_s.wait(t, d_hm[(e * 2 + h) * 16 + 15])
                        wdma.wait(t, w_wo[e])
                        for tt in range(4):
                            if oi >= 1:
                                ev_s.wait(t, oa_flat[oi - 1])
                            for k in range(16):
                                for half in range(2):
                                    inst = t.matmul(ps_o0[:, half * 512:(half + 1) * 512],
                                                    hw_sb[:, k, tt * 128:(tt + 1) * 128],
                                                    wo_sb[:, k, half * 512:(half + 1) * 512],
                                                    start=(k == 0), stop=(k == 15),
                                                    skip_group_check=True)
                            tk = pe_s.push(inst)
                            assert tk == p_out[(e, h, tt)]
                            oi += 1
            bk.tensor(pesec)

            def acsec(s, q=q):
                for e in range(8):
                    for h in range(2):
                        for m in range(16):
                            gi = (e * 2 + h) * 16 + m
                            pe_s.wait(s, p_g[gi])
                            if gi >= 2:
                                ev_s.wait(s, d_hm[gi - 2])
                            tk = ac_s.push(s.activation(gs_sb[gi % 2][:, :], ps_gp[gi % 2],
                                                        AF.Silu, bias=bg_sb[:, m, e:e + 1]))
                            assert tk == a_si[gi]
            bk.scalar(acsec)

            def dvsec(v, q=q):
                for e in range(8):
                    for h in range(2):
                        for m in range(16):
                            gi = (e * 2 + h) * 16 + m
                            if m == 0 and (e, h) != (0, 0):
                                pem, phm = (e, h - 1) if h == 1 else (e - 1, 1)
                                pe_s.wait(v, p_out[(pem, phm, 3)])
                            pe_s.wait(v, p_u[gi])
                            ac_s.wait(v, a_si[gi])
                            tk = ev_s.push(v.scalar_tensor_tensor(hw_sb[:, m, :], ps_up[gi % 2],
                                                                  bu_sb[:, m, e:e + 1],
                                                                  gs_sb[gi % 2][:, :],
                                                                  ALU.add, ALU.mult))
                            assert tk == d_hm[gi]
                        for tt in range(4):
                            it = h * 4 + tt
                            pe_s.wait(v, p_out[(e, h, tt)])
                            tk = ev_s.push(v.scalar_tensor_tensor(acc[:, it, :], ps_o0[:, :],
                                                                  w_tm[:, it, e:e + 1], acc[:, it, :],
                                                                  ALU.mult, ALU.add))
                            assert tk == d_oa[(e, h, tt)]
            bk.vector(dvsec)

            def joinall(v):
                ev_s.wait(v)
                ch.push(v.tensor_copy(r_m1[:, :], r_m1[:, :]))
            bk.vector(joinall)

        with nc.Block() as bk:
            for it in range(NTT):
                tok0 = q * QT + it * 128

                def dmain(sy, tok0=tok0):
                    ch.wait(sy)
                    o_dh.push(sy.dma_start(xtile[:, :], x_in[tok0:tok0 + 128, :]), 16)
                bk.sync(dmain)

                def radd(v, it=it):
                    ch.wait(v)
                    o_dh.wait(v)
                    outs.wait(v)
                    ch.push(v.tensor_tensor(o_tm[:, :], acc[:, it, :], xtile[:, :], ALU.add))
                bk.vector(radd)

                def dmaout(sy, tok0=tok0):
                    ch.wait(sy)
                    outs.push(sy.dma_start(x_out[tok0:tok0 + 128, :], o_tm[:, :]), 16)
                bk.sync(dmaout)

            def fin(sy):
                outs.wait(sy)
            bk.sync(fin)

    return nc


_CACHE = {}


def _get(name, builder):
    if name not in _CACHE:
        nc = builder()
        nc.finalize()
        _CACHE[name] = nc
    return _CACHE[name]


def _host_moe(inputs):
    ident = np.eye(128, dtype=np.float32)
    return {
            "wr": inputs["L0_ffn_W_router"],
            "wg": inputs["L0_ffn_W_gate"],
            "bg": np.ascontiguousarray(np.asarray(inputs["L0_ffn_b_gate"]).reshape(8, 16, 128).transpose(2, 1, 0)),
            "wu": inputs["L0_ffn_W_up"],
            "bu": np.ascontiguousarray(np.asarray(inputs["L0_ffn_b_up"]).reshape(8, 16, 128).transpose(2, 1, 0)),
            "wo": inputs["L0_ffn_W_out"],
            "bo": inputs["L0_ffn_b_out"],
            "ident": ident,
    }


def run_moe(x, inputs):
    nc = _get("moe", build_moe)
    base = _host_moe(inputs)
    in_maps = []
    for c in range(NCORE):
        m = dict(base)
        m["x_in"] = np.ascontiguousarray(x[2 * c:2 * c + 2].reshape(TOK, DM))
        in_maps.append(m)
    res = run_bass_kernel_spmd(nc, in_maps, core_ids=list(range(NCORE)))
    out = np.concatenate([r["x_out"].reshape(BPC, T, DM) for r in res.results], axis=0)
    return out


def build_full():
    import contextlib
    nc = bass.Bass("TRN2", target_bir_lowering=False, debug=False)
    ps_a = nc.alloc_psum_tensor("f_psa", [128, 1024], F32)
    ps_b = nc.alloc_psum_tensor("f_psb", [128, 1024], F32)
    ps_c = nc.alloc_psum_tensor("f_psc", [128, 512], F32)
    ps_d = nc.alloc_psum_tensor("f_psd", [128, 512], F32)
    trA = nc.alloc_psum_tensor("f_trA", [128, 1024], BF16)
    trB = nc.alloc_psum_tensor("f_trB", [128, 1024], BF16)
    mamba_pmap = {"mps_a0": ps_a[:, 0:512], "mps_a1": ps_a[:, 512:1024],
                  "mps_b0": ps_b[:, 0:512], "mps_b1": ps_b[:, 512:1024],
                  "mps_c": ps_c[:, :], "mps_d": ps_d[:, :],
                  "mps_tr0": trA, "mps_tr1": trB}

    x0 = nc.dram_tensor("x_in", [BPC, T, DM], F32, kind="ExternalInput").ap()
    x1_sc = nc.dram_tensor("x1_sc", [BPC, T, DM], F32, kind="Internal").ap()
    x2_sc = nc.dram_tensor("x2_sc", [BPC, T, DM], F32, kind="Internal").ap()
    ag_in = nc.dram_tensor("ag_in", [BPC * T, DM], BF16, kind="Internal").ap()
    ag_out = nc.dram_tensor("ag_out", [NCORE * BPC * T, DM], BF16, kind="Internal").ap()
    x_out = nc.dram_tensor("x_out", [16, 256, DM], F32, kind="ExternalOutput").ap()

    with contextlib.ExitStack() as st:
        fc = FC(nc=nc, prefix="m_", stack=st,
                dmap={"x_in": x0, "x_out": x1_sc},
                pmap=mamba_pmap)
        build_mamba(fc=fc)

    with contextlib.ExitStack() as st:
        fc = FC(nc=nc, prefix="o_", stack=st,
                dmap={"x_in": x1_sc.rearrange("b t d -> (b t) d"),
                      "x_out": x2_sc.rearrange("b t d -> (b t) d")},
                pmap={"ps_tr0": trA, "ps_tr1": trB, "ps_r": ps_b[:, 0:8],
                      "ps_g": ps_c, "ps_u": ps_d, "ps_o0": ps_a, "ps_o1": ps_b})
        build_moe(fc=fc)

    with contextlib.ExitStack() as st:
        fc = FC(nc=nc, prefix="t_", stack=st, out_swdge=True,
                dmap={"x_in": x2_sc,
                      "x_out": ag_in.rearrange("(b t) d -> b t d", b=BPC)},
                pmap={"t_psbig": ps_a, "t_psbig2": ps_b, "t_pssm": ps_c,
                      "t_pstr0": trA, "t_pstr1": trB})
        build_ttt(fc=fc)

    ccs = nc.alloc_semaphore("f_ccs")
    with nc.Block() as bk:
        def ccsec(g):
            g.collective_compute("AllGather", mybir.AluOpType.bypass,
                                 replica_groups=[list(range(NCORE))],
                                 ins=[ag_in.opt()], outs=[ag_out.opt()]).then_inc(ccs, 1)
            g.wait_ge(ccs, 1)
        bk.gpsimd(ccsec)

    with contextlib.ExitStack() as st:
        fc = FC(nc=nc, prefix="l_", stack=st, lstm_gather=True,
                dmap={"agout": ag_out, "x_out": x_out},
                pmap={"l_psf": ps_a[:, 0:512], "l_psi": ps_a[:, 512:1024],
                      "l_psg": ps_b[:, 0:512], "l_psf2": ps_b[:, 512:1024],
                      "l_psi2": ps_c, "l_psg2": ps_d,
                      "l_pstr0": trA, "l_pstr1": trB})
        build_lstm(fc=fc)

    return nc


def kernel(**inputs):
    nc = _get("full", build_full)
    x0 = np.asarray(inputs["L0_attn_x"], dtype=np.float32)
    mb = {("m_" + k): v for k, v in _host_mamba(inputs).items()}
    ob = {("o_" + k): v for k, v in _host_moe(inputs).items()}
    tb = {("t_" + k): v for k, v in _host_ttt(inputs).items()}
    lb = {("l_" + k): v for k, v in _host_lstm(inputs).items()}
    in_maps = []
    for cc in range(NCORE):
        sel = np.zeros((128, 32), np.uint32)
        for bb in range(16):
            for tt in range(2):
                sel[:, bb * 2 + tt] = ((bb // 2) * 4096 + (bb % 2) * 2048
                                       + 256 * cc + tt * 128 + np.arange(128))
        m = {}
        m.update(mb)
        m.update(ob)
        m.update(tb)
        m.update(lb)
        m["x_in"] = np.ascontiguousarray(x0[2 * cc:2 * cc + 2])
        m["l_tsel"] = sel
        in_maps.append(m)
    res = run_bass_kernel_spmd(nc, in_maps, core_ids=list(range(NCORE)))
    return np.concatenate([r["x_out"] for r in res.results], axis=1)


def kernel_unfused(**inputs):
    x = np.asarray(inputs["L0_attn_x"], dtype=np.float32)
    x1 = run_mamba(x, inputs)
    x2 = run_moe(x1, inputs)
    x3 = run_ttt(x2, inputs)
    x4 = run_lstm(x3, inputs)
    return x4




def build_mamba(dbg=False, fc=None):
    import os
    SKIP_SCAN = os.environ.get("SKIP_SCAN") == "1"
    SKIP_DT = os.environ.get("SKIP_DT") == "1"
    SKIP_IP = os.environ.get("SKIP_IP") == "1"
    fc = fc or FC()
    nc = fc.nc if fc.nc is not None else bass.Bass("TRN2", target_bir_lowering=False, debug=False)
    TC = 512
    NCH = T // TC
    NG = DI // 8

    x_in = fc.dram(nc, "x_in", [BPC, T, DM], F32)
    inw_d = fc.dram(nc, "in_w", [DM, 2 * DI], F32)
    inb_d = fc.dram(nc, "in_b", [128, 32], F32)
    cw_d = fc.dram(nc, "conv_w", [128, 16, 4], F32)
    cb_d = fc.dram(nc, "conv_b", [128, 16], F32)
    ap_d = fc.dram(nc, "A_perm", [128, 256], F32)
    dp_d = fc.dram(nc, "Dp_col", [128, 16], F32)
    dtw_d = fc.dram(nc, "dt_w", [DI, DI], F32)
    dtb_d = fc.dram(nc, "dt_b", [128, 16], F32)
    bcw_d = fc.dram(nc, "bc_w", [DI, 32], F32)
    bcb_d = fc.dram(nc, "bc_b", [32, 1], F32)
    ow_d = fc.dram(nc, "out_w", [DI, DM], F32)
    ob_d = fc.dram(nc, "out_b", [128, 8], F32)
    sel_d = fc.dram(nc, "SEL", [128, 16, 128], F32)
    red_d = fc.dram(nc, "RED", [128, 16, 128], F32)
    sbc_d = fc.dram(nc, "selbc", [32, 2, 128], F32)
    id_d = fc.dram(nc, "ident", [128, 128], F32)
    x_out = fc.dram(nc, "x_out", [BPC, T, DM], F32, kind="ExternalOutput")

    xz_sc = fc.dram(nc, "xz_sc", [BPC, 16, 128, T], BF16, kind="Internal")
    gt_sc = fc.dram(nc, "gt_sc", [BPC, 16, 128, T], BF16, kind="Internal")

    inb_sb = fc.sb(nc, "inb_sb", [128, 32], F32)
    cw_sb = fc.sb(nc, "cw_sb", [128, 16, 4], F32)
    cb_sb = fc.sb(nc, "cb_sb", [128, 16], F32)
    ap_sb = fc.sb(nc, "ap_sb", [128, 256], F32)
    dp_sb = fc.sb(nc, "dp_sb", [128, 16], F32)
    dtb_sb = fc.sb(nc, "dtb_sb", [128, 16], F32)
    bcb_sb = fc.sb(nc, "bcb_sb", [32, 1], F32)
    ob_sb = fc.sb(nc, "ob_sb", [128, 8], F32)
    sel_sb = fc.sb(nc, "sel_sb", [128, 16, 128], BF16)
    red_sb = fc.sb(nc, "red_sb", [128, 16, 128], BF16)
    sbc_sb = fc.sb(nc, "sbc_sb", [32, 2, 128], BF16)
    id_bf = fc.sb(nc, "id_bf", [128, 128], BF16)
    eps_col = fc.sb(nc, "eps_col", [128, 1], F32)
    A1 = fc.sb(nc, "m_A1", [128, 8 * 2 * DI], BF16)
    inw_sb = A1[:, :].rearrange("p (k m) -> p k m", k=8)
    dtw_sb = A1[:, :].rearrange("p (k m) -> p k m", k=16)
    ow_sb = fc.sb(nc, "m_ow", [128, 16, DM], BF16)
    bcw_sb = fc.sb(nc, "m_bcw", [128, 16, 32], BF16)
    ar1 = fc.sb(nc, "m_ar1", [128, 16, TC + 3], BF16)
    xz_fm = ar1[:, :, 0:TC]
    xzh = ar1[:, :, :]
    yg_fm = ar1[:, :, 0:TC]
    gtc_fm = fc.sb(nc, "m_gt_fm", [128, 16, TC], BF16)
    xc_fm = fc.sb(nc, "m_xc_fm", [128, 16, TC], BF16)
    dt_fm = fc.sb(nc, "m_dt_fm", [128, 16, TC], BF16)
    arof = fc.sb(nc, "m_arof", [128, 8, TC], BF16)
    xn_fm = arof[:, :, :]
    of_fm = arof[:, :, :]
    cvr = fc.sb(nc, "m_cvr", [128, TC], F32)
    xtile = fc.sb(nc, "m_xtile", [128, DM], F32)
    xn_tm = fc.sb(nc, "m_xn_tm", [128, DM], BF16)
    lnscr = fc.sb(nc, "m_lnscr", [128, DM], BF16)
    lnm = fc.sb(nc, "m_lnm", [128, 1], F32)
    lns = fc.sb(nc, "m_lns", [128, 1], F32)
    lnr = fc.sb(nc, "m_lnr", [128, 1], F32)
    bc_sb = fc.sb(nc, "m_bc_sb", [32, TC], BF16)
    brep = fc.sb(nc, "m_brep", [128, TC], BF16)
    crep = fc.sb(nc, "m_crep", [128, TC], BF16)
    dA_sb = [fc.sb(nc, f"m_dA{i}", [128, TC], F32) for i in range(2)]
    u_sb = fc.sb(nc, "m_u", [128, TC], F32)
    h_sb = fc.sb(nc, "m_h", [128, TC], BF16)
    hc_sb = [fc.sb(nc, f"m_hc{i}", [128, TC], BF16) for i in range(2)]
    hcar = fc.sb(nc, "m_hcar", [128, 256], F32)
    sg_sb = [fc.sb(nc, f"m_sg{i}", [128, TC], BF16) for i in range(2)]
    t1_sb = fc.sb(nc, "m_t1", [128, TC], BF16)
    dtxJ2 = [xn_tm[:, 0:TC], xn_tm[:, TC:2 * TC]]
    sp_a2 = [lnscr[:, 0:TC], lnscr[:, TC:2 * TC]]
    sp_m2 = [cvr[:, :], xtile[:, 0:TC]]


    mps_a0 = fc.ps(nc, "mps_a0", [128, 512], F32)
    mps_a1 = fc.ps(nc, "mps_a1", [128, 512], F32)
    mps_b0 = fc.ps(nc, "mps_b0", [128, 512], F32)
    mps_b1 = fc.ps(nc, "mps_b1", [128, 512], F32)
    mps_c = fc.ps(nc, "mps_c", [128, 512], F32)
    mps_d = fc.ps(nc, "mps_d", [128, 512], F32)
    ps_dtr = [mps_a0[:, :], mps_a1[:, :]]
    ps_dxr = [mps_b0[:, :], mps_b1[:, :]]
    ps_y = [mps_c[:, :], mps_d[:, :]]
    ps_mm = [mps_a0[:, :], mps_a1[:, :]]
    ps_bc = mps_d[:, :]
    ps_tr = [fc.ps(nc, f"mps_tr{i}", [128, 1024], BF16) for i in range(2)]

    ch = Chain(nc, "m_ch")
    pe_s = Chain(nc, "m_pe")
    ev_s = Chain(nc, "m_ev")
    ac_s = Chain(nc, "m_ac")
    dch = Chain(nc, "m_dch")
    hch = Chain(nc, "m_hch")
    wdma = Chain(nc, "m_wdma")
    outs = Chain(nc, "m_outs")
    outs_h = Chain(nc, "m_outsh")
    ms = Chain(nc, "m_ms")

    with nc.Block() as bk:
        def gsec(g):
            dch.push(g.dma_start(inb_sb[:, :], inb_d), 16)
            dch.push(g.dma_start(cw_sb[:, :, :], cw_d), 16)
            dch.push(g.dma_start(cb_sb[:, :], cb_d), 16)
            dch.push(g.dma_start(ap_sb[:, :], ap_d), 16)
            dch.push(g.dma_start(dp_sb[:, :], dp_d), 16)
            dch.push(g.dma_start(dtb_sb[:, :], dtb_d), 16)
            dch.push(g.dma_start(bcb_sb[:, :], bcb_d), 16)
            dch.push(g.dma_start(ob_sb[:, :], ob_d), 16)
            dch.push(g.dma_start(sel_sb[:, :, :], sel_d), 16)
            dch.push(g.dma_start(red_sb[:, :, :], red_d), 16)
            dch.push(g.dma_start(sbc_sb[:, :, :], sbc_d), 16)
            dch.push(g.dma_start(id_bf[:, :], id_d), 16)
            dch.push(g.dma_start(bcw_sb[:, :, :], bcw_d.rearrange("(k p) c -> p k c", p=128)), 16)
            dch.push(g.dma_start(inw_sb, inw_d.rearrange("(k p) m -> p k m", p=128)), 16)
            g.memset(eps_col[:, :], EPS)
            g.memset(hcar[:, :], 0.0)
        bk.gpsimd(gsec)

    for b in range(BPC):
        for c in range(NCH):
            t0 = c * TC
            with nc.Block() as bk:
                for tt in range(4):
                    def dmain(sy, b=b, t0=t0, tt=tt):
                        ch.wait(sy)
                        hch.push(sy.dma_start(xtile[:, :], x_in[b, t0 + tt * 128:t0 + (tt + 1) * 128, :]), 16)
                    bk.sync(dmain)

                    def vstart(v):
                        dch.wait(v)
                        hch.wait(v)
                        ch.push(v.tensor_copy(lnm[:, :], eps_col[:, :]))
                    bk.vector(vstart)

                    _ln_emit(nc, bk, ch, xtile, xn_tm, lnscr, lnm, lns, lnr, eps_col)

                    pe0, ev0 = pe_s.n, ev_s.n

                    def trans(t, pe0=pe0, ev0=ev0):
                        ch.wait(t)
                        for j in range(8):
                            if j >= 2:
                                ev_s.wait(t, ev0 + (j - 2) + 1)
                            pe_s.push(t.transpose(ps_tr[j % 2][:, 0:128],
                                                  xn_tm[:, j * 128:(j + 1) * 128], id_bf[:, :]))
                    bk.tensor(trans)

                    def evac(v, tt=tt, pe0=pe0, ev0=ev0):
                        for j in range(8):
                            pe_s.wait(v, pe0 + 1 + j)
                            ev_s.push(v.tensor_copy(xn_fm[:, j, tt * 128:(tt + 1) * 128],
                                                    ps_tr[j % 2][:, 0:128]))
                    bk.vector(evac)

                    def joinv(v):
                        ev_s.wait(v)
                        ch.push(v.tensor_copy(lnm[:, :], lnm[:, :]))
                    bk.vector(joinv)

                P0, E0 = pe_s.n, ev_s.n

                def ipmm(t, P0=P0, E0=E0):
                    ch.wait(t)
                    for m in range(32):
                        if m >= 2:
                            ev_s.wait(t, E0 + (m - 2) + 1)
                        for k in range(8):
                            inst = t.matmul(ps_mm[m % 2], inw_sb[:, k, m * 128:(m + 1) * 128],
                                            xn_fm[:, k, :], start=(k == 0), stop=(k == 7),
                                            skip_group_check=True)
                        pe_s.push(inst)
                if not SKIP_IP: bk.tensor(ipmm)

                def ipev(v, P0=P0, E0=E0):
                    for m in range(32):
                        pe_s.wait(v, P0 + m + 1)
                        dst = xz_fm[:, m, :] if m < 16 else gtc_fm[:, m - 16, :]
                        ev_s.push(v.tensor_scalar(dst, ps_mm[m % 2], inb_sb[:, m:m + 1], None, ALU.add))
                if not SKIP_IP: bk.vector(ipev)

                def ipjoin(v):
                    if not SKIP_IP:
                        ev_s.wait(v)
                    ch.push(v.tensor_copy(lnm[:, :], lnm[:, :]))
                bk.vector(ipjoin)

                def st(g, b=b, t0=t0):
                    ch.wait(g)
                    outs.push(g.dma_start(xz_sc[b, :, :, t0:t0 + TC], xz_fm[:, :, :]), 16)
                    outs.push(g.dma_start(gt_sc[b, :, :, t0:t0 + TC], gtc_fm[:, :, :]), 16)
                bk.gpsimd(st)

                def fin(sy):
                    outs.wait(sy)
                bk.sync(fin)

    with nc.Block() as bk:
        def wload(g):
            wdma.push(g.dma_start(dtw_sb, dtw_d.rearrange("(k p) m -> p k m", p=128)), 16)
            wdma.push(g.dma_start(ow_sb[:, :, :], ow_d.rearrange("(k p) m -> p k m", p=128)), 16)
        bk.gpsimd(wload)

    for b in range(BPC):
        with nc.Block() as bk:
            def hrst(v, b=b):
                ch.push(v.tensor_scalar(hcar[:, :], hcar[:, :], 0.0, None, ALU.mult))
            bk.vector(hrst)

        for c in range(NCH):
            t0 = c * TC
            with nc.Block() as bk:
                lo = max(t0 - 1, 0)
                hi = min(t0 + TC + 2, T)
                off = lo - (t0 - 1)

                import os
                def ld(g, b=b, t0=t0, lo=lo, hi=hi, off=off):
                    if os.environ.get("MSFENCE", "1") == "1":
                        ms.push(g.memset(ar1[:, :, :], 0.0))
                        ms.wait(g)
                    else:
                        g.memset(ar1[:, :, :], 0.0)
                    dch.push(g.dma_start(xzh[:, :, off:off + hi - lo], xz_sc[b, :, :, lo:hi]), 16)
                    dch.push(g.dma_start(gtc_fm[:, :, :], gt_sc[b, :, :, t0:t0 + TC]), 16)
                bk.gpsimd(ld)

                for j in range(16):
                    def conv(v, j=j):
                        if j == 0:
                            dch.wait(v)
                            wdma.wait(v)
                        ch.wait(v)
                        t1 = ch.push(v.tensor_scalar(cvr[:, :], xzh[:, j, 0:TC],
                                                     cw_sb[:, j, 0:1], None, ALU.mult))
                        for jj in range(1, 4):
                            ch.wait(v, t1)
                            t1 = ch.push(v.scalar_tensor_tensor(cvr[:, :], xzh[:, j, jj:jj + TC],
                                                                cw_sb[:, j, jj:jj + 1], cvr[:, :],
                                                                ALU.mult, ALU.add))
                    bk.vector(conv)

                    def csil(s, j=j):
                        ch.wait(s)
                        ch.push(s.activation(xc_fm[:, j, :], cvr[:, :], AF.Silu,
                                             bias=cb_sb[:, j:j + 1]))
                    bk.scalar(csil)

                def bcmm(t):
                    ch.wait(t)
                    for k in range(16):
                        inst = t.matmul(ps_bc[0:32, :], bcw_sb[:, k, :], xc_fm[:, k, :],
                                        start=(k == 0), stop=(k == 15))
                    ch.push(inst)
                bk.tensor(bcmm)

                def bcev(v):
                    ch.wait(v)
                    ch.push(v.tensor_scalar(bc_sb[:, :], ps_bc[0:32, :], bcb_sb[:, :], None, ALU.add))
                bk.vector(bcev)

                def repmm(t):
                    ch.wait(t)
                    ch.push(t.matmul(ps_bc[:, :], sbc_sb[:, 0, :], bc_sb[:, :], start=True, stop=True))
                bk.tensor(repmm)

                def repev(v):
                    ch.wait(v)
                    ch.push(v.tensor_copy(brep[:, :], ps_bc[:, :]))
                bk.vector(repev)

                def repmm2(t):
                    ch.wait(t)
                    ch.push(t.matmul(ps_bc[:, :], sbc_sb[:, 1, :], bc_sb[:, :], start=True, stop=True))
                bk.tensor(repmm2)

                def repev2(v):
                    ch.wait(v)
                    ch.push(v.tensor_copy(crep[:, :], ps_bc[:, :]))
                bk.vector(repev2)

                dtP0, dtA0, dtD0 = pe_s.n, ac_s.n, ev_s.n
                a_ln = {m: dtA0 + 3 * m + 3 for m in range(16)}
                d_max = {m: dtD0 + 2 * m + 1 for m in range(16)}
                d_add = {m: dtD0 + 2 * m + 2 for m in range(16)}
                p_dt = {m: dtP0 + m + 1 for m in range(16)}

                def dtmm(t):
                    ch.wait(t)
                    for m in range(16):
                        if m >= 2:
                            ac_s.wait(t, a_ln[m - 2])
                            ev_s.wait(t, d_max[m - 2])
                        for k in range(16):
                            inst = t.matmul(ps_mm[m % 2], dtw_sb[:, k, m * 128:(m + 1) * 128],
                                            xc_fm[:, k, :], start=(k == 0), stop=(k == 15),
                                            skip_group_check=True)
                        tk = pe_s.push(inst)
                        assert tk == p_dt[m]
                if not SKIP_DT: bk.tensor(dtmm)

                def dtsp1(s):
                    for m in range(16):
                        pe_s.wait(s, p_dt[m])
                        if m >= 2:
                            ev_s.wait(s, d_add[m - 2])
                        ac_s.push(s.activation(sp_a2[m % 2][:, :], ps_mm[m % 2], AF.Abs,
                                               bias=dtb_sb[:, m:m + 1]))
                        ac_s.wait(s)
                        ac_s.push(s.activation(sp_a2[m % 2][:, :], sp_a2[m % 2][:, :], AF.Exp, scale=-1.0))
                        ac_s.wait(s)
                        tk = ac_s.push(s.activation(sp_a2[m % 2][:, :], sp_a2[m % 2][:, :], AF.Ln, bias=1.0))
                        assert tk == a_ln[m]
                if not SKIP_DT: bk.scalar(dtsp1)

                def dtsp2(v):
                    for m in range(16):
                        pe_s.wait(v, p_dt[m])
                        ac_s.wait(v, dtA0 + 3 * m + 1)
                        tk = ev_s.push(v.tensor_scalar(sp_m2[m % 2][:, :], ps_mm[m % 2],
                                                       dtb_sb[:, m:m + 1], 0.0, ALU.add, ALU.max))
                        assert tk == d_max[m]
                        ac_s.wait(v, a_ln[m])
                        ev_s.wait(v, d_max[m])
                        tk = ev_s.push(v.tensor_tensor(dt_fm[:, m, :], sp_m2[m % 2][:, :],
                                                       sp_a2[m % 2][:, :], ALU.add))
                        assert tk == d_add[m]
                if not SKIP_DT: bk.vector(dtsp2)

                def dtjoin(v):
                    if not SKIP_DT:
                        ev_s.wait(v)
                    ch.push(v.tensor_copy(lnm[:, :], lnm[:, :]))
                bk.vector(dtjoin)

                NGC = 256
                P0s, A0s, D0s = pe_s.n, ac_s.n, ev_s.n
                p_dtr, p_dxr, p_y = {}, {}, {}
                cnt = P0s
                for g in range(NGC):
                    if g >= 1:
                        cnt += 1; p_y[g - 1] = cnt
                    cnt += 1; p_dtr[g] = cnt
                    cnt += 1; p_dxr[g] = cnt
                cnt += 1; p_y[NGC - 1] = cnt
                a_dA, a_sg = {}, {}
                cnt = A0s
                for g in range(NGC):
                    cnt += 1; a_dA[g] = cnt
                    if g % 16 == 15:
                        cnt += 1; a_sg[g // 16] = cnt
                d_dtx, d_u, d_scan, d_carry, d_hc, d_t1, d_yg = {}, {}, {}, {}, {}, {}, {}
                cnt = D0s
                for g in range(NGC):
                    J, j = g // 16, g % 16
                    if j == 0:
                        cnt += 1; d_dtx[J] = cnt
                    cnt += 1; d_u[g] = cnt
                    cnt += 1; d_scan[g] = cnt
                    cnt += 1; d_carry[g] = cnt
                    cnt += 1; d_hc[g] = cnt
                    if j == 15:
                        cnt += 1; d_t1[J] = cnt
                        cnt += 1; d_yg[J] = cnt

                def sc_pe(t):
                    ch.wait(t)
                    for g in range(NGC):
                        J, j = g // 16, g % 16
                        if g >= 1:
                            gy = g - 1
                            Jy, jy = gy // 16, gy % 16
                            if jy == 0 and Jy >= 2:
                                ev_s.wait(t, d_yg[Jy - 2])
                            ev_s.wait(t, d_hc[gy])
                            tk = pe_s.push(t.matmul(ps_y[Jy % 2], red_sb[:, jy, :], hc_sb[gy % 2][:, :],
                                                    start=(jy == 0), stop=(jy == 15),
                                                    skip_group_check=True))
                            assert tk == p_y[gy]
                        if j == 0:
                            ev_s.wait(t, d_dtx[J])
                        if g >= 2:
                            ac_s.wait(t, a_dA[g - 2])
                            ev_s.wait(t, d_u[g - 2])
                        tk = pe_s.push(t.matmul(ps_dtr[g % 2], sel_sb[:, j, :], dt_fm[:, J, :],
                                                start=True, stop=True, skip_group_check=True))
                        assert tk == p_dtr[g]
                        tk = pe_s.push(t.matmul(ps_dxr[g % 2], sel_sb[:, j, :], dtxJ2[J % 2][:, :],
                                                start=True, stop=True, skip_group_check=True))
                        assert tk == p_dxr[g]
                    gy = NGC - 1
                    ev_s.wait(t, d_hc[gy])
                    tk = pe_s.push(t.matmul(ps_y[(gy // 16) % 2], red_sb[:, 15, :], hc_sb[gy % 2][:, :],
                                            start=False, stop=True, skip_group_check=True))
                    assert tk == p_y[gy]
                if not SKIP_SCAN: bk.tensor(sc_pe)

                def sc_ac(s, b=b):
                    for g in range(NGC):
                        J = g // 16
                        pe_s.wait(s, p_dtr[g])
                        if g >= 2:
                            ev_s.wait(s, d_scan[g - 2])
                        tk = ac_s.push(s.activation(dA_sb[g % 2][:, :], ps_dtr[g % 2], AF.Exp,
                                                    scale=ap_sb[:, g:g + 1]))
                        assert tk == a_dA[g]
                        if g % 16 == 15:
                            if J >= 2:
                                ev_s.wait(s, d_yg[J - 2])
                            tk = ac_s.push(s.activation(sg_sb[J % 2][:, :], gtc_fm[:, J, :], AF.Silu))
                            assert tk == a_sg[J]
                if not SKIP_SCAN: bk.scalar(sc_ac)

                def sc_dv(v):
                    for g in range(NGC):
                        J, j = g // 16, g % 16
                        if j == 0:
                            if J >= 2:
                                pe_s.wait(v, p_dxr[16 * (J - 2) + 15])
                            tk = ev_s.push(v.tensor_tensor(dtxJ2[J % 2][:, :], dt_fm[:, J, :],
                                                           xc_fm[:, J, :], ALU.mult))
                            assert tk == d_dtx[J]
                        pe_s.wait(v, p_dxr[g])
                        tk = ev_s.push(v.tensor_tensor(u_sb[:, :], ps_dxr[g % 2], brep[:, :], ALU.mult))
                        assert tk == d_u[g]
                        ac_s.wait(v, a_dA[g])
                        ev_s.wait(v, d_u[g])
                        tk = ev_s.push(v.tensor_tensor_scan(h_sb[:, :], dA_sb[g % 2][:, :], u_sb[:, :],
                                                            hcar[:, g:g + 1], ALU.mult, ALU.add))
                        assert tk == d_scan[g]
                        ev_s.wait(v, d_scan[g])
                        tk = ev_s.push(v.tensor_copy(hcar[:, g:g + 1], h_sb[:, TC - 1:TC]))
                        assert tk == d_carry[g]
                        if g >= 2:
                            pe_s.wait(v, p_y[g - 2])
                        tk = ev_s.push(v.tensor_tensor(hc_sb[g % 2][:, :], h_sb[:, :], crep[:, :], ALU.mult))
                        assert tk == d_hc[g]
                        if j == 15:
                            pe_s.wait(v, p_y[g])
                            ac_s.wait(v, a_sg[J])
                            tk = ev_s.push(v.scalar_tensor_tensor(t1_sb[:, :], xc_fm[:, J, :],
                                                                  dp_sb[:, J:J + 1], ps_y[J % 2],
                                                                  ALU.mult, ALU.add))
                            assert tk == d_t1[J]
                            ev_s.wait(v, d_t1[J])
                            tk = ev_s.push(v.tensor_tensor(yg_fm[:, J, :], t1_sb[:, :],
                                                           sg_sb[J % 2][:, :], ALU.mult))
                            assert tk == d_yg[J]
                if not SKIP_SCAN: bk.vector(sc_dv)

                def scjoin(v):
                    if not SKIP_SCAN:
                        ev_s.wait(v)
                    ch.push(v.tensor_copy(lnm[:, :], lnm[:, :]))
                bk.vector(scjoin)

                oP0, oE0 = pe_s.n, ev_s.n

                def omm(t, oP0=oP0, oE0=oE0):
                    ch.wait(t)
                    for m in range(8):
                        if m >= 2:
                            ev_s.wait(t, oE0 + (m - 2) + 1)
                        for k in range(16):
                            inst = t.matmul(ps_mm[m % 2], ow_sb[:, k, m * 128:(m + 1) * 128],
                                            yg_fm[:, k, :], start=(k == 0), stop=(k == 15),
                                            skip_group_check=True)
                        pe_s.push(inst)
                bk.tensor(omm)

                def oev(v, oP0=oP0, oE0=oE0):
                    for m in range(8):
                        pe_s.wait(v, oP0 + m + 1)
                        ev_s.push(v.tensor_scalar(of_fm[:, m, :], ps_mm[m % 2], ob_sb[:, m:m + 1],
                                                  None, ALU.add))
                bk.vector(oev)

                def ojoin(v):
                    ev_s.wait(v)
                    ch.push(v.tensor_copy(lnm[:, :], lnm[:, :]))
                bk.vector(ojoin)

                for tt in range(4):
                    def dmx(sy, b=b, t0=t0, tt=tt):
                        ch.wait(sy)
                        outs_h.wait(sy)
                        hch.push(sy.dma_start(xtile[:, :], x_in[b, t0 + tt * 128:t0 + (tt + 1) * 128, :]), 16)
                    bk.sync(dmx)

                    pe0, ev0 = pe_s.n, ev_s.n

                    def etr(t, tt=tt, pe0=pe0, ev0=ev0):
                        ch.wait(t)
                        for m in range(8):
                            if m >= 2:
                                ev_s.wait(t, ev0 + (m - 2) + 1)
                            pe_s.push(t.transpose(ps_tr[m % 2][:, 0:128],
                                                  of_fm[:, m, tt * 128:(tt + 1) * 128], id_bf[:, :]))
                    bk.tensor(etr)

                    def eev(v, tt=tt, pe0=pe0, ev0=ev0):
                        hch.wait(v)
                        for m in range(8):
                            pe_s.wait(v, pe0 + 1 + m)
                            ev_s.push(v.tensor_tensor(xtile[:, m * 128:(m + 1) * 128],
                                                      ps_tr[m % 2][:, 0:128],
                                                      xtile[:, m * 128:(m + 1) * 128], ALU.add))
                    bk.vector(eev)

                    def ejoin(v):
                        ev_s.wait(v)
                        ch.push(v.tensor_copy(lnm[:, :], lnm[:, :]))
                    bk.vector(ejoin)

                    def dmo(sy, b=b, t0=t0, tt=tt):
                        ch.wait(sy)
                        outs_h.push(sy.dma_start(x_out[b, t0 + tt * 128:t0 + (tt + 1) * 128, :],
                                                 xtile[:, :]), 16)
                    bk.sync(dmo)

                def fin(sy):
                    outs_h.wait(sy)
                bk.sync(fin)

    return nc


def _host_mamba(inputs):
    ident = np.eye(128, dtype=np.float32)
    A = np.asarray(inputs["L0_attn_A"])
    A_perm = np.ascontiguousarray(A.reshape(256, 128).T)
    SEL = np.zeros((128, 16, 128), np.float32)
    for r in range(128):
        for jj in range(16):
            for p in range(128):
                if r == 8 * jj + p // 16:
                    SEL[r, jj, p] = 1.0
    RED = np.ascontiguousarray(SEL.transpose(2, 1, 0))
    selbc = np.zeros((32, 2, 128), np.float32)
    for p in range(128):
        selbc[p % 16, 0, p] = 1.0
        selbc[16 + p % 16, 1, p] = 1.0
    conv_w = np.asarray(inputs["L0_attn_conv_w"]).reshape(4, DI)
    base = {
        "in_w": inputs["L0_attn_in_w"],
        "in_b": np.ascontiguousarray(np.asarray(inputs["L0_attn_in_b"]).reshape(32, 128).T),
        "conv_w": np.ascontiguousarray(conv_w.reshape(4, 16, 128).transpose(2, 1, 0)),
        "conv_b": np.ascontiguousarray(np.asarray(inputs["L0_attn_conv_b"]).reshape(16, 128).T),
        "A_perm": A_perm,
        "Dp_col": np.ascontiguousarray(np.asarray(inputs["L0_attn_Dp"]).reshape(16, 128).T),
        "dt_w": inputs["L0_attn_dt_w"],
        "dt_b": np.ascontiguousarray(np.asarray(inputs["L0_attn_dt_b"]).reshape(16, 128).T),
        "bc_w": np.ascontiguousarray(np.concatenate([np.asarray(inputs["L0_attn_B_w"]),
                                                     np.asarray(inputs["L0_attn_C_w"])], axis=1)),
        "bc_b": np.ascontiguousarray(np.concatenate([np.asarray(inputs["L0_attn_B_b"]),
                                                     np.asarray(inputs["L0_attn_C_b"])]).reshape(32, 1)),
        "out_w": inputs["L0_attn_out_w"],
        "out_b": np.ascontiguousarray(np.asarray(inputs["L0_attn_out_b"]).reshape(8, 128).T),
        "SEL": SEL,
        "RED": RED,
        "selbc": selbc,
        "ident": ident,
    }
    return base


def run_mamba(x, inputs):
    nc = _get("mamba", build_mamba)
    base = _host_mamba(inputs)
    in_maps = []
    for cc in range(NCORE):
        m = dict(base)
        m["x_in"] = np.ascontiguousarray(x[2 * cc:2 * cc + 2])
        in_maps.append(m)
    res = run_bass_kernel_spmd(nc, in_maps, core_ids=list(range(NCORE)))
    return np.concatenate([r["x_out"] for r in res.results], axis=0)




def build_ttt(fc=None):
    fc = fc or FC()
    nc = fc.nc if fc.nc is not None else bass.Bass("TRN2", target_bir_lowering=False, debug=False)
    CK = 128
    NCK = T // CK
    LR = 0.01
    ALPHA = 2.0 / DM
    LAM = 2.0 * LR / DM

    x_in = fc.dram(nc, "x_in", [BPC, T, DM], F32)
    thk_d = fc.dram(nc, "thK_T", [DM, DM], F32)
    thv_d = fc.dram(nc, "thV_T", [DM, DM], F32)
    thq_d = fc.dram(nc, "thQ_T", [DM, DM], F32)
    w0t_d = fc.dram(nc, "W0_T", [DM, DM], F32)
    b0_d = fc.dram(nc, "b0", [1, DM], F32)
    mst_d = fc.dram(nc, "Mstrict", [128, 128], F32)
    min_d = fc.dram(nc, "Mincl", [128, 128], F32)
    id_d = fc.dram(nc, "ident", [128, 128], F32)
    x_out = fc.dram(nc, "x_out", [BPC, T, DM], F32, kind="ExternalOutput")

    thk = fc.sb(nc, "t_thk", [128, 8, DM], BF16)
    thq = fc.sb(nc, "t_thq", [128, 8, DM], BF16)
    thvn = fc.sb(nc, "t_thvn", [128, 8, DM], BF16)
    wt = [fc.sb(nc, f"t_wt{b}", [128, 8, DM], BF16) for b in range(BPC)]
    bst = fc.sb(nc, "t_bst", [1, DM], F32)
    bst_bf = fc.sb(nc, "t_bstbf", [1, DM], BF16)
    mstr = fc.sb(nc, "t_mstr", [128, 128], BF16)
    mincl = fc.sb(nc, "t_mincl", [128, 128], BF16)
    id_bf = fc.sb(nc, "t_id", [128, 128], BF16)
    ones_r = fc.sb(nc, "t_ones_r", [1, 128], BF16)
    ones_c = fc.sb(nc, "t_ones_c", [128, 1], BF16)
    eps_col = fc.sb(nc, "t_eps", [128, 1], F32)
    kfm = fc.sb(nc, "t_kfm", [128, 8, CK], BF16)
    qfm = fc.sb(nc, "t_qfm", [128, 8, CK], BF16)
    ktm = fc.sb(nc, "t_ktm", [128, 8, 128], BF16)
    base_sb = fc.sb(nc, "t_base", [128, DM], BF16)
    gs_tm = fc.sb(nc, "t_gs", [128, DM], BF16)
    nmat = fc.sb(nc, "t_n", [128, 128], BF16)
    pmat = fc.sb(nc, "t_p", [128, 128], BF16)
    ptmat = fc.sb(nc, "t_pt", [128, 128], BF16)
    tmat = fc.sb(nc, "t_t", [128, 128], BF16)
    umat = fc.sb(nc, "t_u", [128, 128], BF16)
    xn_fm = fc.sb(nc, "t_xnfm", [128, 8, CK], BF16)
    xtile = fc.sb(nc, "t_xtile", [128, DM], F32)
    xn_tm = fc.sb(nc, "t_xntm", [128, DM], BF16)
    lnscr = fc.sb(nc, "t_lnscr", [128, DM], BF16)
    lnm = fc.sb(nc, "t_lnm", [128, 1], F32)
    lns = fc.sb(nc, "t_lns", [128, 1], F32)
    lnr = fc.sb(nc, "t_lnr", [128, 1], F32)
    o_tm = fc.sb(nc, "t_otm", [128, DM], F32)

    ps_big = fc.ps(nc, "t_psbig", [128, DM], F32)
    ps_big2 = fc.ps(nc, "t_psbig2", [128, DM], F32)
    ps_sm = fc.ps(nc, "t_pssm", [128, 512], F32)
    ps_kq = [ps_sm[:, 0:128], ps_big2[:, 0:128]]
    ps_dw = [ps_big[:, :], ps_big2[:, :]]
    ps_tr = [fc.ps(nc, f"t_pstr{i}", [128, 1024], BF16) for i in range(2)]

    ch = Chain(nc, "t_ch")
    pe_s = Chain(nc, "t_pe")
    ev_s = Chain(nc, "t_ev")
    ac_s = Chain(nc, "t_ac")
    dch = Chain(nc, "t_dch")
    hch = Chain(nc, "t_hch")
    outs = Chain(nc, "t_outs")

    with nc.Block() as bk:
        def gsec(g):
            dch.push(g.dma_start(thk[:, :, :], thk_d.rearrange("(k p) m -> p k m", p=128)), 16)
            dch.push(g.dma_start(thq[:, :, :], thq_d.rearrange("(k p) m -> p k m", p=128)), 16)
            dch.push(g.dma_start(thvn[:, :, :], thv_d.rearrange("(k p) m -> p k m", p=128)), 16)
            for b in range(BPC):
                dch.push(g.dma_start(wt[b][:, :, :], w0t_d.rearrange("(k p) m -> p k m", p=128)), 16)
            dch.push(g.dma_start(mstr[:, :], mst_d), 16)
            dch.push(g.dma_start(mincl[:, :], min_d), 16)
            dch.push(g.dma_start(id_bf[:, :], id_d), 16)
            g.memset(ones_r[:, :], 1.0)
            g.memset(ones_c[:, :], 1.0)
            g.memset(eps_col[:, :], EPS)
        bk.gpsimd(gsec)

        def vneg(s):
            dch.wait(s)
            for j in range(8):
                ch.push(s.activation(thvn[:, j, :], thvn[:, j, :], AF.Identity, scale=-1.0))
        bk.scalar(vneg)

    for b in range(BPC):
        with nc.Block() as bk:
            def binit(g, b=b):
                dch.push(g.dma_start(bst[:, :], b0_d), 16)
            bk.gpsimd(binit)

            def binit2(v):
                dch.wait(v)
                ch.push(v.tensor_copy(bst_bf[:, :], bst[:, :]))
            bk.vector(binit2)

        for c in range(NCK):
            t0 = c * CK
            with nc.Block() as bk:
                def dmain(sy, b=b, t0=t0):
                    ch.wait(sy)
                    hch.push(sy.dma_start(xtile[:, :], x_in[b, t0:t0 + CK, :]), 16)
                bk.sync(dmain)

                def vstart(v):
                    dch.wait(v)
                    hch.wait(v)
                    ch.push(v.tensor_copy(lnm[:, :], eps_col[:, :]))
                bk.vector(vstart)

                _ln_emit(nc, bk, ch, xtile, xn_tm, lnscr, lnm, lns, lnr, eps_col)

                pe0, ev0 = pe_s.n, ev_s.n

                def trans(t, pe0=pe0, ev0=ev0):
                    ch.wait(t)
                    for j in range(8):
                        if j >= 2:
                            ev_s.wait(t, ev0 + (j - 2) + 1)
                        pe_s.push(t.transpose(ps_tr[j % 2][:, 0:128],
                                              xn_tm[:, j * 128:(j + 1) * 128], id_bf[:, :]))
                bk.tensor(trans)

                def evac(v, pe0=pe0, ev0=ev0):
                    for j in range(8):
                        pe_s.wait(v, pe0 + 1 + j)
                        ev_s.push(v.tensor_copy(xn_fm[:, j, :], ps_tr[j % 2][:, 0:128]))
                bk.vector(evac)

                def joinv(v):
                    ev_s.wait(v)
                    ch.push(v.tensor_copy(lnm[:, :], lnm[:, :]))
                bk.vector(joinv)

                kqP0, kqE0 = pe_s.n, ev_s.n

                def kqmm2(t, kqP0=kqP0, kqE0=kqE0):
                    ch.wait(t)
                    ii = 0
                    for dst, th in [(kfm, thk), (qfm, thq)]:
                        for i in range(8):
                            if ii >= 2:
                                ev_s.wait(t, kqE0 + ii - 1)
                            for k in range(8):
                                inst = t.matmul(ps_kq[ii % 2], th[:, k, i * 128:(i + 1) * 128],
                                                xn_fm[:, k, :], start=(k == 0), stop=(k == 7),
                                                skip_group_check=True)
                            pe_s.push(inst)
                            ii += 1
                bk.tensor(kqmm2)

                def kqev(v, kqP0=kqP0, kqE0=kqE0):
                    ii = 0
                    for dst, th in [(kfm, thk), (qfm, thq)]:
                        for i in range(8):
                            pe_s.wait(v, kqP0 + ii + 1)
                            ev_s.push(v.tensor_copy(dst[:, i, :], ps_kq[ii % 2]))
                            ii += 1
                bk.vector(kqev)

                def kqjoin(v):
                    ev_s.wait(v)
                    ch.push(v.tensor_copy(lnm[:, :], lnm[:, :]))
                bk.vector(kqjoin)

                def basemm(t, b=b):
                    ch.wait(t)
                    for k in range(8):
                        for hh in range(2):
                            t.matmul(ps_big[:, hh * 512:(hh + 1) * 512], kfm[:, k, :],
                                     wt[b][:, k, hh * 512:(hh + 1) * 512],
                                     start=(k == 0), stop=False)
                    for hh in range(2):
                        t.matmul(ps_big[:, hh * 512:(hh + 1) * 512], ones_r[:, :],
                                 bst_bf[0:1, hh * 512:(hh + 1) * 512], start=False, stop=False)
                    for k in range(8):
                        for hh in range(2):
                            inst = t.matmul(ps_big[:, hh * 512:(hh + 1) * 512], xn_fm[:, k, :],
                                            thvn[:, k, hh * 512:(hh + 1) * 512],
                                            start=False, stop=(k == 7))
                    ch.push(inst)
                bk.tensor(basemm)

                def baseev(s):
                    ch.wait(s)
                    ch.push(s.activation(base_sb[:, :], ps_big[:, :], AF.Identity, scale=ALPHA))
                bk.scalar(baseev)

                def kkmm(t):
                    ch.wait(t)
                    for k in range(8):
                        t.matmul(ps_sm[:, 0:128], kfm[:, k, :], kfm[:, k, :],
                                 start=(k == 0), stop=False)
                    inst = t.matmul(ps_sm[:, 0:128], ones_r[:, :],
                                    ones_r[:, :], start=False, stop=True)
                    ch.push(inst)
                bk.tensor(kkmm)

                def nev(v):
                    ch.wait(v)
                    ch.push(v.scalar_tensor_tensor(nmat[:, :], ps_sm[:, 0:128], -LAM,
                                                   mstr[:, :], ALU.mult, ALU.mult))
                bk.vector(nev)

                def tinit(v):
                    ch.wait(v)
                    t1 = ch.push(v.tensor_tensor(tmat[:, :], nmat[:, :], id_bf[:, :], ALU.add))
                    ch.wait(v, t1)
                    ch.push(v.tensor_copy(pmat[:, :], nmat[:, :]))
                bk.vector(tinit)

                for it in range(6):
                    def sq1(t, it=it):
                        ch.wait(t)
                        ch.push(t.transpose(ps_tr[0][:, 0:128], pmat[:, :], id_bf[:, :]))
                    bk.tensor(sq1)

                    def sq2(v, it=it):
                        ch.wait(v)
                        ch.push(v.tensor_copy(ptmat[:, :], ps_tr[0][:, 0:128]))
                    bk.vector(sq2)

                    def sq3(t, it=it):
                        ch.wait(t)
                        ch.push(t.matmul(ps_sm[:, 0:128], ptmat[:, :], pmat[:, :],
                                         start=True, stop=True))
                        ch.push(t.matmul(ps_sm[:, 128:256], ptmat[:, :], tmat[:, :],
                                         start=True, stop=True))
                    bk.tensor(sq3)

                    def sq4(v, it=it):
                        ch.wait(v)
                        t1 = ch.push(v.tensor_copy(pmat[:, :], ps_sm[:, 0:128]))
                        ch.wait(v, t1)
                        ch.push(v.tensor_tensor(tmat[:, :], tmat[:, :], ps_sm[:, 128:256], ALU.add))
                    bk.vector(sq4)

                def kqmm(t):
                    ch.wait(t)
                    for k in range(8):
                        t.matmul(ps_sm[:, 0:128], kfm[:, k, :], qfm[:, k, :],
                                 start=(k == 0), stop=False)
                    inst = t.matmul(ps_sm[:, 0:128], ones_r[:, :], ones_r[:, :],
                                    start=False, stop=True)
                    ch.push(inst)
                bk.tensor(kqmm)

                def uev(v):
                    ch.wait(v)
                    ch.push(v.tensor_tensor(umat[:, :], ps_sm[:, 0:128], mincl[:, :], ALU.mult))
                bk.vector(uev)

                def gmm(t):
                    ch.wait(t)
                    for hh in range(2):
                        inst = t.matmul(ps_big[:, hh * 512:(hh + 1) * 512], tmat[:, :],
                                        base_sb[:, hh * 512:(hh + 1) * 512], start=True, stop=True)
                    ch.push(inst)
                bk.tensor(gmm)

                def gev(s):
                    ch.wait(s)
                    ch.push(s.activation(gs_tm[:, :], ps_big[:, :], AF.Identity, scale=-LR))
                bk.scalar(gev)

                def omm(t, b=b):
                    ch.wait(t)
                    for k in range(8):
                        for hh in range(2):
                            t.matmul(ps_big[:, hh * 512:(hh + 1) * 512], qfm[:, k, :],
                                     wt[b][:, k, hh * 512:(hh + 1) * 512],
                                     start=(k == 0), stop=False)
                    for hh in range(2):
                        t.matmul(ps_big[:, hh * 512:(hh + 1) * 512], ones_r[:, :],
                                 bst_bf[0:1, hh * 512:(hh + 1) * 512], start=False, stop=False)
                    for hh in range(2):
                        inst = t.matmul(ps_big[:, hh * 512:(hh + 1) * 512], umat[:, :],
                                        gs_tm[:, hh * 512:(hh + 1) * 512], start=False, stop=True)
                    ch.push(inst)
                bk.tensor(omm)

                def dmx(sy, b=b, t0=t0):
                    ch.wait(sy)
                    outs.wait(sy)
                    hch.push(sy.dma_start(xtile[:, :], x_in[b, t0:t0 + CK, :]), 16)
                bk.sync(dmx)

                def oev(v):
                    ch.wait(v)
                    hch.wait(v)
                    ch.push(v.tensor_tensor(o_tm[:, :], ps_big[:, :], xtile[:, :], ALU.add))
                bk.vector(oev)

                def dmo(e, b=b, t0=t0):
                    ch.wait(e)
                    outs.push(e.dma_start(x_out[b, t0:t0 + CK, :], o_tm[:, :]), 16)
                if fc.out_swdge:
                    bk.gpsimd(dmo)
                else:
                    bk.sync(dmo)

                pe0, ev0 = pe_s.n, ev_s.n

                def ktr(t, pe0=pe0, ev0=ev0):
                    ch.wait(t)
                    for i in range(8):
                        if i >= 2:
                            ev_s.wait(t, ev0 + (i - 2) + 1)
                        pe_s.push(t.transpose(ps_tr[i % 2][:, 0:128], kfm[:, i, :], id_bf[:, :]))
                bk.tensor(ktr)

                def kev(v, pe0=pe0, ev0=ev0):
                    for i in range(8):
                        pe_s.wait(v, pe0 + 1 + i)
                        ev_s.push(v.tensor_copy(ktm[:, i, :], ps_tr[i % 2][:, 0:128]))
                bk.vector(kev)

                def kjoin(v):
                    ev_s.wait(v)
                    ch.push(v.tensor_copy(lnm[:, :], lnm[:, :]))
                bk.vector(kjoin)

                dwP0, dwE0 = pe_s.n, ev_s.n

                def dwmm(t, dwP0=dwP0, dwE0=dwE0):
                    ch.wait(t)
                    for i in range(8):
                        if i >= 2:
                            ev_s.wait(t, dwE0 + i - 1)
                        for hh in range(2):
                            inst = t.matmul(ps_dw[i % 2][:, hh * 512:(hh + 1) * 512], ktm[:, i, :],
                                            gs_tm[:, hh * 512:(hh + 1) * 512], start=True, stop=True,
                                            skip_group_check=True)
                        pe_s.push(inst)
                bk.tensor(dwmm)

                def dwup(v, b=b, dwP0=dwP0, dwE0=dwE0):
                    for i in range(8):
                        pe_s.wait(v, dwP0 + i + 1)
                        ev_s.push(v.tensor_tensor(wt[b][:, i, :], ps_dw[i % 2][:, :],
                                                  wt[b][:, i, :], ALU.add))
                bk.vector(dwup)

                def dwjoin(v):
                    ev_s.wait(v)
                    ch.push(v.tensor_copy(lnm[:, :], lnm[:, :]))
                bk.vector(dwjoin)

                for hh in range(2):
                    def bmm(t, hh=hh):
                        ch.wait(t)
                        ch.push(t.matmul(ps_sm[0:1, :], ones_c[:, :],
                                         gs_tm[:, hh * 512:(hh + 1) * 512], start=True, stop=True))
                    bk.tensor(bmm)

                    def bup(v, hh=hh):
                        ch.wait(v)
                        ch.push(v.tensor_tensor(bst[0:1, hh * 512:(hh + 1) * 512], ps_sm[0:1, :],
                                                bst[0:1, hh * 512:(hh + 1) * 512], ALU.add))
                    bk.vector(bup)

                def bbf(v):
                    ch.wait(v)
                    ch.push(v.tensor_copy(bst_bf[:, :], bst[:, :]))
                bk.vector(bbf)

                def fin(sy):
                    outs.wait(sy)
                bk.sync(fin)

    return nc


def _host_ttt(inputs):
    ident = np.eye(128, dtype=np.float32)
    ii, jj = np.meshgrid(np.arange(128), np.arange(128), indexing="ij")
    Mstrict = (ii < jj).astype(np.float32)
    Mincl = (ii <= jj).astype(np.float32)
    base = {
        "thK_T": np.ascontiguousarray(np.asarray(inputs["L1_attn_theta_K"]).T),
        "thV_T": np.ascontiguousarray(np.asarray(inputs["L1_attn_theta_V"]).T),
        "thQ_T": np.ascontiguousarray(np.asarray(inputs["L1_attn_theta_Q"]).T),
        "W0_T": np.ascontiguousarray(np.asarray(inputs["L1_attn_W0"]).T),
        "b0": np.ascontiguousarray(np.asarray(inputs["L1_attn_b0"]).reshape(1, DM)),
        "Mstrict": Mstrict,
        "Mincl": Mincl,
        "ident": ident,
    }
    return base


def run_ttt(x, inputs):
    nc = _get("ttt", build_ttt)
    base = _host_ttt(inputs)
    in_maps = []
    for cc in range(NCORE):
        m = dict(base)
        m["x_in"] = np.ascontiguousarray(x[2 * cc:2 * cc + 2])
        in_maps.append(m)
    res = run_bass_kernel_spmd(nc, in_maps, core_ids=list(range(NCORE)))
    return np.concatenate([r["x_out"] for r in res.results], axis=0)




def build_lstm(fc=None):
    fc = fc or FC()
    nc = fc.nc if fc.nc is not None else bass.Bass("TRN2", target_bir_lowering=False, debug=False)
    NB = 16
    TS = 256
    if fc.lstm_gather:
        agout_d = fc.dmap["agout"]
        tsel_d = fc.dram(nc, "tsel", [128, 32], mybir.dt.uint32)
    else:
        x_in = fc.dram(nc, "x_in", [NB, TS, DM], F32)
    aw_d = fc.dram(nc, "aw", [6, DM, DM], F32)
    ow2_d = fc.dram(nc, "ow2", [2, DM, DM], F32)
    gb_d = fc.dram(nc, "gb", [128, 3, 8], F32)
    bo_d = fc.dram(nc, "bo", [128, 8], F32)
    ic_d = fc.dram(nc, "in_c", [128, 8], F32)
    id_d = fc.dram(nc, "ident", [128, 128], F32)
    x_out = fc.dram(nc, "x_out", [NB, TS, DM], F32, kind="ExternalOutput")

    AW = fc.sb(nc, "l_AW", [128, 6, 8, DM], BF16)
    wv = [AW[:, i, :, :] for i in range(6)]
    xn3 = fc.sb(nc, "l_xn3", [128, 8, NB * TS], BF16)
    c_fm = fc.sb(nc, "l_c", [128, 8, TS], F32)
    c_bf = [fc.sb(nc, f"l_cbf{i}", [128, 8, TS], BF16) for i in range(2)]
    f_sb = [fc.sb(nc, f"l_f{i}", [128, TS], BF16) for i in range(2)]
    i_sb = [fc.sb(nc, f"l_i{i}", [128, TS], BF16) for i in range(2)]
    g_sb = [fc.sb(nc, f"l_g{i}", [128, TS], BF16) for i in range(2)]
    tt1 = fc.sb(nc, "l_tt1", [128, TS], F32)
    tt2 = fc.sb(nc, "l_tt2", [128, TS], F32)
    gb_sb = fc.sb(nc, "l_gb", [128, 3, 8], F32)
    bo_sb = fc.sb(nc, "l_bo", [128, 8], F32)
    ic_sb = fc.sb(nc, "l_ic", [128, 8], F32)
    onef = fc.sb(nc, "l_onef", [128, TS], F32)
    id_bf = fc.sb(nc, "l_id", [128, 128], BF16)
    eps_col = fc.sb(nc, "l_eps", [128, 1], F32)
    h_fm = fc.sb(nc, "l_hfm", [128, 8, TS], BF16)
    xtile = fc.sb(nc, "l_xtile", [128, DM], F32)
    xn_tm = fc.sb(nc, "l_xntm", [128, DM], BF16)
    lnscr = fc.sb(nc, "l_lnscr", [128, DM], BF16)
    lnm = fc.sb(nc, "l_lnm", [128, 1], F32)
    lns = fc.sb(nc, "l_lns", [128, 1], F32)
    lnr = fc.sb(nc, "l_lnr", [128, 1], F32)
    o_tm = fc.sb(nc, "l_otm", [128, DM], F32)
    if fc.lstm_gather:
        xg_bf = fc.sb(nc, "l_xgbf", [128, DM], BF16)
        tsel_sb = fc.sb(nc, "l_tsel", [128, 32], mybir.dt.uint32)

    ps_f = fc.ps(nc, "l_psf", [128, 512], F32)
    ps_i = fc.ps(nc, "l_psi", [128, 512], F32)
    ps_g = fc.ps(nc, "l_psg", [128, 512], F32)
    ps_f2 = fc.ps(nc, "l_psf2", [128, 512], F32)
    ps_i2 = fc.ps(nc, "l_psi2", [128, 512], F32)
    ps_g2 = fc.ps(nc, "l_psg2", [128, 512], F32)
    ps_fp = [ps_f[:, 0:TS], ps_f2[:, 0:TS]]
    ps_ip = [ps_i[:, 0:TS], ps_i2[:, 0:TS]]
    ps_gp2 = [ps_g[:, 0:TS], ps_g2[:, 0:TS]]
    ps_tr = [fc.ps(nc, f"l_pstr{i}", [128, 1024], BF16) for i in range(2)]

    ch = Chain(nc, "l_ch")
    pe_s = Chain(nc, "l_pe")
    ev_s = Chain(nc, "l_ev")
    ac_s = Chain(nc, "l_ac")
    dch = Chain(nc, "l_dch")
    hch = Chain(nc, "l_hch")
    wdma = Chain(nc, "l_wdma")
    outs = Chain(nc, "l_outs")

    with nc.Block() as bk:
        def gsec(g):
            wdma.push(g.dma_start(AW[:, :, :, :],
                                  aw_d.rearrange("w (k p) m -> p w k m", p=128)), 16)
            dch.push(g.dma_start(gb_sb[:, :, :], gb_d), 16)
            dch.push(g.dma_start(bo_sb[:, :], bo_d), 16)
            dch.push(g.dma_start(ic_sb[:, :], ic_d), 16)
            dch.push(g.dma_start(id_bf[:, :], id_d), 16)
            g.memset(eps_col[:, :], EPS)
            g.memset(onef[:, :], 1.0)
            if fc.lstm_gather:
                dch.push(g.dma_start(tsel_sb[:, :], tsel_d), 16)
        bk.gpsimd(gsec)

        def cinit(s):
            dch.wait(s)
            for m in range(8):
                ch.push(s.activation(c_fm[:, m, :], onef[:, :], AF.Identity,
                                     scale=ic_sb[:, m:m + 1]))
        bk.scalar(cinit)

        def cbf0(v):
            ch.wait(v)
            ch.push(v.tensor_copy(c_bf[0][:, :, :], c_fm[:, :, :]))
        bk.vector(cbf0)

    for b in range(16):
        for tt in range(2):
            with nc.Block() as bk:
                if fc.lstm_gather:
                    def dmain(g, b=b, tt=tt):
                        ch.wait(g)
                        dch.push(g.indirect_dma_start(
                            xg_bf[:, :], None, agout_d,
                            bass.IndirectOffsetOnAxis(ap=tsel_sb[:, b * 2 + tt:b * 2 + tt + 1],
                                                      axis=0)), 16)
                    bk.gpsimd(dmain)

                    def vstart(v):
                        dch.wait(v)
                        ch.push(v.tensor_copy(xtile[:, :], xg_bf[:, :]))
                    bk.vector(vstart)
                else:
                    def dmain(sy, b=b, tt=tt):
                        ch.wait(sy)
                        hch.push(sy.dma_start(xtile[:, :], x_in[b, tt * 128:(tt + 1) * 128, :]), 16)
                    bk.sync(dmain)

                    def vstart(v):
                        dch.wait(v)
                        hch.wait(v)
                        ch.push(v.tensor_copy(lnm[:, :], eps_col[:, :]))
                    bk.vector(vstart)

                _ln_emit(nc, bk, ch, xtile, xn_tm, lnscr, lnm, lns, lnr, eps_col)

                pe0, ev0 = pe_s.n, ev_s.n

                def trans(t, pe0=pe0, ev0=ev0):
                    ch.wait(t)
                    for j in range(8):
                        if j >= 2:
                            ev_s.wait(t, ev0 + (j - 2) + 1)
                        pe_s.push(t.transpose(ps_tr[j % 2][:, 0:128],
                                              xn_tm[:, j * 128:(j + 1) * 128], id_bf[:, :]))
                bk.tensor(trans)

                def evac(v, b=b, tt=tt, pe0=pe0, ev0=ev0):
                    col = b * TS + tt * 128
                    for j in range(8):
                        pe_s.wait(v, pe0 + 1 + j)
                        ev_s.push(v.tensor_copy(xn3[:, j, col:col + 128], ps_tr[j % 2][:, 0:128]))
                bk.vector(evac)

                def joinv(v):
                    ev_s.wait(v)
                    ch.push(v.tensor_copy(lnm[:, :], lnm[:, :]))
                bk.vector(joinv)

    with nc.Block() as bk:
        P0n = pe_s.n
        p_gate = {}
        cnt = P0n
        for b in range(16):
            for m in range(8):
                for gidx in range(3):
                    cnt += 1; p_gate[(b, m, gidx)] = cnt
        A0n = ac_s.n
        a_g = {}
        cnt = A0n
        for b in range(16):
            for m in range(8):
                for gidx in range(3):
                    cnt += 1; a_g[(b, m, gidx)] = cnt
        E0n = ev_s.n
        d_cbf = {}
        cnt = E0n
        for b in range(16):
            for m in range(8):
                cnt += 4; d_cbf[(b, m)] = cnt

        def pesec(t):
            ch.wait(t)
            wdma.wait(t)
            for b in range(16):
                col = b * TS
                for m in range(8):
                    mi = b * 8 + m
                    if mi >= 2:
                        pb, pm = divmod(mi - 2, 8)
                        ac_s.wait(t, a_g[(pb, pm, 2)])
                    for (pslist, wi_, ui_, gidx) in [(ps_fp, 0, 1, 0), (ps_ip, 2, 3, 1),
                                                     (ps_gp2, 4, 5, 2)]:
                        ps = pslist[mi % 2]
                        for k in range(8):
                            t.matmul(ps, wv[wi_][:, k, m * 128:(m + 1) * 128],
                                     xn3[:, k, col:col + TS], start=(k == 0), stop=False,
                                     skip_group_check=True)
                        if m == 0 and gidx == 0 and b >= 1:
                            ev_s.wait(t, d_cbf[(b - 1, 7)])
                        for k in range(8):
                            inst = t.matmul(ps, wv[ui_][:, k, m * 128:(m + 1) * 128],
                                            c_bf[b % 2][:, k, :], start=False, stop=(k == 7),
                                            skip_group_check=True)
                        pe_s.push(inst)
        bk.tensor(pesec)

        def acsec(s):
            for b in range(16):
                for m in range(8):
                    mi = b * 8 + m
                    if mi >= 2:
                        pb, pm = divmod(mi - 2, 8)
                        ev_s.wait(s, d_cbf[(pb, pm)])
                    pe_s.wait(s, p_gate[(b, m, 0)])
                    ac_s.push(s.activation(f_sb[mi % 2][:, :], ps_fp[mi % 2], AF.Sigmoid,
                                           bias=gb_sb[:, 0, m:m + 1]))
                    pe_s.wait(s, p_gate[(b, m, 1)])
                    ac_s.push(s.activation(i_sb[mi % 2][:, :], ps_ip[mi % 2], AF.Sigmoid,
                                           bias=gb_sb[:, 1, m:m + 1]))
                    pe_s.wait(s, p_gate[(b, m, 2)])
                    ac_s.push(s.activation(g_sb[mi % 2][:, :], ps_gp2[mi % 2], AF.Tanh,
                                           bias=gb_sb[:, 2, m:m + 1]))
        bk.scalar(acsec)

        def dvsec(v):
            for b in range(16):
                for m in range(8):
                    mi = b * 8 + m
                    ac_s.wait(v, a_g[(b, m, 2)])
                    ev_s.push(v.tensor_tensor(tt1[:, :], f_sb[mi % 2][:, :], c_fm[:, m, :], ALU.mult))
                    ev_s.push(v.tensor_tensor(tt2[:, :], i_sb[mi % 2][:, :], g_sb[mi % 2][:, :], ALU.mult))
                    ev_s.wait(v)
                    ev_s.push(v.tensor_tensor(c_fm[:, m, :], tt1[:, :], tt2[:, :], ALU.add))
                    ev_s.wait(v)
                    ev_s.push(v.tensor_copy(c_bf[(b + 1) % 2][:, m, :], c_fm[:, m, :]))
        bk.vector(dvsec)

        def join(v):
            ev_s.wait(v)
            ch.push(v.tensor_copy(lnm[:, :], lnm[:, :]))
        bk.vector(join)

    with nc.Block() as bk:
        def wload2(g):
            wdma.push(g.dma_start(AW[:, 0:2, :, :],
                                  ow2_d.rearrange("w (k p) m -> p w k m", p=128)), 16)
        bk.gpsimd(wload2)

    for b in range(16):
        col = b * TS
        with nc.Block() as bk:
            for m in range(8):
                def hmm(t, b=b, m=m, col=col):
                    wdma.wait(t)
                    ch.wait(t)
                    for k in range(8):
                        t.matmul(ps_f[:, 0:TS], wv[0][:, k, m * 128:(m + 1) * 128],
                                 xn3[:, k, col:col + TS], start=(k == 0), stop=False)
                    for k in range(8):
                        inst = t.matmul(ps_f[:, 0:TS], wv[1][:, k, m * 128:(m + 1) * 128],
                                        c_bf[0][:, k, :], start=False, stop=(k == 7))
                    ch.push(inst)
                bk.tensor(hmm)

                def hact(s, m=m):
                    ch.wait(s)
                    ch.push(s.activation(h_fm[:, m, :], ps_f[:, 0:TS], AF.Tanh,
                                         bias=bo_sb[:, m:m + 1]))
                bk.scalar(hact)

            for tt in range(2):
                if fc.lstm_gather:
                    def dmx(g, b=b, tt=tt):
                        ch.wait(g)
                        outs.wait(g)
                        dch.push(g.indirect_dma_start(
                            xg_bf[:, :], None, agout_d,
                            bass.IndirectOffsetOnAxis(ap=tsel_sb[:, b * 2 + tt:b * 2 + tt + 1],
                                                      axis=0)), 16)
                    bk.gpsimd(dmx)

                    def vcast(v):
                        dch.wait(v)
                        ch.push(v.tensor_copy(xtile[:, :], xg_bf[:, :]))
                    bk.vector(vcast)
                else:
                    def dmx(sy, b=b, tt=tt):
                        ch.wait(sy)
                        outs.wait(sy)
                        hch.push(sy.dma_start(xtile[:, :], x_in[b, tt * 128:(tt + 1) * 128, :]), 16)
                    bk.sync(dmx)

                pe0, ev0 = pe_s.n, ev_s.n

                def etr(t, tt=tt, pe0=pe0, ev0=ev0):
                    ch.wait(t)
                    for m in range(8):
                        if m >= 2:
                            ev_s.wait(t, ev0 + (m - 2) + 1)
                        pe_s.push(t.transpose(ps_tr[m % 2][:, 0:128],
                                              h_fm[:, m, tt * 128:(tt + 1) * 128], id_bf[:, :]))
                bk.tensor(etr)

                def eev(v, tt=tt, pe0=pe0, ev0=ev0):
                    if not fc.lstm_gather:
                        hch.wait(v)
                    else:
                        ch.wait(v)
                    for m in range(8):
                        pe_s.wait(v, pe0 + 1 + m)
                        ev_s.push(v.tensor_tensor(o_tm[:, m * 128:(m + 1) * 128],
                                                  ps_tr[m % 2][:, 0:128],
                                                  xtile[:, m * 128:(m + 1) * 128], ALU.add))
                bk.vector(eev)

                def ejoin(v):
                    ev_s.wait(v)
                    ch.push(v.tensor_copy(lnm[:, :], lnm[:, :]))
                bk.vector(ejoin)

                def dmo(sy, b=b, tt=tt):
                    ch.wait(sy)
                    outs.push(sy.dma_start(x_out[b, tt * 128:(tt + 1) * 128, :], o_tm[:, :]), 16)
                bk.sync(dmo)

            def fin(sy):
                outs.wait(sy)
            bk.sync(fin)

    return nc


def _host_lstm(inputs):
    ident = np.eye(128, dtype=np.float32)
    aw = np.stack([np.asarray(inputs["L1_ffn_" + n]).T for n in
                   ["wf", "uf", "wi", "ui", "wc", "uc"]])
    ow2 = np.stack([np.asarray(inputs["L1_ffn_wo"]).T, np.asarray(inputs["L1_ffn_uo"]).T])
    gb = np.stack([np.asarray(inputs["L1_ffn_" + n]).reshape(8, 128).T for n in
                   ["bf", "bi", "bc"]], axis=1)
    base = {
        "aw": np.ascontiguousarray(aw),
        "ow2": np.ascontiguousarray(ow2),
        "gb": np.ascontiguousarray(gb),
        "bo": np.ascontiguousarray(np.asarray(inputs["L1_ffn_bo"]).reshape(8, 128).T),
        "in_c": np.ascontiguousarray(np.asarray(inputs["L1_ffn_in_c"]).reshape(8, 128).T),
        "ident": ident,
    }
    return base


def run_lstm(x, inputs):
    nc = _get("lstm", build_lstm)
    base = _host_lstm(inputs)
    in_maps = []
    for cc in range(NCORE):
        m = dict(base)
        m["x_in"] = np.ascontiguousarray(x[:, 256 * cc:256 * (cc + 1), :])
        in_maps.append(m)
    res = run_bass_kernel_spmd(nc, in_maps, core_ids=list(range(NCORE)))
    return np.concatenate([r["x_out"] for r in res.results], axis=1)


# version 06
# speedup vs baseline: 1.5935x; 1.0470x over previous
import numpy as np
import concourse.bass as bass
import concourse.mybir as mybir
from concourse.bass_utils import run_bass_kernel_spmd

F32 = mybir.dt.float32
BF16 = mybir.dt.bfloat16
AF = mybir.ActivationFunctionType
ALU = mybir.AluOpType
AX = mybir.AxisListType

NCORE = 8
BPC = 2
T = 2048
DM = 1024
DI = 2048
DS = 16
TOK = BPC * T
EPS = 1e-5


class Chain:

    def __init__(self, nc, name):
        self.h = nc.alloc_semaphore(name)
        self.n = 0

    def push(self, inst, k=1):
        inst.then_inc(self.h, k)
        self.n += k
        return self.n

    def wait(self, eng, tok=None):
        eng.wait_ge(self.h, self.n if tok is None else tok)




class FC:

    def __init__(self, nc=None, prefix="", dmap=None, pmap=None, stack=None,
                 lstm_gather=False, out_swdge=False):
        self.nc = nc
        self.prefix = prefix
        self.dmap = dmap or {}
        self.pmap = pmap or {}
        self.stack = stack
        self.lstm_gather = lstm_gather
        self.out_swdge = out_swdge

    def dram(self, nc, name, shape, dtype, kind="ExternalInput"):
        if name in self.dmap:
            return self.dmap[name]
        return nc.dram_tensor(self.prefix + name, shape, dtype, kind=kind).ap()

    def sb(self, nc, name, shape, dtype):
        if self.stack is not None:
            return self.stack.enter_context(nc.sbuf_tensor(self.prefix + name, shape, dtype))
        return nc.alloc_sbuf_tensor(name, shape, dtype)

    def ps(self, nc, name, shape, dtype):
        if name in self.pmap:
            return self.pmap[name]
        return nc.alloc_psum_tensor(name, shape, dtype)

def _ln_emit(nc, bk, ch, xtile, xn_tm, lnscr, lnm, lns, lnr, eps_col=None):

    def vpart1(v):
        ch.wait(v)
        t1 = ch.push(v.tensor_reduce(lnm[:, :], xtile[:, :], AX.X, ALU.add))
        ch.wait(v, t1)
        t2 = ch.push(v.tensor_scalar(lnm[:, :], lnm[:, :], 1.0 / DM, None, ALU.mult))
        ch.wait(v, t2)
        ch.push(v.tensor_scalar(xtile[:, :], xtile[:, :], lnm[:, :], None, ALU.subtract))
    bk.vector(vpart1)

    def spart(s):
        ch.wait(s)
        t1 = ch.push(s.activation(lnscr[:, :], xtile[:, :], AF.Square, accum_out=lns[:, :]))
        ch.wait(s, t1)
        ch.push(s.activation(lnr[:, :], lns[:, :], AF.Sqrt, scale=1.0 / DM, bias=eps_col[:, :]))
    bk.scalar(spart)

    def vpart2(v):
        ch.wait(v)
        t1 = ch.push(v.reciprocal(lnr[:, :], lnr[:, :]))
        ch.wait(v, t1)
        ch.push(v.tensor_scalar(xn_tm[:, :], xtile[:, :], lnr[:, :], None, ALU.mult))
    bk.vector(vpart2)


def build_moe(dbg=False, fc=None):
    fc = fc or FC()
    nc = fc.nc if fc.nc is not None else bass.Bass("TRN2", target_bir_lowering=False, debug=False)
    x_in = fc.dram(nc, "x_in", [TOK, DM], F32)
    wr_d = fc.dram(nc, "wr", [DM, 8], F32)
    wg_d = fc.dram(nc, "wg", [8, DM, DI], F32)
    bg_d = fc.dram(nc, "bg", [128, 16, 8], F32)
    wu_d = fc.dram(nc, "wu", [8, DM, DI], F32)
    bu_d = fc.dram(nc, "bu", [128, 16, 8], F32)
    wo_d = fc.dram(nc, "wo", [8, DI, DM], F32)
    bo_d = fc.dram(nc, "bo", [8, DM], F32)
    id_d = fc.dram(nc, "ident", [128, 128], F32)
    x_out = fc.dram(nc, "x_out", [TOK, DM], F32, kind="ExternalOutput")
    if dbg:
        d_xq = fc.dram(nc, "d_xq", [128, 8, 1024], F32, kind="ExternalOutput")
        d_w = fc.dram(nc, "d_w", [128, 8, 8], F32, kind="ExternalOutput")
        d_acc0 = fc.dram(nc, "d_acc0", [128, 8, DM], F32, kind="ExternalOutput")
        d_acc = fc.dram(nc, "d_acc", [128, 8, DM], F32, kind="ExternalOutput")
        d_hw = fc.dram(nc, "d_hw", [128, 16, 512], F32, kind="ExternalOutput")
        d_acc1 = fc.dram(nc, "d_acc1", [128, 8, DM], F32, kind="ExternalOutput")

    QT = 1024
    NQ = TOK // QT
    NTT = QT // 128

    wr_sb = fc.sb(nc, "wr_sb", [128, 8, 8], BF16)
    bg_sb = fc.sb(nc, "bg_sb", [128, 16, 8], F32)
    bu_sb = fc.sb(nc, "bu_sb", [128, 16, 8], F32)
    bo_sb = fc.sb(nc, "bo_sb", [8, DM], BF16)
    id_bf = fc.sb(nc, "id_bf", [128, 128], BF16)
    xq_fm = fc.sb(nc, "xq_fm", [128, 8, QT], BF16)
    w_tm = fc.sb(nc, "w_tm", [128, NTT, 8], F32)
    w_fmT = fc.sb(nc, "w_fmT", [8, QT], BF16)
    acc = fc.sb(nc, "acc", [128, NTT, DM], F32)
    wg_sb = fc.sb(nc, "wg_sb", [128, 8, DI], BF16)
    wu_sb = fc.sb(nc, "wu_sb", [128, 8, DI], BF16)
    wo_sb = fc.sb(nc, "wo_sb", [128, 16, DM], BF16)
    hw_sb = fc.sb(nc, "hw_sb", [128, 16, 512], BF16)
    gs_sb = [fc.sb(nc, f"gs_sb{i}", [128, 512], BF16) for i in range(2)]
    xtile = fc.sb(nc, "xtile", [128, DM], F32)
    xn_tm = fc.sb(nc, "xn_tm", [128, DM], BF16)
    lnscr = fc.sb(nc, "lnscr", [128, DM], BF16)
    lnm = fc.sb(nc, "lnm", [128, 1], F32)
    lns = fc.sb(nc, "lns", [128, 1], F32)
    lnr = fc.sb(nc, "lnr", [128, 1], F32)
    r_pr = fc.sb(nc, "r_pr", [128, 8], F32)
    r_m1 = fc.sb(nc, "r_m1", [128, 1], F32)
    r_m2 = fc.sb(nc, "r_m2", [128, 1], F32)
    r_k1 = fc.sb(nc, "r_k1", [128, 8], F32)
    r_k2 = fc.sb(nc, "r_k2", [128, 8], F32)
    r_t = fc.sb(nc, "r_t", [128, 8], F32)
    r_den = fc.sb(nc, "r_den", [128, 1], F32)
    w_bf = fc.sb(nc, "w_bf", [128, 8], BF16)
    eps_col = fc.sb(nc, "eps_col", [128, 1], F32)
    o_tm = fc.sb(nc, "o_tm", [128, DM], F32)

    ps_tr = [fc.ps(nc, f"ps_tr{i}", [128, 1024], BF16) for i in range(2)]
    ps_g = fc.ps(nc, "ps_g", [128, 512], F32)
    ps_u = fc.ps(nc, "ps_u", [128, 512], F32)
    ps_o0 = fc.ps(nc, "ps_o0", [128, DM], F32)
    ps_o1 = fc.ps(nc, "ps_o1", [128, DM], F32)
    ps_o = [ps_o0, ps_o0]
    ps_gp = [ps_g[:, :], ps_o1[:, 0:512]]
    ps_up = [ps_u[:, :], ps_o1[:, 512:1024]]
    ps_r = ps_o1[:, 0:8]

    ch = Chain(nc, "ch")
    pe_s = Chain(nc, "pe_s")
    ev_s = Chain(nc, "ev_s")
    ac_s = Chain(nc, "ac_s")
    o_dg = Chain(nc, "o_dg")
    o_dh = Chain(nc, "o_dh")
    wdma = Chain(nc, "wdma")
    outs = Chain(nc, "outs")

    with nc.Block() as bk:
        def gsec(g):
            o_dg.push(g.dma_start(wr_sb[:, :, :], wr_d.rearrange("(k p) e -> p k e", p=128)), 16)
            o_dg.push(g.dma_start(bg_sb[:, :, :], bg_d), 16)
            o_dg.push(g.dma_start(bu_sb[:, :, :], bu_d), 16)
            o_dg.push(g.dma_start(bo_sb[:, :], bo_d), 16)
            o_dg.push(g.dma_start(id_bf[:, :], id_d), 16)
            g.memset(eps_col[:, :], EPS)
        bk.gpsimd(gsec)

    for q in range(NQ):
        with nc.Block() as bk:
            for it in range(NTT):
                tok0 = q * QT + it * 128

                def dmain(sy, tok0=tok0):
                    ch.wait(sy)
                    o_dh.push(sy.dma_start(xtile[:, :], x_in[tok0:tok0 + 128, :]), 16)
                bk.sync(dmain)

                def vbridge(v):
                    o_dg.wait(v)
                    o_dh.wait(v)
                    ch.push(v.tensor_copy(lnm[:, :], eps_col[:, :]))
                bk.vector(vbridge)

                _ln_emit(nc, bk, ch, xtile, xn_tm, lnscr, lnm, lns, lnr, eps_col)

                pe0, ev0 = pe_s.n, ev_s.n

                def trans(t, pe0=pe0, ev0=ev0):
                    ch.wait(t)
                    for j in range(8):
                        if j >= 2:
                            ev_s.wait(t, ev0 + (j - 2) + 1)
                        pe_s.push(t.transpose(ps_tr[j % 2][:, 0:128],
                                              xn_tm[:, j * 128:(j + 1) * 128], id_bf[:, :]))
                bk.tensor(trans)

                def evac(v, it=it, pe0=pe0, ev0=ev0):
                    for j in range(8):
                        pe_s.wait(v, pe0 + 1 + j)
                        ev_s.push(v.tensor_copy(xq_fm[:, j, it * 128:(it + 1) * 128],
                                                ps_tr[j % 2][:, 0:128]))
                bk.vector(evac)

                def joinv(v):
                    ev_s.wait(v)
                    ch.push(v.tensor_copy(lnm[:, :], lnm[:, :]))
                bk.vector(joinv)

                def rmm(t, it=it):
                    ch.wait(t)
                    for k in range(8):
                        inst = t.matmul(ps_r[:, :], xq_fm[:, k, it * 128:(it + 1) * 128],
                                        wr_sb[:, k, :], start=(k == 0), stop=(k == 7))
                    ch.push(inst)
                bk.tensor(rmm)

                def rsoft1(v):
                    ch.wait(v)
                    t1 = ch.push(v.tensor_copy(r_pr[:, :], ps_r[:, :]))
                    ch.wait(v, t1)
                    t2 = ch.push(v.tensor_reduce(r_m1[:, :], r_pr[:, :], AX.X, ALU.max))
                    ch.wait(v, t2)
                    ch.push(v.tensor_scalar(r_pr[:, :], r_pr[:, :], r_m1[:, :], None, ALU.subtract))
                bk.vector(rsoft1)

                def rexp(s):
                    ch.wait(s)
                    ch.push(s.activation(r_pr[:, :], r_pr[:, :], AF.Exp))
                bk.scalar(rexp)

                def rsoft2(v, it=it):
                    ch.wait(v)
                    seq = [
                        lambda: v.tensor_reduce(r_den[:, :], r_pr[:, :], AX.X, ALU.add),
                        lambda: v.reciprocal(r_den[:, :], r_den[:, :]),
                        lambda: v.tensor_scalar(r_pr[:, :], r_pr[:, :], r_den[:, :], None, ALU.mult),
                        lambda: v.tensor_reduce(r_m1[:, :], r_pr[:, :], AX.X, ALU.max),
                        lambda: v.tensor_scalar(r_k1[:, :], r_pr[:, :], r_m1[:, :], None, ALU.is_ge),
                        lambda: v.tensor_tensor(r_t[:, :], r_k1[:, :], r_pr[:, :], ALU.mult),
                        lambda: v.tensor_tensor(r_t[:, :], r_pr[:, :], r_t[:, :], ALU.subtract),
                        lambda: v.tensor_reduce(r_m2[:, :], r_t[:, :], AX.X, ALU.max),
                        lambda: v.tensor_scalar(r_k2[:, :], r_t[:, :], r_m2[:, :], None, ALU.is_ge),
                        lambda: v.tensor_tensor(r_k1[:, :], r_k1[:, :], r_k2[:, :], ALU.add),
                        lambda: v.tensor_scalar(r_den[:, :], r_m1[:, :], r_m2[:, :], 1e-8, ALU.add, ALU.add),
                        lambda: v.reciprocal(r_den[:, :], r_den[:, :]),
                        lambda: v.tensor_tensor(r_t[:, :], r_pr[:, :], r_k1[:, :], ALU.mult),
                        lambda: v.tensor_scalar(w_tm[:, it, :], r_t[:, :], r_den[:, :], None, ALU.mult),
                        lambda: v.tensor_copy(w_bf[:, :], w_tm[:, it, :]),
                    ]
                    for f in seq:
                        tk = ch.push(f())
                        ch.wait(v, tk)
                bk.vector(rsoft2)

                def wtr(t):
                    ch.wait(t)
                    ch.push(t.transpose(ps_tr[0][0:8, 0:128], w_bf[:, :], id_bf[:, :]))
                bk.tensor(wtr)

                def wev(v, it=it):
                    ch.wait(v)
                    ch.push(v.tensor_copy(w_fmT[0:8, it * 128:(it + 1) * 128],
                                          ps_tr[0][0:8, 0:128]))
                bk.vector(wev)

            for it in range(NTT):
                def bterm(t, it=it):
                    ch.wait(t)
                    for hh in range(2):
                        inst = t.matmul(ps_o[0][:, hh * 512:(hh + 1) * 512],
                                        w_fmT[:, it * 128:(it + 1) * 128],
                                        bo_sb[:, hh * 512:(hh + 1) * 512],
                                        start=True, stop=True)
                    ch.push(inst, 1)
                bk.tensor(bterm)

                def bev(v, it=it):
                    ch.wait(v)
                    ch.push(v.tensor_copy(acc[:, it, :], ps_o[0][:, :]))
                bk.vector(bev)

        if dbg and q == 0:
            with nc.Block() as bk:
                def dump1(g):
                    ch.wait(g)
                    outs.push(g.dma_start(d_xq[:, :, :], xq_fm[:, :, :]), 16)
                    outs.push(g.dma_start(d_w[:, :, :], w_tm[:, :, :]), 16)
                    outs.push(g.dma_start(d_acc0[:, :, :], acc[:, :, :]), 16)
                bk.gpsimd(dump1)

        with nc.Block() as bk:
            W0n = wdma.n
            w_wu = {e: W0n + 48 * e + 32 for e in range(8)}
            w_wo = {e: W0n + 48 * e + 48 for e in range(8)}
            P0n = pe_s.n
            p_g, p_u, p_out = {}, {}, {}
            cnt = P0n
            for e in range(8):
                for h in range(2):
                    for m in range(16):
                        gi = (e * 2 + h) * 16 + m
                        cnt += 1; p_g[gi] = cnt
                        cnt += 1; p_u[gi] = cnt
                    for tt in range(4):
                        cnt += 1; p_out[(e, h, tt)] = cnt
            A0n = ac_s.n
            a_si = {}
            cnt = A0n
            for gi in range(256):
                cnt += 1; a_si[gi] = cnt
            E0n = ev_s.n
            d_hm, d_oa = {}, {}
            cnt = E0n
            oi = 0
            oa_flat = {}
            for e in range(8):
                for h in range(2):
                    for m in range(16):
                        gi = (e * 2 + h) * 16 + m
                        cnt += 1; d_hm[gi] = cnt
                    for tt in range(4):
                        cnt += 1; d_oa[(e, h, tt)] = cnt
                        oa_flat[oi] = cnt
                        oi += 1

            def wsec(g, q=q):
                for e in range(8):
                    if e >= 1:
                        pe_s.wait(g, p_u[(e * 2) * 16 - 1 + 0] if False else p_u[(e * 2 - 1) * 16 + 15])
                    wdma.push(g.dma_start(wg_sb[:, :, :],
                                          wg_d[e].rearrange("(k p) m -> p k m", p=128)), 16)
                    wdma.push(g.dma_start(wu_sb[:, :, :],
                                          wu_d[e].rearrange("(k p) m -> p k m", p=128)), 16)
                    if e >= 1:
                        pe_s.wait(g, p_out[(e - 1, 1, 3)])
                    wdma.push(g.dma_start(wo_sb[:, :, :],
                                          wo_d[e].rearrange("(k p) m -> p k m", p=128)), 16)
            bk.gpsimd(wsec)

            def pesec(t, q=q):
                ch.wait(t)
                oi = 0
                for e in range(8):
                    wdma.wait(t, w_wu[e])
                    for h in range(2):
                        tok0 = h * 512
                        for m in range(16):
                            gi = (e * 2 + h) * 16 + m
                            if gi >= 2:
                                ac_s.wait(t, a_si[gi - 2])
                                ev_s.wait(t, d_hm[gi - 2])
                            for k in range(8):
                                inst = t.matmul(ps_gp[gi % 2], wg_sb[:, k, m * 128:(m + 1) * 128],
                                                xq_fm[:, k, tok0:tok0 + 512],
                                                start=(k == 0), stop=(k == 7),
                                                skip_group_check=True)
                            tk = pe_s.push(inst)
                            assert tk == p_g[gi]
                            for k in range(8):
                                inst = t.matmul(ps_up[gi % 2], wu_sb[:, k, m * 128:(m + 1) * 128],
                                                xq_fm[:, k, tok0:tok0 + 512],
                                                start=(k == 0), stop=(k == 7),
                                                skip_group_check=True)
                            tk = pe_s.push(inst)
                            assert tk == p_u[gi]
                        ev_s.wait(t, d_hm[(e * 2 + h) * 16 + 15])
                        wdma.wait(t, w_wo[e])
                        for tt in range(4):
                            if oi >= 1:
                                ev_s.wait(t, oa_flat[oi - 1])
                            for k in range(16):
                                for half in range(2):
                                    inst = t.matmul(ps_o0[:, half * 512:(half + 1) * 512],
                                                    hw_sb[:, k, tt * 128:(tt + 1) * 128],
                                                    wo_sb[:, k, half * 512:(half + 1) * 512],
                                                    start=(k == 0), stop=(k == 15),
                                                    skip_group_check=True)
                            tk = pe_s.push(inst)
                            assert tk == p_out[(e, h, tt)]
                            oi += 1
            bk.tensor(pesec)

            def acsec(s, q=q):
                for e in range(8):
                    for h in range(2):
                        for m in range(16):
                            gi = (e * 2 + h) * 16 + m
                            pe_s.wait(s, p_g[gi])
                            if gi >= 2:
                                ev_s.wait(s, d_hm[gi - 2])
                            tk = ac_s.push(s.activation(gs_sb[gi % 2][:, :], ps_gp[gi % 2],
                                                        AF.Silu, bias=bg_sb[:, m, e:e + 1]))
                            assert tk == a_si[gi]
            bk.scalar(acsec)

            def dvsec(v, q=q):
                for e in range(8):
                    for h in range(2):
                        for m in range(16):
                            gi = (e * 2 + h) * 16 + m
                            if m == 0 and (e, h) != (0, 0):
                                pem, phm = (e, h - 1) if h == 1 else (e - 1, 1)
                                pe_s.wait(v, p_out[(pem, phm, 3)])
                            pe_s.wait(v, p_u[gi])
                            ac_s.wait(v, a_si[gi])
                            tk = ev_s.push(v.scalar_tensor_tensor(hw_sb[:, m, :], ps_up[gi % 2],
                                                                  bu_sb[:, m, e:e + 1],
                                                                  gs_sb[gi % 2][:, :],
                                                                  ALU.add, ALU.mult))
                            assert tk == d_hm[gi]
                        for tt in range(4):
                            it = h * 4 + tt
                            pe_s.wait(v, p_out[(e, h, tt)])
                            tk = ev_s.push(v.scalar_tensor_tensor(acc[:, it, :], ps_o0[:, :],
                                                                  w_tm[:, it, e:e + 1], acc[:, it, :],
                                                                  ALU.mult, ALU.add))
                            assert tk == d_oa[(e, h, tt)]
            bk.vector(dvsec)

            def joinall(v):
                ev_s.wait(v)
                ch.push(v.tensor_copy(r_m1[:, :], r_m1[:, :]))
            bk.vector(joinall)

        with nc.Block() as bk:
            for it in range(NTT):
                tok0 = q * QT + it * 128

                def dmain(sy, tok0=tok0):
                    ch.wait(sy)
                    o_dh.push(sy.dma_start(xtile[:, :], x_in[tok0:tok0 + 128, :]), 16)
                bk.sync(dmain)

                def radd(v, it=it):
                    ch.wait(v)
                    o_dh.wait(v)
                    outs.wait(v)
                    ch.push(v.tensor_tensor(o_tm[:, :], acc[:, it, :], xtile[:, :], ALU.add))
                bk.vector(radd)

                def dmaout(sy, tok0=tok0):
                    ch.wait(sy)
                    outs.push(sy.dma_start(x_out[tok0:tok0 + 128, :], o_tm[:, :]), 16)
                bk.sync(dmaout)

            def fin(sy):
                outs.wait(sy)
            bk.sync(fin)

    return nc


_CACHE = {}


def _get(name, builder):
    if name not in _CACHE:
        nc = builder()
        nc.finalize()
        _CACHE[name] = nc
    return _CACHE[name]


def _host_moe(inputs):
    ident = np.eye(128, dtype=np.float32)
    return {
            "wr": inputs["L0_ffn_W_router"],
            "wg": inputs["L0_ffn_W_gate"],
            "bg": np.ascontiguousarray(np.asarray(inputs["L0_ffn_b_gate"]).reshape(8, 16, 128).transpose(2, 1, 0)),
            "wu": inputs["L0_ffn_W_up"],
            "bu": np.ascontiguousarray(np.asarray(inputs["L0_ffn_b_up"]).reshape(8, 16, 128).transpose(2, 1, 0)),
            "wo": inputs["L0_ffn_W_out"],
            "bo": inputs["L0_ffn_b_out"],
            "ident": ident,
    }


def run_moe(x, inputs):
    nc = _get("moe", build_moe)
    base = _host_moe(inputs)
    in_maps = []
    for c in range(NCORE):
        m = dict(base)
        m["x_in"] = np.ascontiguousarray(x[2 * c:2 * c + 2].reshape(TOK, DM))
        in_maps.append(m)
    res = run_bass_kernel_spmd(nc, in_maps, core_ids=list(range(NCORE)))
    out = np.concatenate([r["x_out"].reshape(BPC, T, DM) for r in res.results], axis=0)
    return out


def build_full():
    import contextlib
    nc = bass.Bass("TRN2", target_bir_lowering=False, debug=False)
    ps_a = nc.alloc_psum_tensor("f_psa", [128, 1024], F32)
    ps_b = nc.alloc_psum_tensor("f_psb", [128, 1024], F32)
    ps_c = nc.alloc_psum_tensor("f_psc", [128, 512], F32)
    ps_d = nc.alloc_psum_tensor("f_psd", [128, 512], F32)
    trA = nc.alloc_psum_tensor("f_trA", [128, 1024], BF16)
    trB = nc.alloc_psum_tensor("f_trB", [128, 1024], BF16)
    mamba_pmap = {"mps_a": ps_a, "mps_b": ps_b,
                  "mps_c": ps_c[:, :], "mps_d": ps_d[:, :],
                  "mps_tr0": trA, "mps_tr1": trB}

    x0 = nc.dram_tensor("x_in", [BPC, T, DM], F32, kind="ExternalInput").ap()
    x1_sc = nc.dram_tensor("x1_sc", [BPC, T, DM], F32, kind="Internal").ap()
    x2_sc = nc.dram_tensor("x2_sc", [BPC, T, DM], F32, kind="Internal").ap()
    ag_in = nc.dram_tensor("ag_in", [BPC * T, DM], BF16, kind="Internal").ap()
    ag_out = nc.dram_tensor("ag_out", [NCORE * BPC * T, DM], BF16, kind="Internal").ap()
    x_out = nc.dram_tensor("x_out", [16, 256, DM], F32, kind="ExternalOutput").ap()

    with contextlib.ExitStack() as st:
        fc = FC(nc=nc, prefix="m_", stack=st,
                dmap={"x_in": x0, "x_out": x1_sc},
                pmap=mamba_pmap)
        build_mamba(fc=fc)

    with contextlib.ExitStack() as st:
        fc = FC(nc=nc, prefix="o_", stack=st,
                dmap={"x_in": x1_sc.rearrange("b t d -> (b t) d"),
                      "x_out": x2_sc.rearrange("b t d -> (b t) d")},
                pmap={"ps_tr0": trA, "ps_tr1": trB, "ps_r": ps_b[:, 0:8],
                      "ps_g": ps_c, "ps_u": ps_d, "ps_o0": ps_a, "ps_o1": ps_b})
        build_moe(fc=fc)

    with contextlib.ExitStack() as st:
        fc = FC(nc=nc, prefix="t_", stack=st, out_swdge=True,
                dmap={"x_in": x2_sc,
                      "x_out": ag_in.rearrange("(b t) d -> b t d", b=BPC)},
                pmap={"t_psbig": ps_a, "t_psbig2": ps_b, "t_pssm": ps_c,
                      "t_pstr0": trA, "t_pstr1": trB})
        build_ttt(fc=fc)

    ccs = nc.alloc_semaphore("f_ccs")
    with nc.Block() as bk:
        def ccsec(g):
            g.collective_compute("AllGather", mybir.AluOpType.bypass,
                                 replica_groups=[list(range(NCORE))],
                                 ins=[ag_in.opt()], outs=[ag_out.opt()]).then_inc(ccs, 1)
            g.wait_ge(ccs, 1)
        bk.gpsimd(ccsec)

    with contextlib.ExitStack() as st:
        fc = FC(nc=nc, prefix="l_", stack=st, lstm_gather=True,
                dmap={"agout": ag_out, "x_out": x_out},
                pmap={"l_psf": ps_a[:, 0:512], "l_psi": ps_a[:, 512:1024],
                      "l_psg": ps_b[:, 0:512], "l_psf2": ps_b[:, 512:1024],
                      "l_psi2": ps_c, "l_psg2": ps_d,
                      "l_pstr0": trA, "l_pstr1": trB})
        build_lstm(fc=fc)

    return nc


def kernel(**inputs):
    nc = _get("full", build_full)
    x0 = np.asarray(inputs["L0_attn_x"], dtype=np.float32)
    mb = {("m_" + k): v for k, v in _host_mamba(inputs).items()}
    ob = {("o_" + k): v for k, v in _host_moe(inputs).items()}
    tb = {("t_" + k): v for k, v in _host_ttt(inputs).items()}
    lb = {("l_" + k): v for k, v in _host_lstm(inputs).items()}
    in_maps = []
    for cc in range(NCORE):
        sel = np.zeros((128, 32), np.uint32)
        for bb in range(16):
            for tt in range(2):
                sel[:, bb * 2 + tt] = ((bb // 2) * 4096 + (bb % 2) * 2048
                                       + 256 * cc + tt * 128 + np.arange(128))
        m = {}
        m.update(mb)
        m.update(ob)
        m.update(tb)
        m.update(lb)
        m["x_in"] = np.ascontiguousarray(x0[2 * cc:2 * cc + 2])
        m["l_tsel"] = sel
        in_maps.append(m)
    res = run_bass_kernel_spmd(nc, in_maps, core_ids=list(range(NCORE)))
    return np.concatenate([r["x_out"] for r in res.results], axis=1)


def kernel_unfused(**inputs):
    x = np.asarray(inputs["L0_attn_x"], dtype=np.float32)
    x1 = run_mamba(x, inputs)
    x2 = run_moe(x1, inputs)
    x3 = run_ttt(x2, inputs)
    x4 = run_lstm(x3, inputs)
    return x4




def build_mamba(dbg=False, fc=None):
    import os
    SKIP_SCAN = os.environ.get("SKIP_SCAN") == "1"
    SKIP_DT = os.environ.get("SKIP_DT") == "1"
    SKIP_IP = os.environ.get("SKIP_IP") == "1"
    fc = fc or FC()
    nc = fc.nc if fc.nc is not None else bass.Bass("TRN2", target_bir_lowering=False, debug=False)
    TC = 512
    NCH = T // TC
    NG = DI // 8

    x_in = fc.dram(nc, "x_in", [BPC, T, DM], F32)
    inw_d = fc.dram(nc, "in_w", [DM, 2 * DI], F32)
    inb_d = fc.dram(nc, "in_b", [128, 32], F32)
    cw_d = fc.dram(nc, "conv_w", [128, 16, 4], F32)
    cb_d = fc.dram(nc, "conv_b", [128, 16], F32)
    sela_d = fc.dram(nc, "SELA", [16, 128, 16, 128], F32)
    dp_d = fc.dram(nc, "Dp_col", [128, 16], F32)
    dtw_d = fc.dram(nc, "dt_w", [DI, DI], F32)
    dtb_d = fc.dram(nc, "dt_b", [128, 16], F32)
    bcw_d = fc.dram(nc, "bc_w", [DI, 32], F32)
    bcb_d = fc.dram(nc, "bc_b", [32, 1], F32)
    ow_d = fc.dram(nc, "out_w", [DI, DM], F32)
    ob_d = fc.dram(nc, "out_b", [128, 8], F32)
    sel_d = fc.dram(nc, "SEL", [128, 16, 128], F32)
    red_d = fc.dram(nc, "RED", [128, 16, 128], F32)
    sbc_d = fc.dram(nc, "selbc", [32, 2, 128], F32)
    id_d = fc.dram(nc, "ident", [128, 128], F32)
    x_out = fc.dram(nc, "x_out", [BPC, T, DM], F32, kind="ExternalOutput")

    xz_sc = fc.dram(nc, "xz_sc", [BPC, 16, 128, T], BF16, kind="Internal")
    gt_sc = fc.dram(nc, "gt_sc", [BPC, 16, 128, T], BF16, kind="Internal")

    inb_sb = fc.sb(nc, "inb_sb", [128, 32], F32)
    cw_sb = fc.sb(nc, "cw_sb", [128, 16, 4], F32)
    cb_sb = fc.sb(nc, "cb_sb", [128, 16], F32)
    dp_sb = fc.sb(nc, "dp_sb", [128, 16], F32)
    dtb_sb = fc.sb(nc, "dtb_sb", [128, 16], F32)
    bcb_sb = fc.sb(nc, "bcb_sb", [32, 1], F32)
    ob_sb = fc.sb(nc, "ob_sb", [128, 8], F32)
    sel_sb = fc.sb(nc, "sel_sb", [128, 16, 128], BF16)
    red_sb = fc.sb(nc, "red_sb", [128, 16, 128], BF16)
    sbc_sb = fc.sb(nc, "sbc_sb", [32, 2, 128], BF16)
    id_bf = fc.sb(nc, "id_bf", [128, 128], BF16)
    eps_col = fc.sb(nc, "eps_col", [128, 1], F32)
    A1 = fc.sb(nc, "m_A1", [128, 8 * 2 * DI], BF16)
    inw_sb = A1[:, :].rearrange("p (k m) -> p k m", k=8)
    dtw_sb = A1[:, :].rearrange("p (k m) -> p k m", k=16)
    ow_sb = fc.sb(nc, "m_ow", [128, 16, DM], BF16)
    bcw_sb = fc.sb(nc, "m_bcw", [128, 16, 32], BF16)
    ar1 = fc.sb(nc, "m_ar1", [128, 16, TC + 3], BF16)
    xz_fm = ar1[:, :, 0:TC]
    xzh = ar1[:, :, :]
    yg_fm = ar1[:, :, 0:TC]
    gtc_fm = fc.sb(nc, "m_gt_fm", [128, 8, TC], BF16)
    xc_fm = fc.sb(nc, "m_xc_fm", [128, 16, TC], BF16)
    dt_fm = fc.sb(nc, "m_dt_fm", [128, 16, TC], BF16)
    arof = fc.sb(nc, "m_arof", [128, 8, TC], BF16)
    xn_fm = arof[:, :, :]
    of_fm = arof[:, :, :]
    cvr = fc.sb(nc, "m_cvr", [128, TC], F32)
    xtile = fc.sb(nc, "m_xtile", [128, DM], F32)
    xn_tm = fc.sb(nc, "m_xn_tm", [128, DM], BF16)
    lnscr = fc.sb(nc, "m_lnscr", [128, DM], BF16)
    lnm = fc.sb(nc, "m_lnm", [128, 1], F32)
    lns = fc.sb(nc, "m_lns", [128, 1], F32)
    lnr = fc.sb(nc, "m_lnr", [128, 1], F32)
    bc_sb = fc.sb(nc, "m_bc_sb", [32, TC], BF16)
    brep = fc.sb(nc, "m_brep", [128, TC], BF16)
    crep = fc.sb(nc, "m_crep", [128, TC], BF16)
    dA_sb = fc.sb(nc, "m_dA", [128, 2 * TC], F32)
    u_sb = fc.sb(nc, "m_u", [128, 2 * TC], F32)
    h_sb = fc.sb(nc, "m_h", [128, 2 * TC], BF16)
    hc_sb = [fc.sb(nc, f"m_hc{i}", [128, 2 * TC], BF16) for i in range(2)]
    inj_sb = fc.sb(nc, "m_inj", [128, 1], F32)
    sela_sb = [fc.sb(nc, f"m_sela{i}", [128, 16, 128], BF16) for i in range(2)]
    hcar = fc.sb(nc, "m_hcar", [128, 256], F32)
    sg_sb = [fc.sb(nc, "m_sg0", [128, TC], BF16), lnscr[:, 0:TC]]
    t1_sb = lnscr[:, TC:2 * TC]
    dtxJ2 = [xn_tm[:, 0:TC], xn_tm[:, TC:2 * TC]]
    sp_a2 = [lnscr[:, 0:TC], lnscr[:, TC:2 * TC]]
    sp_m2 = [cvr[:, :], xtile[:, 0:TC]]


    mps_a = fc.ps(nc, "mps_a", [128, 1024], F32)
    mps_b = fc.ps(nc, "mps_b", [128, 1024], F32)
    mps_c = fc.ps(nc, "mps_c", [128, 512], F32)
    mps_d = fc.ps(nc, "mps_d", [128, 512], F32)
    ps_dtr = mps_a[:, :]
    ps_dxr = mps_b[:, :]
    ps_y = [mps_c[:, :], mps_d[:, :]]
    ps_mm = [mps_a[:, 0:512], mps_a[:, 512:1024]]
    ps_bc = mps_d[:, :]
    ps_tr = [fc.ps(nc, f"mps_tr{i}", [128, 1024], BF16) for i in range(2)]

    ch = Chain(nc, "m_ch")
    pe_s = Chain(nc, "m_pe")
    ev_s = Chain(nc, "m_ev")
    ac_s = Chain(nc, "m_ac")
    dch = Chain(nc, "m_dch")
    hch = Chain(nc, "m_hch")
    wdma = Chain(nc, "m_wdma")
    outs = Chain(nc, "m_outs")
    outs_h = Chain(nc, "m_outsh")
    ms = Chain(nc, "m_ms")
    selch = Chain(nc, "m_selch")

    with nc.Block() as bk:
        def gsec(g):
            dch.push(g.dma_start(inb_sb[:, :], inb_d), 16)
            dch.push(g.dma_start(cw_sb[:, :, :], cw_d), 16)
            dch.push(g.dma_start(cb_sb[:, :], cb_d), 16)
            dch.push(g.dma_start(dp_sb[:, :], dp_d), 16)
            dch.push(g.dma_start(dtb_sb[:, :], dtb_d), 16)
            dch.push(g.dma_start(bcb_sb[:, :], bcb_d), 16)
            dch.push(g.dma_start(ob_sb[:, :], ob_d), 16)
            dch.push(g.dma_start(sel_sb[:, :, :], sel_d), 16)
            dch.push(g.dma_start(red_sb[:, :, :], red_d), 16)
            dch.push(g.dma_start(sbc_sb[:, :, :], sbc_d), 16)
            dch.push(g.dma_start(id_bf[:, :], id_d), 16)
            dch.push(g.dma_start(bcw_sb[:, :, :], bcw_d.rearrange("(k p) c -> p k c", p=128)), 16)
            dch.push(g.dma_start(inw_sb, inw_d.rearrange("(k p) m -> p k m", p=128)), 16)
            g.memset(eps_col[:, :], EPS)
            g.memset(hcar[:, :], 0.0)
        bk.gpsimd(gsec)

    for b in range(BPC):
        for c in range(NCH):
            t0 = c * TC
            with nc.Block() as bk:
                for tt in range(4):
                    def dmain(sy, b=b, t0=t0, tt=tt):
                        ch.wait(sy)
                        hch.push(sy.dma_start(xtile[:, :], x_in[b, t0 + tt * 128:t0 + (tt + 1) * 128, :]), 16)
                    bk.sync(dmain)

                    def vstart(v):
                        dch.wait(v)
                        hch.wait(v)
                        ch.push(v.tensor_copy(lnm[:, :], eps_col[:, :]))
                    bk.vector(vstart)

                    _ln_emit(nc, bk, ch, xtile, xn_tm, lnscr, lnm, lns, lnr, eps_col)

                    pe0, ev0 = pe_s.n, ev_s.n

                    def trans(t, pe0=pe0, ev0=ev0):
                        ch.wait(t)
                        for j in range(8):
                            if j >= 2:
                                ev_s.wait(t, ev0 + (j - 2) + 1)
                            pe_s.push(t.transpose(ps_tr[j % 2][:, 0:128],
                                                  xn_tm[:, j * 128:(j + 1) * 128], id_bf[:, :]))
                    bk.tensor(trans)

                    def evac(v, tt=tt, pe0=pe0, ev0=ev0):
                        for j in range(8):
                            pe_s.wait(v, pe0 + 1 + j)
                            ev_s.push(v.tensor_copy(xn_fm[:, j, tt * 128:(tt + 1) * 128],
                                                    ps_tr[j % 2][:, 0:128]))
                    bk.vector(evac)

                    def joinv(v):
                        ev_s.wait(v)
                        ch.push(v.tensor_copy(lnm[:, :], lnm[:, :]))
                    bk.vector(joinv)

                P0, E0 = pe_s.n, ev_s.n

                def ipmm(t, P0=P0, E0=E0):
                    ch.wait(t)
                    for m in range(32):
                        if m >= 2:
                            ev_s.wait(t, E0 + (m - 2) + 1)
                        for k in range(8):
                            inst = t.matmul(ps_mm[m % 2], inw_sb[:, k, m * 128:(m + 1) * 128],
                                            xn_fm[:, k, :], start=(k == 0), stop=(k == 7),
                                            skip_group_check=True)
                        pe_s.push(inst)
                if not SKIP_IP: bk.tensor(ipmm)

                def ipev(v, P0=P0, E0=E0):
                    for m in range(32):
                        pe_s.wait(v, P0 + m + 1)
                        if m < 16:
                            dst = xz_fm[:, m, :]
                        elif m < 24:
                            dst = gtc_fm[:, m - 16, :]
                        else:
                            dst = dt_fm[:, m - 24, :]
                        ev_s.push(v.tensor_scalar(dst, ps_mm[m % 2], inb_sb[:, m:m + 1], None, ALU.add))
                if not SKIP_IP: bk.vector(ipev)

                def ipjoin(v):
                    if not SKIP_IP:
                        ev_s.wait(v)
                    ch.push(v.tensor_copy(lnm[:, :], lnm[:, :]))
                bk.vector(ipjoin)

                def st(g, b=b, t0=t0):
                    ch.wait(g)
                    outs.push(g.dma_start(xz_sc[b, :, :, t0:t0 + TC], xz_fm[:, :, :]), 16)
                    outs.push(g.dma_start(gt_sc[b, 0:8, :, t0:t0 + TC], gtc_fm[:, :, :]), 16)
                    outs.push(g.dma_start(gt_sc[b, 8:16, :, t0:t0 + TC], dt_fm[:, 0:8, :]), 16)
                bk.gpsimd(st)

                def fin(sy):
                    outs.wait(sy)
                bk.sync(fin)

    with nc.Block() as bk:
        def wload(g):
            wdma.push(g.dma_start(dtw_sb, dtw_d.rearrange("(k p) m -> p k m", p=128)), 16)
            wdma.push(g.dma_start(ow_sb[:, :, :], ow_d.rearrange("(k p) m -> p k m", p=128)), 16)
        bk.gpsimd(wload)

    for b in range(BPC):
        with nc.Block() as bk:
            def hrst(v, b=b):
                ch.push(v.tensor_scalar(hcar[:, :], hcar[:, :], 0.0, None, ALU.mult))
            bk.vector(hrst)

        for c in range(NCH):
            t0 = c * TC
            with nc.Block() as bk:
                lo = max(t0 - 1, 0)
                hi = min(t0 + TC + 2, T)
                off = lo - (t0 - 1)

                def ld(g, b=b, t0=t0, lo=lo, hi=hi, off=off):
                    ms.push(g.memset(ar1[:, :, :], 0.0))
                    ms.wait(g)
                    dch.push(g.dma_start(xzh[:, :, off:off + hi - lo], xz_sc[b, :, :, lo:hi]), 16)
                    dch.push(g.dma_start(gtc_fm[:, :, :], gt_sc[b, 0:8, :, t0:t0 + TC]), 16)
                bk.gpsimd(ld)

                for j in range(16):
                    def conv(v, j=j):
                        if j == 0:
                            dch.wait(v)
                            wdma.wait(v)
                        ch.wait(v)
                        t1 = ch.push(v.tensor_scalar(cvr[:, :], xzh[:, j, 0:TC],
                                                     cw_sb[:, j, 0:1], None, ALU.mult))
                        for jj in range(1, 4):
                            ch.wait(v, t1)
                            t1 = ch.push(v.scalar_tensor_tensor(cvr[:, :], xzh[:, j, jj:jj + TC],
                                                                cw_sb[:, j, jj:jj + 1], cvr[:, :],
                                                                ALU.mult, ALU.add))
                    bk.vector(conv)

                    def csil(s, j=j):
                        ch.wait(s)
                        ch.push(s.activation(xc_fm[:, j, :], cvr[:, :], AF.Silu,
                                             bias=cb_sb[:, j:j + 1]))
                    bk.scalar(csil)

                def bcmm(t):
                    ch.wait(t)
                    for k in range(16):
                        inst = t.matmul(ps_bc[0:32, :], bcw_sb[:, k, :], xc_fm[:, k, :],
                                        start=(k == 0), stop=(k == 15))
                    ch.push(inst)
                bk.tensor(bcmm)

                def bcev(v):
                    ch.wait(v)
                    ch.push(v.tensor_scalar(bc_sb[:, :], ps_bc[0:32, :], bcb_sb[:, :], None, ALU.add))
                bk.vector(bcev)

                def repmm(t):
                    ch.wait(t)
                    ch.push(t.matmul(ps_bc[:, :], sbc_sb[:, 0, :], bc_sb[:, :], start=True, stop=True))
                bk.tensor(repmm)

                def repev(v):
                    ch.wait(v)
                    ch.push(v.tensor_copy(brep[:, :], ps_bc[:, :]))
                bk.vector(repev)

                def repmm2(t):
                    ch.wait(t)
                    ch.push(t.matmul(ps_bc[:, :], sbc_sb[:, 1, :], bc_sb[:, :], start=True, stop=True))
                bk.tensor(repmm2)

                def repev2(v):
                    ch.wait(v)
                    ch.push(v.tensor_copy(crep[:, :], ps_bc[:, :]))
                bk.vector(repev2)

                dtP0, dtA0, dtD0 = pe_s.n, ac_s.n, ev_s.n
                a_ln = {m: dtA0 + 3 * m + 3 for m in range(16)}
                d_max = {m: dtD0 + 2 * m + 1 for m in range(16)}
                d_add = {m: dtD0 + 2 * m + 2 for m in range(16)}
                p_dt = {m: dtP0 + m + 1 for m in range(16)}

                def dtmm(t):
                    ch.wait(t)
                    for m in range(16):
                        if m >= 2:
                            ac_s.wait(t, a_ln[m - 2])
                            ev_s.wait(t, d_max[m - 2])
                        for k in range(16):
                            inst = t.matmul(ps_mm[m % 2], dtw_sb[:, k, m * 128:(m + 1) * 128],
                                            xc_fm[:, k, :], start=(k == 0), stop=(k == 15),
                                            skip_group_check=True)
                        tk = pe_s.push(inst)
                        assert tk == p_dt[m]
                if not SKIP_DT: bk.tensor(dtmm)

                def dtsp1(s):
                    for m in range(16):
                        pe_s.wait(s, p_dt[m])
                        if m >= 2:
                            ev_s.wait(s, d_add[m - 2])
                        ac_s.push(s.activation(sp_a2[m % 2][:, :], ps_mm[m % 2], AF.Abs,
                                               bias=dtb_sb[:, m:m + 1]))
                        ac_s.wait(s)
                        ac_s.push(s.activation(sp_a2[m % 2][:, :], sp_a2[m % 2][:, :], AF.Exp, scale=-1.0))
                        ac_s.wait(s)
                        tk = ac_s.push(s.activation(sp_a2[m % 2][:, :], sp_a2[m % 2][:, :], AF.Ln, bias=1.0))
                        assert tk == a_ln[m]
                if not SKIP_DT: bk.scalar(dtsp1)

                def dtsp2(v):
                    for m in range(16):
                        pe_s.wait(v, p_dt[m])
                        ac_s.wait(v, dtA0 + 3 * m + 1)
                        tk = ev_s.push(v.tensor_scalar(sp_m2[m % 2][:, :], ps_mm[m % 2],
                                                       dtb_sb[:, m:m + 1], 0.0, ALU.add, ALU.max))
                        assert tk == d_max[m]
                        ac_s.wait(v, a_ln[m])
                        ev_s.wait(v, d_max[m])
                        tk = ev_s.push(v.tensor_tensor(dt_fm[:, m, :], sp_m2[m % 2][:, :],
                                                       sp_a2[m % 2][:, :], ALU.add))
                        assert tk == d_add[m]
                if not SKIP_DT: bk.vector(dtsp2)

                def dtjoin(v):
                    if not SKIP_DT:
                        ev_s.wait(v)
                    ch.push(v.tensor_copy(lnm[:, :], lnm[:, :]))
                bk.vector(dtjoin)

                NP = 128
                P0s, A0s, D0s = pe_s.n, ac_s.n, ev_s.n
                D0ch = dch.n
                p_dtr, p_dxr, p_y = {}, {}, {}
                cnt = P0s
                for p in range(NP):
                    if p >= 1:
                        cnt += 1; p_y[p - 1] = cnt
                    cnt += 1; p_dtr[p] = cnt
                    cnt += 1; p_dxr[p] = cnt
                cnt += 1; p_y[NP - 1] = cnt
                a_dA, a_sg = {}, {}
                cnt = A0s
                for p in range(NP):
                    cnt += 1; a_dA[p] = cnt
                    if p % 8 == 7:
                        cnt += 1; a_sg[p // 8] = cnt
                d_dtx, d_u, d_scan, d_hc, d_yg = {}, {}, {}, {}, {}
                cnt = D0s
                for p in range(NP):
                    J = p // 8
                    if p % 8 == 0:
                        cnt += 1; d_dtx[J] = cnt
                    cnt += 1; d_u[p] = cnt
                    cnt += 3
                    cnt += 1; d_scan[p] = cnt
                    cnt += 1
                    cnt += 1; d_hc[p] = cnt
                    if p % 8 == 7:
                        cnt += 1
                        cnt += 1; d_yg[J] = cnt
                S0ch = selch.n
                w_sela = {}
                cnt = S0ch
                for J in range(16):
                    cnt += 16; w_sela[J] = cnt
                    if J == 8:
                        cnt += 16; w_gt1 = cnt

                def sc_gp(g, b=b):
                    for J in range(16):
                        if J >= 2:
                            pe_s.wait(g, p_dtr[8 * (J - 2) + 7])
                        selch.push(g.dma_start(sela_sb[J % 2][:, :, :], sela_d[J]), 16)
                        selch.wait(g)
                        if J == 8:
                            ac_s.wait(g, a_sg[7])
                            selch.push(g.dma_start(gtc_fm[:, :, :], gt_sc[b, 8:16, :, t0:t0 + TC]), 16)
                            selch.wait(g)
                bk.gpsimd(sc_gp)

                def sc_pe(t):
                    ch.wait(t)
                    for p in range(NP):
                        J = p // 8
                        j0, j1 = (2 * p) % 16, (2 * p) % 16 + 1
                        if p >= 1:
                            py = p - 1
                            Jy = py // 8
                            jy0, jy1 = (2 * py) % 16, (2 * py) % 16 + 1
                            if py % 8 == 0 and Jy >= 2:
                                ev_s.wait(t, d_yg[Jy - 2])
                            ev_s.wait(t, d_hc[py])
                            t.matmul(ps_y[Jy % 2], red_sb[:, jy0, :], hc_sb[py % 2][:, 0:TC],
                                     start=(jy0 == 0), stop=False, skip_group_check=True)
                            tk = pe_s.push(t.matmul(ps_y[Jy % 2], red_sb[:, jy1, :],
                                                    hc_sb[py % 2][:, TC:2 * TC],
                                                    start=False, stop=(jy1 == 15),
                                                    skip_group_check=True))
                            assert tk == p_y[py]
                        if p % 8 == 0:
                            selch.wait(t, w_sela[J])
                        if p >= 1:
                            ac_s.wait(t, a_dA[p - 1])
                        t.matmul(ps_dtr[:, 0:TC], sela_sb[J % 2][:, j0, :], dt_fm[:, J, :],
                                 start=True, stop=True, skip_group_check=True)
                        tk = pe_s.push(t.matmul(ps_dtr[:, TC:2 * TC], sela_sb[J % 2][:, j1, :],
                                                dt_fm[:, J, :], start=True, stop=True,
                                                skip_group_check=True))
                        assert tk == p_dtr[p]
                        if p % 8 == 0:
                            ev_s.wait(t, d_dtx[J])
                        if p >= 1:
                            ev_s.wait(t, d_u[p - 1])
                        t.matmul(ps_dxr[:, 0:TC], sel_sb[:, j0, :], dtxJ2[J % 2][:, :],
                                 start=True, stop=True, skip_group_check=True)
                        tk = pe_s.push(t.matmul(ps_dxr[:, TC:2 * TC], sel_sb[:, j1, :],
                                                dtxJ2[J % 2][:, :], start=True, stop=True,
                                                skip_group_check=True))
                        assert tk == p_dxr[p]
                    py = NP - 1
                    ev_s.wait(t, d_hc[py])
                    t.matmul(ps_y[(py // 8) % 2], red_sb[:, 14, :], hc_sb[py % 2][:, 0:TC],
                             start=False, stop=False, skip_group_check=True)
                    tk = pe_s.push(t.matmul(ps_y[(py // 8) % 2], red_sb[:, 15, :],
                                            hc_sb[py % 2][:, TC:2 * TC],
                                            start=False, stop=True, skip_group_check=True))
                    assert tk == p_y[py]
                bk.tensor(sc_pe)

                def sc_ac(s, b=b):
                    for p in range(NP):
                        J = p // 8
                        pe_s.wait(s, p_dtr[p])
                        if p >= 1:
                            ev_s.wait(s, d_scan[p - 1])
                        tk = ac_s.push(s.activation(dA_sb[:, :], ps_dtr[:, :], AF.Exp))
                        assert tk == a_dA[p]
                        if p % 8 == 7:
                            if J >= 2:
                                ev_s.wait(s, d_yg[J - 2])
                            if J >= 8:
                                selch.wait(s, w_gt1)
                            else:
                                dch.wait(s, D0ch)
                            tk = ac_s.push(s.activation(sg_sb[J % 2][:, :], gtc_fm[:, J % 8, :],
                                                        AF.Silu))
                            assert tk == a_sg[J]
                bk.scalar(sc_ac)

                def sc_dv(v):
                    u2 = u_sb[:, :].rearrange("q (r t) -> q r t", r=2)
                    h2 = h_sb[:, :].rearrange("q (r t) -> q r t", r=2)
                    dxr2 = ps_dxr.rearrange("q (r t) -> q r t", r=2)
                    for p in range(NP):
                        J = p // 8
                        g0 = 2 * p
                        if p % 8 == 0:
                            if J >= 2:
                                pe_s.wait(v, p_dxr[8 * (J - 2) + 7])
                            tk = ev_s.push(v.tensor_tensor(dtxJ2[J % 2][:, :], dt_fm[:, J, :],
                                                           xc_fm[:, J, :], ALU.mult))
                            assert tk == d_dtx[J]
                        pe_s.wait(v, p_dxr[p])
                        tk = ev_s.push(v.tensor_tensor(
                            u2, dxr2, brep[:, :].unsqueeze(1).broadcast_to([128, 2, TC]),
                            ALU.mult))
                        assert tk == d_u[p]
                        ac_s.wait(v, a_dA[p])
                        ev_s.push(v.tensor_tensor(inj_sb[:, :], dA_sb[:, TC:TC + 1],
                                                  hcar[:, g0 + 1:g0 + 2], ALU.mult))
                        ev_s.wait(v)
                        ev_s.push(v.tensor_tensor(u_sb[:, TC:TC + 1], u_sb[:, TC:TC + 1],
                                                  inj_sb[:, :], ALU.add))
                        ev_s.push(v.tensor_scalar(dA_sb[:, TC:TC + 1], dA_sb[:, TC:TC + 1],
                                                  0.0, None, ALU.mult))
                        ev_s.wait(v)
                        tk = ev_s.push(v.tensor_tensor_scan(h_sb[:, :], dA_sb[:, :], u_sb[:, :],
                                                            hcar[:, g0:g0 + 1], ALU.mult, ALU.add))
                        assert tk == d_scan[p]
                        ev_s.wait(v)
                        ev_s.push(v.tensor_copy(hcar[:, g0:g0 + 2], h2[:, :, TC - 1]))
                        if p >= 2:
                            pe_s.wait(v, p_y[p - 2])
                        tk = ev_s.push(v.tensor_tensor(
                            hc_sb[p % 2][:, :].rearrange("q (r t) -> q r t", r=2), h2,
                            crep[:, :].unsqueeze(1).broadcast_to([128, 2, TC]), ALU.mult))
                        assert tk == d_hc[p]
                        if p % 8 == 7:
                            pe_s.wait(v, p_y[p])
                            ac_s.wait(v, a_sg[J])
                            ev_s.push(v.scalar_tensor_tensor(t1_sb[:, :], xc_fm[:, J, :],
                                                             dp_sb[:, J:J + 1], ps_y[J % 2],
                                                             ALU.mult, ALU.add))
                            ev_s.wait(v)
                            tk = ev_s.push(v.tensor_tensor(yg_fm[:, J, :], t1_sb[:, :],
                                                           sg_sb[J % 2][:, :], ALU.mult))
                            assert tk == d_yg[J]
                bk.vector(sc_dv)

                def scjoin(v):
                    if not SKIP_SCAN:
                        ev_s.wait(v)
                    ch.push(v.tensor_copy(lnm[:, :], lnm[:, :]))
                bk.vector(scjoin)

                oP0, oE0 = pe_s.n, ev_s.n

                def omm(t, oP0=oP0, oE0=oE0):
                    ch.wait(t)
                    for m in range(8):
                        if m >= 2:
                            ev_s.wait(t, oE0 + (m - 2) + 1)
                        for k in range(16):
                            inst = t.matmul(ps_mm[m % 2], ow_sb[:, k, m * 128:(m + 1) * 128],
                                            yg_fm[:, k, :], start=(k == 0), stop=(k == 15),
                                            skip_group_check=True)
                        pe_s.push(inst)
                bk.tensor(omm)

                def oev(v, oP0=oP0, oE0=oE0):
                    for m in range(8):
                        pe_s.wait(v, oP0 + m + 1)
                        ev_s.push(v.tensor_scalar(of_fm[:, m, :], ps_mm[m % 2], ob_sb[:, m:m + 1],
                                                  None, ALU.add))
                bk.vector(oev)

                def ojoin(v):
                    ev_s.wait(v)
                    ch.push(v.tensor_copy(lnm[:, :], lnm[:, :]))
                bk.vector(ojoin)

                for tt in range(4):
                    def dmx(sy, b=b, t0=t0, tt=tt):
                        ch.wait(sy)
                        outs_h.wait(sy)
                        hch.push(sy.dma_start(xtile[:, :], x_in[b, t0 + tt * 128:t0 + (tt + 1) * 128, :]), 16)
                    bk.sync(dmx)

                    pe0, ev0 = pe_s.n, ev_s.n

                    def etr(t, tt=tt, pe0=pe0, ev0=ev0):
                        ch.wait(t)
                        for m in range(8):
                            if m >= 2:
                                ev_s.wait(t, ev0 + (m - 2) + 1)
                            pe_s.push(t.transpose(ps_tr[m % 2][:, 0:128],
                                                  of_fm[:, m, tt * 128:(tt + 1) * 128], id_bf[:, :]))
                    bk.tensor(etr)

                    def eev(v, tt=tt, pe0=pe0, ev0=ev0):
                        hch.wait(v)
                        for m in range(8):
                            pe_s.wait(v, pe0 + 1 + m)
                            ev_s.push(v.tensor_tensor(xtile[:, m * 128:(m + 1) * 128],
                                                      ps_tr[m % 2][:, 0:128],
                                                      xtile[:, m * 128:(m + 1) * 128], ALU.add))
                    bk.vector(eev)

                    def ejoin(v):
                        ev_s.wait(v)
                        ch.push(v.tensor_copy(lnm[:, :], lnm[:, :]))
                    bk.vector(ejoin)

                    def dmo(sy, b=b, t0=t0, tt=tt):
                        ch.wait(sy)
                        outs_h.push(sy.dma_start(x_out[b, t0 + tt * 128:t0 + (tt + 1) * 128, :],
                                                 xtile[:, :]), 16)
                    bk.sync(dmo)

                def fin(sy):
                    outs_h.wait(sy)
                bk.sync(fin)

    return nc


def _host_mamba(inputs):
    ident = np.eye(128, dtype=np.float32)
    A = np.asarray(inputs["L0_attn_A"])
    SELA = np.zeros((16, 128, 16, 128), np.float32)
    pidx = np.arange(128)
    for J in range(16):
        for j in range(16):
            r = 8 * j + pidx // 16
            SELA[J, r, j, pidx] = A[8 * (16 * J + j) + pidx // 16, pidx % 16]
    SEL = np.zeros((128, 16, 128), np.float32)
    for r in range(128):
        for jj in range(16):
            for p in range(128):
                if r == 8 * jj + p // 16:
                    SEL[r, jj, p] = 1.0
    RED = np.ascontiguousarray(SEL.transpose(2, 1, 0))
    selbc = np.zeros((32, 2, 128), np.float32)
    for p in range(128):
        selbc[p % 16, 0, p] = 1.0
        selbc[16 + p % 16, 1, p] = 1.0
    conv_w = np.asarray(inputs["L0_attn_conv_w"]).reshape(4, DI)
    base = {
        "in_w": inputs["L0_attn_in_w"],
        "in_b": np.ascontiguousarray(np.asarray(inputs["L0_attn_in_b"]).reshape(32, 128).T),
        "conv_w": np.ascontiguousarray(conv_w.reshape(4, 16, 128).transpose(2, 1, 0)),
        "conv_b": np.ascontiguousarray(np.asarray(inputs["L0_attn_conv_b"]).reshape(16, 128).T),
        "SELA": SELA,
        "Dp_col": np.ascontiguousarray(np.asarray(inputs["L0_attn_Dp"]).reshape(16, 128).T),
        "dt_w": inputs["L0_attn_dt_w"],
        "dt_b": np.ascontiguousarray(np.asarray(inputs["L0_attn_dt_b"]).reshape(16, 128).T),
        "bc_w": np.ascontiguousarray(np.concatenate([np.asarray(inputs["L0_attn_B_w"]),
                                                     np.asarray(inputs["L0_attn_C_w"])], axis=1)),
        "bc_b": np.ascontiguousarray(np.concatenate([np.asarray(inputs["L0_attn_B_b"]),
                                                     np.asarray(inputs["L0_attn_C_b"])]).reshape(32, 1)),
        "out_w": inputs["L0_attn_out_w"],
        "out_b": np.ascontiguousarray(np.asarray(inputs["L0_attn_out_b"]).reshape(8, 128).T),
        "SEL": SEL,
        "RED": RED,
        "selbc": selbc,
        "ident": ident,
    }
    return base


def run_mamba(x, inputs):
    nc = _get("mamba", build_mamba)
    base = _host_mamba(inputs)
    in_maps = []
    for cc in range(NCORE):
        m = dict(base)
        m["x_in"] = np.ascontiguousarray(x[2 * cc:2 * cc + 2])
        in_maps.append(m)
    res = run_bass_kernel_spmd(nc, in_maps, core_ids=list(range(NCORE)))
    return np.concatenate([r["x_out"] for r in res.results], axis=0)




def build_ttt(fc=None):
    fc = fc or FC()
    nc = fc.nc if fc.nc is not None else bass.Bass("TRN2", target_bir_lowering=False, debug=False)
    CK = 128
    NCK = T // CK
    LR = 0.01
    ALPHA = 2.0 / DM
    LAM = 2.0 * LR / DM

    x_in = fc.dram(nc, "x_in", [BPC, T, DM], F32)
    thk_d = fc.dram(nc, "thK_T", [DM, DM], F32)
    thv_d = fc.dram(nc, "thV_T", [DM, DM], F32)
    thq_d = fc.dram(nc, "thQ_T", [DM, DM], F32)
    w0t_d = fc.dram(nc, "W0_T", [DM, DM], F32)
    b0_d = fc.dram(nc, "b0", [1, DM], F32)
    mst_d = fc.dram(nc, "Mstrict", [128, 128], F32)
    min_d = fc.dram(nc, "Mincl", [128, 128], F32)
    id_d = fc.dram(nc, "ident", [128, 128], F32)
    x_out = fc.dram(nc, "x_out", [BPC, T, DM], F32, kind="ExternalOutput")

    thk = fc.sb(nc, "t_thk", [128, 8, DM], BF16)
    thq = fc.sb(nc, "t_thq", [128, 8, DM], BF16)
    thvn = fc.sb(nc, "t_thvn", [128, 8, DM], BF16)
    wt = [fc.sb(nc, f"t_wt{b}", [128, 8, DM], BF16) for b in range(BPC)]
    bst = fc.sb(nc, "t_bst", [1, DM], F32)
    bst_bf = fc.sb(nc, "t_bstbf", [1, DM], BF16)
    mstr = fc.sb(nc, "t_mstr", [128, 128], BF16)
    mincl = fc.sb(nc, "t_mincl", [128, 128], BF16)
    id_bf = fc.sb(nc, "t_id", [128, 128], BF16)
    ones_r = fc.sb(nc, "t_ones_r", [1, 128], BF16)
    ones_c = fc.sb(nc, "t_ones_c", [128, 1], BF16)
    eps_col = fc.sb(nc, "t_eps", [128, 1], F32)
    kfm = fc.sb(nc, "t_kfm", [128, 8, CK], BF16)
    qfm = fc.sb(nc, "t_qfm", [128, 8, CK], BF16)
    ktm = fc.sb(nc, "t_ktm", [128, 8, 128], BF16)
    base_sb = fc.sb(nc, "t_base", [128, DM], BF16)
    gs_tm = fc.sb(nc, "t_gs", [128, DM], BF16)
    nmat = fc.sb(nc, "t_n", [128, 128], BF16)
    pmat = fc.sb(nc, "t_p", [128, 128], BF16)
    ptmat = fc.sb(nc, "t_pt", [128, 128], BF16)
    tmat = fc.sb(nc, "t_t", [128, 128], BF16)
    umat = fc.sb(nc, "t_u", [128, 128], BF16)
    xn_fm = fc.sb(nc, "t_xnfm", [128, 8, CK], BF16)
    xtile = fc.sb(nc, "t_xtile", [128, DM], F32)
    xn_tm = fc.sb(nc, "t_xntm", [128, DM], BF16)
    lnscr = fc.sb(nc, "t_lnscr", [128, DM], BF16)
    lnm = fc.sb(nc, "t_lnm", [128, 1], F32)
    lns = fc.sb(nc, "t_lns", [128, 1], F32)
    lnr = fc.sb(nc, "t_lnr", [128, 1], F32)
    o_tm = fc.sb(nc, "t_otm", [128, DM], F32)

    ps_big = fc.ps(nc, "t_psbig", [128, DM], F32)
    ps_big2 = fc.ps(nc, "t_psbig2", [128, DM], F32)
    ps_sm = fc.ps(nc, "t_pssm", [128, 512], F32)
    ps_kq = [ps_sm[:, 0:128], ps_big2[:, 0:128]]
    ps_dw = [ps_big[:, :], ps_big2[:, :]]
    ps_tr = [fc.ps(nc, f"t_pstr{i}", [128, 1024], BF16) for i in range(2)]

    ch = Chain(nc, "t_ch")
    pe_s = Chain(nc, "t_pe")
    ev_s = Chain(nc, "t_ev")
    ac_s = Chain(nc, "t_ac")
    dch = Chain(nc, "t_dch")
    hch = Chain(nc, "t_hch")
    outs = Chain(nc, "t_outs")

    with nc.Block() as bk:
        def gsec(g):
            dch.push(g.dma_start(thk[:, :, :], thk_d.rearrange("(k p) m -> p k m", p=128)), 16)
            dch.push(g.dma_start(thq[:, :, :], thq_d.rearrange("(k p) m -> p k m", p=128)), 16)
            dch.push(g.dma_start(thvn[:, :, :], thv_d.rearrange("(k p) m -> p k m", p=128)), 16)
            for b in range(BPC):
                dch.push(g.dma_start(wt[b][:, :, :], w0t_d.rearrange("(k p) m -> p k m", p=128)), 16)
            dch.push(g.dma_start(mstr[:, :], mst_d), 16)
            dch.push(g.dma_start(mincl[:, :], min_d), 16)
            dch.push(g.dma_start(id_bf[:, :], id_d), 16)
            g.memset(ones_r[:, :], 1.0)
            g.memset(ones_c[:, :], 1.0)
            g.memset(eps_col[:, :], EPS)
        bk.gpsimd(gsec)

        def vneg(s):
            dch.wait(s)
            for j in range(8):
                ch.push(s.activation(thvn[:, j, :], thvn[:, j, :], AF.Identity, scale=-1.0))
        bk.scalar(vneg)

    for b in range(BPC):
        with nc.Block() as bk:
            def binit(g, b=b):
                dch.push(g.dma_start(bst[:, :], b0_d), 16)
            bk.gpsimd(binit)

            def binit2(v):
                dch.wait(v)
                ch.push(v.tensor_copy(bst_bf[:, :], bst[:, :]))
            bk.vector(binit2)

        for c in range(NCK):
            t0 = c * CK
            with nc.Block() as bk:
                def dmain(sy, b=b, t0=t0):
                    ch.wait(sy)
                    hch.push(sy.dma_start(xtile[:, :], x_in[b, t0:t0 + CK, :]), 16)
                bk.sync(dmain)

                def vstart(v):
                    dch.wait(v)
                    hch.wait(v)
                    ch.push(v.tensor_copy(lnm[:, :], eps_col[:, :]))
                bk.vector(vstart)

                _ln_emit(nc, bk, ch, xtile, xn_tm, lnscr, lnm, lns, lnr, eps_col)

                pe0, ev0 = pe_s.n, ev_s.n

                def trans(t, pe0=pe0, ev0=ev0):
                    ch.wait(t)
                    for j in range(8):
                        if j >= 2:
                            ev_s.wait(t, ev0 + (j - 2) + 1)
                        pe_s.push(t.transpose(ps_tr[j % 2][:, 0:128],
                                              xn_tm[:, j * 128:(j + 1) * 128], id_bf[:, :]))
                bk.tensor(trans)

                def evac(v, pe0=pe0, ev0=ev0):
                    for j in range(8):
                        pe_s.wait(v, pe0 + 1 + j)
                        ev_s.push(v.tensor_copy(xn_fm[:, j, :], ps_tr[j % 2][:, 0:128]))
                bk.vector(evac)

                def joinv(v):
                    ev_s.wait(v)
                    ch.push(v.tensor_copy(lnm[:, :], lnm[:, :]))
                bk.vector(joinv)

                kqP0, kqE0 = pe_s.n, ev_s.n

                def kqmm2(t, kqP0=kqP0, kqE0=kqE0):
                    ch.wait(t)
                    ii = 0
                    for dst, th in [(kfm, thk), (qfm, thq)]:
                        for i in range(8):
                            if ii >= 2:
                                ev_s.wait(t, kqE0 + ii - 1)
                            for k in range(8):
                                inst = t.matmul(ps_kq[ii % 2], th[:, k, i * 128:(i + 1) * 128],
                                                xn_fm[:, k, :], start=(k == 0), stop=(k == 7),
                                                skip_group_check=True)
                            pe_s.push(inst)
                            ii += 1
                bk.tensor(kqmm2)

                def kqev(v, kqP0=kqP0, kqE0=kqE0):
                    ii = 0
                    for dst, th in [(kfm, thk), (qfm, thq)]:
                        for i in range(8):
                            pe_s.wait(v, kqP0 + ii + 1)
                            ev_s.push(v.tensor_copy(dst[:, i, :], ps_kq[ii % 2]))
                            ii += 1
                bk.vector(kqev)

                def kqjoin(v):
                    ev_s.wait(v)
                    ch.push(v.tensor_copy(lnm[:, :], lnm[:, :]))
                bk.vector(kqjoin)

                def basemm(t, b=b):
                    ch.wait(t)
                    for k in range(8):
                        for hh in range(2):
                            t.matmul(ps_big[:, hh * 512:(hh + 1) * 512], kfm[:, k, :],
                                     wt[b][:, k, hh * 512:(hh + 1) * 512],
                                     start=(k == 0), stop=False)
                    for hh in range(2):
                        t.matmul(ps_big[:, hh * 512:(hh + 1) * 512], ones_r[:, :],
                                 bst_bf[0:1, hh * 512:(hh + 1) * 512], start=False, stop=False)
                    for k in range(8):
                        for hh in range(2):
                            inst = t.matmul(ps_big[:, hh * 512:(hh + 1) * 512], xn_fm[:, k, :],
                                            thvn[:, k, hh * 512:(hh + 1) * 512],
                                            start=False, stop=(k == 7))
                    ch.push(inst)
                bk.tensor(basemm)

                def baseev(s):
                    ch.wait(s)
                    ch.push(s.activation(base_sb[:, :], ps_big[:, :], AF.Identity, scale=ALPHA))
                bk.scalar(baseev)

                def kkmm(t):
                    ch.wait(t)
                    for k in range(8):
                        t.matmul(ps_sm[:, 0:128], kfm[:, k, :], kfm[:, k, :],
                                 start=(k == 0), stop=False)
                    inst = t.matmul(ps_sm[:, 0:128], ones_r[:, :],
                                    ones_r[:, :], start=False, stop=True)
                    ch.push(inst)
                bk.tensor(kkmm)

                def nev(v):
                    ch.wait(v)
                    ch.push(v.scalar_tensor_tensor(nmat[:, :], ps_sm[:, 0:128], -LAM,
                                                   mstr[:, :], ALU.mult, ALU.mult))
                bk.vector(nev)

                def tinit(v):
                    ch.wait(v)
                    t1 = ch.push(v.tensor_tensor(tmat[:, :], nmat[:, :], id_bf[:, :], ALU.add))
                    ch.wait(v, t1)
                    ch.push(v.tensor_copy(pmat[:, :], nmat[:, :]))
                bk.vector(tinit)

                for it in range(6):
                    def sq1(t, it=it):
                        ch.wait(t)
                        ch.push(t.transpose(ps_tr[0][:, 0:128], pmat[:, :], id_bf[:, :]))
                    bk.tensor(sq1)

                    def sq2(v, it=it):
                        ch.wait(v)
                        ch.push(v.tensor_copy(ptmat[:, :], ps_tr[0][:, 0:128]))
                    bk.vector(sq2)

                    def sq3(t, it=it):
                        ch.wait(t)
                        ch.push(t.matmul(ps_sm[:, 0:128], ptmat[:, :], pmat[:, :],
                                         start=True, stop=True))
                        ch.push(t.matmul(ps_sm[:, 128:256], ptmat[:, :], tmat[:, :],
                                         start=True, stop=True))
                    bk.tensor(sq3)

                    def sq4(v, it=it):
                        ch.wait(v)
                        t1 = ch.push(v.tensor_copy(pmat[:, :], ps_sm[:, 0:128]))
                        ch.wait(v, t1)
                        ch.push(v.tensor_tensor(tmat[:, :], tmat[:, :], ps_sm[:, 128:256], ALU.add))
                    bk.vector(sq4)

                def kqmm(t):
                    ch.wait(t)
                    for k in range(8):
                        t.matmul(ps_sm[:, 0:128], kfm[:, k, :], qfm[:, k, :],
                                 start=(k == 0), stop=False)
                    inst = t.matmul(ps_sm[:, 0:128], ones_r[:, :], ones_r[:, :],
                                    start=False, stop=True)
                    ch.push(inst)
                bk.tensor(kqmm)

                def uev(v):
                    ch.wait(v)
                    ch.push(v.tensor_tensor(umat[:, :], ps_sm[:, 0:128], mincl[:, :], ALU.mult))
                bk.vector(uev)

                def gmm(t):
                    ch.wait(t)
                    for hh in range(2):
                        inst = t.matmul(ps_big[:, hh * 512:(hh + 1) * 512], tmat[:, :],
                                        base_sb[:, hh * 512:(hh + 1) * 512], start=True, stop=True)
                    ch.push(inst)
                bk.tensor(gmm)

                def gev(s):
                    ch.wait(s)
                    ch.push(s.activation(gs_tm[:, :], ps_big[:, :], AF.Identity, scale=-LR))
                bk.scalar(gev)

                def omm(t, b=b):
                    ch.wait(t)
                    for k in range(8):
                        for hh in range(2):
                            t.matmul(ps_big[:, hh * 512:(hh + 1) * 512], qfm[:, k, :],
                                     wt[b][:, k, hh * 512:(hh + 1) * 512],
                                     start=(k == 0), stop=False)
                    for hh in range(2):
                        t.matmul(ps_big[:, hh * 512:(hh + 1) * 512], ones_r[:, :],
                                 bst_bf[0:1, hh * 512:(hh + 1) * 512], start=False, stop=False)
                    for hh in range(2):
                        inst = t.matmul(ps_big[:, hh * 512:(hh + 1) * 512], umat[:, :],
                                        gs_tm[:, hh * 512:(hh + 1) * 512], start=False, stop=True)
                    ch.push(inst)
                bk.tensor(omm)

                def dmx(sy, b=b, t0=t0):
                    ch.wait(sy)
                    outs.wait(sy)
                    hch.push(sy.dma_start(xtile[:, :], x_in[b, t0:t0 + CK, :]), 16)
                bk.sync(dmx)

                def oev(v):
                    ch.wait(v)
                    hch.wait(v)
                    ch.push(v.tensor_tensor(o_tm[:, :], ps_big[:, :], xtile[:, :], ALU.add))
                bk.vector(oev)

                def dmo(e, b=b, t0=t0):
                    ch.wait(e)
                    outs.push(e.dma_start(x_out[b, t0:t0 + CK, :], o_tm[:, :]), 16)
                if fc.out_swdge:
                    bk.gpsimd(dmo)
                else:
                    bk.sync(dmo)

                pe0, ev0 = pe_s.n, ev_s.n

                def ktr(t, pe0=pe0, ev0=ev0):
                    ch.wait(t)
                    for i in range(8):
                        if i >= 2:
                            ev_s.wait(t, ev0 + (i - 2) + 1)
                        pe_s.push(t.transpose(ps_tr[i % 2][:, 0:128], kfm[:, i, :], id_bf[:, :]))
                bk.tensor(ktr)

                def kev(v, pe0=pe0, ev0=ev0):
                    for i in range(8):
                        pe_s.wait(v, pe0 + 1 + i)
                        ev_s.push(v.tensor_copy(ktm[:, i, :], ps_tr[i % 2][:, 0:128]))
                bk.vector(kev)

                def kjoin(v):
                    ev_s.wait(v)
                    ch.push(v.tensor_copy(lnm[:, :], lnm[:, :]))
                bk.vector(kjoin)

                dwP0, dwE0 = pe_s.n, ev_s.n

                def dwmm(t, dwP0=dwP0, dwE0=dwE0):
                    ch.wait(t)
                    for i in range(8):
                        if i >= 2:
                            ev_s.wait(t, dwE0 + i - 1)
                        for hh in range(2):
                            inst = t.matmul(ps_dw[i % 2][:, hh * 512:(hh + 1) * 512], ktm[:, i, :],
                                            gs_tm[:, hh * 512:(hh + 1) * 512], start=True, stop=True,
                                            skip_group_check=True)
                        pe_s.push(inst)
                bk.tensor(dwmm)

                def dwup(v, b=b, dwP0=dwP0, dwE0=dwE0):
                    for i in range(8):
                        pe_s.wait(v, dwP0 + i + 1)
                        ev_s.push(v.tensor_tensor(wt[b][:, i, :], ps_dw[i % 2][:, :],
                                                  wt[b][:, i, :], ALU.add))
                bk.vector(dwup)

                def dwjoin(v):
                    ev_s.wait(v)
                    ch.push(v.tensor_copy(lnm[:, :], lnm[:, :]))
                bk.vector(dwjoin)

                for hh in range(2):
                    def bmm(t, hh=hh):
                        ch.wait(t)
                        ch.push(t.matmul(ps_sm[0:1, :], ones_c[:, :],
                                         gs_tm[:, hh * 512:(hh + 1) * 512], start=True, stop=True))
                    bk.tensor(bmm)

                    def bup(v, hh=hh):
                        ch.wait(v)
                        ch.push(v.tensor_tensor(bst[0:1, hh * 512:(hh + 1) * 512], ps_sm[0:1, :],
                                                bst[0:1, hh * 512:(hh + 1) * 512], ALU.add))
                    bk.vector(bup)

                def bbf(v):
                    ch.wait(v)
                    ch.push(v.tensor_copy(bst_bf[:, :], bst[:, :]))
                bk.vector(bbf)

                def fin(sy):
                    outs.wait(sy)
                bk.sync(fin)

    return nc


def _host_ttt(inputs):
    ident = np.eye(128, dtype=np.float32)
    ii, jj = np.meshgrid(np.arange(128), np.arange(128), indexing="ij")
    Mstrict = (ii < jj).astype(np.float32)
    Mincl = (ii <= jj).astype(np.float32)
    base = {
        "thK_T": np.ascontiguousarray(np.asarray(inputs["L1_attn_theta_K"]).T),
        "thV_T": np.ascontiguousarray(np.asarray(inputs["L1_attn_theta_V"]).T),
        "thQ_T": np.ascontiguousarray(np.asarray(inputs["L1_attn_theta_Q"]).T),
        "W0_T": np.ascontiguousarray(np.asarray(inputs["L1_attn_W0"]).T),
        "b0": np.ascontiguousarray(np.asarray(inputs["L1_attn_b0"]).reshape(1, DM)),
        "Mstrict": Mstrict,
        "Mincl": Mincl,
        "ident": ident,
    }
    return base


def run_ttt(x, inputs):
    nc = _get("ttt", build_ttt)
    base = _host_ttt(inputs)
    in_maps = []
    for cc in range(NCORE):
        m = dict(base)
        m["x_in"] = np.ascontiguousarray(x[2 * cc:2 * cc + 2])
        in_maps.append(m)
    res = run_bass_kernel_spmd(nc, in_maps, core_ids=list(range(NCORE)))
    return np.concatenate([r["x_out"] for r in res.results], axis=0)




def build_lstm(fc=None):
    fc = fc or FC()
    nc = fc.nc if fc.nc is not None else bass.Bass("TRN2", target_bir_lowering=False, debug=False)
    NB = 16
    TS = 256
    if fc.lstm_gather:
        agout_d = fc.dmap["agout"]
        tsel_d = fc.dram(nc, "tsel", [128, 32], mybir.dt.uint32)
    else:
        x_in = fc.dram(nc, "x_in", [NB, TS, DM], F32)
    aw_d = fc.dram(nc, "aw", [6, DM, DM], F32)
    ow2_d = fc.dram(nc, "ow2", [2, DM, DM], F32)
    gb_d = fc.dram(nc, "gb", [128, 3, 8], F32)
    bo_d = fc.dram(nc, "bo", [128, 8], F32)
    ic_d = fc.dram(nc, "in_c", [128, 8], F32)
    id_d = fc.dram(nc, "ident", [128, 128], F32)
    x_out = fc.dram(nc, "x_out", [NB, TS, DM], F32, kind="ExternalOutput")

    AW = fc.sb(nc, "l_AW", [128, 6, 8, DM], BF16)
    wv = [AW[:, i, :, :] for i in range(6)]
    xn3 = fc.sb(nc, "l_xn3", [128, 8, NB * TS], BF16)
    c_fm = fc.sb(nc, "l_c", [128, 8, TS], F32)
    c_bf = [fc.sb(nc, f"l_cbf{i}", [128, 8, TS], BF16) for i in range(2)]
    f_sb = [fc.sb(nc, f"l_f{i}", [128, TS], BF16) for i in range(2)]
    i_sb = [fc.sb(nc, f"l_i{i}", [128, TS], BF16) for i in range(2)]
    g_sb = [fc.sb(nc, f"l_g{i}", [128, TS], BF16) for i in range(2)]
    tt1 = fc.sb(nc, "l_tt1", [128, TS], F32)
    tt2 = fc.sb(nc, "l_tt2", [128, TS], F32)
    gb_sb = fc.sb(nc, "l_gb", [128, 3, 8], F32)
    bo_sb = fc.sb(nc, "l_bo", [128, 8], F32)
    ic_sb = fc.sb(nc, "l_ic", [128, 8], F32)
    onef = fc.sb(nc, "l_onef", [128, TS], F32)
    id_bf = fc.sb(nc, "l_id", [128, 128], BF16)
    eps_col = fc.sb(nc, "l_eps", [128, 1], F32)
    h_fm = fc.sb(nc, "l_hfm", [128, 8, TS], BF16)
    xtile = fc.sb(nc, "l_xtile", [128, DM], F32)
    xn_tm = fc.sb(nc, "l_xntm", [128, DM], BF16)
    lnscr = fc.sb(nc, "l_lnscr", [128, DM], BF16)
    lnm = fc.sb(nc, "l_lnm", [128, 1], F32)
    lns = fc.sb(nc, "l_lns", [128, 1], F32)
    lnr = fc.sb(nc, "l_lnr", [128, 1], F32)
    o_tm = fc.sb(nc, "l_otm", [128, DM], F32)
    if fc.lstm_gather:
        xg_bf = fc.sb(nc, "l_xgbf", [128, DM], BF16)
        tsel_sb = fc.sb(nc, "l_tsel", [128, 32], mybir.dt.uint32)

    ps_f = fc.ps(nc, "l_psf", [128, 512], F32)
    ps_i = fc.ps(nc, "l_psi", [128, 512], F32)
    ps_g = fc.ps(nc, "l_psg", [128, 512], F32)
    ps_f2 = fc.ps(nc, "l_psf2", [128, 512], F32)
    ps_i2 = fc.ps(nc, "l_psi2", [128, 512], F32)
    ps_g2 = fc.ps(nc, "l_psg2", [128, 512], F32)
    ps_fp = [ps_f[:, 0:TS], ps_f2[:, 0:TS]]
    ps_ip = [ps_i[:, 0:TS], ps_i2[:, 0:TS]]
    ps_gp2 = [ps_g[:, 0:TS], ps_g2[:, 0:TS]]
    ps_tr = [fc.ps(nc, f"l_pstr{i}", [128, 1024], BF16) for i in range(2)]

    ch = Chain(nc, "l_ch")
    pe_s = Chain(nc, "l_pe")
    ev_s = Chain(nc, "l_ev")
    ac_s = Chain(nc, "l_ac")
    dch = Chain(nc, "l_dch")
    hch = Chain(nc, "l_hch")
    wdma = Chain(nc, "l_wdma")
    outs = Chain(nc, "l_outs")

    with nc.Block() as bk:
        def gsec(g):
            wdma.push(g.dma_start(AW[:, :, :, :],
                                  aw_d.rearrange("w (k p) m -> p w k m", p=128)), 16)
            dch.push(g.dma_start(gb_sb[:, :, :], gb_d), 16)
            dch.push(g.dma_start(bo_sb[:, :], bo_d), 16)
            dch.push(g.dma_start(ic_sb[:, :], ic_d), 16)
            dch.push(g.dma_start(id_bf[:, :], id_d), 16)
            g.memset(eps_col[:, :], EPS)
            g.memset(onef[:, :], 1.0)
            if fc.lstm_gather:
                dch.push(g.dma_start(tsel_sb[:, :], tsel_d), 16)
        bk.gpsimd(gsec)

        def cinit(s):
            dch.wait(s)
            for m in range(8):
                ch.push(s.activation(c_fm[:, m, :], onef[:, :], AF.Identity,
                                     scale=ic_sb[:, m:m + 1]))
        bk.scalar(cinit)

        def cbf0(v):
            ch.wait(v)
            ch.push(v.tensor_copy(c_bf[0][:, :, :], c_fm[:, :, :]))
        bk.vector(cbf0)

    for b in range(16):
        for tt in range(2):
            with nc.Block() as bk:
                if fc.lstm_gather:
                    def dmain(g, b=b, tt=tt):
                        ch.wait(g)
                        dch.push(g.indirect_dma_start(
                            xg_bf[:, :], None, agout_d,
                            bass.IndirectOffsetOnAxis(ap=tsel_sb[:, b * 2 + tt:b * 2 + tt + 1],
                                                      axis=0)), 16)
                    bk.gpsimd(dmain)

                    def vstart(v):
                        dch.wait(v)
                        ch.push(v.tensor_copy(xtile[:, :], xg_bf[:, :]))
                    bk.vector(vstart)
                else:
                    def dmain(sy, b=b, tt=tt):
                        ch.wait(sy)
                        hch.push(sy.dma_start(xtile[:, :], x_in[b, tt * 128:(tt + 1) * 128, :]), 16)
                    bk.sync(dmain)

                    def vstart(v):
                        dch.wait(v)
                        hch.wait(v)
                        ch.push(v.tensor_copy(lnm[:, :], eps_col[:, :]))
                    bk.vector(vstart)

                _ln_emit(nc, bk, ch, xtile, xn_tm, lnscr, lnm, lns, lnr, eps_col)

                pe0, ev0 = pe_s.n, ev_s.n

                def trans(t, pe0=pe0, ev0=ev0):
                    ch.wait(t)
                    for j in range(8):
                        if j >= 2:
                            ev_s.wait(t, ev0 + (j - 2) + 1)
                        pe_s.push(t.transpose(ps_tr[j % 2][:, 0:128],
                                              xn_tm[:, j * 128:(j + 1) * 128], id_bf[:, :]))
                bk.tensor(trans)

                def evac(v, b=b, tt=tt, pe0=pe0, ev0=ev0):
                    col = b * TS + tt * 128
                    for j in range(8):
                        pe_s.wait(v, pe0 + 1 + j)
                        ev_s.push(v.tensor_copy(xn3[:, j, col:col + 128], ps_tr[j % 2][:, 0:128]))
                bk.vector(evac)

                def joinv(v):
                    ev_s.wait(v)
                    ch.push(v.tensor_copy(lnm[:, :], lnm[:, :]))
                bk.vector(joinv)

    with nc.Block() as bk:
        P0n = pe_s.n
        p_gate = {}
        cnt = P0n
        for b in range(16):
            for m in range(8):
                for gidx in range(3):
                    cnt += 1; p_gate[(b, m, gidx)] = cnt
        A0n = ac_s.n
        a_g = {}
        cnt = A0n
        for b in range(16):
            for m in range(8):
                for gidx in range(3):
                    cnt += 1; a_g[(b, m, gidx)] = cnt
        E0n = ev_s.n
        d_cbf = {}
        cnt = E0n
        for b in range(16):
            for m in range(8):
                cnt += 4; d_cbf[(b, m)] = cnt

        def pesec(t):
            ch.wait(t)
            wdma.wait(t)
            for b in range(16):
                col = b * TS
                for m in range(8):
                    mi = b * 8 + m
                    if mi >= 2:
                        pb, pm = divmod(mi - 2, 8)
                        ac_s.wait(t, a_g[(pb, pm, 2)])
                    for (pslist, wi_, ui_, gidx) in [(ps_fp, 0, 1, 0), (ps_ip, 2, 3, 1),
                                                     (ps_gp2, 4, 5, 2)]:
                        ps = pslist[mi % 2]
                        for k in range(8):
                            t.matmul(ps, wv[wi_][:, k, m * 128:(m + 1) * 128],
                                     xn3[:, k, col:col + TS], start=(k == 0), stop=False,
                                     skip_group_check=True)
                        if m == 0 and gidx == 0 and b >= 1:
                            ev_s.wait(t, d_cbf[(b - 1, 7)])
                        for k in range(8):
                            inst = t.matmul(ps, wv[ui_][:, k, m * 128:(m + 1) * 128],
                                            c_bf[b % 2][:, k, :], start=False, stop=(k == 7),
                                            skip_group_check=True)
                        pe_s.push(inst)
        bk.tensor(pesec)

        def acsec(s):
            for b in range(16):
                for m in range(8):
                    mi = b * 8 + m
                    if mi >= 2:
                        pb, pm = divmod(mi - 2, 8)
                        ev_s.wait(s, d_cbf[(pb, pm)])
                    pe_s.wait(s, p_gate[(b, m, 0)])
                    ac_s.push(s.activation(f_sb[mi % 2][:, :], ps_fp[mi % 2], AF.Sigmoid,
                                           bias=gb_sb[:, 0, m:m + 1]))
                    pe_s.wait(s, p_gate[(b, m, 1)])
                    ac_s.push(s.activation(i_sb[mi % 2][:, :], ps_ip[mi % 2], AF.Sigmoid,
                                           bias=gb_sb[:, 1, m:m + 1]))
                    pe_s.wait(s, p_gate[(b, m, 2)])
                    ac_s.push(s.activation(g_sb[mi % 2][:, :], ps_gp2[mi % 2], AF.Tanh,
                                           bias=gb_sb[:, 2, m:m + 1]))
        bk.scalar(acsec)

        def dvsec(v):
            for b in range(16):
                for m in range(8):
                    mi = b * 8 + m
                    ac_s.wait(v, a_g[(b, m, 2)])
                    ev_s.push(v.tensor_tensor(tt1[:, :], f_sb[mi % 2][:, :], c_fm[:, m, :], ALU.mult))
                    ev_s.push(v.tensor_tensor(tt2[:, :], i_sb[mi % 2][:, :], g_sb[mi % 2][:, :], ALU.mult))
                    ev_s.wait(v)
                    ev_s.push(v.tensor_tensor(c_fm[:, m, :], tt1[:, :], tt2[:, :], ALU.add))
                    ev_s.wait(v)
                    ev_s.push(v.tensor_copy(c_bf[(b + 1) % 2][:, m, :], c_fm[:, m, :]))
        bk.vector(dvsec)

        def join(v):
            ev_s.wait(v)
            ch.push(v.tensor_copy(lnm[:, :], lnm[:, :]))
        bk.vector(join)

    with nc.Block() as bk:
        def wload2(g):
            wdma.push(g.dma_start(AW[:, 0:2, :, :],
                                  ow2_d.rearrange("w (k p) m -> p w k m", p=128)), 16)
        bk.gpsimd(wload2)

    for b in range(16):
        col = b * TS
        with nc.Block() as bk:
            for m in range(8):
                def hmm(t, b=b, m=m, col=col):
                    wdma.wait(t)
                    ch.wait(t)
                    for k in range(8):
                        t.matmul(ps_f[:, 0:TS], wv[0][:, k, m * 128:(m + 1) * 128],
                                 xn3[:, k, col:col + TS], start=(k == 0), stop=False)
                    for k in range(8):
                        inst = t.matmul(ps_f[:, 0:TS], wv[1][:, k, m * 128:(m + 1) * 128],
                                        c_bf[0][:, k, :], start=False, stop=(k == 7))
                    ch.push(inst)
                bk.tensor(hmm)

                def hact(s, m=m):
                    ch.wait(s)
                    ch.push(s.activation(h_fm[:, m, :], ps_f[:, 0:TS], AF.Tanh,
                                         bias=bo_sb[:, m:m + 1]))
                bk.scalar(hact)

            for tt in range(2):
                if fc.lstm_gather:
                    def dmx(g, b=b, tt=tt):
                        ch.wait(g)
                        outs.wait(g)
                        dch.push(g.indirect_dma_start(
                            xg_bf[:, :], None, agout_d,
                            bass.IndirectOffsetOnAxis(ap=tsel_sb[:, b * 2 + tt:b * 2 + tt + 1],
                                                      axis=0)), 16)
                    bk.gpsimd(dmx)

                    def vcast(v):
                        dch.wait(v)
                        ch.push(v.tensor_copy(xtile[:, :], xg_bf[:, :]))
                    bk.vector(vcast)
                else:
                    def dmx(sy, b=b, tt=tt):
                        ch.wait(sy)
                        outs.wait(sy)
                        hch.push(sy.dma_start(xtile[:, :], x_in[b, tt * 128:(tt + 1) * 128, :]), 16)
                    bk.sync(dmx)

                pe0, ev0 = pe_s.n, ev_s.n

                def etr(t, tt=tt, pe0=pe0, ev0=ev0):
                    ch.wait(t)
                    for m in range(8):
                        if m >= 2:
                            ev_s.wait(t, ev0 + (m - 2) + 1)
                        pe_s.push(t.transpose(ps_tr[m % 2][:, 0:128],
                                              h_fm[:, m, tt * 128:(tt + 1) * 128], id_bf[:, :]))
                bk.tensor(etr)

                def eev(v, tt=tt, pe0=pe0, ev0=ev0):
                    if not fc.lstm_gather:
                        hch.wait(v)
                    else:
                        ch.wait(v)
                    for m in range(8):
                        pe_s.wait(v, pe0 + 1 + m)
                        ev_s.push(v.tensor_tensor(o_tm[:, m * 128:(m + 1) * 128],
                                                  ps_tr[m % 2][:, 0:128],
                                                  xtile[:, m * 128:(m + 1) * 128], ALU.add))
                bk.vector(eev)

                def ejoin(v):
                    ev_s.wait(v)
                    ch.push(v.tensor_copy(lnm[:, :], lnm[:, :]))
                bk.vector(ejoin)

                def dmo(sy, b=b, tt=tt):
                    ch.wait(sy)
                    outs.push(sy.dma_start(x_out[b, tt * 128:(tt + 1) * 128, :], o_tm[:, :]), 16)
                bk.sync(dmo)

            def fin(sy):
                outs.wait(sy)
            bk.sync(fin)

    return nc


def _host_lstm(inputs):
    ident = np.eye(128, dtype=np.float32)
    aw = np.stack([np.asarray(inputs["L1_ffn_" + n]).T for n in
                   ["wf", "uf", "wi", "ui", "wc", "uc"]])
    ow2 = np.stack([np.asarray(inputs["L1_ffn_wo"]).T, np.asarray(inputs["L1_ffn_uo"]).T])
    gb = np.stack([np.asarray(inputs["L1_ffn_" + n]).reshape(8, 128).T for n in
                   ["bf", "bi", "bc"]], axis=1)
    base = {
        "aw": np.ascontiguousarray(aw),
        "ow2": np.ascontiguousarray(ow2),
        "gb": np.ascontiguousarray(gb),
        "bo": np.ascontiguousarray(np.asarray(inputs["L1_ffn_bo"]).reshape(8, 128).T),
        "in_c": np.ascontiguousarray(np.asarray(inputs["L1_ffn_in_c"]).reshape(8, 128).T),
        "ident": ident,
    }
    return base


def run_lstm(x, inputs):
    nc = _get("lstm", build_lstm)
    base = _host_lstm(inputs)
    in_maps = []
    for cc in range(NCORE):
        m = dict(base)
        m["x_in"] = np.ascontiguousarray(x[:, 256 * cc:256 * (cc + 1), :])
        in_maps.append(m)
    res = run_bass_kernel_spmd(nc, in_maps, core_ids=list(range(NCORE)))
    return np.concatenate([r["x_out"] for r in res.results], axis=1)
